# version 11
# baseline (speedup 1.0000x reference)
import numpy as np
import ml_dtypes
import contextlib
import concourse.bass as bass
import concourse.mybir as mybir
from concourse.bass_utils import run_bass_kernel_spmd
from concourse.alu_op_type import AluOpType as ALU

F32 = mybir.dt.float32
BF16 = mybir.dt.bfloat16
F16 = mybir.dt.float16
AF = mybir.ActivationFunctionType
AX = mybir.AxisListType

NCORES = 8
D = 2048
NH = 16
HD = 128
BLK = 256
DF = 5632
S = 4096
T = 512
NKC = D // 128
NJ = 2 * DF // 128
NJ2 = DF // 128
LN_EPS = 1e-5
AN_EPS = 1e-6
RMS_EPS = 1e-5
R16 = 1536.0

ENGS = ("tensor", "vector", "scalar", "gpsimd", "sync")


class Prog:

    def __init__(self):
        self.ops = []
        self.last_w = {}
        self.readers = {}
        self.bar_deps = []
        self.nbar = 0

    def op(self, engine, fn, reads=(), writes=(), dma=None, tag=""):
        idx = len(self.ops)
        deps = set()
        for b in reads:
            if b in self.last_w:
                deps.add(self.last_w[b])
        for b in writes:
            if b in self.last_w:
                deps.add(self.last_w[b])
            for r in self.readers.get(b, ()):
                deps.add(r)
        deps.update(self.bar_deps)
        deps.discard(idx)
        if dma is not None:
            if not hasattr(self, "dma_eng"):
                self.dma_eng = {}
            assert self.dma_eng.setdefault(dma, engine) == engine, \
                f"dma sem {dma} used from {engine} and {self.dma_eng[dma]}"
        self.ops.append(dict(idx=idx, engine=engine, fn=fn, deps=deps,
                             dma=dma, dma_idx=None, tag=tag, inc=None,
                             is_bar=False))
        for b in reads:
            self.readers.setdefault(b, []).append(idx)
        for b in writes:
            self.last_w[b] = idx
            self.readers[b] = []
        return idx

    def barrier(self):
        self.nbar += 1
        prior = list(range(len(self.ops)))
        bar_idxs = []
        for e in ENGS:
            idx = len(self.ops)
            self.ops.append(dict(idx=idx, engine=e, fn=None, deps=set(prior),
                                 dma=None, dma_idx=None, tag=f"bar{self.nbar}",
                                 inc=None, is_bar=True))
            bar_idxs.append(idx)
        self.bar_deps = bar_idxs

    RING = {"c": 1, "wl": 3, "ws": 2, "wst": 6, "xl": 3, "aw": 2, "tr": 4,
            "rl": 4, "ol": 2, "tb": 1, "tbw": 1, "ao": 2, "wsa": 2, "wla": 3,
            "trk": 4, "trv": 4, "trs": 4, "w2l": 4}

    def resolve(self):
        eng_seq = {}
        seq_ctr = {e: 0 for e in ENGS}
        self.dma_tot = {}
        for o in self.ops:
            eng_seq[o["idx"]] = seq_ctr[o["engine"]]
            seq_ctr[o["engine"]] += 1
            if o["dma"] is not None:
                k = self.dma_tot.get(o["dma"], 0)
                self.dma_tot[o["dma"]] = k + 1
                r = self.RING[o["dma"]]
                o["dma_sub"] = (o["dma"], k % r)
                o["dma_val"] = 16 * (k // r + 1)
        needs_inc = set()
        for o in self.ops:
            for d in o["deps"]:
                dop = self.ops[d]
                if dop["dma"] is None and not (
                        dop["engine"] == "tensor" and o["engine"] == "tensor"):
                    needs_inc.add(d)
        for o in self.ops:
            if o["is_bar"]:
                needs_inc.add(o["idx"])
        self.inc_val = {}
        ctr = {e: 0 for e in ENGS}
        for o in self.ops:
            if o["idx"] in needs_inc:
                ctr[o["engine"]] += 1
                o["inc"] = ctr[o["engine"]]
                self.inc_val[o["idx"]] = ctr[o["engine"]]
        self.eng_seq = eng_seq
        self.by_eng = {e: [o for o in self.ops if o["engine"] == e] for e in ENGS}

    def emit(self, ename, eng_obj, prog_sems, dma_sems):
        waited = {}
        drained_upto = 0
        for o in self.by_eng[ename]:
            need_drain = False
            waits = {}
            for d in sorted(o["deps"]):
                dop = self.ops[d]
                if dop["dma"] is not None:
                    key = ("dma",) + dop["dma_sub"]
                    val = dop["dma_val"]
                    if waits.get(key, 0) < val:
                        waits[key] = val
                elif dop["engine"] == ename:
                    if ename != "tensor":
                        v = self.inc_val[d]
                        key = ("prog", ename)
                        if waits.get(key, 0) < v:
                            waits[key] = v
                else:
                    v = self.inc_val[d]
                    key = ("prog", dop["engine"])
                    if waits.get(key, 0) < v:
                        waits[key] = v
            if o["dma"] is not None and o["dma_val"] > 16:
                key = ("dma",) + o["dma_sub"]
                if waits.get(key, 0) < o["dma_val"] - 16:
                    waits[key] = max(waits.get(key, 0), o["dma_val"] - 16)
            for key, val in sorted(waits.items(), key=lambda kv: str(kv[0])):
                if waited.get(key, 0) >= val:
                    continue
                sem = dma_sems[key[1:]] if key[0] == "dma" else prog_sems[key[1]]
                eng_obj.wait_ge(sem, val)
                waited[key] = val
            if o["is_bar"]:
                inst = eng_obj.drain()
                drained_upto = self.eng_seq[o["idx"]]
            else:
                inst = o["fn"](eng_obj)
            if o["inc"] is not None:
                assert inst is not None, f"op {o['tag']} returned None but needs inc"
                inst.then_inc(prog_sems[ename], 1)
            if o["dma"] is not None:
                inst.then_inc(dma_sems[o["dma_sub"]], 16)


def build_nc(seq=S):
    nt = seq // T
    nxc = seq // 128
    nc = bass.Bass("TRN2", target_bir_lowering=False, debug=False)

    x_h = nc.dram_tensor("x", [seq, D], F32, kind="ExternalInput").ap()
    wq_h = nc.dram_tensor("wq", [NKC, 128, D], F32, kind="ExternalInput").ap()
    wk_h = nc.dram_tensor("wk", [NKC, 128, D], F32, kind="ExternalInput").ap()
    wv_h = nc.dram_tensor("wv", [D, D], F32, kind="ExternalInput").ap()
    wg_h = nc.dram_tensor("wg", [NKC, 128, D], F32, kind="ExternalInput").ap()
    wo_h = nc.dram_tensor("wo", [D, D], F32, kind="ExternalInput").ap()
    w13_h = nc.dram_tensor("w13", [NJ, 128, D], F32, kind="ExternalInput").ap()
    w2_h = nc.dram_tensor("w2t", [DF, D], F32, kind="ExternalInput").ap()
    slope_h = nc.dram_tensor("slope", [1, NH], F32, kind="ExternalInput").ap()
    nwa_h = nc.dram_tensor("nwa", [128, NH], F32, kind="ExternalInput").ap()
    nwf_h = nc.dram_tensor("nwf", [128, NJ2], F32, kind="ExternalInput").ap()
    eye_h = nc.dram_tensor("eye16", [128, 128], BF16, kind="ExternalInput").ap()
    eyef_h = nc.dram_tensor("eyef", [128, 128], F32, kind="ExternalInput").ap()
    prow_h = nc.dram_tensor("prow", [1, 128], F32, kind="ExternalInput").ap()
    mrow_h = nc.dram_tensor("mrow", [1, BLK], F32, kind="ExternalInput").ap()
    onesr_h = nc.dram_tensor("onesrow", [1, 128], F32, kind="ExternalInput").ap()
    mask_h = nc.dram_tensor("mask", [128, 2 * BLK], F32, kind="ExternalInput").ap()
    out_h = nc.dram_tensor("out", [seq, D], F32, kind="ExternalOutput").ap()

    wqs = nc.dram_tensor("wqs", [NKC, 128, D], BF16, kind="Internal").ap()
    wks = nc.dram_tensor("wks", [NKC, 128, D], BF16, kind="Internal").ap()
    wvs = nc.dram_tensor("wvs", [4, NKC, 128, 512], BF16, kind="Internal").ap()
    wgs = nc.dram_tensor("wgs", [NKC, 128, D], BF16, kind="Internal").ap()
    wos = nc.dram_tensor("wos", [4, NKC, 128, 512], BF16, kind="Internal").ap()
    w13s = nc.dram_tensor("w13s", [NJ, 128, D], BF16, kind="Internal").ap()
    w2s = nc.dram_tensor("w2s", [4, NJ2, 128, 512], BF16, kind="Internal").ap()
    xns = nc.dram_tensor("xns", [D, seq], BF16, kind="Internal").ap()
    xqs = nc.dram_tensor("xqs", [D, seq], BF16, kind="Internal").ap()
    aos = nc.dram_tensor("aos", [seq, D], BF16, kind="Internal").ap()
    tbl = nc.dram_tensor("tbl", [2, NH * BLK], F32, kind="Internal").ap()

    P = Prog()

    def dma(qeng, sem, fn, reads, writes, tag=""):
        return P.op(qeng, fn, reads=reads, writes=writes, dma=sem, tag=tag)

    es = contextlib.ExitStack()
    with es:
        sbp = lambda n, sh, dt: es.enter_context(nc.sbuf_tensor(n, sh, dt))
        eye = sbp("eye", [128, 128], BF16)
        eyef = sbp("eyef_s", [128, 128], F32)
        prow = sbp("prow_s", [1, 128], F32)
        mrow = sbp("mrow_s", [1, BLK], F32)
        onesr = sbp("onesr_s", [1, 128], F32)
        ones256 = sbp("ones256", [1, BLK], F32)
        onescol_b = sbp("onescol_b", [128, 1], BF16)
        mask = sbp("mask_s", [128, 2 * BLK], F32)
        nwa = sbp("nwa_s", [128, NH], F32)
        nwf = sbp("nwf_s", [128, NJ2], F32)
        srow = sbp("srow", [1, NH], F32)
        t16 = sbp("t16", [16, BLK], F32)
        bd_vec = sbp("bd_vec", [128, NH], F32)
        cb5 = sbp("cb5", [128, 1], F32)
        cb6 = sbp("cb6", [128, 1], F32)
        cbr = sbp("cbr", [128, 1], F32)
        kdec = sbp("kdec", [128, 2 * NH], F32)
        NWB = 6
        wbuf = [sbp(f"wbuf{i}", [128, D], BF16) for i in range(NWB)]
        wbufS = [sbp(f"wbufS{i}", [128, D], BF16) for i in range(2)]
        wf32 = [sbp(f"wf32{i}", [128, D], F32) for i in range(2)]

        ps = [es.enter_context(nc.psum_tensor(f"ps{i}", [128, 512], F32))
              for i in range(6)]

        prog_sems = {e: es.enter_context(nc.semaphore(f"pg_{e}")) for e in ENGS}
        dma_sems = {}
        for k, r in Prog.RING.items():
            for i in range(r):
                dma_sems[(k, i)] = es.enter_context(nc.semaphore(f"dm_{k}{i}"))

        esAB = contextlib.ExitStack()
        sbab = lambda n, sh, dt: esAB.enter_context(nc.sbuf_tensor(n, sh, dt))
        with esAB:
            qdbc = sbab("qdbc", [128, NH * BLK], BF16)
            ddT = sbab("ddT", [128, NH * 2 * BLK], BF16)
            kv = sbab("kv", [128, NH * HD], F32)

            es1 = contextlib.ExitStack()
            sb1 = lambda n, sh, dt: es1.enter_context(nc.sbuf_tensor(n, sh, dt))
            with es1:
                rA = sb1("rowA", [1, NH * BLK], F32)
                rB = sb1("rowB", [1, NH * BLK], F32)
                rC = sb1("rowC", [1, NH * BLK], F32)
                rD = sb1("rowD", [1, NH * BLK], F32)
                ddw = sb1("ddw", [128, 2 * BLK], F32)
                psB = [es1.enter_context(nc.psum_tensor(f"psB{i}", [128, 1024], BF16))
                       for i in range(2)]
                tstg = [sb1(f"tstg{i}", [128, 1024], BF16) for i in range(2)]
                xb = [sb1(f"xb{i}", [128, D], F32) for i in range(3)]
                u16 = sb1("u16", [128, D], F16)
                xnb = [sb1(f"xnb{i}", [128, D], BF16) for i in range(2)]
                xqb = [sb1(f"xqb{i}", [128, D], BF16) for i in range(2)]
                NST = nxc
                stt_ = sb1("stats", [128, 18 * nxc], F32)
                SK = ["sum", "ss", "mx", "mn", "m", "var", "sd", "rstd", "t0",
                      "t1", "mab", "c", "rc", "a1q", "a0q", "isv", "nis", "nmr"]
                st = {k: (lambda i: lambda g: stt_[:, i * NST:(i + 1) * NST] if g is None
                          else stt_[:, i * NST + g:i * NST + g + 1])(i)
                      for i, k in enumerate(SK)}

                for name, dst, src in [
                        ("eye", eye, eye_h), ("eyef", eyef, eyef_h),
                        ("prow", prow, prow_h), ("mrow", mrow, mrow_h),
                        ("onesr", onesr, onesr_h), ("mask", mask, mask_h),
                        ("nwa", nwa, nwa_h), ("nwf", nwf, nwf_h),
                        ("srow", srow, slope_h)]:
                    dma("sync", "c",
                        (lambda d_, s_: lambda e: e.dma_start(out=d_[:, :], in_=s_))(dst, src),
                        reads=[], writes=[f"c:{name}"], tag=f"ld_{name}")
                P.op("vector", lambda e: e.memset(ones256[:, :], 1.0), writes=["c:ones256"])
                P.op("vector", lambda e: e.memset(onescol_b[:, :], 1.0), writes=["c:onescol"])
                P.op("vector", lambda e: e.memset(cb5[:, :], 1e-5), writes=["c:cb5"])
                P.op("vector", lambda e: e.memset(cb6[:, :], 1e-6), writes=["c:cb6"])
                P.op("vector", lambda e: e.memset(cbr[:, :], R16), writes=["c:cbr"])
                P.op("vector", lambda e: e.memset(kv[:, :], 0.0),
                     writes=["kv"] + [f"kv:{h}" for h in range(NH)])

                P.op("tensor", lambda e: e.matmul(ps[5][0:16, 0:BLK], srow[0:1, :], ones256[0:1, :], start=True, stop=True),
                     reads=["c:srow", "c:ones256"], writes=["ps5"], tag="mmS")
                P.op("vector", lambda e: e.tensor_copy(t16[:, :], ps[5][0:16, 0:BLK]),
                     reads=["ps5"], writes=["t16"], tag="evS")
                dma("gpsimd", "tbw", lambda e: e.dma_start(
                    out=tbl[0:1, :].rearrange("o (p c) -> p (o c)", p=16), in_=t16[:, :]),
                    reads=["t16"], writes=["tbl0"], tag="wS")
                dma("sync", "tb", lambda e: e.dma_start(out=rA[:, :], in_=tbl[0:1, :]),
                    reads=["tbl0"], writes=["rA"], tag="rS")
                P.op("tensor", lambda e: e.matmul(ps[5][0:16, 0:BLK], srow[0:1, :], mrow[0:1, :], start=True, stop=True),
                     reads=["c:srow", "c:mrow", "t16"], writes=["ps5"], tag="mmR")
                P.op("vector", lambda e: e.tensor_copy(t16[:, :], ps[5][0:16, 0:BLK]),
                     reads=["ps5"], writes=["t16"], tag="evR")
                dma("gpsimd", "tbw", lambda e: e.dma_start(
                    out=tbl[1:2, :].rearrange("o (p c) -> p (o c)", p=16), in_=t16[:, :]),
                    reads=["t16"], writes=["tbl1"], tag="wR")
                dma("sync", "tb", lambda e: e.dma_start(out=rB[:, :], in_=tbl[1:2, :]),
                    reads=["tbl1"], writes=["rB"], tag="rR")
                P.op("tensor", lambda e: e.matmul(ps[5][:, 0:NH], onesr[0:1, :], srow[0:1, :], start=True, stop=True),
                     reads=["c:onesr", "c:srow", "t16"], writes=["ps5"], tag="mmB")
                P.op("scalar", lambda e: e.activation(bd_vec[:, :], ps[5][:, 0:NH], AF.Exp, scale=-float(BLK)),
                     reads=["ps5"], writes=["bd_vec"], tag="bd")
                P.op("vector", lambda e: e.tensor_single_scalar(rC[:, :], rA[:, :], -255.0, ALU.mult),
                     reads=["rA"], writes=["rC"], tag="rk0")
                P.op("vector", lambda e: e.tensor_single_scalar(rD[:, :], rA[:, :], -127.0, ALU.mult),
                     reads=["rA"], writes=["rD"], tag="rk1")
                for h in range(NH):
                    for c in range(2):
                        col = 2 * h + c
                        rk = rC if c == 0 else rD
                        P.op("tensor", (lambda h_, col_: lambda e: e.matmul(
                            ps[4][:, col_:col_ + 1], prow[0:1, :],
                            rA[0:1, BLK * h_:BLK * h_ + 1], start=True, stop=False))(h, col),
                            reads=["c:prow", "rA"], writes=["ps4"], tag=f"kda{col}")
                        P.op("tensor", (lambda h_, col_, rk_: lambda e: e.matmul(
                            ps[4][:, col_:col_ + 1], onesr[0:1, :],
                            rk_[0:1, BLK * h_:BLK * h_ + 1], start=False, stop=True))(h, col, rk),
                            reads=["c:onesr", "rC", "rD"], writes=["ps4"], tag=f"kdb{col}")
                P.op("scalar", lambda e: e.activation(kdec[:, :], ps[4][:, 0:2 * NH], AF.Exp),
                     reads=["ps4"], writes=["kdec"], tag="kde")
                P.op("vector", lambda e: e.tensor_single_scalar(rC[:, :], rB[:, :], -1.0, ALU.mult),
                     reads=["rB"], writes=["rC"], tag="rc0")
                P.op("vector", lambda e: e.scalar_tensor_tensor(rD[:, :], rA[:, :], 128.0, rB[:, :], ALU.mult, ALU.subtract),
                     reads=["rA", "rB"], writes=["rD"], tag="rc1")
                for h in range(NH):
                    for c in range(2):
                        bank = 4 + (c % 2)
                        rowc = rC if c == 0 else rD
                        P.op("tensor", (lambda h_, b_: lambda e: e.matmul(
                            ps[b_][:, 0:BLK], prow[0:1, :],
                            rA[0:1, BLK * h_:BLK * (h_ + 1)], start=True, stop=False))(h, bank),
                            reads=["c:prow", "rA", "kdec"], writes=[f"ps{bank}"], tag=f"dda{h}{c}")
                        P.op("tensor", (lambda h_, b_, rc_: lambda e: e.matmul(
                            ps[b_][:, 0:BLK], onesr[0:1, :],
                            rc_[0:1, BLK * h_:BLK * (h_ + 1)], start=False, stop=True))(h, bank, rowc),
                            reads=["c:onesr", "rC", "rD"], writes=[f"ps{bank}"], tag=f"ddb{h}{c}")
                        P.op("vector", (lambda b_: lambda e: e.tensor_scalar(
                            ddw[:, 0:BLK], ps[b_][:, 0:BLK], 1.0, 0.0, ALU.mult, ALU.min))(bank),
                            reads=[f"ps{bank}"], writes=["ddw0"], tag=f"ddc{h}{c}")
                        P.op("scalar", lambda e: e.activation(ddw[:, BLK:2 * BLK], ddw[:, 0:BLK], AF.Exp),
                             reads=["ddw0"], writes=["ddw1"], tag=f"ddd{h}{c}")
                        P.op("vector", (lambda h_, c_: lambda e: e.tensor_mul(
                            ddT[:, (2 * h_ + c_) * BLK:(2 * h_ + c_ + 1) * BLK],
                            ddw[:, BLK:2 * BLK], mask[:, c_ * BLK:(c_ + 1) * BLK]))(h, c),
                            reads=["ddw1", "c:mask"], writes=[f"ddT:{h}:{c}"], tag=f"dde{h}{c}")
                P.op("vector", lambda e: e.tensor_add(rB[:, :], rB[:, :], rA[:, :]),
                     reads=["rA", "rB"], writes=["rB"], tag="r2")
                for cc in range(NH * BLK // 512):
                    bank = cc % 2
                    P.op("tensor", (lambda cc_, b_: lambda e: e.matmul(
                        ps[b_][:, :], onesr[0:1, :], rB[0:1, 512 * cc_:512 * (cc_ + 1)],
                        start=True, stop=True))(cc, bank),
                        reads=["rB", "c:onesr"], writes=[f"ps{bank}"], tag=f"qba{cc}")
                    P.op("scalar", (lambda cc_, b_: lambda e: e.activation(
                        qdbc[:, 512 * cc_:512 * (cc_ + 1)], ps[b_][:, :], AF.Exp, scale=-1.0))(cc, bank),
                        reads=[f"ps{bank}"], writes=[f"qdbc:{cc}"], tag=f"qbb{cc}")

                for ci in range(nxc):
                    slot = ci % 3
                    xk = f"xb:{slot}"
                    dma("scalar", "xl", (lambda ci_, s_: lambda e: e.dma_start(
                        out=xb[s_][:, :], in_=x_h[128 * ci_:128 * (ci_ + 1), :]))(ci, slot),
                        reads=[], writes=[xk], tag=f"ldx{ci}")
                    P.op("scalar", (lambda s_, g_: lambda e: e.activation(
                        u16[:, :], xb[s_][:, :], AF.Square, accum_out=st["ss"](g_)))(slot, ci),
                        reads=[xk], writes=["u16", f"st1:{ci}"], tag=f"sq{ci}")
                    P.op("vector", (lambda s_, g_: lambda e: e.reduce_sum(
                        st["sum"](g_), xb[s_][:, :], axis=AX.X))(slot, ci),
                        reads=[xk], writes=[f"st1:{ci}"], tag=f"rs{ci}")
                    P.op("vector", (lambda s_, g_: lambda e: e.reduce_max(
                        st["mx"](g_), xb[s_][:, :], axis=AX.X))(slot, ci),
                        reads=[xk], writes=[f"st1:{ci}"], tag=f"rx{ci}")
                    P.op("vector", (lambda s_, g_: lambda e: e.tensor_reduce(
                        st["mn"](g_), xb[s_][:, :], AX.X, ALU.min))(slot, ci),
                        reads=[xk], writes=[f"st1:{ci}"], tag=f"rn{ci}")

                def chain_all(e):
                    s_ = lambda k: st[k](None)
                    e.tensor_single_scalar(s_("m"), s_("sum"), 1.0 / D, ALU.mult)
                    e.drain()
                    e.tensor_mul(s_("t0"), s_("m"), s_("m"))
                    e.drain()
                    e.scalar_tensor_tensor(s_("var"), s_("ss"), 1.0 / D, s_("t0"), ALU.mult, ALU.subtract)
                    return e.drain()
                P.op("vector", chain_all,
                     reads=[f"st1:{ci}" for ci in range(nxc)],
                     writes=["st2a"], tag="c1all")
                P.op("scalar", lambda e: e.activation(
                    st["sd"](None), st["var"](None), AF.Sqrt, bias=cb5[:, 0:1]),
                    reads=["st2a", "c:cb5"], writes=["st2b"], tag="c2all")

                def chain3_all(e):
                    s_ = lambda k: st[k](None)
                    e.reciprocal(s_("rstd"), s_("sd"))
                    e.drain()
                    e.tensor_sub(s_("t0"), s_("mx"), s_("m"))
                    e.tensor_sub(s_("t1"), s_("m"), s_("mn"))
                    e.drain()
                    e.tensor_max(s_("mab"), s_("t0"), s_("t1"))
                    e.drain()
                    e.tensor_mul(s_("t0"), s_("mab"), s_("rstd"))
                    e.drain()
                    e.tensor_single_scalar(s_("c"), s_("t0"), 1e-5, ALU.max)
                    e.drain()
                    e.reciprocal(s_("rc"), s_("c"))
                    e.drain()
                    e.scalar_tensor_tensor(s_("a1q"), s_("rstd"), 127.0, s_("rc"), ALU.mult, ALU.mult)
                    e.tensor_single_scalar(s_("isv"), s_("c"), 1.0 / 127.0, ALU.mult)
                    e.tensor_mul(s_("nmr"), s_("m"), s_("rstd"))
                    e.drain()
                    e.tensor_mul(s_("t1"), s_("m"), s_("a1q"))
                    e.tensor_single_scalar(s_("nis"), s_("isv"), -R16, ALU.mult)
                    e.tensor_single_scalar(s_("nmr"), s_("nmr"), -1.0, ALU.mult)
                    e.drain()
                    e.tensor_scalar(s_("a0q"), s_("t1"), -1.0, R16, ALU.mult, ALU.add)
                    return e.drain()
                P.op("vector", chain3_all, reads=["st2a", "st2b"],
                     writes=["st2"], tag="c3all")

                def p2_tr(ci, srcbuf, srckey, dst_hbm, okey, tagp):
                    for half in range(2):
                        pk = f"psB:{half}"
                        for k8 in range(8):
                            kk = 8 * half + k8
                            P.op("tensor", (lambda s_, k_, k8_, h_: lambda e: e.transpose(
                                psB[h_][:, 128 * k8_:128 * (k8_ + 1)],
                                srcbuf[s_][:, 128 * k_:128 * (k_ + 1)], eye[:, :]))(srckey, kk, k8, half),
                                reads=[f"{tagp}bb:{srckey}", "c:eye"], writes=[pk], tag=f"{tagp}t{ci}_{kk}")
                        P.op("vector", (lambda h_: lambda e: e.tensor_copy(
                            tstg[h_][:, :], psB[h_][:, :]))(half),
                            reads=[pk], writes=[f"tstg:{half}"], tag=f"{tagp}e{ci}_{half}")
                        dma("gpsimd", "aw", (lambda ci_, h_: lambda e: e.dma_start(
                            out=dst_hbm[1024 * h_:1024 * (h_ + 1), 128 * ci_:128 * (ci_ + 1)].rearrange(
                                "(k p) c -> p k c", k=8),
                            in_=tstg[h_][:, :].rearrange("p (k c) -> p k c", k=8)))(ci, half),
                            reads=[f"tstg:{half}"], writes=[f"{okey}:{ci // 4}"], tag=f"{tagp}w{ci}_{half}")

                for ci in range(nxc):
                    slot = ci % 3
                    xk = f"xb:{slot}"
                    dma("scalar", "xl", (lambda ci_, s_: lambda e: e.dma_start(
                        out=xb[s_][:, :], in_=x_h[128 * ci_:128 * (ci_ + 1), :]))(ci, slot),
                        reads=[], writes=[xk], tag=f"ldx2_{ci}")
                    P.op("scalar", (lambda s_, g_: lambda e: e.activation(
                        xnb[s_ % 2][:, :], xb[s_][:, :], AF.Identity,
                        bias=st["nmr"](g_), scale=st["rstd"](g_)))(slot, ci),
                        reads=[xk, "st2"], writes=[f"xnbb:{slot % 2}"], tag=f"xn{ci}")
                    p2_tr(ci, xnb, slot % 2, xns, "xns", "xn")
                    P.op("vector", (lambda s_, g_: lambda e: e.tensor_scalar(
                        u16[:, :], xb[s_][:, :], st["a1q"](g_), st["a0q"](g_), ALU.mult, ALU.add))(slot, ci),
                        reads=[xk, "st2"], writes=["u16"], tag=f"u{ci}")
                    P.op("vector", (lambda s_, g_: lambda e: e.tensor_scalar(
                        xqb[s_ % 2][:, :], u16[:, :], st["isv"](g_), st["nis"](g_), ALU.mult, ALU.add))(slot, ci),
                        reads=["u16", "st2"], writes=[f"xqbb:{slot % 2}"], tag=f"xq{ci}")
                    p2_tr(ci, xqb, slot % 2, xqs, "xqs", "xq")

                def stage3(src_ap, dst_ap, nchunk, key, q=("gpsimd", "wl", "gpsimd", "ws")):
                    lq, ls, wq_, ws_ = q
                    for c in range(nchunk):
                        f32b = f"wf32:{c % 2}"
                        bfb = f"wbufS:{c % 2}"
                        dma(lq, ls, (lambda c_: lambda e: e.dma_start(
                            out=wf32[c_ % 2][:, :], in_=src_ap[c_, :, :]))(c),
                            reads=[], writes=[f32b], tag=f"L{key}{c}")
                        P.op("gpsimd", (lambda c_: lambda e: e.tensor_copy(
                            wbufS[c_ % 2][:, :], wf32[c_ % 2][:, :]))(c),
                            reads=[f32b], writes=[bfb], tag=f"C{key}{c}")
                        dma(wq_, ws_, (lambda c_: lambda e: e.dma_start(
                            out=dst_ap[c_, :, :], in_=wbufS[c_ % 2][:, :]))(c),
                            reads=[bfb], writes=[f"{key}:{c}"], tag=f"S{key}{c}")

                def stage2(src_ap, dst_ap, nchunk, key, q=("gpsimd", "wl", "gpsimd", "ws")):
                    lq, ls, wq_, ws_ = q
                    for c in range(nchunk):
                        f32b = f"wf32:{c % 2}"
                        bfb = f"wbufS:{c % 2}"
                        dma(lq, ls, (lambda c_: lambda e: e.dma_start(
                            out=wf32[c_ % 2][:, :], in_=src_ap[128 * c_:128 * (c_ + 1), :]))(c),
                            reads=[], writes=[f32b], tag=f"L{key}{c}")
                        P.op("gpsimd", (lambda c_: lambda e: e.tensor_copy(
                            wbufS[c_ % 2][:, :], wf32[c_ % 2][:, :]))(c),
                            reads=[f32b], writes=[bfb], tag=f"C{key}{c}")
                        dma(wq_, ws_, (lambda c_: lambda e: e.dma_start(
                            out=dst_ap[:, c_, :, :].rearrange("d p c -> p d c"),
                            in_=wbufS[c_ % 2][:, :].rearrange("p (d c) -> p d c", d=4)))(c),
                            reads=[bfb], writes=[f"{key}:{c}"], tag=f"S{key}{c}")

                QA = ("sync", "wla", "scalar", "wsa")
                stage3(wq_h, wqs, NKC, "wqs", QA)
                stage3(wk_h, wks, NKC, "wks", QA)
                stage2(wv_h, wvs, NKC, "wvs", QA)
                stage3(wg_h, wgs, NKC, "wgs", QA)
                stage2(wo_h, wos, NKC, "wos", QA)
                P.stage_ffn = lambda: (stage3(w13_h, w13s, NJ, "w13s"),
                                       stage2(w2_h, w2s, NJ2, "w2s"))

            P.barrier()
            es2 = contextlib.ExitStack()
            sb2 = lambda n, sh, dt: es2.enter_context(nc.sbuf_tensor(n, sh, dt))
            with es2:
                P.stage_ffn()
                xnT = sb2("xnT", [128, NKC * T], BF16)
                qT = sb2("qT", [128, NH * T], BF16)
                kT = sb2("kT", [128, NH * T], BF16)
                v_tok = [sb2(f"v_tok{i}", [128, D], BF16) for i in range(4)]
                psBt = [es2.enter_context(nc.psum_tensor(f"psBt{i}", [128, 1024], BF16))
                        for i in range(2)]
                gateT = sb2("gateT", [128, NH * T], BF16)
                aT = sb2("aT", [128, NH * T], BF16)
                kvb = [sb2(f"kvb{i}", [128, HD], BF16) for i in range(4)]
                qdi = [sb2(f"qdi{i}", [128, BLK], BF16) for i in range(4)]
                qkd = [sb2(f"qkd{i}", [128, 2 * BLK], BF16) for i in range(4)]
                ktok = [sb2(f"ktok{i}", [128, BLK], BF16) for i in range(4)]
                sqs = [sb2(f"sqs{i}", [128, T], BF16) for i in range(4)]
                arow = sb2("arow", [1, T], F32)
                arow2 = sb2("arow2", [1, T], F32)
                rstdA = sb2("rstdA", [128, T], F32)
                aostg = [sb2(f"aostg{i}", [128, 512], BF16) for i in range(2)]

                for t in range(nt):
                    for kk in range(NKC):
                        qq = ("scalar", "tr") if kk % 2 == 0 else ("sync", "trs")
                        dma(qq[0], qq[1], (lambda t_, k_: lambda e: e.dma_start(
                            out=xnT[:, T * k_:T * (k_ + 1)],
                            in_=xns[128 * k_:128 * (k_ + 1), T * t_:T * (t_ + 1)]))(t, kk),
                            reads=[f"xns:{t}"], writes=[f"xnT:{kk}"], tag=f"txn{t}_{kk}")
                    for gi, (wsrc, wkey, dstT, dkey, act) in enumerate([
                            (wqs, "wqs", qT, "qT", AF.Silu),
                            (wks, "wks", kT, "kT", AF.Silu),
                            (wgs, "wgs", gateT, "gateT", AF.Sigmoid)]):
                        for c in range(NKC):
                            slot = (gi * NKC + c) % NWB
                            bank = (gi * NKC + c) % 2
                            dma("sync", "wst", (lambda w_, c_, s_: lambda e: e.dma_start(
                                out=wbuf[s_][:, :], in_=w_[c_, :, :]))(wsrc, c, slot),
                                reads=[f"{wkey}:{c}"], writes=[f"wbuf:{slot}"], tag=f"sw{t}{gi}{c}")
                            for kk in range(NKC):
                                P.op("tensor", (lambda s_, b_, k_: lambda e: e.matmul(
                                    ps[b_][:, :], wbuf[s_][:, 128 * k_:128 * (k_ + 1)],
                                    xnT[:, T * k_:T * (k_ + 1)],
                                    start=(k_ == 0), stop=(k_ == NKC - 1)))(slot, bank, kk),
                                    reads=[f"wbuf:{slot}", f"xnT:{kk}"],
                                    writes=[f"ps{bank}"], tag=f"g{t}_{gi}_{c}_{kk}")
                            P.op("scalar", (lambda b_, c_, d_, a_: lambda e: e.activation(
                                d_[:, T * c_:T * (c_ + 1)], ps[b_][:, :], a_))(bank, c, dstT, act),
                                reads=[f"ps{bank}"], writes=[f"{dkey}:{c}"], tag=f"e{t}_{gi}_{c}")
                    for ec in range(4):
                        for kg in range(NKC // 4):
                            slot = kg % NWB
                            wr = [f"wvs:{4 * kg + i}" for i in range(4)]
                            dma("sync", "wst", (lambda g_, s_, e_: lambda ee: ee.dma_start(
                                out=wbuf[s_][:, :].rearrange("p (k c) -> p k c", k=4),
                                in_=wvs[e_, 4 * g_:4 * (g_ + 1), :, :].rearrange("k p c -> p k c")))(kg, slot, ec),
                                reads=wr, writes=[f"wbuf:{slot}"], tag=f"swv{t}_{ec}_{kg}")
                            for ki in range(4):
                                kk = 4 * kg + ki
                                for tc in range(4):
                                    P.op("tensor", (lambda k_, s_, ki_, tc_: lambda e: e.matmul(
                                        ps[tc_][:, :], xnT[:, T * k_ + 128 * tc_:T * k_ + 128 * (tc_ + 1)],
                                        wbuf[s_][:, 512 * ki_:512 * (ki_ + 1)],
                                        start=(k_ == 0), stop=(k_ == NKC - 1)))(kk, slot, ki, tc),
                                        reads=[f"wbuf:{slot}", f"xnT:{kk}"],
                                        writes=[f"ps{tc}"], tag=f"gv{t}_{ec}_{kk}_{tc}")
                        for tc in range(4):
                            P.op("scalar", (lambda e_, tc_: lambda e: e.activation(
                                v_tok[tc_][:, 512 * e_:512 * (e_ + 1)], ps[tc_][:, :], AF.Silu))(ec, tc),
                                reads=[f"ps{tc}"], writes=[f"vt:{tc}:{ec}"], tag=f"ev{t}_{ec}_{tc}")

                    NU = (T // BLK) * NH

                    def u_hb(u):
                        b, h = divmod(u, NH)
                        return h, b, T * h + b * BLK

                    def attn_pre(u):
                        h, b, hb = u_hb(u)
                        r = u % 4
                        pt = u % 2
                        for cc in range(2):
                            P.op("tensor", (lambda hb_, cc_, p_: lambda e: e.transpose(
                                psBt[p_][:, 128 * cc_:128 * (cc_ + 1)],
                                kT[:, hb_ + 128 * cc_:hb_ + 128 * (cc_ + 1)], eye[:, :]))(hb, cc, pt),
                                reads=[f"kT:{h}", "c:eye"], writes=[f"psBt{pt}"], tag=f"tk{t}_{u}_{cc}")
                        for cc in range(2):
                            P.op("scalar", (lambda h_, cc_, r_, p_: lambda e: e.activation(
                                ktok[r_][:, 128 * cc_:128 * (cc_ + 1)],
                                psBt[p_][:, 128 * cc_:128 * (cc_ + 1)], AF.Copy,
                                scale=kdec[:, 2 * h_ + cc_:2 * h_ + cc_ + 1]))(h, cc, r, pt),
                                reads=[f"psBt{pt}", "kdec"], writes=[f"ktok:{r}"], tag=f"ke{t}_{u}_{cc}")
                        P.op("vector", (lambda h_, hb_, r_: lambda e: e.tensor_mul(
                            qdi[r_][:, :], qT[:, hb_:hb_ + BLK],
                            qdbc[:, BLK * h_:BLK * (h_ + 1)]))(h, hb, r),
                            reads=[f"qT:{h}", f"qdbc:{h // 2}"],
                            writes=[f"qdi:{r}"], tag=f"qd{t}_{u}")
                        P.op("scalar", (lambda h_, r_: lambda e: e.activation(
                            kvb[r_][:, :], kv[:, HD * h_:HD * (h_ + 1)], AF.Copy))(h, r),
                            reads=[f"kv:{h}"], writes=[f"kvb:{r}"], tag=f"kc{t}_{u}")

                    def attn_qk(u):
                        h, b, hb = u_hb(u)
                        pqk = u % 2
                        for cc in range(2):
                            P.op("tensor", (lambda hb_, cc_, p_: lambda e: e.matmul(
                                ps[p_][:, BLK * cc_:BLK * (cc_ + 1)],
                                kT[:, hb_ + 128 * cc_:hb_ + 128 * (cc_ + 1)],
                                qT[:, hb_:hb_ + BLK], start=True, stop=True))(hb, cc, pqk),
                                reads=[f"kT:{h}", f"qT:{h}"], writes=[f"ps{pqk}"], tag=f"qk{t}_{u}_{cc}")
                        P.op("vector", (lambda h_, u_, p_: lambda e: e.tensor_mul(
                            qkd[u_ % 4][:, :], ps[p_][:, 0:2 * BLK],
                            ddT[:, 2 * BLK * h_:2 * BLK * (h_ + 1)]))(h, u, pqk),
                            reads=[f"ps{pqk}", f"ddT:{h}:0", f"ddT:{h}:1"],
                            writes=[f"qkd:{u % 4}"], tag=f"qm{t}_{u}")

                    def attn_body(u):
                        h, b, hb = u_hb(u)
                        r = u % 4
                        po = 2 + u % 2
                        pkv = 4 + u % 2
                        ve = h // 4
                        vsl = lambda cc: (2 * b + cc, h)
                        P.op("tensor", (lambda r_, p_: lambda e: e.matmul(
                            ps[p_][:, 0:BLK], kvb[r_][:, :], qdi[r_][:, :],
                            start=True, stop=False))(r, po),
                            reads=[f"kvb:{r}", f"qdi:{r}"], writes=[f"ps{po}"], tag=f"oi{t}_{u}")
                        for cc in range(2):
                            P.op("tensor", (lambda b_, cc_, h_, r_, p_: lambda e: e.matmul(
                                ps[p_][:, 0:BLK], v_tok[2 * b_ + cc_][:, 128 * h_:128 * (h_ + 1)],
                                qkd[r_][:, BLK * cc_:BLK * (cc_ + 1)],
                                start=False, stop=(cc_ == 1)))(b, cc, h, r, po),
                                reads=[f"vt:{2 * b + cc}:{ve}", f"qkd:{r}"],
                                writes=[f"ps{po}"], tag=f"oa{t}_{u}_{cc}")
                        P.op("vector", (lambda hb_, p_, h_, b_: lambda e: e.tensor_copy(
                            aT[:, hb_:hb_ + BLK], ps[p_][:, 0:BLK]))(hb, po, h, b),
                            reads=[f"ps{po}"], writes=[f"aT:{h}:{b}"], tag=f"oe{t}_{u}")
                        for cc in range(2):
                            P.op("tensor", (lambda b_, cc_, h_, r_, p_: lambda e: e.matmul(
                                ps[p_][:, 0:HD], ktok[r_][:, 128 * cc_:128 * (cc_ + 1)],
                                v_tok[2 * b_ + cc_][:, 128 * h_:128 * (h_ + 1)],
                                start=(cc_ == 0), stop=(cc_ == 1)))(b, cc, h, r, pkv),
                                reads=[f"ktok:{r}", f"vt:{2 * b + cc}:{ve}"],
                                writes=[f"ps{pkv}"], tag=f"km{t}_{u}_{cc}")
                        P.op("vector", (lambda h_, p_: lambda e: e.scalar_tensor_tensor(
                            kv[:, HD * h_:HD * (h_ + 1)], kv[:, HD * h_:HD * (h_ + 1)],
                            bd_vec[:, h_:h_ + 1], ps[p_][:, 0:HD], ALU.mult, ALU.add))(h, pkv),
                            reads=[f"ps{pkv}", f"kv:{h}", "bd_vec"],
                            writes=[f"kv:{h}"], tag=f"ku{t}_{u}")

                    for u in range(3):
                        attn_pre(u)
                    for u in range(NU):
                        attn_qk(u)
                        if u + 3 < NU:
                            attn_pre(u + 3)
                        attn_body(u)

                    def av_mm(h_):
                        P.op("tensor", (lambda hh: lambda e: e.matmul(
                            ps[0][0:1, 0:T], onescol_b[:, 0:1], sqs[hh % 4][:, :],
                            start=(hh == 0), stop=(hh == NH - 1)))(h_),
                            reads=[f"sqs:{h_ % 4}", "c:onescol"], writes=["ps0"], tag=f"av{t}_{h_}")
                    for h in range(NH):
                        P.op("scalar", (lambda h_: lambda e: e.activation(
                            sqs[h_ % 4][:, :], aT[:, T * h_:T * (h_ + 1)], AF.Square))(h),
                            reads=[f"aT:{h}:0", f"aT:{h}:1"], writes=[f"sqs:{h % 4}"], tag=f"as{t}_{h}")
                        if h >= 2:
                            av_mm(h - 2)
                    av_mm(NH - 2)
                    av_mm(NH - 1)
                    P.op("vector", lambda e: e.tensor_single_scalar(
                        arow[0:1, 0:T], ps[0][0:1, 0:T], 1.0 / D, ALU.mult),
                        reads=["ps0"], writes=["arow"], tag=f"an1{t}")
                    P.op("scalar", lambda e: e.activation(arow2[0:1, 0:T], arow[0:1, 0:T], AF.Sqrt, bias=cb6[0:1, 0:1]),
                         reads=["arow", "c:cb6"], writes=["arow2"], tag=f"an2{t}")
                    P.op("vector", lambda e: e.reciprocal(arow[0:1, 0:T], arow2[0:1, 0:T]),
                         reads=["arow2"], writes=["arow"], tag=f"an3{t}")
                    P.op("tensor", lambda e: e.matmul(ps[1][:, 0:T], onesr[0:1, :], arow[0:1, 0:T], start=True, stop=True),
                         reads=["arow", "c:onesr"], writes=["ps1"], tag=f"an4{t}")
                    P.op("vector", lambda e: e.tensor_copy(rstdA[:, 0:T], ps[1][:, 0:T]),
                         reads=["ps1"], writes=["rstdA"], tag=f"an5{t}")
                    for h in range(NH):
                        P.op("vector", (lambda h_: lambda e: e.scalar_tensor_tensor(
                            sqs[h_ % 4][:, :], aT[:, T * h_:T * (h_ + 1)], nwa[:, h_:h_ + 1],
                            rstdA[:, 0:T], ALU.mult, ALU.mult))(h),
                            reads=[f"aT:{h}:0", f"aT:{h}:1", "rstdA", "c:nwa"],
                            writes=[f"sqs:{h % 4}"], tag=f"nm{t}_{h}")
                        P.op("vector", (lambda h_: lambda e: e.tensor_mul(
                            aT[:, T * h_:T * (h_ + 1)], sqs[h_ % 4][:, :],
                            gateT[:, T * h_:T * (h_ + 1)]))(h),
                            reads=[f"sqs:{h % 4}", f"gateT:{h}"],
                            writes=[f"aT:{h}:0", f"aT:{h}:1", f"ga:{h}"], tag=f"gm{t}_{h}")

                    for dchunk in range(4):
                        for kg in range(NKC // 4):
                            slot = kg % NWB
                            wr = [f"wos:{4 * kg + i}" for i in range(4)]
                            dma("sync", "wst", (lambda g_, s_, d_: lambda e: e.dma_start(
                                out=wbuf[s_][:, :].rearrange("p (k c) -> p k c", k=4),
                                in_=wos[d_, 4 * g_:4 * (g_ + 1), :, :].rearrange("k p c -> p k c")))(kg, slot, dchunk),
                                reads=wr, writes=[f"wbuf:{slot}"], tag=f"so{t}_{dchunk}_{kg}")
                            for ki in range(4):
                                kk = 4 * kg + ki
                                for tc in range(4):
                                    P.op("tensor", (lambda k_, s_, ki_, tc_: lambda e: e.matmul(
                                        ps[2 + tc_][:, :],
                                        aT[:, T * k_ + 128 * tc_:T * k_ + 128 * (tc_ + 1)],
                                        wbuf[s_][:, 512 * ki_:512 * (ki_ + 1)],
                                        start=(k_ == 0), stop=(k_ == NKC - 1)))(kk, slot, ki, tc),
                                        reads=[f"wbuf:{slot}", f"ga:{kk}"],
                                        writes=[f"ps{2 + tc}"], tag=f"op{t}_{dchunk}_{kk}_{tc}")
                        for tc in range(4):
                            so = (dchunk * 4 + tc) % 2
                            P.op("scalar", (lambda tc_, s_: lambda e: e.activation(
                                aostg[s_][:, :], ps[2 + tc_][:, :], AF.Copy))(tc, so),
                                reads=[f"ps{2 + tc}"], writes=[f"aostg:{so}"], tag=f"ae{t}_{dchunk}_{tc}")
                            dma("scalar", "ao", (lambda t_, tc_, d_, s_: lambda e: e.dma_start(
                                out=aos[T * t_ + 128 * tc_:T * t_ + 128 * (tc_ + 1),
                                        512 * d_:512 * (d_ + 1)],
                                in_=aostg[s_][:, :]))(t, tc, dchunk, so),
                                reads=[f"aostg:{so}"], writes=[f"aos:{t}"], tag=f"aw{t}_{dchunk}_{tc}")

            P.barrier()
        es3 = contextlib.ExitStack()
        sb3 = lambda n, sh, dt: es3.enter_context(nc.sbuf_tensor(n, sh, dt))
        with es3:
            xqT = sb3("xqT", [128, NKC * T], BF16)
            gat = sb3("gat", [128, NJ2 * T], BF16)
            iqT = sb3("iqT", [128, NJ2 * T], BF16)
            a_scr = [sb3(f"ascr{i}", [128, T], F32) for i in range(2)]
            sq2 = [sb3(f"sq2_{i}", [128, T], BF16) for i in range(4)]
            tmpb = [sb3(f"tmpb{i}", [128, T], BF16) for i in range(2)]
            Mb = [sb3(f"Mb{i}", [128, T], BF16) for i in range(2)]
            mbt = [sb3(f"mbt{i}", [128, 128], BF16) for i in range(2)]
            Ub = sb3("Ub", [128, T], F32)
            u16b = [sb3(f"u16b{i}", [128, T], F16) for i in range(2)]
            qrow = sb3("qrow", [1, T], F32)
            qt_ = sb3("qstats", [128, 88], F32)
            QK = ["ss", "M", "var", "sd", "rstd", "c", "rc", "s2", "is2", "U"]
            iqs = [sb3(f"iqs{i}", [128, T], F32) for i in range(2)]
            ps6 = es3.enter_context(nc.psum_tensor("ps6", [128, 512], F32))
            ps7 = es3.enter_context(nc.psum_tensor("ps7", [128, 512], F32))
            ps67 = [ps6, ps7]
            wbuf2 = [sb3(f"wbuf2_{i}", [128, 2048], BF16) for i in range(2)]
            xres = [sb3(f"xres{i}", [128, 512], F32) for i in range(2)]
            aores = [sb3(f"aores{i}", [128, 512], BF16) for i in range(2)]
            ostg = [sb3(f"ostg{i}", [128, 512], F32) for i in range(2)]

            def p4_txq(t):
                for kk in range(NKC):
                    qq = ("scalar", "tr") if kk % 2 == 0 else ("sync", "trs")
                    dma(qq[0], qq[1], (lambda t_, k_: lambda e: e.dma_start(
                        out=xqT[:, T * k_:T * (k_ + 1)],
                        in_=xqs[128 * k_:128 * (k_ + 1), T * t_:T * (t_ + 1)]))(t, kk),
                        reads=[f"xqs:{t}"], writes=[f"xqT:{kk}"], tag=f"txq{t}_{kk}")

            def p4_w13(t):
                pq = t % 2
                P.op("vector", (lambda p_: lambda e: e.memset(Mb[p_][:, :], 0.0))(pq),
                     reads=[], writes=[f"Mb{pq}"], tag=f"mz{t}")

                def ss_mm(j_):
                    P.op("tensor", (lambda jx: lambda e: e.matmul(
                        ps[4][0:1, 0:T], onescol_b[:, 0:1], sq2[jx % 4][:, :],
                        start=(jx == 0), stop=(jx == NJ2 - 1)))(j_),
                        reads=[f"sq2:{j_ % 4}", "c:onescol"], writes=["ps4"], tag=f"ss{t}_{j_}")
                for jj in range(NJ2):
                    b1 = 2 * (jj % 2)
                    b3 = b1 + 1
                    for half, (jc, bank) in enumerate([(jj, b1), (jj + NJ2, b3)]):
                        slot = (2 * jj + half) % NWB
                        dma("sync", "wst", (lambda j_, s_: lambda e: e.dma_start(
                            out=wbuf[s_][:, :], in_=w13s[j_, :, :]))(jc, slot),
                            reads=[f"w13s:{jc}"], writes=[f"wbuf:{slot}"], tag=f"s13_{t}_{jc}")
                        for kk in range(NKC):
                            P.op("tensor", (lambda s_, b_, k_: lambda e: e.matmul(
                                ps[b_][:, :], wbuf[s_][:, 128 * k_:128 * (k_ + 1)],
                                xqT[:, T * k_:T * (k_ + 1)], start=(k_ == 0), stop=(k_ == NKC - 1)))(slot, bank, kk),
                                reads=[f"wbuf:{slot}", f"xqT:{kk}"],
                                writes=[f"ps{bank}"], tag=f"w13_{t}_{jc}_{kk}")
                    asl = jj % 2
                    P.op("scalar", (lambda b_, s_: lambda e: e.activation(
                        a_scr[s_][:, :], ps[b_][:, :], AF.Relu))(b1, asl),
                        reads=[f"ps{b1}"], writes=[f"ascr:{asl}"], tag=f"rl{t}_{jj}")
                    P.op("scalar", (lambda s_: lambda e: e.activation(
                        a_scr[s_][:, :], a_scr[s_][:, :], AF.Square))(asl),
                        reads=[f"ascr:{asl}"], writes=[f"ascr:{asl}"], tag=f"sr{t}_{jj}")
                    P.op("vector", (lambda j_, b_, s_: lambda e: e.tensor_mul(
                        gat[:, T * j_:T * (j_ + 1)], a_scr[s_][:, :], ps[b_][:, :]))(jj, b3, asl),
                        reads=[f"ascr:{asl}", f"ps{b3}"], writes=[f"gat:{jj}"], tag=f"gm{t}_{jj}")
                    P.op("gpsimd", (lambda j_, s_: lambda e: e.tensor_mul(
                        sq2[s_][:, :], gat[:, T * j_:T * (j_ + 1)],
                        gat[:, T * j_:T * (j_ + 1)]))(jj, jj % 4),
                        reads=[f"gat:{jj}"], writes=[f"sq2:{jj % 4}"], tag=f"g2{t}_{jj}")
                    if jj >= 2:
                        ss_mm(jj - 2)
                    if jj == NJ2 - 1:
                        ss_mm(NJ2 - 2)
                        ss_mm(NJ2 - 1)
                    P.op("scalar", (lambda j_, s_: lambda e: e.activation(
                        tmpb[s_][:, :], gat[:, T * j_:T * (j_ + 1)], AF.Abs,
                        scale=nwf[:, j_:j_ + 1]))(jj, asl),
                        reads=[f"gat:{jj}", "c:nwf"], writes=[f"tmpb:{asl}"], tag=f"ab{t}_{jj}")
                    P.op("vector", (lambda s_, p_: lambda e: e.tensor_max(
                        Mb[p_][:, :], Mb[p_][:, :], tmpb[s_][:, :]))(asl, pq),
                        reads=[f"tmpb:{asl}", f"Mb{pq}"], writes=[f"Mb{pq}"], tag=f"mm{t}_{jj}")
                    yield

            def p4_quant2_iq(t):
                pq = t % 2
                qt = {k: qt_[:, 8 * i + 4 * pq:8 * i + 4 * (pq + 1)] for i, k in enumerate(QK)}
                for tc in range(4):
                    dma("scalar", "trv", (lambda tc_, p_: lambda e: e.dma_start(
                        out=mbt[tc_ % 2][:, :], in_=Mb[p_][:, 128 * tc_:128 * (tc_ + 1)],
                        transpose=True))(tc, pq),
                        reads=[f"Mb{pq}"], writes=[f"mbt:{tc % 2}"], tag=f"mx{t}_{tc}")
                    P.op("vector", (lambda tc_, q_: lambda e: e.reduce_max(
                        q_["M"][:, tc_:tc_ + 1], mbt[tc_ % 2][:, :], axis=AX.X))(tc, qt),
                        reads=[f"mbt:{tc % 2}"], writes=[f"qt{pq}:M"], tag=f"mr{t}_{tc}")
                P.op("vector", lambda e: e.tensor_copy(qrow[0:1, 0:T], ps[4][0:1, 0:T]),
                     reads=["ps4"], writes=["qrow"], tag=f"sc{t}")
                for tc in range(4):
                    P.op("tensor", (lambda tc_: lambda e: e.matmul(
                        ps[5][:, tc_:tc_ + 1], qrow[0:1, 128 * tc_:128 * (tc_ + 1)],
                        onesr[0:1, 0:1], start=True, stop=True))(tc),
                        reads=["qrow", "c:onesr"], writes=["ps5"], tag=f"st{t}_{tc}")
                P.op("vector", (lambda q_: lambda e: e.tensor_copy(q_["ss"][:, 0:4], ps[5][:, 0:4]))(qt),
                     reads=["ps5"], writes=[f"qt{pq}:ss"], tag=f"se{t}")
                P.op("vector", (lambda q_: lambda e: e.tensor_single_scalar(
                    q_["var"][:, :], q_["ss"][:, :], 1.0 / DF, ALU.mult))(qt),
                    reads=[f"qt{pq}:ss"], writes=[f"qt{pq}:var"], tag=f"q1{t}")
                P.op("scalar", (lambda q_: lambda e: e.activation(
                    q_["sd"][:, :], q_["var"][:, :], AF.Sqrt, bias=cb5[:, 0:1]))(qt),
                    reads=[f"qt{pq}:var", "c:cb5"], writes=[f"qt{pq}:sd"], tag=f"q2{t}")

                def q2c(e, q_=qt):
                    e.reciprocal(q_["rstd"][:, :], q_["sd"][:, :])
                    e.drain()
                    e.tensor_mul(q_["c"][:, :], q_["rstd"][:, :], q_["M"][:, :])
                    e.drain()
                    e.tensor_single_scalar(q_["c"][:, :], q_["c"][:, :], 1e-5, ALU.max)
                    e.drain()
                    e.reciprocal(q_["rc"][:, :], q_["c"][:, :])
                    e.drain()
                    e.tensor_single_scalar(q_["s2"][:, :], q_["rc"][:, :], 127.0, ALU.mult)
                    e.tensor_single_scalar(q_["is2"][:, :], q_["c"][:, :], 1.0 / 127.0, ALU.mult)
                    e.drain()
                    e.tensor_mul(q_["U"][:, :], q_["rstd"][:, :], q_["s2"][:, :])
                    return e.drain()
                P.op("vector", q2c,
                     reads=[f"qt{pq}:sd", f"qt{pq}:M"],
                     writes=[f"qt{pq}:U", f"qt{pq}:is2"], tag=f"q3{t}")
                for tc in range(4):
                    P.op("tensor", (lambda tc_, q_: lambda e: e.transpose(
                        ps[5][0:1, 128 * tc_:128 * (tc_ + 1)],
                        q_["U"][:, tc_:tc_ + 1], eyef[:, :]))(tc, qt),
                        reads=[f"qt{pq}:U", "c:eyef"], writes=["ps5"], tag=f"ut{t}_{tc}")
                P.op("vector", lambda e: e.tensor_copy(qrow[0:1, 0:T], ps[5][0:1, 0:T]),
                     reads=["ps5"], writes=["qrow"], tag=f"ue{t}")
                P.op("tensor", lambda e: e.matmul(ps[5][:, 0:T], onesr[0:1, :], qrow[0:1, 0:T], start=True, stop=True),
                     reads=["qrow", "c:onesr"], writes=["ps5"], tag=f"ub{t}")
                P.op("vector", lambda e: e.tensor_copy(Ub[:, :], ps[5][:, 0:T]),
                     reads=["ps5"], writes=["Ub"], tag=f"uc{t}")
            def p4_iq(t):
                def i3(jj):
                    s_ = jj % 2
                    P.op("gpsimd", (lambda j_, sl_: lambda e: e.tensor_scalar(
                        iqT[:, T * j_:T * (j_ + 1)], u16b[sl_][:, :], 1.0, -R16,
                        ALU.mult, ALU.add))(jj, s_),
                        reads=[f"u16b:{s_}"], writes=[f"iqT:{jj}"], tag=f"i3{t}_{jj}")
                for jj in range(NJ2):
                    s_ = jj % 2
                    P.op("gpsimd", (lambda j_, sl_: lambda e: e.tensor_mul(
                        iqs[sl_][:, :], gat[:, T * j_:T * (j_ + 1)], Ub[:, :]))(jj, s_),
                        reads=[f"gat:{jj}", "Ub"], writes=[f"iqs:{s_}"], tag=f"i1{t}_{jj}")
                    P.op("scalar", (lambda j_, sl_: lambda e: e.activation(
                        u16b[sl_][:, :], iqs[sl_][:, :], AF.Identity, bias=cbr[:, 0:1],
                        scale=nwf[:, j_:j_ + 1]))(jj, s_),
                        reads=[f"iqs:{s_}", "c:cbr", "c:nwf"], writes=[f"u16b:{s_}"], tag=f"i2{t}_{jj}")
                    if jj >= 1:
                        i3(jj - 1)
                    if jj == NJ2 - 1:
                        i3(jj)
                    yield

            def p4_w2(t):
                pq = t % 2
                qt = {k: qt_[:, 8 * i + 4 * pq:8 * i + 4 * (pq + 1)] for i, k in enumerate(QK)}
                for dchunk in range(4):
                    for half in range(2):
                        for kg in range(NJ2 // 4):
                            slot = kg % 2
                            wr = [f"w2s:{4 * kg + i}" for i in range(4)]
                            dma("sync", "w2l", (lambda g_, s_, d_: lambda e: e.dma_start(
                                out=wbuf2[s_][:, :].rearrange("p (k c) -> p k c", k=4),
                                in_=w2s[d_, 4 * g_:4 * (g_ + 1), :, :].rearrange("k p c -> p k c")))(kg, slot, dchunk),
                                reads=wr, writes=[f"wbuf2:{slot}"], tag=f"s2_{t}_{dchunk}_{half}_{kg}")
                            for ki in range(4):
                                kk = 4 * kg + ki
                                for tco in range(2):
                                    tc = 2 * half + tco
                                    P.op("tensor", (lambda k_, s_, ki_, tc_, b_: lambda e: e.matmul(
                                        ps67[b_][:, :], iqT[:, T * k_ + 128 * tc_:T * k_ + 128 * (tc_ + 1)],
                                        wbuf2[s_][:, 512 * ki_:512 * (ki_ + 1)],
                                        start=(k_ == 0), stop=(k_ == NJ2 - 1)))(kk, slot, ki, tc, tco),
                                        reads=[f"wbuf2:{slot}", f"iqT:{kk}"],
                                        writes=[f"ps{6 + tco}"], tag=f"w2_{t}_{dchunk}_{kk}_{tc}")
                        for tco in range(2):
                            tc = 2 * half + tco
                            sl = tc % 2
                            dma("sync", "rl", (lambda t_, tc_, d_, s_: lambda e: e.dma_start(
                                out=xres[s_][:, :],
                                in_=x_h[T * t_ + 128 * tc_:T * t_ + 128 * (tc_ + 1),
                                        512 * d_:512 * (d_ + 1)]))(t, tc, dchunk, sl),
                                reads=[], writes=[f"xres:{sl}"], tag=f"lx{t}_{dchunk}_{tc}")
                            dma("sync", "rl", (lambda t_, tc_, d_, s_: lambda e: e.dma_start(
                                out=aores[s_][:, :],
                                in_=aos[T * t_ + 128 * tc_:T * t_ + 128 * (tc_ + 1),
                                        512 * d_:512 * (d_ + 1)]))(t, tc, dchunk, sl),
                                reads=[f"aos:{t}"], writes=[f"aores:{sl}"], tag=f"la{t}_{dchunk}_{tc}")
                            P.op("vector", (lambda tc_, s_, b_, q_: lambda e: e.scalar_tensor_tensor(
                                ostg[s_][:, :], ps67[b_][:, :], q_["is2"][:, tc_:tc_ + 1],
                                xres[s_][:, :], ALU.mult, ALU.add))(tc, sl, tco, qt),
                                reads=[f"ps{6 + tco}", f"xres:{sl}", f"qt{pq}:is2"],
                                writes=[f"ostg:{sl}"], tag=f"o1{t}_{dchunk}_{tc}")
                            P.op("gpsimd", (lambda s_: lambda e: e.tensor_add(
                                ostg[s_][:, :], ostg[s_][:, :], aores[s_][:, :]))(sl),
                                reads=[f"ostg:{sl}", f"aores:{sl}"], writes=[f"ostg:{sl}"], tag=f"o2{t}_{dchunk}_{tc}")
                            dma("sync", "ol", (lambda t_, tc_, d_, s_: lambda e: e.dma_start(
                                out=out_h[T * t_ + 128 * tc_:T * t_ + 128 * (tc_ + 1),
                                          512 * d_:512 * (d_ + 1)],
                                in_=ostg[s_][:, :]))(t, tc, dchunk, sl),
                                reads=[f"ostg:{sl}"], writes=[f"odone:{sl}"], tag=f"wo{t}_{dchunk}_{tc}")
                        yield

            W2SLOTS = {6, 11, 16, 21, 26, 31, 36, 41}
            for t in range(nt):
                p4_txq(t)
                if t > 0:
                    p4_quant2_iq(t - 1)
                    for _ in p4_iq(t - 1):
                        pass
                g13 = p4_w13(t)
                g2 = p4_w2(t - 1) if t > 0 else None
                for npair in range(NJ2):
                    next(g13)
                    if g2 is not None and npair in W2SLOTS:
                        next(g2, None)
                for _ in g13:
                    pass
                if g2 is not None:
                    for _ in g2:
                        pass
            p4_quant2_iq(nt - 1)
            for _ in p4_iq(nt - 1):
                pass
            for _ in p4_w2(nt - 1):
                pass

            P.resolve()
            with nc.Block() as blk:
                @blk.sync
                def _(eng):
                    P.emit("sync", eng, prog_sems, dma_sems)
                    r = Prog.RING["ol"]
                    tot = P.dma_tot["ol"]
                    for i in range(r):
                        n_i = tot // r + (1 if i < tot % r else 0)
                        if n_i:
                            eng.wait_ge(dma_sems[("ol", i)], 16 * n_i)

                @blk.vector
                def _(eng):
                    P.emit("vector", eng, prog_sems, dma_sems)

                @blk.scalar
                def _(eng):
                    P.emit("scalar", eng, prog_sems, dma_sems)

                @blk.tensor
                def _(eng):
                    P.emit("tensor", eng, prog_sems, dma_sems)

                @blk.gpsimd
                def _(eng):
                    P.emit("gpsimd", eng, prog_sems, dma_sems)

    return nc


def _host_consts():
    c = {}
    c["eye16"] = np.eye(128, dtype=ml_dtypes.bfloat16)
    c["eyef"] = np.eye(128, dtype=np.float32)
    c["prow"] = np.arange(128, dtype=np.float32).reshape(1, 128)
    c["mrow"] = np.arange(BLK, dtype=np.float32).reshape(1, BLK)
    c["onesrow"] = np.ones((1, 128), dtype=np.float32)
    p = np.arange(128).reshape(128, 1)
    m = np.arange(2 * BLK).reshape(1, 2 * BLK)
    c["mask"] = ((m % BLK) >= (128 * (m // BLK) + p)).astype(np.float32)
    return c


def _blocked_lhsT(wt):
    Dd, ncol = wt.shape
    nk, ncch = Dd // 128, ncol // 128
    return np.ascontiguousarray(
        wt.reshape(nk, 128, ncch, 128).transpose(2, 1, 0, 3).reshape(ncch, 128, Dd))


def kernel(L0_attn_hidden_states, L0_attn_slope_rate, L0_attn_w_qkv,
           L0_attn_w_gate, L0_attn_w_out, L0_attn_norm_w,
           L0_ffn_w13, L0_ffn_w2, L0_ffn_norm_w):
    x = np.ascontiguousarray(np.asarray(L0_attn_hidden_states, dtype=np.float32))
    wqkv = np.asarray(L0_attn_w_qkv, dtype=np.float32).reshape(NH, 3, HD, D)
    wq = _blocked_lhsT(np.ascontiguousarray(wqkv[:, 0].reshape(D, D).T))
    wk = _blocked_lhsT(np.ascontiguousarray(wqkv[:, 1].reshape(D, D).T))
    wv = np.ascontiguousarray(wqkv[:, 2].reshape(D, D).T)
    wg = _blocked_lhsT(np.ascontiguousarray(np.asarray(L0_attn_w_gate, np.float32).T))
    wo = np.ascontiguousarray(np.asarray(L0_attn_w_out, np.float32).T)
    w13 = _blocked_lhsT(np.ascontiguousarray(np.asarray(L0_ffn_w13, np.float32).T))
    w2t = np.ascontiguousarray(np.asarray(L0_ffn_w2, np.float32).T)
    slope = np.ascontiguousarray(np.asarray(L0_attn_slope_rate, np.float32).reshape(1, NH))
    nwa = np.ascontiguousarray(np.asarray(L0_attn_norm_w, np.float32).reshape(NH, 128).T)
    nwf = np.ascontiguousarray(np.asarray(L0_ffn_norm_w, np.float32).reshape(NJ2, 128).T)
    consts = _host_consts()

    nc = build_nc(seq=S)
    base = dict(wq=wq, wk=wk, wv=wv, wg=wg, wo=wo, w13=w13, w2t=w2t,
                slope=slope, nwa=nwa, nwf=nwf, **consts)
    in_maps = [dict(base, x=np.ascontiguousarray(x[b])) for b in range(NCORES)]
    res = run_bass_kernel_spmd(nc, in_maps, core_ids=list(range(NCORES)))
    out = np.stack([np.asarray(res.results[b]["out"]) for b in range(NCORES)], axis=0)
    return out.astype(np.float32)


# version 12
# speedup vs baseline: 1.0648x; 1.0648x over previous
import numpy as np
import ml_dtypes
import contextlib
import concourse.bass as bass
import concourse.mybir as mybir
from concourse.bass_utils import run_bass_kernel_spmd
from concourse.alu_op_type import AluOpType as ALU

F32 = mybir.dt.float32
BF16 = mybir.dt.bfloat16
F16 = mybir.dt.float16
AF = mybir.ActivationFunctionType
AX = mybir.AxisListType

NCORES = 8
D = 2048
NH = 16
HD = 128
BLK = 256
DF = 5632
S = 4096
T = 512
NKC = D // 128
NJ = 2 * DF // 128
NJ2 = DF // 128
LN_EPS = 1e-5
AN_EPS = 1e-6
RMS_EPS = 1e-5
R16 = 1536.0

ENGS = ("tensor", "vector", "scalar", "gpsimd", "sync")


class Prog:

    def __init__(self):
        self.ops = []
        self.last_w = {}
        self.readers = {}
        self.bar_deps = []
        self.nbar = 0

    def op(self, engine, fn, reads=(), writes=(), dma=None, tag=""):
        idx = len(self.ops)
        deps = set()
        for b in reads:
            if b in self.last_w:
                deps.add(self.last_w[b])
        for b in writes:
            if b in self.last_w:
                deps.add(self.last_w[b])
            for r in self.readers.get(b, ()):
                deps.add(r)
        deps.update(self.bar_deps)
        deps.discard(idx)
        if dma is not None:
            if not hasattr(self, "dma_eng"):
                self.dma_eng = {}
            assert self.dma_eng.setdefault(dma, engine) == engine, \
                f"dma sem {dma} used from {engine} and {self.dma_eng[dma]}"
        self.ops.append(dict(idx=idx, engine=engine, fn=fn, deps=deps,
                             dma=dma, dma_idx=None, tag=tag, inc=None,
                             is_bar=False))
        for b in reads:
            self.readers.setdefault(b, []).append(idx)
        for b in writes:
            self.last_w[b] = idx
            self.readers[b] = []
        return idx

    def barrier(self):
        self.nbar += 1
        prior = list(range(len(self.ops)))
        bar_idxs = []
        for e in ENGS:
            idx = len(self.ops)
            self.ops.append(dict(idx=idx, engine=e, fn=None, deps=set(prior),
                                 dma=None, dma_idx=None, tag=f"bar{self.nbar}",
                                 inc=None, is_bar=True))
            bar_idxs.append(idx)
        self.bar_deps = bar_idxs

    RING = {"c": 1, "wl": 3, "ws": 2, "wst": 6, "xl": 3, "aw": 2, "tr": 4,
            "rl": 4, "ol": 2, "tb": 1, "tbw": 1, "ao": 2, "wsa": 2, "wla": 3,
            "trk": 4, "trv": 4, "trs": 4, "w2l": 4}

    def resolve(self):
        eng_seq = {}
        seq_ctr = {e: 0 for e in ENGS}
        self.dma_tot = {}
        for o in self.ops:
            eng_seq[o["idx"]] = seq_ctr[o["engine"]]
            seq_ctr[o["engine"]] += 1
            if o["dma"] is not None:
                k = self.dma_tot.get(o["dma"], 0)
                self.dma_tot[o["dma"]] = k + 1
                r = self.RING[o["dma"]]
                o["dma_sub"] = (o["dma"], k % r)
                o["dma_val"] = 16 * (k // r + 1)
        needs_inc = set()
        for o in self.ops:
            for d in o["deps"]:
                dop = self.ops[d]
                if dop["dma"] is None and not (
                        dop["engine"] == "tensor" and o["engine"] == "tensor"):
                    needs_inc.add(d)
        for o in self.ops:
            if o["is_bar"]:
                needs_inc.add(o["idx"])
        self.inc_val = {}
        ctr = {e: 0 for e in ENGS}
        for o in self.ops:
            if o["idx"] in needs_inc:
                ctr[o["engine"]] += 1
                o["inc"] = ctr[o["engine"]]
                self.inc_val[o["idx"]] = ctr[o["engine"]]
        self.eng_seq = eng_seq
        self.by_eng = {e: [o for o in self.ops if o["engine"] == e] for e in ENGS}

    def emit(self, ename, eng_obj, prog_sems, dma_sems):
        waited = {}
        drained_upto = 0
        for o in self.by_eng[ename]:
            need_drain = False
            waits = {}
            for d in sorted(o["deps"]):
                dop = self.ops[d]
                if dop["dma"] is not None:
                    key = ("dma",) + dop["dma_sub"]
                    val = dop["dma_val"]
                    if waits.get(key, 0) < val:
                        waits[key] = val
                elif dop["engine"] == ename:
                    if ename != "tensor":
                        v = self.inc_val[d]
                        key = ("prog", ename)
                        if waits.get(key, 0) < v:
                            waits[key] = v
                else:
                    v = self.inc_val[d]
                    key = ("prog", dop["engine"])
                    if waits.get(key, 0) < v:
                        waits[key] = v
            if o["dma"] is not None and o["dma_val"] > 16:
                key = ("dma",) + o["dma_sub"]
                if waits.get(key, 0) < o["dma_val"] - 16:
                    waits[key] = max(waits.get(key, 0), o["dma_val"] - 16)
            for key, val in sorted(waits.items(), key=lambda kv: str(kv[0])):
                if waited.get(key, 0) >= val:
                    continue
                sem = dma_sems[key[1:]] if key[0] == "dma" else prog_sems[key[1]]
                eng_obj.wait_ge(sem, val)
                waited[key] = val
            if o["is_bar"]:
                inst = eng_obj.drain()
                drained_upto = self.eng_seq[o["idx"]]
            else:
                inst = o["fn"](eng_obj)
            if o["inc"] is not None:
                assert inst is not None, f"op {o['tag']} returned None but needs inc"
                inst.then_inc(prog_sems[ename], 1)
            if o["dma"] is not None:
                inst.then_inc(dma_sems[o["dma_sub"]], 16)


def build_nc(seq=S):
    nt = seq // T
    nxc = seq // 128
    nc = bass.Bass("TRN2", target_bir_lowering=False, debug=False)

    x_h = nc.dram_tensor("x", [seq, D], F32, kind="ExternalInput").ap()
    wq_h = nc.dram_tensor("wq", [NKC, 128, D], F32, kind="ExternalInput").ap()
    wk_h = nc.dram_tensor("wk", [NKC, 128, D], F32, kind="ExternalInput").ap()
    wv_h = nc.dram_tensor("wv", [D, D], F32, kind="ExternalInput").ap()
    wg_h = nc.dram_tensor("wg", [NKC, 128, D], F32, kind="ExternalInput").ap()
    wo_h = nc.dram_tensor("wo", [D, D], F32, kind="ExternalInput").ap()
    w13_h = nc.dram_tensor("w13", [NJ, 128, D], F32, kind="ExternalInput").ap()
    w2_h = nc.dram_tensor("w2t", [DF, D], F32, kind="ExternalInput").ap()
    slope_h = nc.dram_tensor("slope", [1, NH], F32, kind="ExternalInput").ap()
    nwa_h = nc.dram_tensor("nwa", [128, NH], F32, kind="ExternalInput").ap()
    nwf_h = nc.dram_tensor("nwf", [128, NJ2], F32, kind="ExternalInput").ap()
    eye_h = nc.dram_tensor("eye16", [128, 128], BF16, kind="ExternalInput").ap()
    eyef_h = nc.dram_tensor("eyef", [128, 128], F32, kind="ExternalInput").ap()
    prow_h = nc.dram_tensor("prow", [1, 128], F32, kind="ExternalInput").ap()
    mrow_h = nc.dram_tensor("mrow", [1, BLK], F32, kind="ExternalInput").ap()
    onesr_h = nc.dram_tensor("onesrow", [1, 128], F32, kind="ExternalInput").ap()
    mask_h = nc.dram_tensor("mask", [128, 2 * BLK], F32, kind="ExternalInput").ap()
    out_h = nc.dram_tensor("out", [seq, D], F32, kind="ExternalOutput").ap()

    wqs = nc.dram_tensor("wqs", [NKC, 128, D], BF16, kind="Internal").ap()
    wks = nc.dram_tensor("wks", [NKC, 128, D], BF16, kind="Internal").ap()
    wvs = nc.dram_tensor("wvs", [4, NKC, 128, 512], BF16, kind="Internal").ap()
    wgs = nc.dram_tensor("wgs", [NKC, 128, D], BF16, kind="Internal").ap()
    wos = nc.dram_tensor("wos", [4, NKC, 128, 512], BF16, kind="Internal").ap()
    w13s = nc.dram_tensor("w13s", [NJ, 128, D], BF16, kind="Internal").ap()
    w2s = nc.dram_tensor("w2s", [4, NJ2, 128, 512], BF16, kind="Internal").ap()
    xns = nc.dram_tensor("xns", [D, seq], BF16, kind="Internal").ap()
    xqs = nc.dram_tensor("xqs", [D, seq], BF16, kind="Internal").ap()
    aos = nc.dram_tensor("aos", [seq, D], BF16, kind="Internal").ap()
    tbl = nc.dram_tensor("tbl", [2, NH * BLK], F32, kind="Internal").ap()

    P = Prog()

    def dma(qeng, sem, fn, reads, writes, tag=""):
        return P.op(qeng, fn, reads=reads, writes=writes, dma=sem, tag=tag)

    es = contextlib.ExitStack()
    with es:
        sbp = lambda n, sh, dt: es.enter_context(nc.sbuf_tensor(n, sh, dt))
        eye = sbp("eye", [128, 128], BF16)
        eyef = sbp("eyef_s", [128, 128], F32)
        prow = sbp("prow_s", [1, 128], F32)
        mrow = sbp("mrow_s", [1, BLK], F32)
        onesr = sbp("onesr_s", [1, 128], F32)
        ones256 = sbp("ones256", [1, BLK], F32)
        onescol_b = sbp("onescol_b", [128, 1], BF16)
        mask = sbp("mask_s", [128, 2 * BLK], F32)
        nwa = sbp("nwa_s", [128, NH], F32)
        nwf = sbp("nwf_s", [128, NJ2], F32)
        srow = sbp("srow", [1, NH], F32)
        t16 = sbp("t16", [16, BLK], F32)
        bd_vec = sbp("bd_vec", [128, NH], F32)
        cb5 = sbp("cb5", [128, 1], F32)
        cb6 = sbp("cb6", [128, 1], F32)
        cbr = sbp("cbr", [128, 1], F32)
        kdec = sbp("kdec", [128, 2 * NH], F32)
        NWB = 6
        wbuf = [sbp(f"wbuf{i}", [128, D], BF16) for i in range(NWB)]
        wbufS = [sbp(f"wbufS{i}", [128, D], BF16) for i in range(2)]
        wf32 = [sbp(f"wf32{i}", [128, D], F32) for i in range(2)]

        ps = [es.enter_context(nc.psum_tensor(f"ps{i}", [128, 512], F32))
              for i in range(6)]

        prog_sems = {e: es.enter_context(nc.semaphore(f"pg_{e}")) for e in ENGS}
        dma_sems = {}
        for k, r in Prog.RING.items():
            for i in range(r):
                dma_sems[(k, i)] = es.enter_context(nc.semaphore(f"dm_{k}{i}"))

        esAB = contextlib.ExitStack()
        sbab = lambda n, sh, dt: esAB.enter_context(nc.sbuf_tensor(n, sh, dt))
        with esAB:
            qdbc = sbab("qdbc", [128, NH * BLK], BF16)
            ddT = sbab("ddT", [128, NH * 2 * BLK], BF16)
            kv = sbab("kv", [128, NH * HD], F32)

            es1 = contextlib.ExitStack()
            sb1 = lambda n, sh, dt: es1.enter_context(nc.sbuf_tensor(n, sh, dt))
            with es1:
                rA = sb1("rowA", [1, NH * BLK], F32)
                rB = sb1("rowB", [1, NH * BLK], F32)
                rC = sb1("rowC", [1, NH * BLK], F32)
                rD = sb1("rowD", [1, NH * BLK], F32)
                ddw = sb1("ddw", [128, 2 * BLK], F32)
                psB = [es1.enter_context(nc.psum_tensor(f"psB{i}", [128, 1024], BF16))
                       for i in range(2)]
                tstg = [sb1(f"tstg{i}", [128, 1024], BF16) for i in range(2)]
                xb = [sb1(f"xb{i}", [128, D], F32) for i in range(3)]
                u16 = sb1("u16", [128, D], F16)
                xnb = [sb1(f"xnb{i}", [128, D], BF16) for i in range(2)]
                xqb = [sb1(f"xqb{i}", [128, D], BF16) for i in range(2)]
                NST = nxc
                stt_ = sb1("stats", [128, 18 * nxc], F32)
                SK = ["sum", "ss", "mx", "mn", "m", "var", "sd", "rstd", "t0",
                      "t1", "mab", "c", "rc", "a1q", "a0q", "isv", "nis", "nmr"]
                st = {k: (lambda i: lambda g: stt_[:, i * NST:(i + 1) * NST] if g is None
                          else stt_[:, i * NST + g:i * NST + g + 1])(i)
                      for i, k in enumerate(SK)}

                for name, dst, src in [
                        ("eye", eye, eye_h), ("eyef", eyef, eyef_h),
                        ("prow", prow, prow_h), ("mrow", mrow, mrow_h),
                        ("onesr", onesr, onesr_h), ("mask", mask, mask_h),
                        ("nwa", nwa, nwa_h), ("nwf", nwf, nwf_h),
                        ("srow", srow, slope_h)]:
                    dma("sync", "c",
                        (lambda d_, s_: lambda e: e.dma_start(out=d_[:, :], in_=s_))(dst, src),
                        reads=[], writes=[f"c:{name}"], tag=f"ld_{name}")
                P.op("vector", lambda e: e.memset(ones256[:, :], 1.0), writes=["c:ones256"])
                P.op("vector", lambda e: e.memset(onescol_b[:, :], 1.0), writes=["c:onescol"])
                P.op("vector", lambda e: e.memset(cb5[:, :], 1e-5), writes=["c:cb5"])
                P.op("vector", lambda e: e.memset(cb6[:, :], 1e-6), writes=["c:cb6"])
                P.op("vector", lambda e: e.memset(cbr[:, :], R16), writes=["c:cbr"])
                P.op("vector", lambda e: e.memset(kv[:, :], 0.0),
                     writes=["kv"] + [f"kv:{h}" for h in range(NH)])

                P.op("tensor", lambda e: e.matmul(ps[5][0:16, 0:BLK], srow[0:1, :], ones256[0:1, :], start=True, stop=True),
                     reads=["c:srow", "c:ones256"], writes=["ps5"], tag="mmS")
                P.op("vector", lambda e: e.tensor_copy(t16[:, :], ps[5][0:16, 0:BLK]),
                     reads=["ps5"], writes=["t16"], tag="evS")
                dma("gpsimd", "tbw", lambda e: e.dma_start(
                    out=tbl[0:1, :].rearrange("o (p c) -> p (o c)", p=16), in_=t16[:, :]),
                    reads=["t16"], writes=["tbl0"], tag="wS")
                dma("sync", "tb", lambda e: e.dma_start(out=rA[:, :], in_=tbl[0:1, :]),
                    reads=["tbl0"], writes=["rA"], tag="rS")
                P.op("tensor", lambda e: e.matmul(ps[5][0:16, 0:BLK], srow[0:1, :], mrow[0:1, :], start=True, stop=True),
                     reads=["c:srow", "c:mrow", "t16"], writes=["ps5"], tag="mmR")
                P.op("vector", lambda e: e.tensor_copy(t16[:, :], ps[5][0:16, 0:BLK]),
                     reads=["ps5"], writes=["t16"], tag="evR")
                dma("gpsimd", "tbw", lambda e: e.dma_start(
                    out=tbl[1:2, :].rearrange("o (p c) -> p (o c)", p=16), in_=t16[:, :]),
                    reads=["t16"], writes=["tbl1"], tag="wR")
                dma("sync", "tb", lambda e: e.dma_start(out=rB[:, :], in_=tbl[1:2, :]),
                    reads=["tbl1"], writes=["rB"], tag="rR")
                P.op("tensor", lambda e: e.matmul(ps[5][:, 0:NH], onesr[0:1, :], srow[0:1, :], start=True, stop=True),
                     reads=["c:onesr", "c:srow", "t16"], writes=["ps5"], tag="mmB")
                P.op("scalar", lambda e: e.activation(bd_vec[:, :], ps[5][:, 0:NH], AF.Exp, scale=-float(BLK)),
                     reads=["ps5"], writes=["bd_vec"], tag="bd")
                P.op("vector", lambda e: e.tensor_single_scalar(rC[:, :], rA[:, :], -255.0, ALU.mult),
                     reads=["rA"], writes=["rC"], tag="rk0")
                P.op("vector", lambda e: e.tensor_single_scalar(rD[:, :], rA[:, :], -127.0, ALU.mult),
                     reads=["rA"], writes=["rD"], tag="rk1")
                for h in range(NH):
                    for c in range(2):
                        col = 2 * h + c
                        rk = rC if c == 0 else rD
                        P.op("tensor", (lambda h_, col_: lambda e: e.matmul(
                            ps[4][:, col_:col_ + 1], prow[0:1, :],
                            rA[0:1, BLK * h_:BLK * h_ + 1], start=True, stop=False))(h, col),
                            reads=["c:prow", "rA"], writes=["ps4"], tag=f"kda{col}")
                        P.op("tensor", (lambda h_, col_, rk_: lambda e: e.matmul(
                            ps[4][:, col_:col_ + 1], onesr[0:1, :],
                            rk_[0:1, BLK * h_:BLK * h_ + 1], start=False, stop=True))(h, col, rk),
                            reads=["c:onesr", "rC", "rD"], writes=["ps4"], tag=f"kdb{col}")
                P.op("scalar", lambda e: e.activation(kdec[:, :], ps[4][:, 0:2 * NH], AF.Exp),
                     reads=["ps4"], writes=["kdec"], tag="kde")
                P.op("vector", lambda e: e.tensor_single_scalar(rC[:, :], rB[:, :], -1.0, ALU.mult),
                     reads=["rB"], writes=["rC"], tag="rc0")
                P.op("vector", lambda e: e.scalar_tensor_tensor(rD[:, :], rA[:, :], 128.0, rB[:, :], ALU.mult, ALU.subtract),
                     reads=["rA", "rB"], writes=["rD"], tag="rc1")
                for h in range(NH):
                    for c in range(2):
                        bank = 4 + (c % 2)
                        rowc = rC if c == 0 else rD
                        P.op("tensor", (lambda h_, b_: lambda e: e.matmul(
                            ps[b_][:, 0:BLK], prow[0:1, :],
                            rA[0:1, BLK * h_:BLK * (h_ + 1)], start=True, stop=False))(h, bank),
                            reads=["c:prow", "rA", "kdec"], writes=[f"ps{bank}"], tag=f"dda{h}{c}")
                        P.op("tensor", (lambda h_, b_, rc_: lambda e: e.matmul(
                            ps[b_][:, 0:BLK], onesr[0:1, :],
                            rc_[0:1, BLK * h_:BLK * (h_ + 1)], start=False, stop=True))(h, bank, rowc),
                            reads=["c:onesr", "rC", "rD"], writes=[f"ps{bank}"], tag=f"ddb{h}{c}")
                        P.op("vector", (lambda b_: lambda e: e.tensor_scalar(
                            ddw[:, 0:BLK], ps[b_][:, 0:BLK], 1.0, 0.0, ALU.mult, ALU.min))(bank),
                            reads=[f"ps{bank}"], writes=["ddw0"], tag=f"ddc{h}{c}")
                        P.op("scalar", lambda e: e.activation(ddw[:, BLK:2 * BLK], ddw[:, 0:BLK], AF.Exp),
                             reads=["ddw0"], writes=["ddw1"], tag=f"ddd{h}{c}")
                        P.op("vector", (lambda h_, c_: lambda e: e.tensor_mul(
                            ddT[:, (2 * h_ + c_) * BLK:(2 * h_ + c_ + 1) * BLK],
                            ddw[:, BLK:2 * BLK], mask[:, c_ * BLK:(c_ + 1) * BLK]))(h, c),
                            reads=["ddw1", "c:mask"], writes=[f"ddT:{h}:{c}"], tag=f"dde{h}{c}")
                P.op("vector", lambda e: e.tensor_add(rB[:, :], rB[:, :], rA[:, :]),
                     reads=["rA", "rB"], writes=["rB"], tag="r2")
                for cc in range(NH * BLK // 512):
                    bank = cc % 2
                    P.op("tensor", (lambda cc_, b_: lambda e: e.matmul(
                        ps[b_][:, :], onesr[0:1, :], rB[0:1, 512 * cc_:512 * (cc_ + 1)],
                        start=True, stop=True))(cc, bank),
                        reads=["rB", "c:onesr"], writes=[f"ps{bank}"], tag=f"qba{cc}")
                    P.op("scalar", (lambda cc_, b_: lambda e: e.activation(
                        qdbc[:, 512 * cc_:512 * (cc_ + 1)], ps[b_][:, :], AF.Exp, scale=-1.0))(cc, bank),
                        reads=[f"ps{bank}"], writes=[f"qdbc:{cc}"], tag=f"qbb{cc}")

                for ci in range(nxc):
                    slot = ci % 3
                    xk = f"xb:{slot}"
                    dma("scalar", "xl", (lambda ci_, s_: lambda e: e.dma_start(
                        out=xb[s_][:, :], in_=x_h[128 * ci_:128 * (ci_ + 1), :]))(ci, slot),
                        reads=[], writes=[xk], tag=f"ldx{ci}")
                    P.op("scalar", (lambda s_, g_: lambda e: e.activation(
                        u16[:, :], xb[s_][:, :], AF.Square, accum_out=st["ss"](g_)))(slot, ci),
                        reads=[xk], writes=["u16", f"st1:{ci}"], tag=f"sq{ci}")
                    P.op("vector", (lambda s_, g_: lambda e: e.reduce_sum(
                        st["sum"](g_), xb[s_][:, :], axis=AX.X))(slot, ci),
                        reads=[xk], writes=[f"st1:{ci}"], tag=f"rs{ci}")
                    P.op("vector", (lambda s_, g_: lambda e: e.reduce_max(
                        st["mx"](g_), xb[s_][:, :], axis=AX.X))(slot, ci),
                        reads=[xk], writes=[f"st1:{ci}"], tag=f"rx{ci}")
                    P.op("vector", (lambda s_, g_: lambda e: e.tensor_reduce(
                        st["mn"](g_), xb[s_][:, :], AX.X, ALU.min))(slot, ci),
                        reads=[xk], writes=[f"st1:{ci}"], tag=f"rn{ci}")

                def chain_all(e):
                    s_ = lambda k: st[k](None)
                    e.tensor_single_scalar(s_("m"), s_("sum"), 1.0 / D, ALU.mult)
                    e.drain()
                    e.tensor_mul(s_("t0"), s_("m"), s_("m"))
                    e.drain()
                    e.scalar_tensor_tensor(s_("var"), s_("ss"), 1.0 / D, s_("t0"), ALU.mult, ALU.subtract)
                    return e.drain()
                P.op("vector", chain_all,
                     reads=[f"st1:{ci}" for ci in range(nxc)],
                     writes=["st2a"], tag="c1all")
                P.op("scalar", lambda e: e.activation(
                    st["sd"](None), st["var"](None), AF.Sqrt, bias=cb5[:, 0:1]),
                    reads=["st2a", "c:cb5"], writes=["st2b"], tag="c2all")

                def chain3_all(e):
                    s_ = lambda k: st[k](None)
                    e.reciprocal(s_("rstd"), s_("sd"))
                    e.drain()
                    e.tensor_sub(s_("t0"), s_("mx"), s_("m"))
                    e.tensor_sub(s_("t1"), s_("m"), s_("mn"))
                    e.drain()
                    e.tensor_max(s_("mab"), s_("t0"), s_("t1"))
                    e.drain()
                    e.tensor_mul(s_("t0"), s_("mab"), s_("rstd"))
                    e.drain()
                    e.tensor_single_scalar(s_("c"), s_("t0"), 1e-5, ALU.max)
                    e.drain()
                    e.reciprocal(s_("rc"), s_("c"))
                    e.drain()
                    e.scalar_tensor_tensor(s_("a1q"), s_("rstd"), 127.0, s_("rc"), ALU.mult, ALU.mult)
                    e.tensor_single_scalar(s_("isv"), s_("c"), 1.0 / 127.0, ALU.mult)
                    e.tensor_mul(s_("nmr"), s_("m"), s_("rstd"))
                    e.drain()
                    e.tensor_mul(s_("t1"), s_("m"), s_("a1q"))
                    e.tensor_single_scalar(s_("nis"), s_("isv"), -R16, ALU.mult)
                    e.tensor_single_scalar(s_("nmr"), s_("nmr"), -1.0, ALU.mult)
                    e.drain()
                    e.tensor_scalar(s_("a0q"), s_("t1"), -1.0, R16, ALU.mult, ALU.add)
                    return e.drain()
                P.op("vector", chain3_all, reads=["st2a", "st2b"],
                     writes=["st2"], tag="c3all")

                def p2_tr(ci, srcbuf, srckey, dst_hbm, okey, tagp):
                    for half in range(2):
                        pk = f"psB:{half}"
                        for k8 in range(8):
                            kk = 8 * half + k8
                            P.op("tensor", (lambda s_, k_, k8_, h_: lambda e: e.transpose(
                                psB[h_][:, 128 * k8_:128 * (k8_ + 1)],
                                srcbuf[s_][:, 128 * k_:128 * (k_ + 1)], eye[:, :]))(srckey, kk, k8, half),
                                reads=[f"{tagp}bb:{srckey}", "c:eye"], writes=[pk], tag=f"{tagp}t{ci}_{kk}")
                        P.op("vector", (lambda h_: lambda e: e.tensor_copy(
                            tstg[h_][:, :], psB[h_][:, :]))(half),
                            reads=[pk], writes=[f"tstg:{half}"], tag=f"{tagp}e{ci}_{half}")
                        dma("gpsimd", "aw", (lambda ci_, h_: lambda e: e.dma_start(
                            out=dst_hbm[1024 * h_:1024 * (h_ + 1), 128 * ci_:128 * (ci_ + 1)].rearrange(
                                "(k p) c -> p k c", k=8),
                            in_=tstg[h_][:, :].rearrange("p (k c) -> p k c", k=8)))(ci, half),
                            reads=[f"tstg:{half}"], writes=[f"{okey}:{ci // 4}"], tag=f"{tagp}w{ci}_{half}")

                for ci in range(nxc):
                    slot = ci % 3
                    xk = f"xb:{slot}"
                    dma("scalar", "xl", (lambda ci_, s_: lambda e: e.dma_start(
                        out=xb[s_][:, :], in_=x_h[128 * ci_:128 * (ci_ + 1), :]))(ci, slot),
                        reads=[], writes=[xk], tag=f"ldx2_{ci}")
                    P.op("scalar", (lambda s_, g_: lambda e: e.activation(
                        xnb[s_ % 2][:, :], xb[s_][:, :], AF.Identity,
                        bias=st["nmr"](g_), scale=st["rstd"](g_)))(slot, ci),
                        reads=[xk, "st2"], writes=[f"xnbb:{slot % 2}"], tag=f"xn{ci}")
                    p2_tr(ci, xnb, slot % 2, xns, "xns", "xn")
                    P.op("vector", (lambda s_, g_: lambda e: e.tensor_scalar(
                        u16[:, :], xb[s_][:, :], st["a1q"](g_), st["a0q"](g_), ALU.mult, ALU.add))(slot, ci),
                        reads=[xk, "st2"], writes=["u16"], tag=f"u{ci}")
                    P.op("vector", (lambda s_, g_: lambda e: e.tensor_scalar(
                        xqb[s_ % 2][:, :], u16[:, :], st["isv"](g_), st["nis"](g_), ALU.mult, ALU.add))(slot, ci),
                        reads=["u16", "st2"], writes=[f"xqbb:{slot % 2}"], tag=f"xq{ci}")
                    p2_tr(ci, xqb, slot % 2, xqs, "xqs", "xq")

                def stage3(src_ap, dst_ap, nchunk, key, q=("gpsimd", "wl", "gpsimd", "ws")):
                    lq, ls, wq_, ws_ = q
                    for c in range(nchunk):
                        f32b = f"wf32:{c % 2}"
                        bfb = f"wbufS:{c % 2}"
                        dma(lq, ls, (lambda c_: lambda e: e.dma_start(
                            out=wf32[c_ % 2][:, :], in_=src_ap[c_, :, :]))(c),
                            reads=[], writes=[f32b], tag=f"L{key}{c}")
                        P.op("gpsimd", (lambda c_: lambda e: e.tensor_copy(
                            wbufS[c_ % 2][:, :], wf32[c_ % 2][:, :]))(c),
                            reads=[f32b], writes=[bfb], tag=f"C{key}{c}")
                        dma(wq_, ws_, (lambda c_: lambda e: e.dma_start(
                            out=dst_ap[c_, :, :], in_=wbufS[c_ % 2][:, :]))(c),
                            reads=[bfb], writes=[f"{key}:{c}"], tag=f"S{key}{c}")

                def stage2(src_ap, dst_ap, nchunk, key, q=("gpsimd", "wl", "gpsimd", "ws")):
                    lq, ls, wq_, ws_ = q
                    for c in range(nchunk):
                        f32b = f"wf32:{c % 2}"
                        bfb = f"wbufS:{c % 2}"
                        dma(lq, ls, (lambda c_: lambda e: e.dma_start(
                            out=wf32[c_ % 2][:, :], in_=src_ap[128 * c_:128 * (c_ + 1), :]))(c),
                            reads=[], writes=[f32b], tag=f"L{key}{c}")
                        P.op("gpsimd", (lambda c_: lambda e: e.tensor_copy(
                            wbufS[c_ % 2][:, :], wf32[c_ % 2][:, :]))(c),
                            reads=[f32b], writes=[bfb], tag=f"C{key}{c}")
                        dma(wq_, ws_, (lambda c_: lambda e: e.dma_start(
                            out=dst_ap[:, c_, :, :].rearrange("d p c -> p d c"),
                            in_=wbufS[c_ % 2][:, :].rearrange("p (d c) -> p d c", d=4)))(c),
                            reads=[bfb], writes=[f"{key}:{c}"], tag=f"S{key}{c}")

                QA = ("sync", "wla", "scalar", "wsa")
                stage3(wq_h, wqs, NKC, "wqs", QA)
                stage3(wk_h, wks, NKC, "wks", QA)
                stage2(wv_h, wvs, NKC, "wvs", QA)
                stage3(wg_h, wgs, NKC, "wgs", QA)
                stage2(wo_h, wos, NKC, "wos", QA)
                P.stage_ffn = lambda: (stage3(w13_h, w13s, NJ, "w13s"),
                                       stage2(w2_h, w2s, NJ2, "w2s"))

            P.barrier()
            es2 = contextlib.ExitStack()
            sb2 = lambda n, sh, dt: es2.enter_context(nc.sbuf_tensor(n, sh, dt))
            with es2:
                P.stage_ffn()
                xnT = sb2("xnT", [128, NKC * T], BF16)
                qT = sb2("qT", [128, NH * T], BF16)
                kT = sb2("kT", [128, NH * T], BF16)
                v_tok = [sb2(f"v_tok{i}", [128, D], BF16) for i in range(4)]
                psBt = [es2.enter_context(nc.psum_tensor(f"psBt{i}", [128, 1024], BF16))
                        for i in range(2)]
                gateT = sb2("gateT", [128, NH * T], BF16)
                aT = sb2("aT", [128, NH * T], BF16)
                kvb = [sb2(f"kvb{i}", [128, HD], BF16) for i in range(4)]
                qdi = [sb2(f"qdi{i}", [128, BLK], BF16) for i in range(4)]
                qkd = [sb2(f"qkd{i}", [128, 2 * BLK], BF16) for i in range(4)]
                ktok = [sb2(f"ktok{i}", [128, BLK], BF16) for i in range(4)]
                sqs = [sb2(f"sqs{i}", [128, T], BF16) for i in range(4)]
                arow = sb2("arow", [1, T], F32)
                arow2 = sb2("arow2", [1, T], F32)
                rstdA = sb2("rstdA", [128, T], F32)
                aostg = [sb2(f"aostg{i}", [128, 512], BF16) for i in range(2)]

                for t in range(nt):
                    for kk in range(NKC):
                        qq = ("scalar", "tr") if kk % 2 == 0 else ("sync", "trs")
                        dma(qq[0], qq[1], (lambda t_, k_: lambda e: e.dma_start(
                            out=xnT[:, T * k_:T * (k_ + 1)],
                            in_=xns[128 * k_:128 * (k_ + 1), T * t_:T * (t_ + 1)]))(t, kk),
                            reads=[f"xns:{t}"], writes=[f"xnT:{kk}"], tag=f"txn{t}_{kk}")
                    for gi, (wsrc, wkey, dstT, dkey, act) in enumerate([
                            (wqs, "wqs", qT, "qT", AF.Silu),
                            (wks, "wks", kT, "kT", AF.Silu),
                            (wgs, "wgs", gateT, "gateT", AF.Sigmoid)]):
                        for c in range(NKC):
                            slot = (gi * NKC + c) % NWB
                            bank = (gi * NKC + c) % 2
                            dma("sync", "wst", (lambda w_, c_, s_: lambda e: e.dma_start(
                                out=wbuf[s_][:, :], in_=w_[c_, :, :]))(wsrc, c, slot),
                                reads=[f"{wkey}:{c}"], writes=[f"wbuf:{slot}"], tag=f"sw{t}{gi}{c}")
                            for kk in range(NKC):
                                P.op("tensor", (lambda s_, b_, k_: lambda e: e.matmul(
                                    ps[b_][:, :], wbuf[s_][:, 128 * k_:128 * (k_ + 1)],
                                    xnT[:, T * k_:T * (k_ + 1)],
                                    start=(k_ == 0), stop=(k_ == NKC - 1)))(slot, bank, kk),
                                    reads=[f"wbuf:{slot}", f"xnT:{kk}"],
                                    writes=[f"ps{bank}"], tag=f"g{t}_{gi}_{c}_{kk}")
                            P.op("scalar", (lambda b_, c_, d_, a_: lambda e: e.activation(
                                d_[:, T * c_:T * (c_ + 1)], ps[b_][:, :], a_))(bank, c, dstT, act),
                                reads=[f"ps{bank}"], writes=[f"{dkey}:{c}"], tag=f"e{t}_{gi}_{c}")
                    for ec in range(4):
                        for kg in range(NKC // 4):
                            slot = kg % NWB
                            wr = [f"wvs:{4 * kg + i}" for i in range(4)]
                            dma("sync", "wst", (lambda g_, s_, e_: lambda ee: ee.dma_start(
                                out=wbuf[s_][:, :].rearrange("p (k c) -> p k c", k=4),
                                in_=wvs[e_, 4 * g_:4 * (g_ + 1), :, :].rearrange("k p c -> p k c")))(kg, slot, ec),
                                reads=wr, writes=[f"wbuf:{slot}"], tag=f"swv{t}_{ec}_{kg}")
                            for ki in range(4):
                                kk = 4 * kg + ki
                                for tc in range(4):
                                    P.op("tensor", (lambda k_, s_, ki_, tc_: lambda e: e.matmul(
                                        ps[tc_][:, :], xnT[:, T * k_ + 128 * tc_:T * k_ + 128 * (tc_ + 1)],
                                        wbuf[s_][:, 512 * ki_:512 * (ki_ + 1)],
                                        start=(k_ == 0), stop=(k_ == NKC - 1)))(kk, slot, ki, tc),
                                        reads=[f"wbuf:{slot}", f"xnT:{kk}"],
                                        writes=[f"ps{tc}"], tag=f"gv{t}_{ec}_{kk}_{tc}")
                        for tc in range(4):
                            P.op("scalar", (lambda e_, tc_: lambda e: e.activation(
                                v_tok[tc_][:, 512 * e_:512 * (e_ + 1)], ps[tc_][:, :], AF.Silu))(ec, tc),
                                reads=[f"ps{tc}"], writes=[f"vt:{tc}:{ec}"], tag=f"ev{t}_{ec}_{tc}")

                    NU = (T // BLK) * NH

                    def u_hb(u):
                        b, h = divmod(u, NH)
                        return h, b, T * h + b * BLK

                    def attn_pre(u):
                        h, b, hb = u_hb(u)
                        r = u % 4
                        pt = u % 2
                        for cc in range(2):
                            P.op("tensor", (lambda hb_, cc_, p_: lambda e: e.transpose(
                                psBt[p_][:, 128 * cc_:128 * (cc_ + 1)],
                                kT[:, hb_ + 128 * cc_:hb_ + 128 * (cc_ + 1)], eye[:, :]))(hb, cc, pt),
                                reads=[f"kT:{h}", "c:eye"], writes=[f"psBt{pt}"], tag=f"tk{t}_{u}_{cc}")
                        for cc in range(2):
                            P.op("scalar", (lambda h_, cc_, r_, p_: lambda e: e.activation(
                                ktok[r_][:, 128 * cc_:128 * (cc_ + 1)],
                                psBt[p_][:, 128 * cc_:128 * (cc_ + 1)], AF.Copy,
                                scale=kdec[:, 2 * h_ + cc_:2 * h_ + cc_ + 1]))(h, cc, r, pt),
                                reads=[f"psBt{pt}", "kdec"], writes=[f"ktok:{r}"], tag=f"ke{t}_{u}_{cc}")
                        P.op("vector", (lambda h_, hb_, r_: lambda e: e.tensor_mul(
                            qdi[r_][:, :], qT[:, hb_:hb_ + BLK],
                            qdbc[:, BLK * h_:BLK * (h_ + 1)]))(h, hb, r),
                            reads=[f"qT:{h}", f"qdbc:{h // 2}"],
                            writes=[f"qdi:{r}"], tag=f"qd{t}_{u}")
                        P.op("scalar", (lambda h_, r_: lambda e: e.activation(
                            kvb[r_][:, :], kv[:, HD * h_:HD * (h_ + 1)], AF.Copy))(h, r),
                            reads=[f"kv:{h}"], writes=[f"kvb:{r}"], tag=f"kc{t}_{u}")

                    def attn_qk(u):
                        h, b, hb = u_hb(u)
                        pqk = u % 2
                        for cc in range(2):
                            P.op("tensor", (lambda hb_, cc_, p_: lambda e: e.matmul(
                                ps[p_][:, BLK * cc_:BLK * (cc_ + 1)],
                                kT[:, hb_ + 128 * cc_:hb_ + 128 * (cc_ + 1)],
                                qT[:, hb_:hb_ + BLK], start=True, stop=True))(hb, cc, pqk),
                                reads=[f"kT:{h}", f"qT:{h}"], writes=[f"ps{pqk}"], tag=f"qk{t}_{u}_{cc}")
                        P.op("vector", (lambda h_, u_, p_: lambda e: e.tensor_mul(
                            qkd[u_ % 4][:, :], ps[p_][:, 0:2 * BLK],
                            ddT[:, 2 * BLK * h_:2 * BLK * (h_ + 1)]))(h, u, pqk),
                            reads=[f"ps{pqk}", f"ddT:{h}:0", f"ddT:{h}:1"],
                            writes=[f"qkd:{u % 4}"], tag=f"qm{t}_{u}")

                    def attn_body(u):
                        h, b, hb = u_hb(u)
                        r = u % 4
                        po = 2 + u % 2
                        pkv = 4 + u % 2
                        ve = h // 4
                        vsl = lambda cc: (2 * b + cc, h)
                        P.op("tensor", (lambda r_, p_: lambda e: e.matmul(
                            ps[p_][:, 0:BLK], kvb[r_][:, :], qdi[r_][:, :],
                            start=True, stop=False))(r, po),
                            reads=[f"kvb:{r}", f"qdi:{r}"], writes=[f"ps{po}"], tag=f"oi{t}_{u}")
                        for cc in range(2):
                            P.op("tensor", (lambda b_, cc_, h_, r_, p_: lambda e: e.matmul(
                                ps[p_][:, 0:BLK], v_tok[2 * b_ + cc_][:, 128 * h_:128 * (h_ + 1)],
                                qkd[r_][:, BLK * cc_:BLK * (cc_ + 1)],
                                start=False, stop=(cc_ == 1)))(b, cc, h, r, po),
                                reads=[f"vt:{2 * b + cc}:{ve}", f"qkd:{r}"],
                                writes=[f"ps{po}"], tag=f"oa{t}_{u}_{cc}")
                        P.op("vector", (lambda hb_, p_, h_, b_: lambda e: e.tensor_copy(
                            aT[:, hb_:hb_ + BLK], ps[p_][:, 0:BLK]))(hb, po, h, b),
                            reads=[f"ps{po}"], writes=[f"aT:{h}:{b}"], tag=f"oe{t}_{u}")
                        for cc in range(2):
                            P.op("tensor", (lambda b_, cc_, h_, r_, p_: lambda e: e.matmul(
                                ps[p_][:, 0:HD], ktok[r_][:, 128 * cc_:128 * (cc_ + 1)],
                                v_tok[2 * b_ + cc_][:, 128 * h_:128 * (h_ + 1)],
                                start=(cc_ == 0), stop=(cc_ == 1)))(b, cc, h, r, pkv),
                                reads=[f"ktok:{r}", f"vt:{2 * b + cc}:{ve}"],
                                writes=[f"ps{pkv}"], tag=f"km{t}_{u}_{cc}")
                        P.op("vector", (lambda h_, p_: lambda e: e.scalar_tensor_tensor(
                            kv[:, HD * h_:HD * (h_ + 1)], kv[:, HD * h_:HD * (h_ + 1)],
                            bd_vec[:, h_:h_ + 1], ps[p_][:, 0:HD], ALU.mult, ALU.add))(h, pkv),
                            reads=[f"ps{pkv}", f"kv:{h}", "bd_vec"],
                            writes=[f"kv:{h}"], tag=f"ku{t}_{u}")

                    for u in range(3):
                        attn_pre(u)
                    for u in range(NU):
                        attn_qk(u)
                        if u + 3 < NU:
                            attn_pre(u + 3)
                        attn_body(u)

                    def av_mm(h_):
                        P.op("tensor", (lambda hh: lambda e: e.matmul(
                            ps[0][0:1, 0:T], onescol_b[:, 0:1], sqs[hh % 4][:, :],
                            start=(hh == 0), stop=(hh == NH - 1)))(h_),
                            reads=[f"sqs:{h_ % 4}", "c:onescol"], writes=["ps0"], tag=f"av{t}_{h_}")
                    for h in range(NH):
                        P.op("scalar", (lambda h_: lambda e: e.activation(
                            sqs[h_ % 4][:, :], aT[:, T * h_:T * (h_ + 1)], AF.Square))(h),
                            reads=[f"aT:{h}:0", f"aT:{h}:1"], writes=[f"sqs:{h % 4}"], tag=f"as{t}_{h}")
                        if h >= 2:
                            av_mm(h - 2)
                    av_mm(NH - 2)
                    av_mm(NH - 1)
                    P.op("vector", lambda e: e.tensor_single_scalar(
                        arow[0:1, 0:T], ps[0][0:1, 0:T], 1.0 / D, ALU.mult),
                        reads=["ps0"], writes=["arow"], tag=f"an1{t}")
                    P.op("scalar", lambda e: e.activation(arow2[0:1, 0:T], arow[0:1, 0:T], AF.Sqrt, bias=cb6[0:1, 0:1]),
                         reads=["arow", "c:cb6"], writes=["arow2"], tag=f"an2{t}")
                    P.op("vector", lambda e: e.reciprocal(arow[0:1, 0:T], arow2[0:1, 0:T]),
                         reads=["arow2"], writes=["arow"], tag=f"an3{t}")
                    P.op("tensor", lambda e: e.matmul(ps[1][:, 0:T], onesr[0:1, :], arow[0:1, 0:T], start=True, stop=True),
                         reads=["arow", "c:onesr"], writes=["ps1"], tag=f"an4{t}")
                    P.op("vector", lambda e: e.tensor_copy(rstdA[:, 0:T], ps[1][:, 0:T]),
                         reads=["ps1"], writes=["rstdA"], tag=f"an5{t}")
                    for h in range(NH):
                        P.op("vector", (lambda h_: lambda e: e.scalar_tensor_tensor(
                            sqs[h_ % 4][:, :], aT[:, T * h_:T * (h_ + 1)], nwa[:, h_:h_ + 1],
                            rstdA[:, 0:T], ALU.mult, ALU.mult))(h),
                            reads=[f"aT:{h}:0", f"aT:{h}:1", "rstdA", "c:nwa"],
                            writes=[f"sqs:{h % 4}"], tag=f"nm{t}_{h}")
                        P.op("vector", (lambda h_: lambda e: e.tensor_mul(
                            aT[:, T * h_:T * (h_ + 1)], sqs[h_ % 4][:, :],
                            gateT[:, T * h_:T * (h_ + 1)]))(h),
                            reads=[f"sqs:{h % 4}", f"gateT:{h}"],
                            writes=[f"aT:{h}:0", f"aT:{h}:1", f"ga:{h}"], tag=f"gm{t}_{h}")

                    for dchunk in range(4):
                        for kg in range(NKC // 4):
                            slot = kg % NWB
                            wr = [f"wos:{4 * kg + i}" for i in range(4)]
                            dma("sync", "wst", (lambda g_, s_, d_: lambda e: e.dma_start(
                                out=wbuf[s_][:, :].rearrange("p (k c) -> p k c", k=4),
                                in_=wos[d_, 4 * g_:4 * (g_ + 1), :, :].rearrange("k p c -> p k c")))(kg, slot, dchunk),
                                reads=wr, writes=[f"wbuf:{slot}"], tag=f"so{t}_{dchunk}_{kg}")
                            for ki in range(4):
                                kk = 4 * kg + ki
                                for tc in range(4):
                                    P.op("tensor", (lambda k_, s_, ki_, tc_: lambda e: e.matmul(
                                        ps[2 + tc_][:, :],
                                        aT[:, T * k_ + 128 * tc_:T * k_ + 128 * (tc_ + 1)],
                                        wbuf[s_][:, 512 * ki_:512 * (ki_ + 1)],
                                        start=(k_ == 0), stop=(k_ == NKC - 1)))(kk, slot, ki, tc),
                                        reads=[f"wbuf:{slot}", f"ga:{kk}"],
                                        writes=[f"ps{2 + tc}"], tag=f"op{t}_{dchunk}_{kk}_{tc}")
                        for tc in range(4):
                            so = (dchunk * 4 + tc) % 2
                            P.op("scalar", (lambda tc_, s_: lambda e: e.activation(
                                aostg[s_][:, :], ps[2 + tc_][:, :], AF.Copy))(tc, so),
                                reads=[f"ps{2 + tc}"], writes=[f"aostg:{so}"], tag=f"ae{t}_{dchunk}_{tc}")
                            dma("scalar", "ao", (lambda t_, tc_, d_, s_: lambda e: e.dma_start(
                                out=aos[T * t_ + 128 * tc_:T * t_ + 128 * (tc_ + 1),
                                        512 * d_:512 * (d_ + 1)],
                                in_=aostg[s_][:, :]))(t, tc, dchunk, so),
                                reads=[f"aostg:{so}"], writes=[f"aos:{t}"], tag=f"aw{t}_{dchunk}_{tc}")

            P.barrier()
        es3 = contextlib.ExitStack()
        sb3 = lambda n, sh, dt: es3.enter_context(nc.sbuf_tensor(n, sh, dt))
        with es3:
            xqT = sb3("xqT", [128, NKC * T], BF16)
            gat = sb3("gat", [128, NJ2 * T], BF16)
            iqT = sb3("iqT", [128, NJ2 * T], BF16)
            a_scr = [sb3(f"ascr{i}", [128, T], F32) for i in range(2)]
            sq2 = [sb3(f"sq2_{i}", [128, T], BF16) for i in range(4)]
            tmpb = [sb3(f"tmpb{i}", [128, T], BF16) for i in range(2)]
            Mb = [sb3(f"Mb{i}", [128, T], BF16) for i in range(2)]
            mbt = [sb3(f"mbt{i}", [128, 128], BF16) for i in range(2)]
            Ub = sb3("Ub", [128, T], F32)
            u16b = [sb3(f"u16b{i}", [128, T], F16) for i in range(2)]
            qrow = sb3("qrow", [1, T], F32)
            qt_ = sb3("qstats", [128, 88], F32)
            QK = ["ss", "M", "var", "sd", "rstd", "c", "rc", "s2", "is2", "U"]
            iqs = [sb3(f"iqs{i}", [128, T], F32) for i in range(2)]
            ps6 = es3.enter_context(nc.psum_tensor("ps6", [128, 512], F32))
            ps7 = es3.enter_context(nc.psum_tensor("ps7", [128, 512], F32))
            ps67 = [ps6, ps7]
            wbuf2 = [sb3(f"wbuf2_{i}", [128, 2048], BF16) for i in range(2)]
            xres = [sb3(f"xres{i}", [128, 512], F32) for i in range(2)]
            aores = [sb3(f"aores{i}", [128, 512], BF16) for i in range(2)]
            ostg = [sb3(f"ostg{i}", [128, 512], F32) for i in range(2)]

            def p4_txq(t):
                for kk in range(NKC):
                    qq = ("scalar", "tr") if kk % 2 == 0 else ("sync", "trs")
                    dma(qq[0], qq[1], (lambda t_, k_: lambda e: e.dma_start(
                        out=xqT[:, T * k_:T * (k_ + 1)],
                        in_=xqs[128 * k_:128 * (k_ + 1), T * t_:T * (t_ + 1)]))(t, kk),
                        reads=[f"xqs:{t}"], writes=[f"xqT:{kk}"], tag=f"txq{t}_{kk}")

            def p4_w13(t):
                pq = t % 2
                P.op("vector", (lambda p_: lambda e: e.memset(Mb[p_][:, :], 0.0))(pq),
                     reads=[], writes=[f"Mb{pq}"], tag=f"mz{t}")

                def ss_mm(j_):
                    P.op("tensor", (lambda jx: lambda e: e.matmul(
                        ps[4][0:1, 0:T], onescol_b[:, 0:1], sq2[jx % 4][:, :],
                        start=(jx == 0), stop=(jx == NJ2 - 1)))(j_),
                        reads=[f"sq2:{j_ % 4}", "c:onescol"], writes=["ps4"], tag=f"ss{t}_{j_}")
                for jj in range(NJ2):
                    b1 = 2 * (jj % 2)
                    b3 = b1 + 1
                    for half, (jc, bank) in enumerate([(jj, b1), (jj + NJ2, b3)]):
                        slot = (2 * jj + half) % NWB
                        dma("sync", "wst", (lambda j_, s_: lambda e: e.dma_start(
                            out=wbuf[s_][:, :], in_=w13s[j_, :, :]))(jc, slot),
                            reads=[f"w13s:{jc}"], writes=[f"wbuf:{slot}"], tag=f"s13_{t}_{jc}")
                        for kk in range(NKC):
                            P.op("tensor", (lambda s_, b_, k_: lambda e: e.matmul(
                                ps[b_][:, :], wbuf[s_][:, 128 * k_:128 * (k_ + 1)],
                                xqT[:, T * k_:T * (k_ + 1)], start=(k_ == 0), stop=(k_ == NKC - 1)))(slot, bank, kk),
                                reads=[f"wbuf:{slot}", f"xqT:{kk}"],
                                writes=[f"ps{bank}"], tag=f"w13_{t}_{jc}_{kk}")
                    asl = jj % 2
                    P.op("scalar", (lambda b_, s_: lambda e: e.activation(
                        a_scr[s_][:, :], ps[b_][:, :], AF.Relu))(b1, asl),
                        reads=[f"ps{b1}"], writes=[f"ascr:{asl}"], tag=f"rl{t}_{jj}")
                    P.op("scalar", (lambda s_: lambda e: e.activation(
                        a_scr[s_][:, :], a_scr[s_][:, :], AF.Square))(asl),
                        reads=[f"ascr:{asl}"], writes=[f"ascr:{asl}"], tag=f"sr{t}_{jj}")
                    P.op("vector", (lambda j_, b_, s_: lambda e: e.tensor_mul(
                        gat[:, T * j_:T * (j_ + 1)], a_scr[s_][:, :], ps[b_][:, :]))(jj, b3, asl),
                        reads=[f"ascr:{asl}", f"ps{b3}"], writes=[f"gat:{jj}"], tag=f"gm{t}_{jj}")
                    P.op("gpsimd", (lambda j_, s_: lambda e: e.tensor_mul(
                        sq2[s_][:, :], gat[:, T * j_:T * (j_ + 1)],
                        gat[:, T * j_:T * (j_ + 1)]))(jj, jj % 4),
                        reads=[f"gat:{jj}"], writes=[f"sq2:{jj % 4}"], tag=f"g2{t}_{jj}")
                    if jj >= 2:
                        ss_mm(jj - 2)
                    if jj == NJ2 - 1:
                        ss_mm(NJ2 - 2)
                        ss_mm(NJ2 - 1)
                    P.op("scalar", (lambda j_, s_: lambda e: e.activation(
                        tmpb[s_][:, :], gat[:, T * j_:T * (j_ + 1)], AF.Abs,
                        scale=nwf[:, j_:j_ + 1]))(jj, asl),
                        reads=[f"gat:{jj}", "c:nwf"], writes=[f"tmpb:{asl}"], tag=f"ab{t}_{jj}")
                    P.op("vector", (lambda s_, p_: lambda e: e.tensor_max(
                        Mb[p_][:, :], Mb[p_][:, :], tmpb[s_][:, :]))(asl, pq),
                        reads=[f"tmpb:{asl}", f"Mb{pq}"], writes=[f"Mb{pq}"], tag=f"mm{t}_{jj}")
                    yield

            def p4_quant2_iq(t):
                pq = t % 2
                qt = {k: qt_[:, 8 * i + 4 * pq:8 * i + 4 * (pq + 1)] for i, k in enumerate(QK)}
                for tc in range(4):
                    dma("scalar", "trv", (lambda tc_, p_: lambda e: e.dma_start(
                        out=mbt[tc_ % 2][:, :], in_=Mb[p_][:, 128 * tc_:128 * (tc_ + 1)],
                        transpose=True))(tc, pq),
                        reads=[f"Mb{pq}"], writes=[f"mbt:{tc % 2}"], tag=f"mx{t}_{tc}")
                    P.op("vector", (lambda tc_, q_: lambda e: e.reduce_max(
                        q_["M"][:, tc_:tc_ + 1], mbt[tc_ % 2][:, :], axis=AX.X))(tc, qt),
                        reads=[f"mbt:{tc % 2}"], writes=[f"qt{pq}:M"], tag=f"mr{t}_{tc}")
                P.op("vector", lambda e: e.tensor_copy(qrow[0:1, 0:T], ps[4][0:1, 0:T]),
                     reads=["ps4"], writes=["qrow"], tag=f"sc{t}")
                for tc in range(4):
                    P.op("tensor", (lambda tc_: lambda e: e.matmul(
                        ps[5][:, tc_:tc_ + 1], qrow[0:1, 128 * tc_:128 * (tc_ + 1)],
                        onesr[0:1, 0:1], start=True, stop=True))(tc),
                        reads=["qrow", "c:onesr"], writes=["ps5"], tag=f"st{t}_{tc}")
                P.op("vector", (lambda q_: lambda e: e.tensor_copy(q_["ss"][:, 0:4], ps[5][:, 0:4]))(qt),
                     reads=["ps5"], writes=[f"qt{pq}:ss"], tag=f"se{t}")
                P.op("vector", (lambda q_: lambda e: e.tensor_single_scalar(
                    q_["var"][:, :], q_["ss"][:, :], 1.0 / DF, ALU.mult))(qt),
                    reads=[f"qt{pq}:ss"], writes=[f"qt{pq}:var"], tag=f"q1{t}")
                P.op("scalar", (lambda q_: lambda e: e.activation(
                    q_["sd"][:, :], q_["var"][:, :], AF.Sqrt, bias=cb5[:, 0:1]))(qt),
                    reads=[f"qt{pq}:var", "c:cb5"], writes=[f"qt{pq}:sd"], tag=f"q2{t}")

                def q2c(e, q_=qt):
                    e.reciprocal(q_["rstd"][:, :], q_["sd"][:, :])
                    e.drain()
                    e.tensor_mul(q_["c"][:, :], q_["rstd"][:, :], q_["M"][:, :])
                    e.drain()
                    e.tensor_single_scalar(q_["c"][:, :], q_["c"][:, :], 1e-5, ALU.max)
                    e.drain()
                    e.reciprocal(q_["rc"][:, :], q_["c"][:, :])
                    e.drain()
                    e.tensor_single_scalar(q_["s2"][:, :], q_["rc"][:, :], 127.0, ALU.mult)
                    e.tensor_single_scalar(q_["is2"][:, :], q_["c"][:, :], 1.0 / 127.0, ALU.mult)
                    e.drain()
                    e.tensor_mul(q_["U"][:, :], q_["rstd"][:, :], q_["s2"][:, :])
                    return e.drain()
                P.op("vector", q2c,
                     reads=[f"qt{pq}:sd", f"qt{pq}:M"],
                     writes=[f"qt{pq}:U", f"qt{pq}:is2"], tag=f"q3{t}")
                for tc in range(4):
                    P.op("tensor", (lambda tc_, q_: lambda e: e.transpose(
                        ps[5][0:1, 128 * tc_:128 * (tc_ + 1)],
                        q_["U"][:, tc_:tc_ + 1], eyef[:, :]))(tc, qt),
                        reads=[f"qt{pq}:U", "c:eyef"], writes=["ps5"], tag=f"ut{t}_{tc}")
                P.op("vector", lambda e: e.tensor_copy(qrow[0:1, 0:T], ps[5][0:1, 0:T]),
                     reads=["ps5"], writes=["qrow"], tag=f"ue{t}")
                P.op("tensor", lambda e: e.matmul(ps[5][:, 0:T], onesr[0:1, :], qrow[0:1, 0:T], start=True, stop=True),
                     reads=["qrow", "c:onesr"], writes=["ps5"], tag=f"ub{t}")
                P.op("vector", lambda e: e.tensor_copy(Ub[:, :], ps[5][:, 0:T]),
                     reads=["ps5"], writes=["Ub"], tag=f"uc{t}")
            def p4_iq(t):
                def i3(jj):
                    s_ = jj % 2
                    P.op("gpsimd", (lambda j_, sl_: lambda e: e.tensor_scalar(
                        iqT[:, T * j_:T * (j_ + 1)], u16b[sl_][:, :], 1.0, -R16,
                        ALU.mult, ALU.add))(jj, s_),
                        reads=[f"u16b:{s_}"], writes=[f"iqT:{jj}"], tag=f"i3{t}_{jj}")
                for jj in range(NJ2):
                    s_ = jj % 2
                    P.op("gpsimd", (lambda j_, sl_: lambda e: e.tensor_mul(
                        iqs[sl_][:, :], gat[:, T * j_:T * (j_ + 1)], Ub[:, :]))(jj, s_),
                        reads=[f"gat:{jj}", "Ub"], writes=[f"iqs:{s_}"], tag=f"i1{t}_{jj}")
                    P.op("scalar", (lambda j_, sl_: lambda e: e.activation(
                        u16b[sl_][:, :], iqs[sl_][:, :], AF.Identity, bias=cbr[:, 0:1],
                        scale=nwf[:, j_:j_ + 1]))(jj, s_),
                        reads=[f"iqs:{s_}", "c:cbr", "c:nwf"], writes=[f"u16b:{s_}"], tag=f"i2{t}_{jj}")
                    if jj >= 1:
                        i3(jj - 1)
                    if jj == NJ2 - 1:
                        i3(jj)
                    yield

            def p4_w2(t):
                pq = t % 2
                qt = {k: qt_[:, 8 * i + 4 * pq:8 * i + 4 * (pq + 1)] for i, k in enumerate(QK)}
                for dchunk in range(4):
                    for half in range(2):
                        for kg in range(NJ2 // 4):
                            slot = kg % 2
                            wr = [f"w2s:{4 * kg + i}" for i in range(4)]
                            dma("sync", "w2l", (lambda g_, s_, d_: lambda e: e.dma_start(
                                out=wbuf2[s_][:, :].rearrange("p (k c) -> p k c", k=4),
                                in_=w2s[d_, 4 * g_:4 * (g_ + 1), :, :].rearrange("k p c -> p k c")))(kg, slot, dchunk),
                                reads=wr, writes=[f"wbuf2:{slot}"], tag=f"s2_{t}_{dchunk}_{half}_{kg}")
                            for ki in range(4):
                                kk = 4 * kg + ki
                                for tco in range(2):
                                    tc = 2 * half + tco
                                    P.op("tensor", (lambda k_, s_, ki_, tc_, b_: lambda e: e.matmul(
                                        ps67[b_][:, :], iqT[:, T * k_ + 128 * tc_:T * k_ + 128 * (tc_ + 1)],
                                        wbuf2[s_][:, 512 * ki_:512 * (ki_ + 1)],
                                        start=(k_ == 0), stop=(k_ == NJ2 - 1)))(kk, slot, ki, tc, tco),
                                        reads=[f"wbuf2:{slot}", f"iqT:{kk}"],
                                        writes=[f"ps{6 + tco}"], tag=f"w2_{t}_{dchunk}_{kk}_{tc}")
                        for tco in range(2):
                            tc = 2 * half + tco
                            sl = tc % 2
                            dma("sync", "rl", (lambda t_, tc_, d_, s_: lambda e: e.dma_start(
                                out=xres[s_][:, :],
                                in_=x_h[T * t_ + 128 * tc_:T * t_ + 128 * (tc_ + 1),
                                        512 * d_:512 * (d_ + 1)]))(t, tc, dchunk, sl),
                                reads=[], writes=[f"xres:{sl}"], tag=f"lx{t}_{dchunk}_{tc}")
                            dma("sync", "rl", (lambda t_, tc_, d_, s_: lambda e: e.dma_start(
                                out=aores[s_][:, :],
                                in_=aos[T * t_ + 128 * tc_:T * t_ + 128 * (tc_ + 1),
                                        512 * d_:512 * (d_ + 1)]))(t, tc, dchunk, sl),
                                reads=[f"aos:{t}"], writes=[f"aores:{sl}"], tag=f"la{t}_{dchunk}_{tc}")
                            P.op("vector", (lambda tc_, s_, b_, q_: lambda e: e.scalar_tensor_tensor(
                                ostg[s_][:, :], ps67[b_][:, :], q_["is2"][:, tc_:tc_ + 1],
                                xres[s_][:, :], ALU.mult, ALU.add))(tc, sl, tco, qt),
                                reads=[f"ps{6 + tco}", f"xres:{sl}", f"qt{pq}:is2"],
                                writes=[f"ostg:{sl}"], tag=f"o1{t}_{dchunk}_{tc}")
                            P.op("gpsimd", (lambda s_: lambda e: e.tensor_add(
                                ostg[s_][:, :], ostg[s_][:, :], aores[s_][:, :]))(sl),
                                reads=[f"ostg:{sl}", f"aores:{sl}"], writes=[f"ostg:{sl}"], tag=f"o2{t}_{dchunk}_{tc}")
                            dma("sync", "ol", (lambda t_, tc_, d_, s_: lambda e: e.dma_start(
                                out=out_h[T * t_ + 128 * tc_:T * t_ + 128 * (tc_ + 1),
                                          512 * d_:512 * (d_ + 1)],
                                in_=ostg[s_][:, :]))(t, tc, dchunk, sl),
                                reads=[f"ostg:{sl}"], writes=[f"odone:{sl}"], tag=f"wo{t}_{dchunk}_{tc}")
                        yield

            W2SLOTS = {16, 20, 24, 28, 32, 36, 40}
            for t in range(nt):
                p4_txq(t)
                if t > 0:
                    p4_quant2_iq(t - 1)
                g13 = p4_w13(t)
                giq = p4_iq(t - 1) if t > 0 else None
                g2 = p4_w2(t - 1) if t > 0 else None
                for npair in range(NJ2):
                    if giq is not None:
                        next(giq, None)
                        next(giq, None)
                        next(giq, None)
                    next(g13)
                    if g2 is not None and npair in W2SLOTS:
                        next(g2, None)
                for _ in g13:
                    pass
                if giq is not None:
                    for _ in giq:
                        pass
                if g2 is not None:
                    for _ in g2:
                        pass
            p4_quant2_iq(nt - 1)
            for _ in p4_iq(nt - 1):
                pass
            for _ in p4_w2(nt - 1):
                pass

            P.resolve()
            with nc.Block() as blk:
                @blk.sync
                def _(eng):
                    P.emit("sync", eng, prog_sems, dma_sems)
                    r = Prog.RING["ol"]
                    tot = P.dma_tot["ol"]
                    for i in range(r):
                        n_i = tot // r + (1 if i < tot % r else 0)
                        if n_i:
                            eng.wait_ge(dma_sems[("ol", i)], 16 * n_i)

                @blk.vector
                def _(eng):
                    P.emit("vector", eng, prog_sems, dma_sems)

                @blk.scalar
                def _(eng):
                    P.emit("scalar", eng, prog_sems, dma_sems)

                @blk.tensor
                def _(eng):
                    P.emit("tensor", eng, prog_sems, dma_sems)

                @blk.gpsimd
                def _(eng):
                    P.emit("gpsimd", eng, prog_sems, dma_sems)

    return nc


def _host_consts():
    c = {}
    c["eye16"] = np.eye(128, dtype=ml_dtypes.bfloat16)
    c["eyef"] = np.eye(128, dtype=np.float32)
    c["prow"] = np.arange(128, dtype=np.float32).reshape(1, 128)
    c["mrow"] = np.arange(BLK, dtype=np.float32).reshape(1, BLK)
    c["onesrow"] = np.ones((1, 128), dtype=np.float32)
    p = np.arange(128).reshape(128, 1)
    m = np.arange(2 * BLK).reshape(1, 2 * BLK)
    c["mask"] = ((m % BLK) >= (128 * (m // BLK) + p)).astype(np.float32)
    return c


def _blocked_lhsT(wt):
    Dd, ncol = wt.shape
    nk, ncch = Dd // 128, ncol // 128
    return np.ascontiguousarray(
        wt.reshape(nk, 128, ncch, 128).transpose(2, 1, 0, 3).reshape(ncch, 128, Dd))


def kernel(L0_attn_hidden_states, L0_attn_slope_rate, L0_attn_w_qkv,
           L0_attn_w_gate, L0_attn_w_out, L0_attn_norm_w,
           L0_ffn_w13, L0_ffn_w2, L0_ffn_norm_w):
    x = np.ascontiguousarray(np.asarray(L0_attn_hidden_states, dtype=np.float32))
    wqkv = np.asarray(L0_attn_w_qkv, dtype=np.float32).reshape(NH, 3, HD, D)
    wq = _blocked_lhsT(np.ascontiguousarray(wqkv[:, 0].reshape(D, D).T))
    wk = _blocked_lhsT(np.ascontiguousarray(wqkv[:, 1].reshape(D, D).T))
    wv = np.ascontiguousarray(wqkv[:, 2].reshape(D, D).T)
    wg = _blocked_lhsT(np.ascontiguousarray(np.asarray(L0_attn_w_gate, np.float32).T))
    wo = np.ascontiguousarray(np.asarray(L0_attn_w_out, np.float32).T)
    w13 = _blocked_lhsT(np.ascontiguousarray(np.asarray(L0_ffn_w13, np.float32).T))
    w2t = np.ascontiguousarray(np.asarray(L0_ffn_w2, np.float32).T)
    slope = np.ascontiguousarray(np.asarray(L0_attn_slope_rate, np.float32).reshape(1, NH))
    nwa = np.ascontiguousarray(np.asarray(L0_attn_norm_w, np.float32).reshape(NH, 128).T)
    nwf = np.ascontiguousarray(np.asarray(L0_ffn_norm_w, np.float32).reshape(NJ2, 128).T)
    consts = _host_consts()

    nc = build_nc(seq=S)
    base = dict(wq=wq, wk=wk, wv=wv, wg=wg, wo=wo, w13=w13, w2t=w2t,
                slope=slope, nwa=nwa, nwf=nwf, **consts)
    in_maps = [dict(base, x=np.ascontiguousarray(x[b])) for b in range(NCORES)]
    res = run_bass_kernel_spmd(nc, in_maps, core_ids=list(range(NCORES)))
    out = np.stack([np.asarray(res.results[b]["out"]) for b in range(NCORES)], axis=0)
    return out.astype(np.float32)


# version 16
# speedup vs baseline: 1.1760x; 1.1045x over previous
import numpy as np
import ml_dtypes
import contextlib
import concourse.bass as bass
import concourse.mybir as mybir
from concourse.bass_utils import run_bass_kernel_spmd
from concourse.alu_op_type import AluOpType as ALU

F32 = mybir.dt.float32
BF16 = mybir.dt.bfloat16
F16 = mybir.dt.float16
AF = mybir.ActivationFunctionType
AX = mybir.AxisListType

NCORES = 8
D = 2048
NH = 16
HD = 128
BLK = 256
DF = 5632
S = 4096
T = 512
NKC = D // 128
NJ = 2 * DF // 128
NJ2 = DF // 128
LN_EPS = 1e-5
AN_EPS = 1e-6
RMS_EPS = 1e-5
R16 = 1536.0

ENGS = ("tensor", "vector", "scalar", "gpsimd", "sync")


class Prog:

    def __init__(self):
        self.ops = []
        self.last_w = {}
        self.readers = {}
        self.bar_deps = []
        self.nbar = 0

    def op(self, engine, fn, reads=(), writes=(), dma=None, tag=""):
        idx = len(self.ops)
        deps = set()
        for b in reads:
            if b in self.last_w:
                deps.add(self.last_w[b])
        for b in writes:
            if b in self.last_w:
                deps.add(self.last_w[b])
            for r in self.readers.get(b, ()):
                deps.add(r)
        deps.update(self.bar_deps)
        deps.discard(idx)
        if dma is not None:
            if not hasattr(self, "dma_eng"):
                self.dma_eng = {}
            assert self.dma_eng.setdefault(dma, engine) == engine, \
                f"dma sem {dma} used from {engine} and {self.dma_eng[dma]}"
        self.ops.append(dict(idx=idx, engine=engine, fn=fn, deps=deps,
                             dma=dma, dma_idx=None, tag=tag, inc=None,
                             is_bar=False))
        for b in reads:
            self.readers.setdefault(b, []).append(idx)
        for b in writes:
            self.last_w[b] = idx
            self.readers[b] = []
        return idx

    def barrier(self):
        self.nbar += 1
        prior = list(range(len(self.ops)))
        bar_idxs = []
        for e in ENGS:
            idx = len(self.ops)
            self.ops.append(dict(idx=idx, engine=e, fn=None, deps=set(prior),
                                 dma=None, dma_idx=None, tag=f"bar{self.nbar}",
                                 inc=None, is_bar=True))
            bar_idxs.append(idx)
        self.bar_deps = bar_idxs

    RING = {"c": 1, "wl": 3, "ws": 2, "wst": 6, "xl": 3, "aw": 2, "tr": 4,
            "rl": 4, "ol": 2, "tb": 1, "tbw": 1, "ao": 2, "wsa": 2, "wla": 3,
            "trk": 4, "trv": 4, "trs": 4, "w2l": 4}

    def resolve(self):
        eng_seq = {}
        seq_ctr = {e: 0 for e in ENGS}
        self.dma_tot = {}
        for o in self.ops:
            eng_seq[o["idx"]] = seq_ctr[o["engine"]]
            seq_ctr[o["engine"]] += 1
            if o["dma"] is not None:
                k = self.dma_tot.get(o["dma"], 0)
                self.dma_tot[o["dma"]] = k + 1
                r = self.RING[o["dma"]]
                o["dma_sub"] = (o["dma"], k % r)
                o["dma_val"] = 16 * (k // r + 1)
        needs_inc = set()
        for o in self.ops:
            for d in o["deps"]:
                dop = self.ops[d]
                if dop["dma"] is None and not (
                        dop["engine"] == "tensor" and o["engine"] == "tensor"):
                    needs_inc.add(d)
        for o in self.ops:
            if o["is_bar"]:
                needs_inc.add(o["idx"])
        self.inc_val = {}
        ctr = {e: 0 for e in ENGS}
        for o in self.ops:
            if o["idx"] in needs_inc:
                ctr[o["engine"]] += 1
                o["inc"] = ctr[o["engine"]]
                self.inc_val[o["idx"]] = ctr[o["engine"]]
        self.eng_seq = eng_seq
        self.by_eng = {e: [o for o in self.ops if o["engine"] == e] for e in ENGS}

    def emit(self, ename, eng_obj, prog_sems, dma_sems):
        waited = {}
        drained_upto = 0
        for o in self.by_eng[ename]:
            need_drain = False
            waits = {}
            for d in sorted(o["deps"]):
                dop = self.ops[d]
                if dop["dma"] is not None:
                    key = ("dma",) + dop["dma_sub"]
                    val = dop["dma_val"]
                    if waits.get(key, 0) < val:
                        waits[key] = val
                elif dop["engine"] == ename:
                    if ename != "tensor":
                        v = self.inc_val[d]
                        key = ("prog", ename)
                        if waits.get(key, 0) < v:
                            waits[key] = v
                else:
                    v = self.inc_val[d]
                    key = ("prog", dop["engine"])
                    if waits.get(key, 0) < v:
                        waits[key] = v
            if o["dma"] is not None and o["dma_val"] > 16:
                key = ("dma",) + o["dma_sub"]
                if waits.get(key, 0) < o["dma_val"] - 16:
                    waits[key] = max(waits.get(key, 0), o["dma_val"] - 16)
            for key, val in sorted(waits.items(), key=lambda kv: str(kv[0])):
                if waited.get(key, 0) >= val:
                    continue
                sem = dma_sems[key[1:]] if key[0] == "dma" else prog_sems[key[1]]
                eng_obj.wait_ge(sem, val)
                waited[key] = val
            if o["is_bar"]:
                inst = eng_obj.drain()
                drained_upto = self.eng_seq[o["idx"]]
            else:
                inst = o["fn"](eng_obj)
            if o["inc"] is not None:
                assert inst is not None, f"op {o['tag']} returned None but needs inc"
                inst.then_inc(prog_sems[ename], 1)
            if o["dma"] is not None:
                inst.then_inc(dma_sems[o["dma_sub"]], 16)


def build_nc(seq=S):
    nt = seq // T
    nxc = seq // 128
    nc = bass.Bass("TRN2", target_bir_lowering=False, debug=False)

    x_h = nc.dram_tensor("x", [seq, D], F32, kind="ExternalInput").ap()
    wq_h = nc.dram_tensor("wq", [NKC, 128, D], F32, kind="ExternalInput").ap()
    wk_h = nc.dram_tensor("wk", [NKC, 128, D], F32, kind="ExternalInput").ap()
    wv_h = nc.dram_tensor("wv", [D, D], F32, kind="ExternalInput").ap()
    wg_h = nc.dram_tensor("wg", [NKC, 128, D], F32, kind="ExternalInput").ap()
    wo_h = nc.dram_tensor("wo", [D, D], F32, kind="ExternalInput").ap()
    w13_h = nc.dram_tensor("w13", [NJ, 128, D], F32, kind="ExternalInput").ap()
    w2_h = nc.dram_tensor("w2t", [DF, D], F32, kind="ExternalInput").ap()
    slope_h = nc.dram_tensor("slope", [1, NH], F32, kind="ExternalInput").ap()
    nwa_h = nc.dram_tensor("nwa", [128, NH], F32, kind="ExternalInput").ap()
    nwf_h = nc.dram_tensor("nwf", [128, NJ2], F32, kind="ExternalInput").ap()
    eye_h = nc.dram_tensor("eye16", [128, 128], BF16, kind="ExternalInput").ap()
    eyef_h = nc.dram_tensor("eyef", [128, 128], F32, kind="ExternalInput").ap()
    prow_h = nc.dram_tensor("prow", [1, 128], F32, kind="ExternalInput").ap()
    mrow_h = nc.dram_tensor("mrow", [1, BLK], F32, kind="ExternalInput").ap()
    onesr_h = nc.dram_tensor("onesrow", [1, 128], F32, kind="ExternalInput").ap()
    mask_h = nc.dram_tensor("mask", [128, 2 * BLK], BF16, kind="ExternalInput").ap()
    out_h = nc.dram_tensor("out", [seq, D], F32, kind="ExternalOutput").ap()

    wqs = nc.dram_tensor("wqs", [NKC, 128, D], BF16, kind="Internal").ap()
    wks = nc.dram_tensor("wks", [NKC, 128, D], BF16, kind="Internal").ap()
    wvs = nc.dram_tensor("wvs", [4, NKC, 128, 512], BF16, kind="Internal").ap()
    wgs = nc.dram_tensor("wgs", [NKC, 128, D], BF16, kind="Internal").ap()
    wos = nc.dram_tensor("wos", [4, NKC, 128, 512], BF16, kind="Internal").ap()
    w13s = nc.dram_tensor("w13s", [NJ, 128, D], BF16, kind="Internal").ap()
    w2s = nc.dram_tensor("w2s", [4, NJ2, 128, 512], BF16, kind="Internal").ap()
    xns = nc.dram_tensor("xns", [D, seq], BF16, kind="Internal").ap()
    xqs = nc.dram_tensor("xqs", [D, seq], BF16, kind="Internal").ap()
    aos = nc.dram_tensor("aos", [seq, D], BF16, kind="Internal").ap()
    tbl = nc.dram_tensor("tbl", [2, NH * BLK], F32, kind="Internal").ap()

    P = Prog()

    def dma(qeng, sem, fn, reads, writes, tag=""):
        return P.op(qeng, fn, reads=reads, writes=writes, dma=sem, tag=tag)

    es = contextlib.ExitStack()
    with es:
        sbp = lambda n, sh, dt: es.enter_context(nc.sbuf_tensor(n, sh, dt))
        eye = sbp("eye", [128, 128], BF16)

        onesr = sbp("onesr_s", [1, 128], F32)
        onescol_b = sbp("onescol_b", [128, 1], BF16)
        mask = sbp("mask_s", [128, 2 * BLK], BF16)
        nwa = sbp("nwa_s", [128, NH], F32)
        nwf = sbp("nwf_s", [128, NJ2], F32)
        srow = sbp("srow", [1, NH], F32)
        bd_vec = sbp("bd_vec", [128, NH], F32)
        cb5 = sbp("cb5", [128, 1], F32)
        cb6 = sbp("cb6", [128, 1], F32)
        cbr = sbp("cbr", [128, 1], F32)
        kdec = sbp("kdec", [128, 2 * NH], F32)
        NWB = 5
        wbuf = [sbp(f"wbuf{i}", [128, D], BF16) for i in range(NWB)]
        wbufS = [sbp(f"wbufS{i}", [128, D], BF16) for i in range(2)]
        wf32 = [sbp(f"wf32{i}", [128, D], F32) for i in range(2)]

        ps = [es.enter_context(nc.psum_tensor(f"ps{i}", [128, 512], F32))
              for i in range(6)]

        prog_sems = {e: es.enter_context(nc.semaphore(f"pg_{e}")) for e in ENGS}
        dma_sems = {}
        for k, r in Prog.RING.items():
            for i in range(r):
                dma_sems[(k, i)] = es.enter_context(nc.semaphore(f"dm_{k}{i}"))

        esAB = contextlib.ExitStack()
        sbab = lambda n, sh, dt: esAB.enter_context(nc.sbuf_tensor(n, sh, dt))
        with esAB:
            qdbc = sbab("qdbc", [128, NH * BLK], BF16)
            ddT = sbab("ddT", [128, NH * 2 * BLK], BF16)
            kv = sbab("kv", [128, NH * HD], F32)

            es1 = contextlib.ExitStack()
            sb1 = lambda n, sh, dt: es1.enter_context(nc.sbuf_tensor(n, sh, dt))
            with es1:
                rA = sb1("rowA", [1, NH * BLK], F32)
                rB = sb1("rowB", [1, NH * BLK], F32)
                rC = sb1("rowC", [1, NH * BLK], F32)
                rD = sb1("rowD", [1, NH * BLK], F32)
                ddw = sb1("ddw", [128, 2 * BLK], F32)
                t16 = sb1("t16", [16, BLK], F32)
                prow = sb1("prow_s", [1, 128], F32)
                mrow = sb1("mrow_s", [1, BLK], F32)
                ones256 = sb1("ones256", [1, BLK], F32)
                psB = [es1.enter_context(nc.psum_tensor(f"psB{i}", [128, 1024], BF16))
                       for i in range(2)]
                tstg = [sb1(f"tstg{i}", [128, 1024], BF16) for i in range(2)]
                xb = [sb1(f"xb{i}", [128, D], F32) for i in range(3)]
                u16 = sb1("u16", [128, D], F16)
                xnb = [sb1(f"xnb{i}", [128, D], BF16) for i in range(2)]
                xqb = [sb1(f"xqb{i}", [128, D], BF16) for i in range(2)]
                NST = nxc
                stt_ = sb1("stats", [128, 18 * nxc], F32)
                SK = ["sum", "ss", "mx", "mn", "m", "var", "sd", "rstd", "t0",
                      "t1", "mab", "c", "rc", "a1q", "a0q", "isv", "nis", "nmr"]
                st = {k: (lambda i: lambda g: stt_[:, i * NST:(i + 1) * NST] if g is None
                          else stt_[:, i * NST + g:i * NST + g + 1])(i)
                      for i, k in enumerate(SK)}

                for name, dst, src in [
                        ("eye", eye, eye_h),
                        ("prow", prow, prow_h), ("mrow", mrow, mrow_h),
                        ("onesr", onesr, onesr_h), ("mask", mask, mask_h),
                        ("nwa", nwa, nwa_h), ("nwf", nwf, nwf_h),
                        ("srow", srow, slope_h)]:
                    dma("sync", "c",
                        (lambda d_, s_: lambda e: e.dma_start(out=d_[:, :], in_=s_))(dst, src),
                        reads=[], writes=[f"c:{name}"], tag=f"ld_{name}")
                P.op("vector", lambda e: e.memset(ones256[:, :], 1.0), writes=["c:ones256"])
                P.op("vector", lambda e: e.memset(onescol_b[:, :], 1.0), writes=["c:onescol"])
                P.op("vector", lambda e: e.memset(cb5[:, :], 1e-5), writes=["c:cb5"])
                P.op("vector", lambda e: e.memset(cb6[:, :], 1e-6), writes=["c:cb6"])
                P.op("vector", lambda e: e.memset(cbr[:, :], R16), writes=["c:cbr"])
                P.op("vector", lambda e: e.memset(kv[:, :], 0.0),
                     writes=["kv"] + [f"kv:{h}" for h in range(NH)])

                P.op("tensor", lambda e: e.matmul(ps[5][0:16, 0:BLK], srow[0:1, :], ones256[0:1, :], start=True, stop=True),
                     reads=["c:srow", "c:ones256"], writes=["ps5"], tag="mmS")
                P.op("vector", lambda e: e.tensor_copy(t16[:, :], ps[5][0:16, 0:BLK]),
                     reads=["ps5"], writes=["t16"], tag="evS")
                dma("gpsimd", "tbw", lambda e: e.dma_start(
                    out=tbl[0:1, :].rearrange("o (p c) -> p (o c)", p=16), in_=t16[:, :]),
                    reads=["t16"], writes=["tbl0"], tag="wS")
                dma("sync", "tb", lambda e: e.dma_start(out=rA[:, :], in_=tbl[0:1, :]),
                    reads=["tbl0"], writes=["rA"], tag="rS")
                P.op("tensor", lambda e: e.matmul(ps[5][0:16, 0:BLK], srow[0:1, :], mrow[0:1, :], start=True, stop=True),
                     reads=["c:srow", "c:mrow", "t16"], writes=["ps5"], tag="mmR")
                P.op("vector", lambda e: e.tensor_copy(t16[:, :], ps[5][0:16, 0:BLK]),
                     reads=["ps5"], writes=["t16"], tag="evR")
                dma("gpsimd", "tbw", lambda e: e.dma_start(
                    out=tbl[1:2, :].rearrange("o (p c) -> p (o c)", p=16), in_=t16[:, :]),
                    reads=["t16"], writes=["tbl1"], tag="wR")
                dma("sync", "tb", lambda e: e.dma_start(out=rB[:, :], in_=tbl[1:2, :]),
                    reads=["tbl1"], writes=["rB"], tag="rR")
                P.op("tensor", lambda e: e.matmul(ps[5][:, 0:NH], onesr[0:1, :], srow[0:1, :], start=True, stop=True),
                     reads=["c:onesr", "c:srow", "t16"], writes=["ps5"], tag="mmB")
                P.op("scalar", lambda e: e.activation(bd_vec[:, :], ps[5][:, 0:NH], AF.Exp, scale=-float(BLK)),
                     reads=["ps5"], writes=["bd_vec"], tag="bd")
                P.op("vector", lambda e: e.tensor_single_scalar(rC[:, :], rA[:, :], -255.0, ALU.mult),
                     reads=["rA"], writes=["rC"], tag="rk0")
                P.op("vector", lambda e: e.tensor_single_scalar(rD[:, :], rA[:, :], -127.0, ALU.mult),
                     reads=["rA"], writes=["rD"], tag="rk1")
                for h in range(NH):
                    for c in range(2):
                        col = 2 * h + c
                        rk = rC if c == 0 else rD
                        P.op("tensor", (lambda h_, col_: lambda e: e.matmul(
                            ps[4][:, col_:col_ + 1], prow[0:1, :],
                            rA[0:1, BLK * h_:BLK * h_ + 1], start=True, stop=False))(h, col),
                            reads=["c:prow", "rA"], writes=["ps4"], tag=f"kda{col}")
                        P.op("tensor", (lambda h_, col_, rk_: lambda e: e.matmul(
                            ps[4][:, col_:col_ + 1], onesr[0:1, :],
                            rk_[0:1, BLK * h_:BLK * h_ + 1], start=False, stop=True))(h, col, rk),
                            reads=["c:onesr", "rC", "rD"], writes=["ps4"], tag=f"kdb{col}")
                P.op("scalar", lambda e: e.activation(kdec[:, :], ps[4][:, 0:2 * NH], AF.Exp),
                     reads=["ps4"], writes=["kdec"], tag="kde")
                P.op("vector", lambda e: e.tensor_single_scalar(rC[:, :], rB[:, :], -1.0, ALU.mult),
                     reads=["rB"], writes=["rC"], tag="rc0")
                P.op("vector", lambda e: e.scalar_tensor_tensor(rD[:, :], rA[:, :], 128.0, rB[:, :], ALU.mult, ALU.subtract),
                     reads=["rA", "rB"], writes=["rD"], tag="rc1")
                for h in range(NH):
                    for c in range(2):
                        bank = 4 + (c % 2)
                        rowc = rC if c == 0 else rD
                        P.op("tensor", (lambda h_, b_: lambda e: e.matmul(
                            ps[b_][:, 0:BLK], prow[0:1, :],
                            rA[0:1, BLK * h_:BLK * (h_ + 1)], start=True, stop=False))(h, bank),
                            reads=["c:prow", "rA", "kdec"], writes=[f"ps{bank}"], tag=f"dda{h}{c}")
                        P.op("tensor", (lambda h_, b_, rc_: lambda e: e.matmul(
                            ps[b_][:, 0:BLK], onesr[0:1, :],
                            rc_[0:1, BLK * h_:BLK * (h_ + 1)], start=False, stop=True))(h, bank, rowc),
                            reads=["c:onesr", "rC", "rD"], writes=[f"ps{bank}"], tag=f"ddb{h}{c}")
                        P.op("vector", (lambda b_: lambda e: e.tensor_scalar(
                            ddw[:, 0:BLK], ps[b_][:, 0:BLK], 1.0, 0.0, ALU.mult, ALU.min))(bank),
                            reads=[f"ps{bank}"], writes=["ddw0"], tag=f"ddc{h}{c}")
                        P.op("scalar", lambda e: e.activation(ddw[:, BLK:2 * BLK], ddw[:, 0:BLK], AF.Exp),
                             reads=["ddw0"], writes=["ddw1"], tag=f"ddd{h}{c}")
                        P.op("vector", (lambda h_, c_: lambda e: e.tensor_mul(
                            ddT[:, (2 * h_ + c_) * BLK:(2 * h_ + c_ + 1) * BLK],
                            ddw[:, BLK:2 * BLK], mask[:, c_ * BLK:(c_ + 1) * BLK]))(h, c),
                            reads=["ddw1", "c:mask"], writes=[f"ddT:{h}:{c}"], tag=f"dde{h}{c}")
                P.op("vector", lambda e: e.tensor_add(rB[:, :], rB[:, :], rA[:, :]),
                     reads=["rA", "rB"], writes=["rB"], tag="r2")
                for cc in range(NH * BLK // 512):
                    bank = cc % 2
                    P.op("tensor", (lambda cc_, b_: lambda e: e.matmul(
                        ps[b_][:, :], onesr[0:1, :], rB[0:1, 512 * cc_:512 * (cc_ + 1)],
                        start=True, stop=True))(cc, bank),
                        reads=["rB", "c:onesr"], writes=[f"ps{bank}"], tag=f"qba{cc}")
                    P.op("scalar", (lambda cc_, b_: lambda e: e.activation(
                        qdbc[:, 512 * cc_:512 * (cc_ + 1)], ps[b_][:, :], AF.Exp, scale=-1.0))(cc, bank),
                        reads=[f"ps{bank}"], writes=[f"qdbc:{cc}"], tag=f"qbb{cc}")

                def stage3(src_ap, dst_ap, nchunk, key, q=("gpsimd", "wl", "gpsimd", "ws")):
                    lq, ls, wq_, ws_ = q
                    for c in range(nchunk):
                        f32b = f"wf32:{c % 2}"
                        bfb = f"wbufS:{c % 2}"
                        dma(lq, ls, (lambda c_: lambda e: e.dma_start(
                            out=wf32[c_ % 2][:, :], in_=src_ap[c_, :, :]))(c),
                            reads=[], writes=[f32b], tag=f"L{key}{c}")
                        P.op("gpsimd", (lambda c_: lambda e: e.tensor_copy(
                            wbufS[c_ % 2][:, :], wf32[c_ % 2][:, :]))(c),
                            reads=[f32b], writes=[bfb], tag=f"C{key}{c}")
                        dma(wq_, ws_, (lambda c_: lambda e: e.dma_start(
                            out=dst_ap[c_, :, :], in_=wbufS[c_ % 2][:, :]))(c),
                            reads=[bfb], writes=[f"{key}:{c}"], tag=f"S{key}{c}")

                def stage2(src_ap, dst_ap, nchunk, key, q=("gpsimd", "wl", "gpsimd", "ws")):
                    lq, ls, wq_, ws_ = q
                    for c in range(nchunk):
                        f32b = f"wf32:{c % 2}"
                        bfb = f"wbufS:{c % 2}"
                        dma(lq, ls, (lambda c_: lambda e: e.dma_start(
                            out=wf32[c_ % 2][:, :], in_=src_ap[128 * c_:128 * (c_ + 1), :]))(c),
                            reads=[], writes=[f32b], tag=f"L{key}{c}")
                        P.op("gpsimd", (lambda c_: lambda e: e.tensor_copy(
                            wbufS[c_ % 2][:, :], wf32[c_ % 2][:, :]))(c),
                            reads=[f32b], writes=[bfb], tag=f"C{key}{c}")
                        dma(wq_, ws_, (lambda c_: lambda e: e.dma_start(
                            out=dst_ap[:, c_, :, :].rearrange("d p c -> p d c"),
                            in_=wbufS[c_ % 2][:, :].rearrange("p (d c) -> p d c", d=4)))(c),
                            reads=[bfb], writes=[f"{key}:{c}"], tag=f"S{key}{c}")

                QA = ("sync", "wla", "gpsimd", "ws")
                stage3(wq_h, wqs, NKC, "wqs", QA)
                stage3(wk_h, wks, NKC, "wks", QA)
                stage2(wv_h, wvs, NKC, "wvs", QA)
                stage3(wg_h, wgs, NKC, "wgs", QA)
                stage2(wo_h, wos, NKC, "wos", QA)
                P.stage_ffn = lambda: (stage3(w13_h, w13s, NJ, "w13s"),
                                       stage2(w2_h, w2s, NJ2, "w2s"))

                for ci in range(nxc):
                    slot = ci % 3
                    xk = f"xb:{slot}"
                    dma("scalar", "xl", (lambda ci_, s_: lambda e: e.dma_start(
                        out=xb[s_][:, :], in_=x_h[128 * ci_:128 * (ci_ + 1), :]))(ci, slot),
                        reads=[], writes=[xk], tag=f"ldx{ci}")
                    P.op("scalar", (lambda s_, g_: lambda e: e.activation(
                        u16[:, :], xb[s_][:, :], AF.Square, accum_out=st["ss"](g_)))(slot, ci),
                        reads=[xk], writes=["u16", f"st1:{ci}"], tag=f"sq{ci}")
                    P.op("vector", (lambda s_, g_: lambda e: e.reduce_sum(
                        st["sum"](g_), xb[s_][:, :], axis=AX.X))(slot, ci),
                        reads=[xk], writes=[f"st1:{ci}"], tag=f"rs{ci}")
                    P.op("vector", (lambda s_, g_: lambda e: e.reduce_max(
                        st["mx"](g_), xb[s_][:, :], axis=AX.X))(slot, ci),
                        reads=[xk], writes=[f"st1:{ci}"], tag=f"rx{ci}")
                    P.op("vector", (lambda s_, g_: lambda e: e.tensor_reduce(
                        st["mn"](g_), xb[s_][:, :], AX.X, ALU.min))(slot, ci),
                        reads=[xk], writes=[f"st1:{ci}"], tag=f"rn{ci}")

                def chain_all(e):
                    s_ = lambda k: st[k](None)
                    e.tensor_single_scalar(s_("m"), s_("sum"), 1.0 / D, ALU.mult)
                    e.drain()
                    e.tensor_mul(s_("t0"), s_("m"), s_("m"))
                    e.drain()
                    e.scalar_tensor_tensor(s_("var"), s_("ss"), 1.0 / D, s_("t0"), ALU.mult, ALU.subtract)
                    return e.drain()
                P.op("vector", chain_all,
                     reads=[f"st1:{ci}" for ci in range(nxc)],
                     writes=["st2a"], tag="c1all")
                P.op("scalar", lambda e: e.activation(
                    st["sd"](None), st["var"](None), AF.Sqrt, bias=cb5[:, 0:1]),
                    reads=["st2a", "c:cb5"], writes=["st2b"], tag="c2all")

                def chain3_all(e):
                    s_ = lambda k: st[k](None)
                    e.reciprocal(s_("rstd"), s_("sd"))
                    e.drain()
                    e.tensor_sub(s_("t0"), s_("mx"), s_("m"))
                    e.tensor_sub(s_("t1"), s_("m"), s_("mn"))
                    e.drain()
                    e.tensor_max(s_("mab"), s_("t0"), s_("t1"))
                    e.drain()
                    e.tensor_mul(s_("t0"), s_("mab"), s_("rstd"))
                    e.drain()
                    e.tensor_single_scalar(s_("c"), s_("t0"), 1e-5, ALU.max)
                    e.drain()
                    e.reciprocal(s_("rc"), s_("c"))
                    e.drain()
                    e.scalar_tensor_tensor(s_("a1q"), s_("rstd"), 127.0, s_("rc"), ALU.mult, ALU.mult)
                    e.tensor_single_scalar(s_("isv"), s_("c"), 1.0 / 127.0, ALU.mult)
                    e.tensor_mul(s_("nmr"), s_("m"), s_("rstd"))
                    e.drain()
                    e.tensor_mul(s_("t1"), s_("m"), s_("a1q"))
                    e.tensor_single_scalar(s_("nis"), s_("isv"), -R16, ALU.mult)
                    e.tensor_single_scalar(s_("nmr"), s_("nmr"), -1.0, ALU.mult)
                    e.drain()
                    e.tensor_scalar(s_("a0q"), s_("t1"), -1.0, R16, ALU.mult, ALU.add)
                    return e.drain()
                P.op("vector", chain3_all, reads=["st2a", "st2b"],
                     writes=["st2"], tag="c3all")

                def p2_tr(ci, srcbuf, srckey, dst_hbm, okey, tagp):
                    for half in range(2):
                        pk = f"psB:{half}"
                        for k8 in range(8):
                            kk = 8 * half + k8
                            P.op("tensor", (lambda s_, k_, k8_, h_: lambda e: e.transpose(
                                psB[h_][:, 128 * k8_:128 * (k8_ + 1)],
                                srcbuf[s_][:, 128 * k_:128 * (k_ + 1)], eye[:, :]))(srckey, kk, k8, half),
                                reads=[f"{tagp}bb:{srckey}", "c:eye"], writes=[pk], tag=f"{tagp}t{ci}_{kk}")
                        P.op("vector", (lambda h_: lambda e: e.tensor_copy(
                            tstg[h_][:, :], psB[h_][:, :]))(half),
                            reads=[pk], writes=[f"tstg:{half}"], tag=f"{tagp}e{ci}_{half}")
                        dma("gpsimd", "aw", (lambda ci_, h_: lambda e: e.dma_start(
                            out=dst_hbm[1024 * h_:1024 * (h_ + 1), 128 * ci_:128 * (ci_ + 1)].rearrange(
                                "(k p) c -> p k c", k=8),
                            in_=tstg[h_][:, :].rearrange("p (k c) -> p k c", k=8)))(ci, half),
                            reads=[f"tstg:{half}"], writes=[f"{okey}:{ci // 4}"], tag=f"{tagp}w{ci}_{half}")

                for ci in range(nxc):
                    slot = ci % 3
                    xk = f"xb:{slot}"
                    dma("scalar", "xl", (lambda ci_, s_: lambda e: e.dma_start(
                        out=xb[s_][:, :], in_=x_h[128 * ci_:128 * (ci_ + 1), :]))(ci, slot),
                        reads=[], writes=[xk], tag=f"ldx2_{ci}")
                    P.op("scalar", (lambda s_, g_: lambda e: e.activation(
                        xnb[s_ % 2][:, :], xb[s_][:, :], AF.Identity,
                        bias=st["nmr"](g_), scale=st["rstd"](g_)))(slot, ci),
                        reads=[xk, "st2"], writes=[f"xnbb:{slot % 2}"], tag=f"xn{ci}")
                    p2_tr(ci, xnb, slot % 2, xns, "xns", "xn")
                    P.op("vector", (lambda s_, g_: lambda e: e.tensor_scalar(
                        u16[:, :], xb[s_][:, :], st["a1q"](g_), st["a0q"](g_), ALU.mult, ALU.add))(slot, ci),
                        reads=[xk, "st2"], writes=["u16"], tag=f"u{ci}")
                    P.op("vector", (lambda s_, g_: lambda e: e.tensor_scalar(
                        xqb[s_ % 2][:, :], u16[:, :], st["isv"](g_), st["nis"](g_), ALU.mult, ALU.add))(slot, ci),
                        reads=["u16", "st2"], writes=[f"xqbb:{slot % 2}"], tag=f"xq{ci}")
                    p2_tr(ci, xqb, slot % 2, xqs, "xqs", "xq")

            P.barrier()
            es2 = contextlib.ExitStack()
            sb2 = lambda n, sh, dt: es2.enter_context(nc.sbuf_tensor(n, sh, dt))
            with es2:
                P.stage_ffn()
                xnT = [sb2(f"xnT{i}", [128, NKC * T], BF16) for i in range(2)]
                qT = sb2("qT", [128, NH * T], BF16)
                kT = sb2("kT", [128, NH * T], BF16)
                v_tok = [sb2(f"v_tok{i}", [128, D], BF16) for i in range(4)]
                psBt = [es2.enter_context(nc.psum_tensor(f"psBt{i}", [128, 1024], BF16))
                        for i in range(2)]
                gateT = sb2("gateT", [128, NH * T], BF16)
                aT = sb2("aT", [128, NH * T], BF16)
                kvb = [sb2(f"kvb{i}", [128, HD], BF16) for i in range(4)]
                qdi = [sb2(f"qdi{i}", [128, BLK], BF16) for i in range(4)]
                qkd = [sb2(f"qkd{i}", [128, 2 * BLK], BF16) for i in range(2)]
                ktok = [sb2(f"ktok{i}", [128, BLK], BF16) for i in range(4)]
                sqs0 = sb2("sqs0", [128, T], BF16)
                arow = sb2("arow", [1, T], F32)
                arow2 = sb2("arow2", [1, T], F32)
                rstdA = sb2("rstdA", [128, T], F32)
                aostg = [sb2(f"aostg{i}", [128, 512], BF16) for i in range(2)]

                for t in range(nt):
                    xp = t % 2
                    xnTc = xnT[xp]
                    for kk in range(NKC):
                        qq = ("scalar", "tr") if kk % 2 == 0 else ("sync", "trs")
                        dma(qq[0], qq[1], (lambda t_, k_, b_: lambda e: e.dma_start(
                            out=b_[:, T * k_:T * (k_ + 1)],
                            in_=xns[128 * k_:128 * (k_ + 1), T * t_:T * (t_ + 1)]))(t, kk, xnTc),
                            reads=[f"xns:{t}"], writes=[f"xnT{xp}:{kk}"], tag=f"txn{t}_{kk}")
                    for gi, (wsrc, wkey, dstT, dkey, act) in enumerate([
                            (wqs, "wqs", qT, "qT", AF.Silu),
                            (wks, "wks", kT, "kT", AF.Silu),
                            (wgs, "wgs", gateT, "gateT", AF.Sigmoid)]):
                        for c in range(NKC):
                            slot = (gi * NKC + c) % NWB
                            bank = (gi * NKC + c) % 2
                            dma("sync", "wst", (lambda w_, c_, s_: lambda e: e.dma_start(
                                out=wbuf[s_][:, :], in_=w_[c_, :, :]))(wsrc, c, slot),
                                reads=[f"{wkey}:{c}"], writes=[f"wbuf:{slot}"], tag=f"sw{t}{gi}{c}")
                            for kk in range(NKC):
                                P.op("tensor", (lambda s_, b_, k_, x_: lambda e: e.matmul(
                                    ps[b_][:, :], wbuf[s_][:, 128 * k_:128 * (k_ + 1)],
                                    x_[:, T * k_:T * (k_ + 1)],
                                    start=(k_ == 0), stop=(k_ == NKC - 1)))(slot, bank, kk, xnTc),
                                    reads=[f"wbuf:{slot}", f"xnT{xp}:{kk}"],
                                    writes=[f"ps{bank}"], tag=f"g{t}_{gi}_{c}_{kk}")
                            P.op("scalar", (lambda b_, c_, d_, a_: lambda e: e.activation(
                                d_[:, T * c_:T * (c_ + 1)], ps[b_][:, :], a_))(bank, c, dstT, act),
                                reads=[f"ps{bank}"], writes=[f"{dkey}:{c}"], tag=f"e{t}_{gi}_{c}")
                    for ec in range(4):
                        for kg in range(NKC // 4):
                            slot = kg % NWB
                            wr = [f"wvs:{4 * kg + i}" for i in range(4)]
                            dma("sync", "wst", (lambda g_, s_, e_: lambda ee: ee.dma_start(
                                out=wbuf[s_][:, :].rearrange("p (k c) -> p k c", k=4),
                                in_=wvs[e_, 4 * g_:4 * (g_ + 1), :, :].rearrange("k p c -> p k c")))(kg, slot, ec),
                                reads=wr, writes=[f"wbuf:{slot}"], tag=f"swv{t}_{ec}_{kg}")
                            for ki in range(4):
                                kk = 4 * kg + ki
                                for tc in range(4):
                                    P.op("tensor", (lambda k_, s_, ki_, tc_, x_: lambda e: e.matmul(
                                        ps[tc_][:, :], x_[:, T * k_ + 128 * tc_:T * k_ + 128 * (tc_ + 1)],
                                        wbuf[s_][:, 512 * ki_:512 * (ki_ + 1)],
                                        start=(k_ == 0), stop=(k_ == NKC - 1)))(kk, slot, ki, tc, xnTc),
                                        reads=[f"wbuf:{slot}", f"xnT{xp}:{kk}"],
                                        writes=[f"ps{tc}"], tag=f"gv{t}_{ec}_{kk}_{tc}")
                        for tc in range(4):
                            P.op("scalar", (lambda e_, tc_: lambda e: e.activation(
                                v_tok[tc_][:, 512 * e_:512 * (e_ + 1)], ps[tc_][:, :], AF.Silu))(ec, tc),
                                reads=[f"ps{tc}"], writes=[f"vt:{tc}:{ec}"], tag=f"ev{t}_{ec}_{tc}")

                    NU = (T // BLK) * NH

                    def u_hb(u):
                        b, h = divmod(u, NH)
                        return h, b, T * h + b * BLK

                    def attn_pre(u):
                        h, b, hb = u_hb(u)
                        r = u % 4
                        r2 = u % 2
                        pt = u % 2
                        for cc in range(2):
                            P.op("tensor", (lambda hb_, cc_, p_: lambda e: e.transpose(
                                psBt[p_][:, 128 * cc_:128 * (cc_ + 1)],
                                kT[:, hb_ + 128 * cc_:hb_ + 128 * (cc_ + 1)], eye[:, :]))(hb, cc, pt),
                                reads=[f"kT:{h}", "c:eye"], writes=[f"psBt{pt}"], tag=f"tk{t}_{u}_{cc}")
                        for cc in range(2):
                            P.op("scalar", (lambda h_, cc_, r_, p_: lambda e: e.activation(
                                ktok[r_][:, 128 * cc_:128 * (cc_ + 1)],
                                psBt[p_][:, 128 * cc_:128 * (cc_ + 1)], AF.Copy,
                                scale=kdec[:, 2 * h_ + cc_:2 * h_ + cc_ + 1]))(h, cc, r, pt),
                                reads=[f"psBt{pt}", "kdec"], writes=[f"ktok:{r}"], tag=f"ke{t}_{u}_{cc}")
                        P.op("vector", (lambda h_, hb_, r_: lambda e: e.tensor_mul(
                            qdi[r_][:, :], qT[:, hb_:hb_ + BLK],
                            qdbc[:, BLK * h_:BLK * (h_ + 1)]))(h, hb, r),
                            reads=[f"qT:{h}", f"qdbc:{h // 2}"],
                            writes=[f"qdi:{r}"], tag=f"qd{t}_{u}")
                        P.op("scalar", (lambda h_, r_: lambda e: e.activation(
                            kvb[r_][:, :], kv[:, HD * h_:HD * (h_ + 1)], AF.Copy))(h, r),
                            reads=[f"kv:{h}"], writes=[f"kvb:{r}"], tag=f"kc{t}_{u}")

                    def attn_qk(u):
                        h, b, hb = u_hb(u)
                        pqk = u % 2
                        for cc in range(2):
                            P.op("tensor", (lambda hb_, cc_, p_: lambda e: e.matmul(
                                ps[p_][:, BLK * cc_:BLK * (cc_ + 1)],
                                kT[:, hb_ + 128 * cc_:hb_ + 128 * (cc_ + 1)],
                                qT[:, hb_:hb_ + BLK], start=True, stop=True))(hb, cc, pqk),
                                reads=[f"kT:{h}", f"qT:{h}"], writes=[f"ps{pqk}"], tag=f"qk{t}_{u}_{cc}")
                        P.op("vector", (lambda h_, u_, p_: lambda e: e.tensor_mul(
                            qkd[u_ % 2][:, :], ps[p_][:, 0:2 * BLK],
                            ddT[:, 2 * BLK * h_:2 * BLK * (h_ + 1)]))(h, u, pqk),
                            reads=[f"ps{pqk}", f"ddT:{h}:0", f"ddT:{h}:1"],
                            writes=[f"qkd:{u % 2}"], tag=f"qm{t}_{u}")

                    def attn_body(u):
                        h, b, hb = u_hb(u)
                        r = u % 4
                        r2 = u % 2
                        po = 2 + u % 2
                        pkv = 4 + u % 2
                        ve = h // 4
                        vsl = lambda cc: (2 * b + cc, h)
                        P.op("tensor", (lambda r_, p_: lambda e: e.matmul(
                            ps[p_][:, 0:BLK], kvb[r_][:, :], qdi[r_][:, :],
                            start=True, stop=False))(r, po),
                            reads=[f"kvb:{r}", f"qdi:{r}"], writes=[f"ps{po}"], tag=f"oi{t}_{u}")
                        for cc in range(2):
                            P.op("tensor", (lambda b_, cc_, h_, r_, p_: lambda e: e.matmul(
                                ps[p_][:, 0:BLK], v_tok[2 * b_ + cc_][:, 128 * h_:128 * (h_ + 1)],
                                qkd[u % 2][:, BLK * cc_:BLK * (cc_ + 1)],
                                start=False, stop=(cc_ == 1)))(b, cc, h, r, po),
                                reads=[f"vt:{2 * b + cc}:{ve}", f"qkd:{u % 2}"],
                                writes=[f"ps{po}"], tag=f"oa{t}_{u}_{cc}")
                        P.op("vector", (lambda hb_, p_, h_, b_: lambda e: e.tensor_copy(
                            aT[:, hb_:hb_ + BLK], ps[p_][:, 0:BLK]))(hb, po, h, b),
                            reads=[f"ps{po}"], writes=[f"aT:{h}:{b}"], tag=f"oe{t}_{u}")
                        for cc in range(2):
                            P.op("tensor", (lambda b_, cc_, h_, r_, p_: lambda e: e.matmul(
                                ps[p_][:, 0:HD], ktok[r_][:, 128 * cc_:128 * (cc_ + 1)],
                                v_tok[2 * b_ + cc_][:, 128 * h_:128 * (h_ + 1)],
                                start=(cc_ == 0), stop=(cc_ == 1)))(b, cc, h, r, pkv),
                                reads=[f"ktok:{r}", f"vt:{2 * b + cc}:{ve}"],
                                writes=[f"ps{pkv}"], tag=f"km{t}_{u}_{cc}")
                        P.op("vector", (lambda h_, p_: lambda e: e.scalar_tensor_tensor(
                            kv[:, HD * h_:HD * (h_ + 1)], kv[:, HD * h_:HD * (h_ + 1)],
                            bd_vec[:, h_:h_ + 1], ps[p_][:, 0:HD], ALU.mult, ALU.add))(h, pkv),
                            reads=[f"ps{pkv}", f"kv:{h}", "bd_vec"],
                            writes=[f"kv:{h}"], tag=f"ku{t}_{u}")

                    for u in range(3):
                        attn_pre(u)
                    for u in range(NU):
                        attn_qk(u)
                        if u + 3 < NU:
                            attn_pre(u + 3)
                        attn_body(u)

                    sqs = [qkd[0][:, :], qkd[1][:, :], sqs0[:, :]]
                    skey = ["qkd:0", "qkd:1", "sqs:0"]
                    def av_mm(h_):
                        P.op("tensor", (lambda hh: lambda e: e.matmul(
                            ps[0][0:1, 0:T], onescol_b[:, 0:1], sqs[hh % 3],
                            start=(hh == 0), stop=(hh == NH - 1)))(h_),
                            reads=[skey[h_ % 3], "c:onescol"], writes=["ps0"], tag=f"av{t}_{h_}")
                    for h in range(NH):
                        P.op("scalar", (lambda h_: lambda e: e.activation(
                            sqs[h_ % 3], aT[:, T * h_:T * (h_ + 1)], AF.Square))(h),
                            reads=[f"aT:{h}:0", f"aT:{h}:1"], writes=[skey[h % 3]], tag=f"as{t}_{h}")
                        if h >= 2:
                            av_mm(h - 2)
                    av_mm(NH - 2)
                    av_mm(NH - 1)
                    P.op("vector", lambda e: e.tensor_single_scalar(
                        arow[0:1, 0:T], ps[0][0:1, 0:T], 1.0 / D, ALU.mult),
                        reads=["ps0"], writes=["arow"], tag=f"an1{t}")
                    P.op("scalar", lambda e: e.activation(arow2[0:1, 0:T], arow[0:1, 0:T], AF.Sqrt, bias=cb6[0:1, 0:1]),
                         reads=["arow", "c:cb6"], writes=["arow2"], tag=f"an2{t}")
                    P.op("vector", lambda e: e.reciprocal(arow[0:1, 0:T], arow2[0:1, 0:T]),
                         reads=["arow2"], writes=["arow"], tag=f"an3{t}")
                    P.op("tensor", lambda e: e.matmul(ps[1][:, 0:T], onesr[0:1, :], arow[0:1, 0:T], start=True, stop=True),
                         reads=["arow", "c:onesr"], writes=["ps1"], tag=f"an4{t}")
                    P.op("vector", lambda e: e.tensor_copy(rstdA[:, 0:T], ps[1][:, 0:T]),
                         reads=["ps1"], writes=["rstdA"], tag=f"an5{t}")
                    for h in range(NH):
                        P.op("vector", (lambda h_: lambda e: e.scalar_tensor_tensor(
                            sqs[h_ % 3], aT[:, T * h_:T * (h_ + 1)], nwa[:, h_:h_ + 1],
                            rstdA[:, 0:T], ALU.mult, ALU.mult))(h),
                            reads=[f"aT:{h}:0", f"aT:{h}:1", "rstdA", "c:nwa"],
                            writes=[skey[h % 3]], tag=f"nm{t}_{h}")
                        P.op("vector", (lambda h_: lambda e: e.tensor_mul(
                            aT[:, T * h_:T * (h_ + 1)], sqs[h_ % 3],
                            gateT[:, T * h_:T * (h_ + 1)]))(h),
                            reads=[skey[h % 3], f"gateT:{h}"],
                            writes=[f"aT:{h}:0", f"aT:{h}:1", f"ga:{h}"], tag=f"gm{t}_{h}")

                    for dchunk in range(4):
                        for kg in range(NKC // 4):
                            slot = kg % NWB
                            wr = [f"wos:{4 * kg + i}" for i in range(4)]
                            dma("sync", "wst", (lambda g_, s_, d_: lambda e: e.dma_start(
                                out=wbuf[s_][:, :].rearrange("p (k c) -> p k c", k=4),
                                in_=wos[d_, 4 * g_:4 * (g_ + 1), :, :].rearrange("k p c -> p k c")))(kg, slot, dchunk),
                                reads=wr, writes=[f"wbuf:{slot}"], tag=f"so{t}_{dchunk}_{kg}")
                            for ki in range(4):
                                kk = 4 * kg + ki
                                for tc in range(4):
                                    P.op("tensor", (lambda k_, s_, ki_, tc_: lambda e: e.matmul(
                                        ps[2 + tc_][:, :],
                                        aT[:, T * k_ + 128 * tc_:T * k_ + 128 * (tc_ + 1)],
                                        wbuf[s_][:, 512 * ki_:512 * (ki_ + 1)],
                                        start=(k_ == 0), stop=(k_ == NKC - 1)))(kk, slot, ki, tc),
                                        reads=[f"wbuf:{slot}", f"ga:{kk}"],
                                        writes=[f"ps{2 + tc}"], tag=f"op{t}_{dchunk}_{kk}_{tc}")
                        for tc in range(4):
                            so = (dchunk * 4 + tc) % 2
                            P.op("scalar", (lambda tc_, s_: lambda e: e.activation(
                                aostg[s_][:, :], ps[2 + tc_][:, :], AF.Copy))(tc, so),
                                reads=[f"ps{2 + tc}"], writes=[f"aostg:{so}"], tag=f"ae{t}_{dchunk}_{tc}")
                            dma("scalar", "ao", (lambda t_, tc_, d_, s_: lambda e: e.dma_start(
                                out=aos[T * t_ + 128 * tc_:T * t_ + 128 * (tc_ + 1),
                                        512 * d_:512 * (d_ + 1)],
                                in_=aostg[s_][:, :]))(t, tc, dchunk, so),
                                reads=[f"aostg:{so}"], writes=[f"aos:{t}"], tag=f"aw{t}_{dchunk}_{tc}")

            P.barrier()
        es3 = contextlib.ExitStack()
        sb3 = lambda n, sh, dt: es3.enter_context(nc.sbuf_tensor(n, sh, dt))
        with es3:
            xqT = sb3("xqT", [128, NKC * T], BF16)
            gat = sb3("gat", [128, NJ2 * T], BF16)
            iqT = sb3("iqT", [128, NJ2 * T], BF16)
            a_scr = [sb3(f"ascr{i}", [128, T], F32) for i in range(2)]
            sq2 = [sb3(f"sq2_{i}", [128, T], BF16) for i in range(4)]
            tmpb = [sb3(f"tmpb{i}", [128, T], BF16) for i in range(2)]
            Mb = [sb3(f"Mb{i}", [128, T], BF16) for i in range(2)]
            mbt = [sb3(f"mbt{i}", [128, 128], BF16) for i in range(2)]
            Ub = sb3("Ub", [128, T], F32)
            u16b = [sb3(f"u16b{i}", [128, T], F16) for i in range(2)]
            qrow = sb3("qrow", [1, T], F32)
            qt_ = sb3("qstats", [128, 88], F32)
            QK = ["ss", "M", "var", "sd", "rstd", "c", "rc", "s2", "is2", "U"]
            iqs = [sb3(f"iqs{i}", [128, T], F32) for i in range(2)]
            eyef = sb3("eyef_s", [128, 128], F32)
            ps6 = es3.enter_context(nc.psum_tensor("ps6", [128, 512], F32))
            ps7 = es3.enter_context(nc.psum_tensor("ps7", [128, 512], F32))
            ps67 = [ps6, ps7]
            wbuf2 = [sb3(f"wbuf2_{i}", [128, 2048], BF16) for i in range(2)]
            xres = [sb3(f"xres{i}", [128, 512], F32) for i in range(2)]
            aores = [sb3(f"aores{i}", [128, 512], BF16) for i in range(2)]
            ostg = [sb3(f"ostg{i}", [128, 512], F32) for i in range(2)]

            def p4_txq(t):
                for kk in range(NKC):
                    qq = ("scalar", "tr") if kk % 2 == 0 else ("sync", "trs")
                    dma(qq[0], qq[1], (lambda t_, k_: lambda e: e.dma_start(
                        out=xqT[:, T * k_:T * (k_ + 1)],
                        in_=xqs[128 * k_:128 * (k_ + 1), T * t_:T * (t_ + 1)]))(t, kk),
                        reads=[f"xqs:{t}"], writes=[f"xqT:{kk}"], tag=f"txq{t}_{kk}")

            def p4_w13(t):
                pq = t % 2
                P.op("vector", (lambda p_: lambda e: e.memset(Mb[p_][:, :], 0.0))(pq),
                     reads=[], writes=[f"Mb{pq}"], tag=f"mz{t}")

                def ss_mm(j_):
                    P.op("tensor", (lambda jx: lambda e: e.matmul(
                        ps[4][0:1, 0:T], onescol_b[:, 0:1], sq2[jx % 4][:, :],
                        start=(jx == 0), stop=(jx == NJ2 - 1)))(j_),
                        reads=[f"sq2:{j_ % 4}", "c:onescol"], writes=["ps4"], tag=f"ss{t}_{j_}")
                for jj in range(NJ2):
                    b1 = 2 * (jj % 2)
                    b3 = b1 + 1
                    for half, (jc, bank) in enumerate([(jj, b1), (jj + NJ2, b3)]):
                        slot = (2 * jj + half) % NWB
                        dma("sync", "wst", (lambda j_, s_: lambda e: e.dma_start(
                            out=wbuf[s_][:, :], in_=w13s[j_, :, :]))(jc, slot),
                            reads=[f"w13s:{jc}"], writes=[f"wbuf:{slot}"], tag=f"s13_{t}_{jc}")
                        for kk in range(NKC):
                            P.op("tensor", (lambda s_, b_, k_: lambda e: e.matmul(
                                ps[b_][:, :], wbuf[s_][:, 128 * k_:128 * (k_ + 1)],
                                xqT[:, T * k_:T * (k_ + 1)], start=(k_ == 0), stop=(k_ == NKC - 1)))(slot, bank, kk),
                                reads=[f"wbuf:{slot}", f"xqT:{kk}"],
                                writes=[f"ps{bank}"], tag=f"w13_{t}_{jc}_{kk}")
                    asl = jj % 2
                    P.op("scalar", (lambda b_, s_: lambda e: e.activation(
                        a_scr[s_][:, :], ps[b_][:, :], AF.Relu))(b1, asl),
                        reads=[f"ps{b1}"], writes=[f"ascr:{asl}"], tag=f"rl{t}_{jj}")
                    P.op("scalar", (lambda s_: lambda e: e.activation(
                        a_scr[s_][:, :], a_scr[s_][:, :], AF.Square))(asl),
                        reads=[f"ascr:{asl}"], writes=[f"ascr:{asl}"], tag=f"sr{t}_{jj}")
                    P.op("vector", (lambda j_, b_, s_: lambda e: e.tensor_mul(
                        gat[:, T * j_:T * (j_ + 1)], a_scr[s_][:, :], ps[b_][:, :]))(jj, b3, asl),
                        reads=[f"ascr:{asl}", f"ps{b3}"], writes=[f"gat:{jj}"], tag=f"gm{t}_{jj}")
                    P.op("gpsimd", (lambda j_, s_: lambda e: e.tensor_mul(
                        sq2[s_][:, :], gat[:, T * j_:T * (j_ + 1)],
                        gat[:, T * j_:T * (j_ + 1)]))(jj, jj % 4),
                        reads=[f"gat:{jj}"], writes=[f"sq2:{jj % 4}"], tag=f"g2{t}_{jj}")
                    if jj >= 2:
                        ss_mm(jj - 2)
                    if jj == NJ2 - 1:
                        ss_mm(NJ2 - 2)
                        ss_mm(NJ2 - 1)
                    P.op("scalar", (lambda j_, s_: lambda e: e.activation(
                        tmpb[s_][:, :], gat[:, T * j_:T * (j_ + 1)], AF.Abs,
                        scale=nwf[:, j_:j_ + 1]))(jj, asl),
                        reads=[f"gat:{jj}", "c:nwf"], writes=[f"tmpb:{asl}"], tag=f"ab{t}_{jj}")
                    P.op("vector", (lambda s_, p_: lambda e: e.tensor_max(
                        Mb[p_][:, :], Mb[p_][:, :], tmpb[s_][:, :]))(asl, pq),
                        reads=[f"tmpb:{asl}", f"Mb{pq}"], writes=[f"Mb{pq}"], tag=f"mm{t}_{jj}")
                    yield

            def p4_quant2_iq(t):
                pq = t % 2
                qt = {k: qt_[:, 8 * i + 4 * pq:8 * i + 4 * (pq + 1)] for i, k in enumerate(QK)}
                for tc in range(4):
                    dma("scalar", "trv", (lambda tc_, p_: lambda e: e.dma_start(
                        out=mbt[tc_ % 2][:, :], in_=Mb[p_][:, 128 * tc_:128 * (tc_ + 1)],
                        transpose=True))(tc, pq),
                        reads=[f"Mb{pq}"], writes=[f"mbt:{tc % 2}"], tag=f"mx{t}_{tc}")
                    P.op("vector", (lambda tc_, q_: lambda e: e.reduce_max(
                        q_["M"][:, tc_:tc_ + 1], mbt[tc_ % 2][:, :], axis=AX.X))(tc, qt),
                        reads=[f"mbt:{tc % 2}"], writes=[f"qt{pq}:M"], tag=f"mr{t}_{tc}")
                P.op("vector", lambda e: e.tensor_copy(qrow[0:1, 0:T], ps[4][0:1, 0:T]),
                     reads=["ps4"], writes=["qrow"], tag=f"sc{t}")
                for tc in range(4):
                    P.op("tensor", (lambda tc_: lambda e: e.matmul(
                        ps[5][:, tc_:tc_ + 1], qrow[0:1, 128 * tc_:128 * (tc_ + 1)],
                        onesr[0:1, 0:1], start=True, stop=True))(tc),
                        reads=["qrow", "c:onesr"], writes=["ps5"], tag=f"st{t}_{tc}")
                P.op("vector", (lambda q_: lambda e: e.tensor_copy(q_["ss"][:, 0:4], ps[5][:, 0:4]))(qt),
                     reads=["ps5"], writes=[f"qt{pq}:ss"], tag=f"se{t}")
                P.op("vector", (lambda q_: lambda e: e.tensor_single_scalar(
                    q_["var"][:, :], q_["ss"][:, :], 1.0 / DF, ALU.mult))(qt),
                    reads=[f"qt{pq}:ss"], writes=[f"qt{pq}:var"], tag=f"q1{t}")
                P.op("scalar", (lambda q_: lambda e: e.activation(
                    q_["sd"][:, :], q_["var"][:, :], AF.Sqrt, bias=cb5[:, 0:1]))(qt),
                    reads=[f"qt{pq}:var", "c:cb5"], writes=[f"qt{pq}:sd"], tag=f"q2{t}")

                def q2c(e, q_=qt):
                    e.reciprocal(q_["rstd"][:, :], q_["sd"][:, :])
                    e.drain()
                    e.tensor_mul(q_["c"][:, :], q_["rstd"][:, :], q_["M"][:, :])
                    e.drain()
                    e.tensor_single_scalar(q_["c"][:, :], q_["c"][:, :], 1e-5, ALU.max)
                    e.drain()
                    e.reciprocal(q_["rc"][:, :], q_["c"][:, :])
                    e.drain()
                    e.tensor_single_scalar(q_["s2"][:, :], q_["rc"][:, :], 127.0, ALU.mult)
                    e.tensor_single_scalar(q_["is2"][:, :], q_["c"][:, :], 1.0 / 127.0, ALU.mult)
                    e.drain()
                    e.tensor_mul(q_["U"][:, :], q_["rstd"][:, :], q_["s2"][:, :])
                    return e.drain()
                P.op("vector", q2c,
                     reads=[f"qt{pq}:sd", f"qt{pq}:M"],
                     writes=[f"qt{pq}:U", f"qt{pq}:is2"], tag=f"q3{t}")
                for tc in range(4):
                    P.op("tensor", (lambda tc_, q_: lambda e: e.transpose(
                        ps[5][0:1, 128 * tc_:128 * (tc_ + 1)],
                        q_["U"][:, tc_:tc_ + 1], eyef[:, :]))(tc, qt),
                        reads=[f"qt{pq}:U", "c:eyef"], writes=["ps5"], tag=f"ut{t}_{tc}")
                P.op("vector", lambda e: e.tensor_copy(qrow[0:1, 0:T], ps[5][0:1, 0:T]),
                     reads=["ps5"], writes=["qrow"], tag=f"ue{t}")
                P.op("tensor", lambda e: e.matmul(ps[5][:, 0:T], onesr[0:1, :], qrow[0:1, 0:T], start=True, stop=True),
                     reads=["qrow", "c:onesr"], writes=["ps5"], tag=f"ub{t}")
                P.op("vector", lambda e: e.tensor_copy(Ub[:, :], ps[5][:, 0:T]),
                     reads=["ps5"], writes=["Ub"], tag=f"uc{t}")
            def p4_iq(t):
                def i3(jj):
                    s_ = jj % 2
                    P.op("gpsimd", (lambda j_, sl_: lambda e: e.tensor_scalar(
                        iqT[:, T * j_:T * (j_ + 1)], u16b[sl_][:, :], 1.0, -R16,
                        ALU.mult, ALU.add))(jj, s_),
                        reads=[f"u16b:{s_}"], writes=[f"iqT:{jj}"], tag=f"i3{t}_{jj}")
                for jj in range(NJ2):
                    s_ = jj % 2
                    P.op("gpsimd", (lambda j_, sl_: lambda e: e.tensor_mul(
                        iqs[sl_][:, :], gat[:, T * j_:T * (j_ + 1)], Ub[:, :]))(jj, s_),
                        reads=[f"gat:{jj}", "Ub"], writes=[f"iqs:{s_}"], tag=f"i1{t}_{jj}")
                    P.op("scalar", (lambda j_, sl_: lambda e: e.activation(
                        u16b[sl_][:, :], iqs[sl_][:, :], AF.Identity, bias=cbr[:, 0:1],
                        scale=nwf[:, j_:j_ + 1]))(jj, s_),
                        reads=[f"iqs:{s_}", "c:cbr", "c:nwf"], writes=[f"u16b:{s_}"], tag=f"i2{t}_{jj}")
                    if jj >= 1:
                        i3(jj - 1)
                    if jj == NJ2 - 1:
                        i3(jj)
                    yield

            def p4_w2(t, giq):
                if giq is not None:
                    next(giq, None)
                    next(giq, None)
                pq = t % 2
                qt = {k: qt_[:, 8 * i + 4 * pq:8 * i + 4 * (pq + 1)] for i, k in enumerate(QK)}
                for dchunk in range(4):
                    for kg in range(NJ2 // 4):
                        slot = kg % 2
                        wr = [f"w2s:{4 * kg + i}" for i in range(4)]
                        dma("sync", "w2l", (lambda g_, s_, d_: lambda e: e.dma_start(
                            out=wbuf2[s_][:, :].rearrange("p (k c) -> p k c", k=4),
                            in_=w2s[d_, 4 * g_:4 * (g_ + 1), :, :].rearrange("k p c -> p k c")))(kg, slot, dchunk),
                            reads=wr, writes=[f"wbuf2:{slot}"], tag=f"s2_{t}_{dchunk}_{kg}")
                        for ki in range(4):
                            kk = 4 * kg + ki
                            if giq is not None and dchunk == 0:
                                next(giq, None)
                            for tc in range(4):
                                P.op("tensor", (lambda k_, s_, ki_, tc_: lambda e: e.matmul(
                                    ps[tc_][:, :], iqT[:, T * k_ + 128 * tc_:T * k_ + 128 * (tc_ + 1)],
                                    wbuf2[s_][:, 512 * ki_:512 * (ki_ + 1)],
                                    start=(k_ == 0), stop=(k_ == NJ2 - 1)))(kk, slot, ki, tc),
                                    reads=[f"wbuf2:{slot}", f"iqT:{kk}"],
                                    writes=[f"ps{tc}"], tag=f"w2_{t}_{dchunk}_{kk}_{tc}")
                    for tc in range(4):
                        sl = tc % 2
                        dma("scalar", "rl", (lambda t_, tc_, d_, s_: lambda e: e.dma_start(
                            out=xres[s_][:, :],
                            in_=x_h[T * t_ + 128 * tc_:T * t_ + 128 * (tc_ + 1),
                                    512 * d_:512 * (d_ + 1)]))(t, tc, dchunk, sl),
                            reads=[], writes=[f"xres:{sl}"], tag=f"lx{t}_{dchunk}_{tc}")
                        dma("scalar", "rl", (lambda t_, tc_, d_, s_: lambda e: e.dma_start(
                            out=aores[s_][:, :],
                            in_=aos[T * t_ + 128 * tc_:T * t_ + 128 * (tc_ + 1),
                                    512 * d_:512 * (d_ + 1)]))(t, tc, dchunk, sl),
                            reads=[f"aos:{t}"], writes=[f"aores:{sl}"], tag=f"la{t}_{dchunk}_{tc}")
                        P.op("vector", (lambda tc_, s_, q_: lambda e: e.scalar_tensor_tensor(
                            ostg[s_][:, :], ps[tc_][:, :], q_["is2"][:, tc_:tc_ + 1],
                            xres[s_][:, :], ALU.mult, ALU.add))(tc, sl, qt),
                            reads=[f"ps{tc}", f"xres:{sl}", f"qt{pq}:is2"],
                            writes=[f"ostg:{sl}"], tag=f"o1{t}_{dchunk}_{tc}")
                        P.op("gpsimd", (lambda s_: lambda e: e.tensor_add(
                            ostg[s_][:, :], ostg[s_][:, :], aores[s_][:, :]))(sl),
                            reads=[f"ostg:{sl}", f"aores:{sl}"], writes=[f"ostg:{sl}"], tag=f"o2{t}_{dchunk}_{tc}")
                        dma("gpsimd", "ol", (lambda t_, tc_, d_, s_: lambda e: e.dma_start(
                            out=out_h[T * t_ + 128 * tc_:T * t_ + 128 * (tc_ + 1),
                                      512 * d_:512 * (d_ + 1)],
                            in_=ostg[s_][:, :]))(t, tc, dchunk, sl),
                            reads=[f"ostg:{sl}"], writes=[f"odone:{sl}"], tag=f"wo{t}_{dchunk}_{tc}")

            dma("sync", "c", lambda e: e.dma_start(out=eyef[:, :], in_=eyef_h),
                reads=[], writes=["c:eyef"], tag="ld_eyef")
            for t in range(nt):
                p4_txq(t)
                g13 = p4_w13(t)
                for _ in g13:
                    pass
                p4_quant2_iq(t)
                giq = p4_iq(t)
                p4_w2(t, giq)
                for _ in giq:
                    pass


            P.resolve()
            with nc.Block() as blk:
                @blk.sync
                def _(eng):
                    P.emit("sync", eng, prog_sems, dma_sems)
                    r = Prog.RING["ol"]
                    tot = P.dma_tot["ol"]
                    for i in range(r):
                        n_i = tot // r + (1 if i < tot % r else 0)
                        if n_i:
                            eng.wait_ge(dma_sems[("ol", i)], 16 * n_i)

                @blk.vector
                def _(eng):
                    P.emit("vector", eng, prog_sems, dma_sems)

                @blk.scalar
                def _(eng):
                    P.emit("scalar", eng, prog_sems, dma_sems)

                @blk.tensor
                def _(eng):
                    P.emit("tensor", eng, prog_sems, dma_sems)

                @blk.gpsimd
                def _(eng):
                    P.emit("gpsimd", eng, prog_sems, dma_sems)

    return nc


def _host_consts():
    c = {}
    c["eye16"] = np.eye(128, dtype=ml_dtypes.bfloat16)
    c["eyef"] = np.eye(128, dtype=np.float32)
    c["prow"] = np.arange(128, dtype=np.float32).reshape(1, 128)
    c["mrow"] = np.arange(BLK, dtype=np.float32).reshape(1, BLK)
    c["onesrow"] = np.ones((1, 128), dtype=np.float32)
    p = np.arange(128).reshape(128, 1)
    m = np.arange(2 * BLK).reshape(1, 2 * BLK)
    c["mask"] = ((m % BLK) >= (128 * (m // BLK) + p)).astype(ml_dtypes.bfloat16)
    return c


def _blocked_lhsT(wt):
    Dd, ncol = wt.shape
    nk, ncch = Dd // 128, ncol // 128
    return np.ascontiguousarray(
        wt.reshape(nk, 128, ncch, 128).transpose(2, 1, 0, 3).reshape(ncch, 128, Dd))


def kernel(L0_attn_hidden_states, L0_attn_slope_rate, L0_attn_w_qkv,
           L0_attn_w_gate, L0_attn_w_out, L0_attn_norm_w,
           L0_ffn_w13, L0_ffn_w2, L0_ffn_norm_w):
    x = np.ascontiguousarray(np.asarray(L0_attn_hidden_states, dtype=np.float32))
    wqkv = np.asarray(L0_attn_w_qkv, dtype=np.float32).reshape(NH, 3, HD, D)
    wq = _blocked_lhsT(np.ascontiguousarray(wqkv[:, 0].reshape(D, D).T))
    wk = _blocked_lhsT(np.ascontiguousarray(wqkv[:, 1].reshape(D, D).T))
    wv = np.ascontiguousarray(wqkv[:, 2].reshape(D, D).T)
    wg = _blocked_lhsT(np.ascontiguousarray(np.asarray(L0_attn_w_gate, np.float32).T))
    wo = np.ascontiguousarray(np.asarray(L0_attn_w_out, np.float32).T)
    w13 = _blocked_lhsT(np.ascontiguousarray(np.asarray(L0_ffn_w13, np.float32).T))
    w2t = np.ascontiguousarray(np.asarray(L0_ffn_w2, np.float32).T)
    slope = np.ascontiguousarray(np.asarray(L0_attn_slope_rate, np.float32).reshape(1, NH))
    nwa = np.ascontiguousarray(np.asarray(L0_attn_norm_w, np.float32).reshape(NH, 128).T)
    nwf = np.ascontiguousarray(np.asarray(L0_ffn_norm_w, np.float32).reshape(NJ2, 128).T)
    consts = _host_consts()

    nc = build_nc(seq=S)
    base = dict(wq=wq, wk=wk, wv=wv, wg=wg, wo=wo, w13=w13, w2t=w2t,
                slope=slope, nwa=nwa, nwf=nwf, **consts)
    in_maps = [dict(base, x=np.ascontiguousarray(x[b])) for b in range(NCORES)]
    res = run_bass_kernel_spmd(nc, in_maps, core_ids=list(range(NCORES)))
    out = np.stack([np.asarray(res.results[b]["out"]) for b in range(NCORES)], axis=0)
    return out.astype(np.float32)


# version 17
# speedup vs baseline: 1.2170x; 1.0348x over previous
import numpy as np
import ml_dtypes
import contextlib
import concourse.bass as bass
import concourse.mybir as mybir
from concourse.bass_utils import run_bass_kernel_spmd
from concourse.alu_op_type import AluOpType as ALU

F32 = mybir.dt.float32
BF16 = mybir.dt.bfloat16
F16 = mybir.dt.float16
AF = mybir.ActivationFunctionType
AX = mybir.AxisListType

NCORES = 8
D = 2048
NH = 16
HD = 128
BLK = 256
DF = 5632
S = 4096
T = 512
NKC = D // 128
NJ = 2 * DF // 128
NJ2 = DF // 128
LN_EPS = 1e-5
AN_EPS = 1e-6
RMS_EPS = 1e-5
R16 = 1536.0

ENGS = ("tensor", "vector", "scalar", "gpsimd", "sync")


class Prog:

    def __init__(self):
        self.ops = []
        self.last_w = {}
        self.readers = {}
        self.bar_deps = []
        self.nbar = 0

    def op(self, engine, fn, reads=(), writes=(), dma=None, tag=""):
        idx = len(self.ops)
        deps = set()
        for b in reads:
            if b in self.last_w:
                deps.add(self.last_w[b])
        for b in writes:
            if b in self.last_w:
                deps.add(self.last_w[b])
            for r in self.readers.get(b, ()):
                deps.add(r)
        deps.update(self.bar_deps)
        deps.discard(idx)
        if dma is not None:
            if not hasattr(self, "dma_eng"):
                self.dma_eng = {}
            assert self.dma_eng.setdefault(dma, engine) == engine, \
                f"dma sem {dma} used from {engine} and {self.dma_eng[dma]}"
        self.ops.append(dict(idx=idx, engine=engine, fn=fn, deps=deps,
                             dma=dma, dma_idx=None, tag=tag, inc=None,
                             is_bar=False))
        for b in reads:
            self.readers.setdefault(b, []).append(idx)
        for b in writes:
            self.last_w[b] = idx
            self.readers[b] = []
        return idx

    def barrier(self):
        self.nbar += 1
        prior = list(range(len(self.ops)))
        bar_idxs = []
        for e in ENGS:
            idx = len(self.ops)
            self.ops.append(dict(idx=idx, engine=e, fn=None, deps=set(prior),
                                 dma=None, dma_idx=None, tag=f"bar{self.nbar}",
                                 inc=None, is_bar=True))
            bar_idxs.append(idx)
        self.bar_deps = bar_idxs

    RING = {"c": 1, "wl": 3, "ws": 2, "wst": 6, "xl": 3, "aw": 2, "tr": 4,
            "rl": 4, "ol": 2, "tb": 1, "tbw": 1, "ao": 2, "wsa": 2, "wla": 3,
            "trk": 4, "trv": 4, "trs": 4, "w2l": 4}

    def resolve(self):
        eng_seq = {}
        seq_ctr = {e: 0 for e in ENGS}
        self.dma_tot = {}
        for o in self.ops:
            eng_seq[o["idx"]] = seq_ctr[o["engine"]]
            seq_ctr[o["engine"]] += 1
            if o["dma"] is not None:
                k = self.dma_tot.get(o["dma"], 0)
                self.dma_tot[o["dma"]] = k + 1
                r = self.RING[o["dma"]]
                o["dma_sub"] = (o["dma"], k % r)
                o["dma_val"] = 16 * (k // r + 1)
        needs_inc = set()
        for o in self.ops:
            for d in o["deps"]:
                dop = self.ops[d]
                if dop["dma"] is None and not (
                        dop["engine"] == "tensor" and o["engine"] == "tensor"):
                    needs_inc.add(d)
        for o in self.ops:
            if o["is_bar"]:
                needs_inc.add(o["idx"])
        self.inc_val = {}
        ctr = {e: 0 for e in ENGS}
        for o in self.ops:
            if o["idx"] in needs_inc:
                ctr[o["engine"]] += 1
                o["inc"] = ctr[o["engine"]]
                self.inc_val[o["idx"]] = ctr[o["engine"]]
        self.eng_seq = eng_seq
        self.by_eng = {e: [o for o in self.ops if o["engine"] == e] for e in ENGS}

    def emit(self, ename, eng_obj, prog_sems, dma_sems):
        waited = {}
        drained_upto = 0
        for o in self.by_eng[ename]:
            need_drain = False
            waits = {}
            for d in sorted(o["deps"]):
                dop = self.ops[d]
                if dop["dma"] is not None:
                    key = ("dma",) + dop["dma_sub"]
                    val = dop["dma_val"]
                    if waits.get(key, 0) < val:
                        waits[key] = val
                elif dop["engine"] == ename:
                    if ename != "tensor":
                        v = self.inc_val[d]
                        key = ("prog", ename)
                        if waits.get(key, 0) < v:
                            waits[key] = v
                else:
                    v = self.inc_val[d]
                    key = ("prog", dop["engine"])
                    if waits.get(key, 0) < v:
                        waits[key] = v
            if o["dma"] is not None and o["dma_val"] > 16:
                key = ("dma",) + o["dma_sub"]
                if waits.get(key, 0) < o["dma_val"] - 16:
                    waits[key] = max(waits.get(key, 0), o["dma_val"] - 16)
            for key, val in sorted(waits.items(), key=lambda kv: str(kv[0])):
                if waited.get(key, 0) >= val:
                    continue
                sem = dma_sems[key[1:]] if key[0] == "dma" else prog_sems[key[1]]
                eng_obj.wait_ge(sem, val)
                waited[key] = val
            if o["is_bar"]:
                inst = eng_obj.drain()
                drained_upto = self.eng_seq[o["idx"]]
            else:
                inst = o["fn"](eng_obj)
            if o["inc"] is not None:
                assert inst is not None, f"op {o['tag']} returned None but needs inc"
                inst.then_inc(prog_sems[ename], 1)
            if o["dma"] is not None:
                inst.then_inc(dma_sems[o["dma_sub"]], 16)


def build_nc(seq=S):
    nt = seq // T
    nxc = seq // 128
    nc = bass.Bass("TRN2", target_bir_lowering=False, debug=False)

    x_h = nc.dram_tensor("x", [seq, D], F32, kind="ExternalInput").ap()
    wq_h = nc.dram_tensor("wq", [NKC, 128, D], F32, kind="ExternalInput").ap()
    wk_h = nc.dram_tensor("wk", [NKC, 128, D], F32, kind="ExternalInput").ap()
    wv_h = nc.dram_tensor("wv", [D, D], F32, kind="ExternalInput").ap()
    wg_h = nc.dram_tensor("wg", [NKC, 128, D], F32, kind="ExternalInput").ap()
    wo_h = nc.dram_tensor("wo", [D, D], F32, kind="ExternalInput").ap()
    w13_h = nc.dram_tensor("w13", [NJ, 128, D], F32, kind="ExternalInput").ap()
    w2_h = nc.dram_tensor("w2t", [DF, D], F32, kind="ExternalInput").ap()
    slope_h = nc.dram_tensor("slope", [1, NH], F32, kind="ExternalInput").ap()
    nwa_h = nc.dram_tensor("nwa", [128, NH], F32, kind="ExternalInput").ap()
    nwf_h = nc.dram_tensor("nwf", [128, NJ2], F32, kind="ExternalInput").ap()
    eye_h = nc.dram_tensor("eye16", [128, 128], BF16, kind="ExternalInput").ap()
    eyef_h = nc.dram_tensor("eyef", [128, 128], F32, kind="ExternalInput").ap()
    prow_h = nc.dram_tensor("prow", [1, 128], F32, kind="ExternalInput").ap()
    mrow_h = nc.dram_tensor("mrow", [1, BLK], F32, kind="ExternalInput").ap()
    onesr_h = nc.dram_tensor("onesrow", [1, 128], F32, kind="ExternalInput").ap()
    mask_h = nc.dram_tensor("mask", [128, 2 * BLK], BF16, kind="ExternalInput").ap()
    out_h = nc.dram_tensor("out", [seq, D], F32, kind="ExternalOutput").ap()

    wqs = nc.dram_tensor("wqs", [NKC, 128, D], BF16, kind="Internal").ap()
    wks = nc.dram_tensor("wks", [NKC, 128, D], BF16, kind="Internal").ap()
    wvs = nc.dram_tensor("wvs", [4, NKC, 128, 512], BF16, kind="Internal").ap()
    wgs = nc.dram_tensor("wgs", [NKC, 128, D], BF16, kind="Internal").ap()
    wos = nc.dram_tensor("wos", [4, NKC, 128, 512], BF16, kind="Internal").ap()
    w13s = nc.dram_tensor("w13s", [NJ, 128, D], BF16, kind="Internal").ap()
    w2s = nc.dram_tensor("w2s", [4, NJ2, 128, 512], BF16, kind="Internal").ap()
    xns = nc.dram_tensor("xns", [D, seq], BF16, kind="Internal").ap()
    xqs = nc.dram_tensor("xqs", [D, seq], BF16, kind="Internal").ap()
    aos = nc.dram_tensor("aos", [seq, D], BF16, kind="Internal").ap()
    tbl = nc.dram_tensor("tbl", [2, NH * BLK], F32, kind="Internal").ap()

    P = Prog()

    def dma(qeng, sem, fn, reads, writes, tag=""):
        return P.op(qeng, fn, reads=reads, writes=writes, dma=sem, tag=tag)

    es = contextlib.ExitStack()
    with es:
        sbp = lambda n, sh, dt: es.enter_context(nc.sbuf_tensor(n, sh, dt))
        eye = sbp("eye", [128, 128], BF16)

        onesr = sbp("onesr_s", [1, 128], F32)
        onescol_b = sbp("onescol_b", [128, 1], BF16)
        mask = sbp("mask_s", [128, 2 * BLK], BF16)
        nwa = sbp("nwa_s", [128, NH], F32)
        nwf = sbp("nwf_s", [128, NJ2], F32)
        srow = sbp("srow", [1, NH], F32)
        bd_vec = sbp("bd_vec", [128, NH], F32)
        cb5 = sbp("cb5", [128, 1], F32)
        cb6 = sbp("cb6", [128, 1], F32)
        cbr = sbp("cbr", [128, 1], F32)
        kdec = sbp("kdec", [128, 2 * NH], F32)
        NWB = 5
        wbuf = [sbp(f"wbuf{i}", [128, D], BF16) for i in range(NWB)]
        wbufS = [sbp(f"wbufS{i}", [128, D], BF16) for i in range(2)]
        wf32 = [sbp(f"wf32{i}", [128, D], F32) for i in range(2)]

        ps = [es.enter_context(nc.psum_tensor(f"ps{i}", [128, 512], F32))
              for i in range(6)]

        prog_sems = {e: es.enter_context(nc.semaphore(f"pg_{e}")) for e in ENGS}
        dma_sems = {}
        for k, r in Prog.RING.items():
            for i in range(r):
                dma_sems[(k, i)] = es.enter_context(nc.semaphore(f"dm_{k}{i}"))

        esAB = contextlib.ExitStack()
        sbab = lambda n, sh, dt: esAB.enter_context(nc.sbuf_tensor(n, sh, dt))
        with esAB:
            qdbc = sbab("qdbc", [128, NH * BLK], BF16)
            ddT = sbab("ddT", [128, NH * 2 * BLK], BF16)
            kv = sbab("kv", [128, NH * HD], F32)

            es1 = contextlib.ExitStack()
            sb1 = lambda n, sh, dt: es1.enter_context(nc.sbuf_tensor(n, sh, dt))
            with es1:
                rA = sb1("rowA", [1, NH * BLK], F32)
                rB = sb1("rowB", [1, NH * BLK], F32)
                rC = sb1("rowC", [1, NH * BLK], F32)
                rD = sb1("rowD", [1, NH * BLK], F32)
                ddw = sb1("ddw", [128, 2 * BLK], F32)
                t16 = sb1("t16", [16, BLK], F32)
                prow = sb1("prow_s", [1, 128], F32)
                mrow = sb1("mrow_s", [1, BLK], F32)
                ones256 = sb1("ones256", [1, BLK], F32)
                psB = [es1.enter_context(nc.psum_tensor(f"psB{i}", [128, 1024], BF16))
                       for i in range(2)]
                tstg = [sb1(f"tstg{i}", [128, 1024], BF16) for i in range(2)]
                xb = [sb1(f"xb{i}", [128, D], F32) for i in range(3)]
                u16 = sb1("u16", [128, D], F16)
                xnb = [sb1(f"xnb{i}", [128, D], BF16) for i in range(2)]
                xqb = [sb1(f"xqb{i}", [128, D], BF16) for i in range(2)]
                NST = nxc
                stt_ = sb1("stats", [128, 18 * nxc], F32)
                SK = ["sum", "ss", "mx", "mn", "m", "var", "sd", "rstd", "t0",
                      "t1", "mab", "c", "rc", "a1q", "a0q", "isv", "nis", "nmr"]
                st = {k: (lambda i: lambda g: stt_[:, i * NST:(i + 1) * NST] if g is None
                          else stt_[:, i * NST + g:i * NST + g + 1])(i)
                      for i, k in enumerate(SK)}

                for name, dst, src in [
                        ("eye", eye, eye_h),
                        ("prow", prow, prow_h), ("mrow", mrow, mrow_h),
                        ("onesr", onesr, onesr_h), ("mask", mask, mask_h),
                        ("nwa", nwa, nwa_h), ("nwf", nwf, nwf_h),
                        ("srow", srow, slope_h)]:
                    dma("sync", "c",
                        (lambda d_, s_: lambda e: e.dma_start(out=d_[:, :], in_=s_))(dst, src),
                        reads=[], writes=[f"c:{name}"], tag=f"ld_{name}")
                P.op("vector", lambda e: e.memset(ones256[:, :], 1.0), writes=["c:ones256"])
                P.op("vector", lambda e: e.memset(onescol_b[:, :], 1.0), writes=["c:onescol"])
                P.op("vector", lambda e: e.memset(cb5[:, :], 1e-5), writes=["c:cb5"])
                P.op("vector", lambda e: e.memset(cb6[:, :], 1e-6), writes=["c:cb6"])
                P.op("vector", lambda e: e.memset(cbr[:, :], R16), writes=["c:cbr"])
                P.op("vector", lambda e: e.memset(kv[:, :], 0.0),
                     writes=["kv"] + [f"kv:{h}" for h in range(NH)])

                P.op("tensor", lambda e: e.matmul(ps[5][0:16, 0:BLK], srow[0:1, :], ones256[0:1, :], start=True, stop=True),
                     reads=["c:srow", "c:ones256"], writes=["ps5"], tag="mmS")
                P.op("vector", lambda e: e.tensor_copy(t16[:, :], ps[5][0:16, 0:BLK]),
                     reads=["ps5"], writes=["t16"], tag="evS")
                dma("gpsimd", "tbw", lambda e: e.dma_start(
                    out=tbl[0:1, :].rearrange("o (p c) -> p (o c)", p=16), in_=t16[:, :]),
                    reads=["t16"], writes=["tbl0"], tag="wS")
                dma("sync", "tb", lambda e: e.dma_start(out=rA[:, :], in_=tbl[0:1, :]),
                    reads=["tbl0"], writes=["rA"], tag="rS")
                P.op("tensor", lambda e: e.matmul(ps[5][0:16, 0:BLK], srow[0:1, :], mrow[0:1, :], start=True, stop=True),
                     reads=["c:srow", "c:mrow", "t16"], writes=["ps5"], tag="mmR")
                P.op("vector", lambda e: e.tensor_copy(t16[:, :], ps[5][0:16, 0:BLK]),
                     reads=["ps5"], writes=["t16"], tag="evR")
                dma("gpsimd", "tbw", lambda e: e.dma_start(
                    out=tbl[1:2, :].rearrange("o (p c) -> p (o c)", p=16), in_=t16[:, :]),
                    reads=["t16"], writes=["tbl1"], tag="wR")
                dma("sync", "tb", lambda e: e.dma_start(out=rB[:, :], in_=tbl[1:2, :]),
                    reads=["tbl1"], writes=["rB"], tag="rR")
                P.op("tensor", lambda e: e.matmul(ps[5][:, 0:NH], onesr[0:1, :], srow[0:1, :], start=True, stop=True),
                     reads=["c:onesr", "c:srow", "t16"], writes=["ps5"], tag="mmB")
                P.op("scalar", lambda e: e.activation(bd_vec[:, :], ps[5][:, 0:NH], AF.Exp, scale=-float(BLK)),
                     reads=["ps5"], writes=["bd_vec"], tag="bd")
                P.op("vector", lambda e: e.tensor_single_scalar(rC[:, :], rA[:, :], -255.0, ALU.mult),
                     reads=["rA"], writes=["rC"], tag="rk0")
                P.op("vector", lambda e: e.tensor_single_scalar(rD[:, :], rA[:, :], -127.0, ALU.mult),
                     reads=["rA"], writes=["rD"], tag="rk1")
                for h in range(NH):
                    for c in range(2):
                        col = 2 * h + c
                        rk = rC if c == 0 else rD
                        P.op("tensor", (lambda h_, col_: lambda e: e.matmul(
                            ps[4][:, col_:col_ + 1], prow[0:1, :],
                            rA[0:1, BLK * h_:BLK * h_ + 1], start=True, stop=False))(h, col),
                            reads=["c:prow", "rA"], writes=["ps4"], tag=f"kda{col}")
                        P.op("tensor", (lambda h_, col_, rk_: lambda e: e.matmul(
                            ps[4][:, col_:col_ + 1], onesr[0:1, :],
                            rk_[0:1, BLK * h_:BLK * h_ + 1], start=False, stop=True))(h, col, rk),
                            reads=["c:onesr", "rC", "rD"], writes=["ps4"], tag=f"kdb{col}")
                P.op("scalar", lambda e: e.activation(kdec[:, :], ps[4][:, 0:2 * NH], AF.Exp),
                     reads=["ps4"], writes=["kdec"], tag="kde")
                P.op("vector", lambda e: e.tensor_single_scalar(rC[:, :], rB[:, :], -1.0, ALU.mult),
                     reads=["rB"], writes=["rC"], tag="rc0")
                P.op("vector", lambda e: e.scalar_tensor_tensor(rD[:, :], rA[:, :], 128.0, rB[:, :], ALU.mult, ALU.subtract),
                     reads=["rA", "rB"], writes=["rD"], tag="rc1")
                for h in range(NH):
                    for c in range(2):
                        bank = 4 + (c % 2)
                        rowc = rC if c == 0 else rD
                        P.op("tensor", (lambda h_, b_: lambda e: e.matmul(
                            ps[b_][:, 0:BLK], prow[0:1, :],
                            rA[0:1, BLK * h_:BLK * (h_ + 1)], start=True, stop=False))(h, bank),
                            reads=["c:prow", "rA", "kdec"], writes=[f"ps{bank}"], tag=f"dda{h}{c}")
                        P.op("tensor", (lambda h_, b_, rc_: lambda e: e.matmul(
                            ps[b_][:, 0:BLK], onesr[0:1, :],
                            rc_[0:1, BLK * h_:BLK * (h_ + 1)], start=False, stop=True))(h, bank, rowc),
                            reads=["c:onesr", "rC", "rD"], writes=[f"ps{bank}"], tag=f"ddb{h}{c}")
                        P.op("vector", (lambda b_: lambda e: e.tensor_scalar(
                            ddw[:, 0:BLK], ps[b_][:, 0:BLK], 1.0, 0.0, ALU.mult, ALU.min))(bank),
                            reads=[f"ps{bank}"], writes=["ddw0"], tag=f"ddc{h}{c}")
                        P.op("scalar", lambda e: e.activation(ddw[:, BLK:2 * BLK], ddw[:, 0:BLK], AF.Exp),
                             reads=["ddw0"], writes=["ddw1"], tag=f"ddd{h}{c}")
                        P.op("vector", (lambda h_, c_: lambda e: e.tensor_mul(
                            ddT[:, (2 * h_ + c_) * BLK:(2 * h_ + c_ + 1) * BLK],
                            ddw[:, BLK:2 * BLK], mask[:, c_ * BLK:(c_ + 1) * BLK]))(h, c),
                            reads=["ddw1", "c:mask"], writes=[f"ddT:{h}:{c}"], tag=f"dde{h}{c}")
                P.op("vector", lambda e: e.tensor_add(rB[:, :], rB[:, :], rA[:, :]),
                     reads=["rA", "rB"], writes=["rB"], tag="r2")
                for cc in range(NH * BLK // 512):
                    bank = cc % 2
                    P.op("tensor", (lambda cc_, b_: lambda e: e.matmul(
                        ps[b_][:, :], onesr[0:1, :], rB[0:1, 512 * cc_:512 * (cc_ + 1)],
                        start=True, stop=True))(cc, bank),
                        reads=["rB", "c:onesr"], writes=[f"ps{bank}"], tag=f"qba{cc}")
                    P.op("scalar", (lambda cc_, b_: lambda e: e.activation(
                        qdbc[:, 512 * cc_:512 * (cc_ + 1)], ps[b_][:, :], AF.Exp, scale=-1.0))(cc, bank),
                        reads=[f"ps{bank}"], writes=[f"qdbc:{cc}"], tag=f"qbb{cc}")

                def stage3(src_ap, dst_ap, nchunk, key, q=("gpsimd", "wl", "gpsimd", "ws")):
                    lq, ls, wq_, ws_ = q
                    for c in range(nchunk):
                        f32b = f"wf32:{c % 2}"
                        bfb = f"wbufS:{c % 2}"
                        dma(lq, ls, (lambda c_: lambda e: e.dma_start(
                            out=wf32[c_ % 2][:, :], in_=src_ap[c_, :, :]))(c),
                            reads=[], writes=[f32b], tag=f"L{key}{c}")
                        P.op("gpsimd", (lambda c_: lambda e: e.tensor_copy(
                            wbufS[c_ % 2][:, :], wf32[c_ % 2][:, :]))(c),
                            reads=[f32b], writes=[bfb], tag=f"C{key}{c}")
                        dma(wq_, ws_, (lambda c_: lambda e: e.dma_start(
                            out=dst_ap[c_, :, :], in_=wbufS[c_ % 2][:, :]))(c),
                            reads=[bfb], writes=[f"{key}:{c}"], tag=f"S{key}{c}")

                def stage2(src_ap, dst_ap, nchunk, key, q=("gpsimd", "wl", "gpsimd", "ws")):
                    lq, ls, wq_, ws_ = q
                    for c in range(nchunk):
                        f32b = f"wf32:{c % 2}"
                        bfb = f"wbufS:{c % 2}"
                        dma(lq, ls, (lambda c_: lambda e: e.dma_start(
                            out=wf32[c_ % 2][:, :], in_=src_ap[128 * c_:128 * (c_ + 1), :]))(c),
                            reads=[], writes=[f32b], tag=f"L{key}{c}")
                        P.op("gpsimd", (lambda c_: lambda e: e.tensor_copy(
                            wbufS[c_ % 2][:, :], wf32[c_ % 2][:, :]))(c),
                            reads=[f32b], writes=[bfb], tag=f"C{key}{c}")
                        dma(wq_, ws_, (lambda c_: lambda e: e.dma_start(
                            out=dst_ap[:, c_, :, :].rearrange("d p c -> p d c"),
                            in_=wbufS[c_ % 2][:, :].rearrange("p (d c) -> p d c", d=4)))(c),
                            reads=[bfb], writes=[f"{key}:{c}"], tag=f"S{key}{c}")

                QA = ("sync", "wla", "gpsimd", "ws")
                stage3(wq_h, wqs, NKC, "wqs", QA)
                stage3(wk_h, wks, NKC, "wks", QA)
                stage2(wv_h, wvs, NKC, "wvs", QA)
                stage3(wg_h, wgs, NKC, "wgs", QA)
                stage2(wo_h, wos, NKC, "wos", QA)
                P.stage_ffn = lambda: (stage3(w13_h, w13s, NJ, "w13s"),
                                       stage2(w2_h, w2s, NJ2, "w2s"))

                for ci in range(nxc):
                    slot = ci % 3
                    xk = f"xb:{slot}"
                    dma("scalar", "xl", (lambda ci_, s_: lambda e: e.dma_start(
                        out=xb[s_][:, :], in_=x_h[128 * ci_:128 * (ci_ + 1), :]))(ci, slot),
                        reads=[], writes=[xk], tag=f"ldx{ci}")
                    P.op("scalar", (lambda s_, g_: lambda e: e.activation(
                        u16[:, :], xb[s_][:, :], AF.Square, accum_out=st["ss"](g_)))(slot, ci),
                        reads=[xk], writes=["u16", f"st1:{ci}"], tag=f"sq{ci}")
                    P.op("vector", (lambda s_, g_: lambda e: e.reduce_sum(
                        st["sum"](g_), xb[s_][:, :], axis=AX.X))(slot, ci),
                        reads=[xk], writes=[f"st1:{ci}"], tag=f"rs{ci}")
                    P.op("vector", (lambda s_, g_: lambda e: e.reduce_max(
                        st["mx"](g_), xb[s_][:, :], axis=AX.X))(slot, ci),
                        reads=[xk], writes=[f"st1:{ci}"], tag=f"rx{ci}")
                    P.op("vector", (lambda s_, g_: lambda e: e.tensor_reduce(
                        st["mn"](g_), xb[s_][:, :], AX.X, ALU.min))(slot, ci),
                        reads=[xk], writes=[f"st1:{ci}"], tag=f"rn{ci}")

                def chain_all(e):
                    s_ = lambda k: st[k](None)
                    e.tensor_single_scalar(s_("m"), s_("sum"), 1.0 / D, ALU.mult)
                    e.drain()
                    e.tensor_mul(s_("t0"), s_("m"), s_("m"))
                    e.drain()
                    e.scalar_tensor_tensor(s_("var"), s_("ss"), 1.0 / D, s_("t0"), ALU.mult, ALU.subtract)
                    return e.drain()
                P.op("vector", chain_all,
                     reads=[f"st1:{ci}" for ci in range(nxc)],
                     writes=["st2a"], tag="c1all")
                P.op("scalar", lambda e: e.activation(
                    st["sd"](None), st["var"](None), AF.Sqrt, bias=cb5[:, 0:1]),
                    reads=["st2a", "c:cb5"], writes=["st2b"], tag="c2all")

                def chain3_all(e):
                    s_ = lambda k: st[k](None)
                    e.reciprocal(s_("rstd"), s_("sd"))
                    e.drain()
                    e.tensor_sub(s_("t0"), s_("mx"), s_("m"))
                    e.tensor_sub(s_("t1"), s_("m"), s_("mn"))
                    e.drain()
                    e.tensor_max(s_("mab"), s_("t0"), s_("t1"))
                    e.drain()
                    e.tensor_mul(s_("t0"), s_("mab"), s_("rstd"))
                    e.drain()
                    e.tensor_single_scalar(s_("c"), s_("t0"), 1e-5, ALU.max)
                    e.drain()
                    e.reciprocal(s_("rc"), s_("c"))
                    e.drain()
                    e.scalar_tensor_tensor(s_("a1q"), s_("rstd"), 127.0, s_("rc"), ALU.mult, ALU.mult)
                    e.tensor_single_scalar(s_("isv"), s_("c"), 1.0 / 127.0, ALU.mult)
                    e.tensor_mul(s_("nmr"), s_("m"), s_("rstd"))
                    e.drain()
                    e.tensor_mul(s_("t1"), s_("m"), s_("a1q"))
                    e.tensor_single_scalar(s_("nis"), s_("isv"), -R16, ALU.mult)
                    e.tensor_single_scalar(s_("nmr"), s_("nmr"), -1.0, ALU.mult)
                    e.drain()
                    e.tensor_scalar(s_("a0q"), s_("t1"), -1.0, R16, ALU.mult, ALU.add)
                    return e.drain()
                P.op("vector", chain3_all, reads=["st2a", "st2b"],
                     writes=["st2"], tag="c3all")

                def p2_tr(ci, srcbuf, srckey, dst_hbm, okey, tagp):
                    for half in range(2):
                        pk = f"psB:{half}"
                        for k8 in range(8):
                            kk = 8 * half + k8
                            P.op("tensor", (lambda s_, k_, k8_, h_: lambda e: e.transpose(
                                psB[h_][:, 128 * k8_:128 * (k8_ + 1)],
                                srcbuf[s_][:, 128 * k_:128 * (k_ + 1)], eye[:, :]))(srckey, kk, k8, half),
                                reads=[f"{tagp}bb:{srckey}", "c:eye"], writes=[pk], tag=f"{tagp}t{ci}_{kk}")
                        P.op("vector", (lambda h_: lambda e: e.tensor_copy(
                            tstg[h_][:, :], psB[h_][:, :]))(half),
                            reads=[pk], writes=[f"tstg:{half}"], tag=f"{tagp}e{ci}_{half}")
                        dma("gpsimd", "aw", (lambda ci_, h_: lambda e: e.dma_start(
                            out=dst_hbm[1024 * h_:1024 * (h_ + 1), 128 * ci_:128 * (ci_ + 1)].rearrange(
                                "(k p) c -> p k c", k=8),
                            in_=tstg[h_][:, :].rearrange("p (k c) -> p k c", k=8)))(ci, half),
                            reads=[f"tstg:{half}"], writes=[f"{okey}:{ci // 4}"], tag=f"{tagp}w{ci}_{half}")

                for ci in range(nxc):
                    slot = ci % 3
                    xk = f"xb:{slot}"
                    dma("scalar", "xl", (lambda ci_, s_: lambda e: e.dma_start(
                        out=xb[s_][:, :], in_=x_h[128 * ci_:128 * (ci_ + 1), :]))(ci, slot),
                        reads=[], writes=[xk], tag=f"ldx2_{ci}")
                    P.op("scalar", (lambda s_, g_: lambda e: e.activation(
                        xnb[s_ % 2][:, :], xb[s_][:, :], AF.Identity,
                        bias=st["nmr"](g_), scale=st["rstd"](g_)))(slot, ci),
                        reads=[xk, "st2"], writes=[f"xnbb:{slot % 2}"], tag=f"xn{ci}")
                    p2_tr(ci, xnb, slot % 2, xns, "xns", "xn")
                    P.op("vector", (lambda s_, g_: lambda e: e.tensor_scalar(
                        u16[:, :], xb[s_][:, :], st["a1q"](g_), st["a0q"](g_), ALU.mult, ALU.add))(slot, ci),
                        reads=[xk, "st2"], writes=["u16"], tag=f"u{ci}")
                    P.op("vector", (lambda s_, g_: lambda e: e.tensor_scalar(
                        xqb[s_ % 2][:, :], u16[:, :], st["isv"](g_), st["nis"](g_), ALU.mult, ALU.add))(slot, ci),
                        reads=["u16", "st2"], writes=[f"xqbb:{slot % 2}"], tag=f"xq{ci}")
                    p2_tr(ci, xqb, slot % 2, xqs, "xqs", "xq")

            P.barrier()
            es2 = contextlib.ExitStack()
            sb2 = lambda n, sh, dt: es2.enter_context(nc.sbuf_tensor(n, sh, dt))
            with es2:
                P.stage_ffn()
                xnT = [sb2(f"xnT{i}", [128, NKC * T], BF16) for i in range(2)]
                qT = sb2("qT", [128, NH * T], BF16)
                kT = sb2("kT", [128, NH * T], BF16)
                v_tok = [sb2(f"v_tok{i}", [128, D], BF16) for i in range(4)]
                psBt = [es2.enter_context(nc.psum_tensor(f"psBt{i}", [128, 1024], BF16))
                        for i in range(2)]
                gateT = sb2("gateT", [128, NH * T], BF16)
                aT = sb2("aT", [128, NH * T], BF16)
                kvb = [sb2(f"kvb{i}", [128, HD], BF16) for i in range(4)]
                qdi = [sb2(f"qdi{i}", [128, BLK], BF16) for i in range(4)]
                qkd = [sb2(f"qkd{i}", [128, 2 * BLK], BF16) for i in range(2)]
                ktok = [sb2(f"ktok{i}", [128, BLK], BF16) for i in range(4)]
                sqs0 = sb2("sqs0", [128, T], BF16)
                arow = sb2("arow", [1, T], F32)
                arow2 = sb2("arow2", [1, T], F32)
                rstdA = sb2("rstdA", [128, T], F32)
                aostg = [sb2(f"aostg{i}", [128, 512], BF16) for i in range(2)]

                for t in range(nt):
                    xp = t % 2
                    xnTc = xnT[xp]
                    for kk in range(NKC):
                        qq = ("scalar", "tr") if kk % 2 == 0 else ("sync", "trs")
                        dma(qq[0], qq[1], (lambda t_, k_, b_: lambda e: e.dma_start(
                            out=b_[:, T * k_:T * (k_ + 1)],
                            in_=xns[128 * k_:128 * (k_ + 1), T * t_:T * (t_ + 1)]))(t, kk, xnTc),
                            reads=[f"xns:{t}"], writes=[f"xnT{xp}:{kk}"], tag=f"txn{t}_{kk}")
                    for gi, (wsrc, wkey, dstT, dkey, act) in enumerate([
                            (wqs, "wqs", qT, "qT", AF.Silu),
                            (wks, "wks", kT, "kT", AF.Silu),
                            (wgs, "wgs", gateT, "gateT", AF.Sigmoid)]):
                        for c in range(NKC):
                            slot = (gi * NKC + c) % NWB
                            bank = (gi * NKC + c) % 2
                            dma("sync", "wst", (lambda w_, c_, s_: lambda e: e.dma_start(
                                out=wbuf[s_][:, :], in_=w_[c_, :, :]))(wsrc, c, slot),
                                reads=[f"{wkey}:{c}"], writes=[f"wbuf:{slot}"], tag=f"sw{t}{gi}{c}")
                            for kk in range(NKC):
                                P.op("tensor", (lambda s_, b_, k_, x_: lambda e: e.matmul(
                                    ps[b_][:, :], wbuf[s_][:, 128 * k_:128 * (k_ + 1)],
                                    x_[:, T * k_:T * (k_ + 1)],
                                    start=(k_ == 0), stop=(k_ == NKC - 1)))(slot, bank, kk, xnTc),
                                    reads=[f"wbuf:{slot}", f"xnT{xp}:{kk}"],
                                    writes=[f"ps{bank}"], tag=f"g{t}_{gi}_{c}_{kk}")
                            P.op("scalar", (lambda b_, c_, d_, a_: lambda e: e.activation(
                                d_[:, T * c_:T * (c_ + 1)], ps[b_][:, :], a_))(bank, c, dstT, act),
                                reads=[f"ps{bank}"], writes=[f"{dkey}:{c}"], tag=f"e{t}_{gi}_{c}")
                    for ec in range(4):
                        for kg in range(NKC // 4):
                            slot = kg % NWB
                            wr = [f"wvs:{4 * kg + i}" for i in range(4)]
                            dma("sync", "wst", (lambda g_, s_, e_: lambda ee: ee.dma_start(
                                out=wbuf[s_][:, :].rearrange("p (k c) -> p k c", k=4),
                                in_=wvs[e_, 4 * g_:4 * (g_ + 1), :, :].rearrange("k p c -> p k c")))(kg, slot, ec),
                                reads=wr, writes=[f"wbuf:{slot}"], tag=f"swv{t}_{ec}_{kg}")
                            for ki in range(4):
                                kk = 4 * kg + ki
                                for tc in range(4):
                                    P.op("tensor", (lambda k_, s_, ki_, tc_, x_: lambda e: e.matmul(
                                        ps[tc_][:, :], x_[:, T * k_ + 128 * tc_:T * k_ + 128 * (tc_ + 1)],
                                        wbuf[s_][:, 512 * ki_:512 * (ki_ + 1)],
                                        start=(k_ == 0), stop=(k_ == NKC - 1)))(kk, slot, ki, tc, xnTc),
                                        reads=[f"wbuf:{slot}", f"xnT{xp}:{kk}"],
                                        writes=[f"ps{tc}"], tag=f"gv{t}_{ec}_{kk}_{tc}")
                        for tc in range(4):
                            P.op("scalar", (lambda e_, tc_: lambda e: e.activation(
                                v_tok[tc_][:, 512 * e_:512 * (e_ + 1)], ps[tc_][:, :], AF.Silu))(ec, tc),
                                reads=[f"ps{tc}"], writes=[f"vt:{tc}:{ec}"], tag=f"ev{t}_{ec}_{tc}")

                    NU = (T // BLK) * NH

                    def u_hb(u):
                        b, h = divmod(u, NH)
                        return h, b, T * h + b * BLK

                    def attn_pre(u):
                        h, b, hb = u_hb(u)
                        r = u % 4
                        r2 = u % 2
                        pt = u % 2
                        for cc in range(2):
                            P.op("tensor", (lambda hb_, cc_, p_: lambda e: e.transpose(
                                psBt[p_][:, 128 * cc_:128 * (cc_ + 1)],
                                kT[:, hb_ + 128 * cc_:hb_ + 128 * (cc_ + 1)], eye[:, :]))(hb, cc, pt),
                                reads=[f"kT:{h}", "c:eye"], writes=[f"psBt{pt}"], tag=f"tk{t}_{u}_{cc}")
                        for cc in range(2):
                            P.op("scalar", (lambda h_, cc_, r_, p_: lambda e: e.activation(
                                ktok[r_][:, 128 * cc_:128 * (cc_ + 1)],
                                psBt[p_][:, 128 * cc_:128 * (cc_ + 1)], AF.Copy,
                                scale=kdec[:, 2 * h_ + cc_:2 * h_ + cc_ + 1]))(h, cc, r, pt),
                                reads=[f"psBt{pt}", "kdec"], writes=[f"ktok:{r}"], tag=f"ke{t}_{u}_{cc}")
                        P.op("vector", (lambda h_, hb_, r_: lambda e: e.tensor_mul(
                            qdi[r_][:, :], qT[:, hb_:hb_ + BLK],
                            qdbc[:, BLK * h_:BLK * (h_ + 1)]))(h, hb, r),
                            reads=[f"qT:{h}", f"qdbc:{h // 2}"],
                            writes=[f"qdi:{r}"], tag=f"qd{t}_{u}")
                        P.op("scalar", (lambda h_, r_: lambda e: e.activation(
                            kvb[r_][:, :], kv[:, HD * h_:HD * (h_ + 1)], AF.Copy))(h, r),
                            reads=[f"kv:{h}"], writes=[f"kvb:{r}"], tag=f"kc{t}_{u}")

                    def attn_qk(u):
                        h, b, hb = u_hb(u)
                        pqk = u % 2
                        for cc in range(2):
                            P.op("tensor", (lambda hb_, cc_, p_: lambda e: e.matmul(
                                ps[p_][:, BLK * cc_:BLK * (cc_ + 1)],
                                kT[:, hb_ + 128 * cc_:hb_ + 128 * (cc_ + 1)],
                                qT[:, hb_:hb_ + BLK], start=True, stop=True))(hb, cc, pqk),
                                reads=[f"kT:{h}", f"qT:{h}"], writes=[f"ps{pqk}"], tag=f"qk{t}_{u}_{cc}")
                        P.op("vector", (lambda h_, u_, p_: lambda e: e.tensor_mul(
                            qkd[u_ % 2][:, :], ps[p_][:, 0:2 * BLK],
                            ddT[:, 2 * BLK * h_:2 * BLK * (h_ + 1)]))(h, u, pqk),
                            reads=[f"ps{pqk}", f"ddT:{h}:0", f"ddT:{h}:1"],
                            writes=[f"qkd:{u % 2}"], tag=f"qm{t}_{u}")

                    def attn_body(u):
                        h, b, hb = u_hb(u)
                        r = u % 4
                        r2 = u % 2
                        po = 2 + u % 2
                        pkv = 4 + u % 2
                        ve = h // 4
                        vsl = lambda cc: (2 * b + cc, h)
                        P.op("tensor", (lambda r_, p_: lambda e: e.matmul(
                            ps[p_][:, 0:BLK], kvb[r_][:, :], qdi[r_][:, :],
                            start=True, stop=False))(r, po),
                            reads=[f"kvb:{r}", f"qdi:{r}"], writes=[f"ps{po}"], tag=f"oi{t}_{u}")
                        for cc in range(2):
                            P.op("tensor", (lambda b_, cc_, h_, r_, p_: lambda e: e.matmul(
                                ps[p_][:, 0:BLK], v_tok[2 * b_ + cc_][:, 128 * h_:128 * (h_ + 1)],
                                qkd[u % 2][:, BLK * cc_:BLK * (cc_ + 1)],
                                start=False, stop=(cc_ == 1)))(b, cc, h, r, po),
                                reads=[f"vt:{2 * b + cc}:{ve}", f"qkd:{u % 2}"],
                                writes=[f"ps{po}"], tag=f"oa{t}_{u}_{cc}")
                        P.op("vector", (lambda hb_, p_, h_, b_: lambda e: e.tensor_copy(
                            aT[:, hb_:hb_ + BLK], ps[p_][:, 0:BLK]))(hb, po, h, b),
                            reads=[f"ps{po}"], writes=[f"aT:{h}:{b}"], tag=f"oe{t}_{u}")
                        for cc in range(2):
                            P.op("tensor", (lambda b_, cc_, h_, r_, p_: lambda e: e.matmul(
                                ps[p_][:, 0:HD], ktok[r_][:, 128 * cc_:128 * (cc_ + 1)],
                                v_tok[2 * b_ + cc_][:, 128 * h_:128 * (h_ + 1)],
                                start=(cc_ == 0), stop=(cc_ == 1)))(b, cc, h, r, pkv),
                                reads=[f"ktok:{r}", f"vt:{2 * b + cc}:{ve}"],
                                writes=[f"ps{pkv}"], tag=f"km{t}_{u}_{cc}")
                        P.op("vector", (lambda h_, p_: lambda e: e.scalar_tensor_tensor(
                            kv[:, HD * h_:HD * (h_ + 1)], kv[:, HD * h_:HD * (h_ + 1)],
                            bd_vec[:, h_:h_ + 1], ps[p_][:, 0:HD], ALU.mult, ALU.add))(h, pkv),
                            reads=[f"ps{pkv}", f"kv:{h}", "bd_vec"],
                            writes=[f"kv:{h}"], tag=f"ku{t}_{u}")

                    for u in range(3):
                        attn_pre(u)
                    attn_qk(0)
                    for u in range(NU):
                        if u + 1 < NU:
                            attn_qk(u + 1)
                        if u + 3 < NU:
                            attn_pre(u + 3)
                        attn_body(u)

                    sqs = [qkd[0][:, :], qkd[1][:, :], sqs0[:, :]]
                    skey = ["qkd:0", "qkd:1", "sqs:0"]
                    def av_mm(h_):
                        P.op("tensor", (lambda hh: lambda e: e.matmul(
                            ps[0][0:1, 0:T], onescol_b[:, 0:1], sqs[hh % 3],
                            start=(hh == 0), stop=(hh == NH - 1)))(h_),
                            reads=[skey[h_ % 3], "c:onescol"], writes=["ps0"], tag=f"av{t}_{h_}")
                    for h in range(NH):
                        P.op("scalar", (lambda h_: lambda e: e.activation(
                            sqs[h_ % 3], aT[:, T * h_:T * (h_ + 1)], AF.Square))(h),
                            reads=[f"aT:{h}:0", f"aT:{h}:1"], writes=[skey[h % 3]], tag=f"as{t}_{h}")
                        if h >= 2:
                            av_mm(h - 2)
                    av_mm(NH - 2)
                    av_mm(NH - 1)
                    P.op("vector", lambda e: e.tensor_single_scalar(
                        arow[0:1, 0:T], ps[0][0:1, 0:T], 1.0 / D, ALU.mult),
                        reads=["ps0"], writes=["arow"], tag=f"an1{t}")
                    P.op("scalar", lambda e: e.activation(arow2[0:1, 0:T], arow[0:1, 0:T], AF.Sqrt, bias=cb6[0:1, 0:1]),
                         reads=["arow", "c:cb6"], writes=["arow2"], tag=f"an2{t}")
                    P.op("vector", lambda e: e.reciprocal(arow[0:1, 0:T], arow2[0:1, 0:T]),
                         reads=["arow2"], writes=["arow"], tag=f"an3{t}")
                    P.op("tensor", lambda e: e.matmul(ps[1][:, 0:T], onesr[0:1, :], arow[0:1, 0:T], start=True, stop=True),
                         reads=["arow", "c:onesr"], writes=["ps1"], tag=f"an4{t}")
                    P.op("vector", lambda e: e.tensor_copy(rstdA[:, 0:T], ps[1][:, 0:T]),
                         reads=["ps1"], writes=["rstdA"], tag=f"an5{t}")
                    for h in range(NH):
                        P.op("vector", (lambda h_: lambda e: e.scalar_tensor_tensor(
                            sqs[h_ % 3], aT[:, T * h_:T * (h_ + 1)], nwa[:, h_:h_ + 1],
                            rstdA[:, 0:T], ALU.mult, ALU.mult))(h),
                            reads=[f"aT:{h}:0", f"aT:{h}:1", "rstdA", "c:nwa"],
                            writes=[skey[h % 3]], tag=f"nm{t}_{h}")
                        P.op("vector", (lambda h_: lambda e: e.tensor_mul(
                            aT[:, T * h_:T * (h_ + 1)], sqs[h_ % 3],
                            gateT[:, T * h_:T * (h_ + 1)]))(h),
                            reads=[skey[h % 3], f"gateT:{h}"],
                            writes=[f"aT:{h}:0", f"aT:{h}:1", f"ga:{h}"], tag=f"gm{t}_{h}")

                    for dchunk in range(4):
                        for kg in range(NKC // 4):
                            slot = kg % NWB
                            wr = [f"wos:{4 * kg + i}" for i in range(4)]
                            dma("sync", "wst", (lambda g_, s_, d_: lambda e: e.dma_start(
                                out=wbuf[s_][:, :].rearrange("p (k c) -> p k c", k=4),
                                in_=wos[d_, 4 * g_:4 * (g_ + 1), :, :].rearrange("k p c -> p k c")))(kg, slot, dchunk),
                                reads=wr, writes=[f"wbuf:{slot}"], tag=f"so{t}_{dchunk}_{kg}")
                            for ki in range(4):
                                kk = 4 * kg + ki
                                for tc in range(4):
                                    P.op("tensor", (lambda k_, s_, ki_, tc_: lambda e: e.matmul(
                                        ps[2 + tc_][:, :],
                                        aT[:, T * k_ + 128 * tc_:T * k_ + 128 * (tc_ + 1)],
                                        wbuf[s_][:, 512 * ki_:512 * (ki_ + 1)],
                                        start=(k_ == 0), stop=(k_ == NKC - 1)))(kk, slot, ki, tc),
                                        reads=[f"wbuf:{slot}", f"ga:{kk}"],
                                        writes=[f"ps{2 + tc}"], tag=f"op{t}_{dchunk}_{kk}_{tc}")
                        for tc in range(4):
                            so = (dchunk * 4 + tc) % 2
                            P.op("scalar", (lambda tc_, s_: lambda e: e.activation(
                                aostg[s_][:, :], ps[2 + tc_][:, :], AF.Copy))(tc, so),
                                reads=[f"ps{2 + tc}"], writes=[f"aostg:{so}"], tag=f"ae{t}_{dchunk}_{tc}")
                            dma("scalar", "ao", (lambda t_, tc_, d_, s_: lambda e: e.dma_start(
                                out=aos[T * t_ + 128 * tc_:T * t_ + 128 * (tc_ + 1),
                                        512 * d_:512 * (d_ + 1)],
                                in_=aostg[s_][:, :]))(t, tc, dchunk, so),
                                reads=[f"aostg:{so}"], writes=[f"aos:{t}"], tag=f"aw{t}_{dchunk}_{tc}")

            P.barrier()
        es3 = contextlib.ExitStack()
        sb3 = lambda n, sh, dt: es3.enter_context(nc.sbuf_tensor(n, sh, dt))
        with es3:
            xqT = sb3("xqT", [128, NKC * T], BF16)
            gat = sb3("gat", [128, NJ2 * T], BF16)
            iqT = sb3("iqT", [128, NJ2 * T], BF16)
            a_scr = [sb3(f"ascr{i}", [128, T], F32) for i in range(2)]
            sq2 = [sb3(f"sq2_{i}", [128, T], BF16) for i in range(4)]
            tmpb = [sb3(f"tmpb{i}", [128, T], BF16) for i in range(2)]
            Mb = [sb3(f"Mb{i}", [128, T], BF16) for i in range(2)]
            mbt = [sb3(f"mbt{i}", [128, 128], BF16) for i in range(2)]
            Ub = sb3("Ub", [128, T], F32)
            u16b = [sb3(f"u16b{i}", [128, T], F16) for i in range(2)]
            qrow = sb3("qrow", [1, T], F32)
            qt_ = sb3("qstats", [128, 88], F32)
            QK = ["ss", "M", "var", "sd", "rstd", "c", "rc", "s2", "is2", "U"]
            iqs = [sb3(f"iqs{i}", [128, T], F32) for i in range(2)]
            eyef = sb3("eyef_s", [128, 128], F32)
            ps6 = es3.enter_context(nc.psum_tensor("ps6", [128, 512], F32))
            ps7 = es3.enter_context(nc.psum_tensor("ps7", [128, 512], F32))
            ps67 = [ps6, ps7]
            wbuf2 = [sb3(f"wbuf2_{i}", [128, 2048], BF16) for i in range(3)]
            xres = [sb3(f"xres{i}", [128, 512], F32) for i in range(2)]
            aores = [sb3(f"aores{i}", [128, 512], BF16) for i in range(2)]
            ostg = [sb3(f"ostg{i}", [128, 512], F32) for i in range(2)]

            def p4_txq(t):
                for kk in range(NKC):
                    qq = ("scalar", "tr") if kk % 2 == 0 else ("sync", "trs")
                    dma(qq[0], qq[1], (lambda t_, k_: lambda e: e.dma_start(
                        out=xqT[:, T * k_:T * (k_ + 1)],
                        in_=xqs[128 * k_:128 * (k_ + 1), T * t_:T * (t_ + 1)]))(t, kk),
                        reads=[f"xqs:{t}"], writes=[f"xqT:{kk}"], tag=f"txq{t}_{kk}")

            def p4_w13(t):
                pq = t % 2
                P.op("vector", (lambda p_: lambda e: e.memset(Mb[p_][:, :], 0.0))(pq),
                     reads=[], writes=[f"Mb{pq}"], tag=f"mz{t}")

                def ss_mm(j_):
                    P.op("tensor", (lambda jx: lambda e: e.matmul(
                        ps[4][0:1, 0:T], onescol_b[:, 0:1], sq2[jx % 4][:, :],
                        start=(jx == 0), stop=(jx == NJ2 - 1)))(j_),
                        reads=[f"sq2:{j_ % 4}", "c:onescol"], writes=["ps4"], tag=f"ss{t}_{j_}")
                for jj in range(NJ2):
                    b1 = 2 * (jj % 2)
                    b3 = b1 + 1
                    for half, (jc, bank) in enumerate([(jj, b1), (jj + NJ2, b3)]):
                        slot = (2 * jj + half) % NWB
                        dma("sync", "wst", (lambda j_, s_: lambda e: e.dma_start(
                            out=wbuf[s_][:, :], in_=w13s[j_, :, :]))(jc, slot),
                            reads=[f"w13s:{jc}"], writes=[f"wbuf:{slot}"], tag=f"s13_{t}_{jc}")
                        for kk in range(NKC):
                            P.op("tensor", (lambda s_, b_, k_: lambda e: e.matmul(
                                ps[b_][:, :], wbuf[s_][:, 128 * k_:128 * (k_ + 1)],
                                xqT[:, T * k_:T * (k_ + 1)], start=(k_ == 0), stop=(k_ == NKC - 1)))(slot, bank, kk),
                                reads=[f"wbuf:{slot}", f"xqT:{kk}"],
                                writes=[f"ps{bank}"], tag=f"w13_{t}_{jc}_{kk}")
                    asl = jj % 2
                    P.op("scalar", (lambda b_, s_: lambda e: e.activation(
                        a_scr[s_][:, :], ps[b_][:, :], AF.Relu))(b1, asl),
                        reads=[f"ps{b1}"], writes=[f"ascr:{asl}"], tag=f"rl{t}_{jj}")
                    P.op("scalar", (lambda s_: lambda e: e.activation(
                        a_scr[s_][:, :], a_scr[s_][:, :], AF.Square))(asl),
                        reads=[f"ascr:{asl}"], writes=[f"ascr:{asl}"], tag=f"sr{t}_{jj}")
                    P.op("vector", (lambda j_, b_, s_: lambda e: e.tensor_mul(
                        gat[:, T * j_:T * (j_ + 1)], a_scr[s_][:, :], ps[b_][:, :]))(jj, b3, asl),
                        reads=[f"ascr:{asl}", f"ps{b3}"], writes=[f"gat:{jj}"], tag=f"gm{t}_{jj}")
                    P.op("gpsimd", (lambda j_, s_: lambda e: e.tensor_mul(
                        sq2[s_][:, :], gat[:, T * j_:T * (j_ + 1)],
                        gat[:, T * j_:T * (j_ + 1)]))(jj, jj % 4),
                        reads=[f"gat:{jj}"], writes=[f"sq2:{jj % 4}"], tag=f"g2{t}_{jj}")
                    if jj >= 2:
                        ss_mm(jj - 2)
                    if jj == NJ2 - 1:
                        ss_mm(NJ2 - 2)
                        ss_mm(NJ2 - 1)
                    P.op("scalar", (lambda j_, s_: lambda e: e.activation(
                        tmpb[s_][:, :], gat[:, T * j_:T * (j_ + 1)], AF.Abs,
                        scale=nwf[:, j_:j_ + 1]))(jj, asl),
                        reads=[f"gat:{jj}", "c:nwf"], writes=[f"tmpb:{asl}"], tag=f"ab{t}_{jj}")
                    P.op("vector", (lambda s_, p_: lambda e: e.tensor_max(
                        Mb[p_][:, :], Mb[p_][:, :], tmpb[s_][:, :]))(asl, pq),
                        reads=[f"tmpb:{asl}", f"Mb{pq}"], writes=[f"Mb{pq}"], tag=f"mm{t}_{jj}")
                    yield

            def p4_quant2_iq(t):
                pq = t % 2
                qt = {k: qt_[:, 8 * i + 4 * pq:8 * i + 4 * (pq + 1)] for i, k in enumerate(QK)}
                for tc in range(4):
                    dma("scalar", "trv", (lambda tc_, p_: lambda e: e.dma_start(
                        out=mbt[tc_ % 2][:, :], in_=Mb[p_][:, 128 * tc_:128 * (tc_ + 1)],
                        transpose=True))(tc, pq),
                        reads=[f"Mb{pq}"], writes=[f"mbt:{tc % 2}"], tag=f"mx{t}_{tc}")
                    P.op("vector", (lambda tc_, q_: lambda e: e.reduce_max(
                        q_["M"][:, tc_:tc_ + 1], mbt[tc_ % 2][:, :], axis=AX.X))(tc, qt),
                        reads=[f"mbt:{tc % 2}"], writes=[f"qt{pq}:M"], tag=f"mr{t}_{tc}")
                P.op("vector", lambda e: e.tensor_copy(qrow[0:1, 0:T], ps[4][0:1, 0:T]),
                     reads=["ps4"], writes=["qrow"], tag=f"sc{t}")
                for tc in range(4):
                    P.op("tensor", (lambda tc_: lambda e: e.matmul(
                        ps[5][:, tc_:tc_ + 1], qrow[0:1, 128 * tc_:128 * (tc_ + 1)],
                        onesr[0:1, 0:1], start=True, stop=True))(tc),
                        reads=["qrow", "c:onesr"], writes=["ps5"], tag=f"st{t}_{tc}")
                P.op("vector", (lambda q_: lambda e: e.tensor_copy(q_["ss"][:, 0:4], ps[5][:, 0:4]))(qt),
                     reads=["ps5"], writes=[f"qt{pq}:ss"], tag=f"se{t}")
                P.op("vector", (lambda q_: lambda e: e.tensor_single_scalar(
                    q_["var"][:, :], q_["ss"][:, :], 1.0 / DF, ALU.mult))(qt),
                    reads=[f"qt{pq}:ss"], writes=[f"qt{pq}:var"], tag=f"q1{t}")
                P.op("scalar", (lambda q_: lambda e: e.activation(
                    q_["sd"][:, :], q_["var"][:, :], AF.Sqrt, bias=cb5[:, 0:1]))(qt),
                    reads=[f"qt{pq}:var", "c:cb5"], writes=[f"qt{pq}:sd"], tag=f"q2{t}")

                def q2c(e, q_=qt):
                    e.reciprocal(q_["rstd"][:, :], q_["sd"][:, :])
                    e.drain()
                    e.tensor_mul(q_["c"][:, :], q_["rstd"][:, :], q_["M"][:, :])
                    e.drain()
                    e.tensor_single_scalar(q_["c"][:, :], q_["c"][:, :], 1e-5, ALU.max)
                    e.drain()
                    e.reciprocal(q_["rc"][:, :], q_["c"][:, :])
                    e.drain()
                    e.tensor_single_scalar(q_["s2"][:, :], q_["rc"][:, :], 127.0, ALU.mult)
                    e.tensor_single_scalar(q_["is2"][:, :], q_["c"][:, :], 1.0 / 127.0, ALU.mult)
                    e.drain()
                    e.tensor_mul(q_["U"][:, :], q_["rstd"][:, :], q_["s2"][:, :])
                    return e.drain()
                P.op("vector", q2c,
                     reads=[f"qt{pq}:sd", f"qt{pq}:M"],
                     writes=[f"qt{pq}:U", f"qt{pq}:is2"], tag=f"q3{t}")
                for tc in range(4):
                    P.op("tensor", (lambda tc_, q_: lambda e: e.transpose(
                        ps[5][0:1, 128 * tc_:128 * (tc_ + 1)],
                        q_["U"][:, tc_:tc_ + 1], eyef[:, :]))(tc, qt),
                        reads=[f"qt{pq}:U", "c:eyef"], writes=["ps5"], tag=f"ut{t}_{tc}")
                P.op("vector", lambda e: e.tensor_copy(qrow[0:1, 0:T], ps[5][0:1, 0:T]),
                     reads=["ps5"], writes=["qrow"], tag=f"ue{t}")
                P.op("tensor", lambda e: e.matmul(ps[5][:, 0:T], onesr[0:1, :], qrow[0:1, 0:T], start=True, stop=True),
                     reads=["qrow", "c:onesr"], writes=["ps5"], tag=f"ub{t}")
                P.op("vector", lambda e: e.tensor_copy(Ub[:, :], ps[5][:, 0:T]),
                     reads=["ps5"], writes=["Ub"], tag=f"uc{t}")
            def p4_iq(t):
                def i3(jj):
                    s_ = jj % 2
                    P.op("gpsimd", (lambda j_, sl_: lambda e: e.tensor_scalar(
                        iqT[:, T * j_:T * (j_ + 1)], u16b[sl_][:, :], 1.0, -R16,
                        ALU.mult, ALU.add))(jj, s_),
                        reads=[f"u16b:{s_}"], writes=[f"iqT:{jj}"], tag=f"i3{t}_{jj}")
                for jj in range(NJ2):
                    s_ = jj % 2
                    P.op("gpsimd", (lambda j_, sl_: lambda e: e.tensor_mul(
                        iqs[sl_][:, :], gat[:, T * j_:T * (j_ + 1)], Ub[:, :]))(jj, s_),
                        reads=[f"gat:{jj}", "Ub"], writes=[f"iqs:{s_}"], tag=f"i1{t}_{jj}")
                    P.op("scalar", (lambda j_, sl_: lambda e: e.activation(
                        u16b[sl_][:, :], iqs[sl_][:, :], AF.Identity, bias=cbr[:, 0:1],
                        scale=nwf[:, j_:j_ + 1]))(jj, s_),
                        reads=[f"iqs:{s_}", "c:cbr", "c:nwf"], writes=[f"u16b:{s_}"], tag=f"i2{t}_{jj}")
                    if jj >= 1:
                        i3(jj - 1)
                    if jj == NJ2 - 1:
                        i3(jj)
                    yield

            def p4_w2(t, giq):
                if giq is not None:
                    next(giq, None)
                    next(giq, None)
                pq = t % 2
                qt = {k: qt_[:, 8 * i + 4 * pq:8 * i + 4 * (pq + 1)] for i, k in enumerate(QK)}
                for dchunk in range(4):
                    for kg in range(NJ2 // 4):
                        slot = kg % 3
                        wr = [f"w2s:{4 * kg + i}" for i in range(4)]
                        dma("sync", "w2l", (lambda g_, s_, d_: lambda e: e.dma_start(
                            out=wbuf2[s_][:, :].rearrange("p (k c) -> p k c", k=4),
                            in_=w2s[d_, 4 * g_:4 * (g_ + 1), :, :].rearrange("k p c -> p k c")))(kg, slot, dchunk),
                            reads=wr, writes=[f"wbuf2:{slot}"], tag=f"s2_{t}_{dchunk}_{kg}")
                        for ki in range(4):
                            kk = 4 * kg + ki
                            if giq is not None and dchunk == 0:
                                next(giq, None)
                            for tc in range(4):
                                P.op("tensor", (lambda k_, s_, ki_, tc_: lambda e: e.matmul(
                                    ps[tc_][:, :], iqT[:, T * k_ + 128 * tc_:T * k_ + 128 * (tc_ + 1)],
                                    wbuf2[s_][:, 512 * ki_:512 * (ki_ + 1)],
                                    start=(k_ == 0), stop=(k_ == NJ2 - 1)))(kk, slot, ki, tc),
                                    reads=[f"wbuf2:{slot}", f"iqT:{kk}"],
                                    writes=[f"ps{tc}"], tag=f"w2_{t}_{dchunk}_{kk}_{tc}")
                    for tc in range(4):
                        sl = tc % 2
                        dma("scalar", "rl", (lambda t_, tc_, d_, s_: lambda e: e.dma_start(
                            out=xres[s_][:, :],
                            in_=x_h[T * t_ + 128 * tc_:T * t_ + 128 * (tc_ + 1),
                                    512 * d_:512 * (d_ + 1)]))(t, tc, dchunk, sl),
                            reads=[], writes=[f"xres:{sl}"], tag=f"lx{t}_{dchunk}_{tc}")
                        dma("scalar", "rl", (lambda t_, tc_, d_, s_: lambda e: e.dma_start(
                            out=aores[s_][:, :],
                            in_=aos[T * t_ + 128 * tc_:T * t_ + 128 * (tc_ + 1),
                                    512 * d_:512 * (d_ + 1)]))(t, tc, dchunk, sl),
                            reads=[f"aos:{t}"], writes=[f"aores:{sl}"], tag=f"la{t}_{dchunk}_{tc}")
                        P.op("vector", (lambda tc_, s_, q_: lambda e: e.scalar_tensor_tensor(
                            ostg[s_][:, :], ps[tc_][:, :], q_["is2"][:, tc_:tc_ + 1],
                            xres[s_][:, :], ALU.mult, ALU.add))(tc, sl, qt),
                            reads=[f"ps{tc}", f"xres:{sl}", f"qt{pq}:is2"],
                            writes=[f"ostg:{sl}"], tag=f"o1{t}_{dchunk}_{tc}")
                        P.op("gpsimd", (lambda s_: lambda e: e.tensor_add(
                            ostg[s_][:, :], ostg[s_][:, :], aores[s_][:, :]))(sl),
                            reads=[f"ostg:{sl}", f"aores:{sl}"], writes=[f"ostg:{sl}"], tag=f"o2{t}_{dchunk}_{tc}")
                        dma("gpsimd", "ol", (lambda t_, tc_, d_, s_: lambda e: e.dma_start(
                            out=out_h[T * t_ + 128 * tc_:T * t_ + 128 * (tc_ + 1),
                                      512 * d_:512 * (d_ + 1)],
                            in_=ostg[s_][:, :]))(t, tc, dchunk, sl),
                            reads=[f"ostg:{sl}"], writes=[f"odone:{sl}"], tag=f"wo{t}_{dchunk}_{tc}")

            dma("sync", "c", lambda e: e.dma_start(out=eyef[:, :], in_=eyef_h),
                reads=[], writes=["c:eyef"], tag="ld_eyef")
            for t in range(nt):
                p4_txq(t)
                g13 = p4_w13(t)
                for _ in g13:
                    pass
                p4_quant2_iq(t)
                giq = p4_iq(t)
                p4_w2(t, giq)
                for _ in giq:
                    pass


            P.resolve()
            with nc.Block() as blk:
                @blk.sync
                def _(eng):
                    P.emit("sync", eng, prog_sems, dma_sems)
                    r = Prog.RING["ol"]
                    tot = P.dma_tot["ol"]
                    for i in range(r):
                        n_i = tot // r + (1 if i < tot % r else 0)
                        if n_i:
                            eng.wait_ge(dma_sems[("ol", i)], 16 * n_i)

                @blk.vector
                def _(eng):
                    P.emit("vector", eng, prog_sems, dma_sems)

                @blk.scalar
                def _(eng):
                    P.emit("scalar", eng, prog_sems, dma_sems)

                @blk.tensor
                def _(eng):
                    P.emit("tensor", eng, prog_sems, dma_sems)

                @blk.gpsimd
                def _(eng):
                    P.emit("gpsimd", eng, prog_sems, dma_sems)

    return nc


def _host_consts():
    c = {}
    c["eye16"] = np.eye(128, dtype=ml_dtypes.bfloat16)
    c["eyef"] = np.eye(128, dtype=np.float32)
    c["prow"] = np.arange(128, dtype=np.float32).reshape(1, 128)
    c["mrow"] = np.arange(BLK, dtype=np.float32).reshape(1, BLK)
    c["onesrow"] = np.ones((1, 128), dtype=np.float32)
    p = np.arange(128).reshape(128, 1)
    m = np.arange(2 * BLK).reshape(1, 2 * BLK)
    c["mask"] = ((m % BLK) >= (128 * (m // BLK) + p)).astype(ml_dtypes.bfloat16)
    return c


def _blocked_lhsT(wt):
    Dd, ncol = wt.shape
    nk, ncch = Dd // 128, ncol // 128
    return np.ascontiguousarray(
        wt.reshape(nk, 128, ncch, 128).transpose(2, 1, 0, 3).reshape(ncch, 128, Dd))


def kernel(L0_attn_hidden_states, L0_attn_slope_rate, L0_attn_w_qkv,
           L0_attn_w_gate, L0_attn_w_out, L0_attn_norm_w,
           L0_ffn_w13, L0_ffn_w2, L0_ffn_norm_w):
    x = np.ascontiguousarray(np.asarray(L0_attn_hidden_states, dtype=np.float32))
    wqkv = np.asarray(L0_attn_w_qkv, dtype=np.float32).reshape(NH, 3, HD, D)
    wq = _blocked_lhsT(np.ascontiguousarray(wqkv[:, 0].reshape(D, D).T))
    wk = _blocked_lhsT(np.ascontiguousarray(wqkv[:, 1].reshape(D, D).T))
    wv = np.ascontiguousarray(wqkv[:, 2].reshape(D, D).T)
    wg = _blocked_lhsT(np.ascontiguousarray(np.asarray(L0_attn_w_gate, np.float32).T))
    wo = np.ascontiguousarray(np.asarray(L0_attn_w_out, np.float32).T)
    w13 = _blocked_lhsT(np.ascontiguousarray(np.asarray(L0_ffn_w13, np.float32).T))
    w2t = np.ascontiguousarray(np.asarray(L0_ffn_w2, np.float32).T)
    slope = np.ascontiguousarray(np.asarray(L0_attn_slope_rate, np.float32).reshape(1, NH))
    nwa = np.ascontiguousarray(np.asarray(L0_attn_norm_w, np.float32).reshape(NH, 128).T)
    nwf = np.ascontiguousarray(np.asarray(L0_ffn_norm_w, np.float32).reshape(NJ2, 128).T)
    consts = _host_consts()

    nc = build_nc(seq=S)
    base = dict(wq=wq, wk=wk, wv=wv, wg=wg, wo=wo, w13=w13, w2t=w2t,
                slope=slope, nwa=nwa, nwf=nwf, **consts)
    in_maps = [dict(base, x=np.ascontiguousarray(x[b])) for b in range(NCORES)]
    res = run_bass_kernel_spmd(nc, in_maps, core_ids=list(range(NCORES)))
    out = np.stack([np.asarray(res.results[b]["out"]) for b in range(NCORES)], axis=0)
    return out.astype(np.float32)


# version 18
# speedup vs baseline: 1.2210x; 1.0033x over previous
import numpy as np
import ml_dtypes
import contextlib
import concourse.bass as bass
import concourse.mybir as mybir
from concourse.bass_utils import run_bass_kernel_spmd
from concourse.alu_op_type import AluOpType as ALU

F32 = mybir.dt.float32
BF16 = mybir.dt.bfloat16
F16 = mybir.dt.float16
AF = mybir.ActivationFunctionType
AX = mybir.AxisListType

NCORES = 8
D = 2048
NH = 16
HD = 128
BLK = 256
DF = 5632
S = 4096
T = 512
NKC = D // 128
NJ = 2 * DF // 128
NJ2 = DF // 128
LN_EPS = 1e-5
AN_EPS = 1e-6
RMS_EPS = 1e-5
R16 = 1536.0

ENGS = ("tensor", "vector", "scalar", "gpsimd", "sync")


class Prog:

    def __init__(self):
        self.ops = []
        self.last_w = {}
        self.readers = {}
        self.bar_deps = []
        self.nbar = 0

    def op(self, engine, fn, reads=(), writes=(), dma=None, tag=""):
        idx = len(self.ops)
        deps = set()
        for b in reads:
            if b in self.last_w:
                deps.add(self.last_w[b])
        for b in writes:
            if b in self.last_w:
                deps.add(self.last_w[b])
            for r in self.readers.get(b, ()):
                deps.add(r)
        deps.update(self.bar_deps)
        deps.discard(idx)
        if dma is not None:
            if not hasattr(self, "dma_eng"):
                self.dma_eng = {}
            assert self.dma_eng.setdefault(dma, engine) == engine, \
                f"dma sem {dma} used from {engine} and {self.dma_eng[dma]}"
        self.ops.append(dict(idx=idx, engine=engine, fn=fn, deps=deps,
                             dma=dma, dma_idx=None, tag=tag, inc=None,
                             is_bar=False))
        for b in reads:
            self.readers.setdefault(b, []).append(idx)
        for b in writes:
            self.last_w[b] = idx
            self.readers[b] = []
        return idx

    def barrier(self):
        self.nbar += 1
        prior = list(range(len(self.ops)))
        bar_idxs = []
        for e in ENGS:
            idx = len(self.ops)
            self.ops.append(dict(idx=idx, engine=e, fn=None, deps=set(prior),
                                 dma=None, dma_idx=None, tag=f"bar{self.nbar}",
                                 inc=None, is_bar=True))
            bar_idxs.append(idx)
        self.bar_deps = bar_idxs

    RING = {"c": 1, "wl": 3, "ws": 2, "wst": 6, "xl": 3, "aw": 2, "tr": 4,
            "rl": 4, "ol": 2, "tb": 1, "tbw": 1, "ao": 2, "wsa": 2, "wla": 3,
            "trk": 4, "trv": 4, "trs": 4, "w2l": 4}

    def resolve(self):
        eng_seq = {}
        seq_ctr = {e: 0 for e in ENGS}
        self.dma_tot = {}
        for o in self.ops:
            eng_seq[o["idx"]] = seq_ctr[o["engine"]]
            seq_ctr[o["engine"]] += 1
            if o["dma"] is not None:
                k = self.dma_tot.get(o["dma"], 0)
                self.dma_tot[o["dma"]] = k + 1
                r = self.RING[o["dma"]]
                o["dma_sub"] = (o["dma"], k % r)
                o["dma_val"] = 16 * (k // r + 1)
        needs_inc = set()
        for o in self.ops:
            for d in o["deps"]:
                dop = self.ops[d]
                if dop["dma"] is None and not (
                        dop["engine"] == "tensor" and o["engine"] == "tensor"):
                    needs_inc.add(d)
        for o in self.ops:
            if o["is_bar"]:
                needs_inc.add(o["idx"])
        self.inc_val = {}
        ctr = {e: 0 for e in ENGS}
        for o in self.ops:
            if o["idx"] in needs_inc:
                ctr[o["engine"]] += 1
                o["inc"] = ctr[o["engine"]]
                self.inc_val[o["idx"]] = ctr[o["engine"]]
        self.eng_seq = eng_seq
        self.by_eng = {e: [o for o in self.ops if o["engine"] == e] for e in ENGS}

    def emit(self, ename, eng_obj, prog_sems, dma_sems):
        waited = {}
        drained_upto = 0
        for o in self.by_eng[ename]:
            need_drain = False
            waits = {}
            for d in sorted(o["deps"]):
                dop = self.ops[d]
                if dop["dma"] is not None:
                    key = ("dma",) + dop["dma_sub"]
                    val = dop["dma_val"]
                    if waits.get(key, 0) < val:
                        waits[key] = val
                elif dop["engine"] == ename:
                    if ename != "tensor":
                        v = self.inc_val[d]
                        key = ("prog", ename)
                        if waits.get(key, 0) < v:
                            waits[key] = v
                else:
                    v = self.inc_val[d]
                    key = ("prog", dop["engine"])
                    if waits.get(key, 0) < v:
                        waits[key] = v
            if o["dma"] is not None and o["dma_val"] > 16:
                key = ("dma",) + o["dma_sub"]
                if waits.get(key, 0) < o["dma_val"] - 16:
                    waits[key] = max(waits.get(key, 0), o["dma_val"] - 16)
            for key, val in sorted(waits.items(), key=lambda kv: str(kv[0])):
                if waited.get(key, 0) >= val:
                    continue
                sem = dma_sems[key[1:]] if key[0] == "dma" else prog_sems[key[1]]
                eng_obj.wait_ge(sem, val)
                waited[key] = val
            if o["is_bar"]:
                inst = eng_obj.drain()
                drained_upto = self.eng_seq[o["idx"]]
            else:
                inst = o["fn"](eng_obj)
            if o["inc"] is not None:
                assert inst is not None, f"op {o['tag']} returned None but needs inc"
                inst.then_inc(prog_sems[ename], 1)
            if o["dma"] is not None:
                inst.then_inc(dma_sems[o["dma_sub"]], 16)


def build_nc(seq=S):
    nt = seq // T
    nxc = seq // 128
    nc = bass.Bass("TRN2", target_bir_lowering=False, debug=False)

    x_h = nc.dram_tensor("x", [seq, D], F32, kind="ExternalInput").ap()
    wq_h = nc.dram_tensor("wq", [NKC, 128, D], F32, kind="ExternalInput").ap()
    wk_h = nc.dram_tensor("wk", [NKC, 128, D], F32, kind="ExternalInput").ap()
    wv_h = nc.dram_tensor("wv", [D, D], F32, kind="ExternalInput").ap()
    wg_h = nc.dram_tensor("wg", [NKC, 128, D], F32, kind="ExternalInput").ap()
    wo_h = nc.dram_tensor("wo", [D, D], F32, kind="ExternalInput").ap()
    w13_h = nc.dram_tensor("w13", [NJ, 128, D], F32, kind="ExternalInput").ap()
    w2_h = nc.dram_tensor("w2t", [DF, D], F32, kind="ExternalInput").ap()
    slope_h = nc.dram_tensor("slope", [1, NH], F32, kind="ExternalInput").ap()
    nwa_h = nc.dram_tensor("nwa", [128, NH], F32, kind="ExternalInput").ap()
    nwf_h = nc.dram_tensor("nwf", [128, NJ2], F32, kind="ExternalInput").ap()
    eye_h = nc.dram_tensor("eye16", [128, 128], BF16, kind="ExternalInput").ap()
    eyef_h = nc.dram_tensor("eyef", [128, 128], F32, kind="ExternalInput").ap()
    prow_h = nc.dram_tensor("prow", [1, 128], F32, kind="ExternalInput").ap()
    mrow_h = nc.dram_tensor("mrow", [1, BLK], F32, kind="ExternalInput").ap()
    onesr_h = nc.dram_tensor("onesrow", [1, 128], F32, kind="ExternalInput").ap()
    mask_h = nc.dram_tensor("mask", [128, 2 * BLK], BF16, kind="ExternalInput").ap()
    out_h = nc.dram_tensor("out", [seq, D], F32, kind="ExternalOutput").ap()

    wqs = nc.dram_tensor("wqs", [NKC, 128, D], BF16, kind="Internal").ap()
    wks = nc.dram_tensor("wks", [NKC, 128, D], BF16, kind="Internal").ap()
    wvs = nc.dram_tensor("wvs", [4, NKC, 128, 512], BF16, kind="Internal").ap()
    wgs = nc.dram_tensor("wgs", [NKC, 128, D], BF16, kind="Internal").ap()
    wos = nc.dram_tensor("wos", [4, NKC, 128, 512], BF16, kind="Internal").ap()
    w13s = nc.dram_tensor("w13s", [NJ, 128, D], BF16, kind="Internal").ap()
    w2s = nc.dram_tensor("w2s", [4, NJ2, 128, 512], BF16, kind="Internal").ap()
    xns = nc.dram_tensor("xns", [D, seq], BF16, kind="Internal").ap()
    xqs = nc.dram_tensor("xqs", [D, seq], BF16, kind="Internal").ap()
    aos = nc.dram_tensor("aos", [seq, D], BF16, kind="Internal").ap()
    tbl = nc.dram_tensor("tbl", [2, NH * BLK], F32, kind="Internal").ap()

    P = Prog()

    def dma(qeng, sem, fn, reads, writes, tag=""):
        return P.op(qeng, fn, reads=reads, writes=writes, dma=sem, tag=tag)

    es = contextlib.ExitStack()
    with es:
        sbp = lambda n, sh, dt: es.enter_context(nc.sbuf_tensor(n, sh, dt))
        eye = sbp("eye", [128, 128], BF16)

        onesr = sbp("onesr_s", [1, 128], F32)
        onescol_b = sbp("onescol_b", [128, 1], BF16)
        mask = sbp("mask_s", [128, 2 * BLK], BF16)
        nwa = sbp("nwa_s", [128, NH], F32)
        nwf = sbp("nwf_s", [128, NJ2], F32)
        srow = sbp("srow", [1, NH], F32)
        bd_vec = sbp("bd_vec", [128, NH], F32)
        cb5 = sbp("cb5", [128, 1], F32)
        cb6 = sbp("cb6", [128, 1], F32)
        cbr = sbp("cbr", [128, 1], F32)
        kdec = sbp("kdec", [128, 2 * NH], F32)
        NWB = 5
        wbuf = [sbp(f"wbuf{i}", [128, D], BF16) for i in range(NWB)]
        wbufS = [sbp(f"wbufS{i}", [128, D], BF16) for i in range(2)]
        wf32 = [sbp(f"wf32{i}", [128, D], F32) for i in range(2)]

        ps = [es.enter_context(nc.psum_tensor(f"ps{i}", [128, 512], F32))
              for i in range(6)]

        prog_sems = {e: es.enter_context(nc.semaphore(f"pg_{e}")) for e in ENGS}
        dma_sems = {}
        for k, r in Prog.RING.items():
            for i in range(r):
                dma_sems[(k, i)] = es.enter_context(nc.semaphore(f"dm_{k}{i}"))

        esAB = contextlib.ExitStack()
        sbab = lambda n, sh, dt: esAB.enter_context(nc.sbuf_tensor(n, sh, dt))
        with esAB:
            qdbc = sbab("qdbc", [128, NH * BLK], BF16)
            ddT = sbab("ddT", [128, NH * 2 * BLK], BF16)
            kv = sbab("kv", [128, NH * HD], F32)

            es1 = contextlib.ExitStack()
            sb1 = lambda n, sh, dt: es1.enter_context(nc.sbuf_tensor(n, sh, dt))
            with es1:
                rA = sb1("rowA", [1, NH * BLK], F32)
                rB = sb1("rowB", [1, NH * BLK], F32)
                rC = sb1("rowC", [1, NH * BLK], F32)
                rD = sb1("rowD", [1, NH * BLK], F32)
                ddw = sb1("ddw", [128, 2 * BLK], F32)
                t16 = sb1("t16", [16, BLK], F32)
                prow = sb1("prow_s", [1, 128], F32)
                mrow = sb1("mrow_s", [1, BLK], F32)
                ones256 = sb1("ones256", [1, BLK], F32)
                psB = [es1.enter_context(nc.psum_tensor(f"psB{i}", [128, 512], F32))
                       for i in range(2)]
                tstg = [sb1(f"tstg{i}", [128, 512], BF16) for i in range(2)]
                xb = [sb1(f"xb{i}", [128, D], F32) for i in range(3)]
                u16 = sb1("u16", [128, D], F16)
                xnb = [sb1(f"xnb{i}", [128, D], BF16) for i in range(2)]
                xqb = [sb1(f"xqb{i}", [128, D], BF16) for i in range(2)]
                NST = nxc
                stt_ = sb1("stats", [128, 18 * nxc], F32)
                SK = ["sum", "ss", "mx", "mn", "m", "var", "sd", "rstd", "t0",
                      "t1", "mab", "c", "rc", "a1q", "a0q", "isv", "nis", "nmr"]
                st = {k: (lambda i: lambda g: stt_[:, i * NST:(i + 1) * NST] if g is None
                          else stt_[:, i * NST + g:i * NST + g + 1])(i)
                      for i, k in enumerate(SK)}

                for name, dst, src in [
                        ("eye", eye, eye_h),
                        ("prow", prow, prow_h), ("mrow", mrow, mrow_h),
                        ("onesr", onesr, onesr_h), ("mask", mask, mask_h),
                        ("nwa", nwa, nwa_h), ("nwf", nwf, nwf_h),
                        ("srow", srow, slope_h)]:
                    dma("sync", "c",
                        (lambda d_, s_: lambda e: e.dma_start(out=d_[:, :], in_=s_))(dst, src),
                        reads=[], writes=[f"c:{name}"], tag=f"ld_{name}")
                P.op("vector", lambda e: e.memset(ones256[:, :], 1.0), writes=["c:ones256"])
                P.op("vector", lambda e: e.memset(onescol_b[:, :], 1.0), writes=["c:onescol"])
                P.op("vector", lambda e: e.memset(cb5[:, :], 1e-5), writes=["c:cb5"])
                P.op("vector", lambda e: e.memset(cb6[:, :], 1e-6), writes=["c:cb6"])
                P.op("vector", lambda e: e.memset(cbr[:, :], R16), writes=["c:cbr"])
                P.op("vector", lambda e: e.memset(kv[:, :], 0.0),
                     writes=["kv"] + [f"kv:{h}" for h in range(NH)])

                P.op("tensor", lambda e: e.matmul(ps[5][0:16, 0:BLK], srow[0:1, :], ones256[0:1, :], start=True, stop=True),
                     reads=["c:srow", "c:ones256"], writes=["ps5"], tag="mmS")
                P.op("vector", lambda e: e.tensor_copy(t16[:, :], ps[5][0:16, 0:BLK]),
                     reads=["ps5"], writes=["t16"], tag="evS")
                dma("gpsimd", "tbw", lambda e: e.dma_start(
                    out=tbl[0:1, :].rearrange("o (p c) -> p (o c)", p=16), in_=t16[:, :]),
                    reads=["t16"], writes=["tbl0"], tag="wS")
                dma("sync", "tb", lambda e: e.dma_start(out=rA[:, :], in_=tbl[0:1, :]),
                    reads=["tbl0"], writes=["rA"], tag="rS")
                P.op("tensor", lambda e: e.matmul(ps[5][0:16, 0:BLK], srow[0:1, :], mrow[0:1, :], start=True, stop=True),
                     reads=["c:srow", "c:mrow", "t16"], writes=["ps5"], tag="mmR")
                P.op("vector", lambda e: e.tensor_copy(t16[:, :], ps[5][0:16, 0:BLK]),
                     reads=["ps5"], writes=["t16"], tag="evR")
                dma("gpsimd", "tbw", lambda e: e.dma_start(
                    out=tbl[1:2, :].rearrange("o (p c) -> p (o c)", p=16), in_=t16[:, :]),
                    reads=["t16"], writes=["tbl1"], tag="wR")
                dma("sync", "tb", lambda e: e.dma_start(out=rB[:, :], in_=tbl[1:2, :]),
                    reads=["tbl1"], writes=["rB"], tag="rR")
                P.op("tensor", lambda e: e.matmul(ps[5][:, 0:NH], onesr[0:1, :], srow[0:1, :], start=True, stop=True),
                     reads=["c:onesr", "c:srow", "t16"], writes=["ps5"], tag="mmB")
                P.op("scalar", lambda e: e.activation(bd_vec[:, :], ps[5][:, 0:NH], AF.Exp, scale=-float(BLK)),
                     reads=["ps5"], writes=["bd_vec"], tag="bd")
                P.op("vector", lambda e: e.tensor_single_scalar(rC[:, :], rA[:, :], -255.0, ALU.mult),
                     reads=["rA"], writes=["rC"], tag="rk0")
                P.op("vector", lambda e: e.tensor_single_scalar(rD[:, :], rA[:, :], -127.0, ALU.mult),
                     reads=["rA"], writes=["rD"], tag="rk1")
                for h in range(NH):
                    for c in range(2):
                        col = 2 * h + c
                        rk = rC if c == 0 else rD
                        P.op("tensor", (lambda h_, col_: lambda e: e.matmul(
                            ps[4][:, col_:col_ + 1], prow[0:1, :],
                            rA[0:1, BLK * h_:BLK * h_ + 1], start=True, stop=False))(h, col),
                            reads=["c:prow", "rA"], writes=["ps4"], tag=f"kda{col}")
                        P.op("tensor", (lambda h_, col_, rk_: lambda e: e.matmul(
                            ps[4][:, col_:col_ + 1], onesr[0:1, :],
                            rk_[0:1, BLK * h_:BLK * h_ + 1], start=False, stop=True))(h, col, rk),
                            reads=["c:onesr", "rC", "rD"], writes=["ps4"], tag=f"kdb{col}")
                P.op("scalar", lambda e: e.activation(kdec[:, :], ps[4][:, 0:2 * NH], AF.Exp),
                     reads=["ps4"], writes=["kdec"], tag="kde")
                P.op("vector", lambda e: e.tensor_single_scalar(rC[:, :], rB[:, :], -1.0, ALU.mult),
                     reads=["rB"], writes=["rC"], tag="rc0")
                P.op("vector", lambda e: e.scalar_tensor_tensor(rD[:, :], rA[:, :], 128.0, rB[:, :], ALU.mult, ALU.subtract),
                     reads=["rA", "rB"], writes=["rD"], tag="rc1")
                for h in range(NH):
                    for c in range(2):
                        bank = 4 + (c % 2)
                        rowc = rC if c == 0 else rD
                        P.op("tensor", (lambda h_, b_: lambda e: e.matmul(
                            ps[b_][:, 0:BLK], prow[0:1, :],
                            rA[0:1, BLK * h_:BLK * (h_ + 1)], start=True, stop=False))(h, bank),
                            reads=["c:prow", "rA", "kdec"], writes=[f"ps{bank}"], tag=f"dda{h}{c}")
                        P.op("tensor", (lambda h_, b_, rc_: lambda e: e.matmul(
                            ps[b_][:, 0:BLK], onesr[0:1, :],
                            rc_[0:1, BLK * h_:BLK * (h_ + 1)], start=False, stop=True))(h, bank, rowc),
                            reads=["c:onesr", "rC", "rD"], writes=[f"ps{bank}"], tag=f"ddb{h}{c}")
                        P.op("vector", (lambda b_: lambda e: e.tensor_scalar(
                            ddw[:, 0:BLK], ps[b_][:, 0:BLK], 1.0, 0.0, ALU.mult, ALU.min))(bank),
                            reads=[f"ps{bank}"], writes=["ddw0"], tag=f"ddc{h}{c}")
                        P.op("scalar", lambda e: e.activation(ddw[:, BLK:2 * BLK], ddw[:, 0:BLK], AF.Exp),
                             reads=["ddw0"], writes=["ddw1"], tag=f"ddd{h}{c}")
                        P.op("vector", (lambda h_, c_: lambda e: e.tensor_mul(
                            ddT[:, (2 * h_ + c_) * BLK:(2 * h_ + c_ + 1) * BLK],
                            ddw[:, BLK:2 * BLK], mask[:, c_ * BLK:(c_ + 1) * BLK]))(h, c),
                            reads=["ddw1", "c:mask"], writes=[f"ddT:{h}:{c}"], tag=f"dde{h}{c}")
                P.op("vector", lambda e: e.tensor_add(rB[:, :], rB[:, :], rA[:, :]),
                     reads=["rA", "rB"], writes=["rB"], tag="r2")
                for cc in range(NH * BLK // 512):
                    bank = cc % 2
                    P.op("tensor", (lambda cc_, b_: lambda e: e.matmul(
                        ps[b_][:, :], onesr[0:1, :], rB[0:1, 512 * cc_:512 * (cc_ + 1)],
                        start=True, stop=True))(cc, bank),
                        reads=["rB", "c:onesr"], writes=[f"ps{bank}"], tag=f"qba{cc}")
                    P.op("scalar", (lambda cc_, b_: lambda e: e.activation(
                        qdbc[:, 512 * cc_:512 * (cc_ + 1)], ps[b_][:, :], AF.Exp, scale=-1.0))(cc, bank),
                        reads=[f"ps{bank}"], writes=[f"qdbc:{cc}"], tag=f"qbb{cc}")

                def stage3(src_ap, dst_ap, nchunk, key, q=("gpsimd", "wl", "gpsimd", "ws")):
                    lq, ls, wq_, ws_ = q
                    for c in range(nchunk):
                        f32b = f"wf32:{c % 2}"
                        bfb = f"wbufS:{c % 2}"
                        dma(lq, ls, (lambda c_: lambda e: e.dma_start(
                            out=wf32[c_ % 2][:, :], in_=src_ap[c_, :, :]))(c),
                            reads=[], writes=[f32b], tag=f"L{key}{c}")
                        P.op("gpsimd", (lambda c_: lambda e: e.tensor_copy(
                            wbufS[c_ % 2][:, :], wf32[c_ % 2][:, :]))(c),
                            reads=[f32b], writes=[bfb], tag=f"C{key}{c}")
                        dma(wq_, ws_, (lambda c_: lambda e: e.dma_start(
                            out=dst_ap[c_, :, :], in_=wbufS[c_ % 2][:, :]))(c),
                            reads=[bfb], writes=[f"{key}:{c}"], tag=f"S{key}{c}")

                def stage2(src_ap, dst_ap, nchunk, key, q=("gpsimd", "wl", "gpsimd", "ws")):
                    lq, ls, wq_, ws_ = q
                    for c in range(nchunk):
                        f32b = f"wf32:{c % 2}"
                        bfb = f"wbufS:{c % 2}"
                        dma(lq, ls, (lambda c_: lambda e: e.dma_start(
                            out=wf32[c_ % 2][:, :], in_=src_ap[128 * c_:128 * (c_ + 1), :]))(c),
                            reads=[], writes=[f32b], tag=f"L{key}{c}")
                        P.op("gpsimd", (lambda c_: lambda e: e.tensor_copy(
                            wbufS[c_ % 2][:, :], wf32[c_ % 2][:, :]))(c),
                            reads=[f32b], writes=[bfb], tag=f"C{key}{c}")
                        dma(wq_, ws_, (lambda c_: lambda e: e.dma_start(
                            out=dst_ap[:, c_, :, :].rearrange("d p c -> p d c"),
                            in_=wbufS[c_ % 2][:, :].rearrange("p (d c) -> p d c", d=4)))(c),
                            reads=[bfb], writes=[f"{key}:{c}"], tag=f"S{key}{c}")

                QA = ("sync", "wla", "gpsimd", "ws")
                stage3(wq_h, wqs, NKC, "wqs", QA)
                stage3(wk_h, wks, NKC, "wks", QA)
                stage2(wv_h, wvs, NKC, "wvs", QA)
                stage3(wg_h, wgs, NKC, "wgs", QA)
                stage2(wo_h, wos, NKC, "wos", QA)
                P.stage_ffn = lambda: (stage3(w13_h, w13s, NJ, "w13s"),
                                       stage2(w2_h, w2s, NJ2, "w2s"))

                for ci in range(nxc):
                    slot = ci % 3
                    xk = f"xb:{slot}"
                    dma("scalar", "xl", (lambda ci_, s_: lambda e: e.dma_start(
                        out=xb[s_][:, :], in_=x_h[128 * ci_:128 * (ci_ + 1), :]))(ci, slot),
                        reads=[], writes=[xk], tag=f"ldx{ci}")
                    P.op("scalar", (lambda s_, g_: lambda e: e.activation(
                        u16[:, :], xb[s_][:, :], AF.Square, accum_out=st["ss"](g_)))(slot, ci),
                        reads=[xk], writes=["u16", f"st1:{ci}"], tag=f"sq{ci}")
                    P.op("vector", (lambda s_, g_: lambda e: e.reduce_sum(
                        st["sum"](g_), xb[s_][:, :], axis=AX.X))(slot, ci),
                        reads=[xk], writes=[f"st1:{ci}"], tag=f"rs{ci}")
                    P.op("vector", (lambda s_, g_: lambda e: e.reduce_max(
                        st["mx"](g_), xb[s_][:, :], axis=AX.X))(slot, ci),
                        reads=[xk], writes=[f"st1:{ci}"], tag=f"rx{ci}")
                    P.op("vector", (lambda s_, g_: lambda e: e.tensor_reduce(
                        st["mn"](g_), xb[s_][:, :], AX.X, ALU.min))(slot, ci),
                        reads=[xk], writes=[f"st1:{ci}"], tag=f"rn{ci}")

                def chain_all(e):
                    s_ = lambda k: st[k](None)
                    e.tensor_single_scalar(s_("m"), s_("sum"), 1.0 / D, ALU.mult)
                    e.drain()
                    e.tensor_mul(s_("t0"), s_("m"), s_("m"))
                    e.drain()
                    e.scalar_tensor_tensor(s_("var"), s_("ss"), 1.0 / D, s_("t0"), ALU.mult, ALU.subtract)
                    return e.drain()
                P.op("vector", chain_all,
                     reads=[f"st1:{ci}" for ci in range(nxc)],
                     writes=["st2a"], tag="c1all")
                P.op("scalar", lambda e: e.activation(
                    st["sd"](None), st["var"](None), AF.Sqrt, bias=cb5[:, 0:1]),
                    reads=["st2a", "c:cb5"], writes=["st2b"], tag="c2all")

                def chain3_all(e):
                    s_ = lambda k: st[k](None)
                    e.reciprocal(s_("rstd"), s_("sd"))
                    e.drain()
                    e.tensor_sub(s_("t0"), s_("mx"), s_("m"))
                    e.tensor_sub(s_("t1"), s_("m"), s_("mn"))
                    e.drain()
                    e.tensor_max(s_("mab"), s_("t0"), s_("t1"))
                    e.drain()
                    e.tensor_mul(s_("t0"), s_("mab"), s_("rstd"))
                    e.drain()
                    e.tensor_single_scalar(s_("c"), s_("t0"), 1e-5, ALU.max)
                    e.drain()
                    e.reciprocal(s_("rc"), s_("c"))
                    e.drain()
                    e.scalar_tensor_tensor(s_("a1q"), s_("rstd"), 127.0, s_("rc"), ALU.mult, ALU.mult)
                    e.tensor_single_scalar(s_("isv"), s_("c"), 1.0 / 127.0, ALU.mult)
                    e.tensor_mul(s_("nmr"), s_("m"), s_("rstd"))
                    e.drain()
                    e.tensor_mul(s_("t1"), s_("m"), s_("a1q"))
                    e.tensor_single_scalar(s_("nis"), s_("isv"), -R16, ALU.mult)
                    e.tensor_single_scalar(s_("nmr"), s_("nmr"), -1.0, ALU.mult)
                    e.drain()
                    e.tensor_scalar(s_("a0q"), s_("t1"), -1.0, R16, ALU.mult, ALU.add)
                    return e.drain()
                P.op("vector", chain3_all, reads=["st2a", "st2b"],
                     writes=["st2"], tag="c3all")

                def p2_tr(ci, srcbuf, srckey, dst_hbm, okey, tagp):
                    for g in range(4):
                        pk = f"psB:{g % 2}"
                        for j in range(4):
                            kk = 4 * g + j
                            P.op("tensor", (lambda s_, k_, j_, g_: lambda e: e.matmul(
                                psB[g_ % 2][:, 128 * j_:128 * (j_ + 1)],
                                srcbuf[s_][:, 128 * k_:128 * (k_ + 1)], eye[:, :],
                                start=True, stop=True))(srckey, kk, j, g),
                                reads=[f"{tagp}bb:{srckey}", "c:eye"], writes=[pk], tag=f"{tagp}t{ci}_{kk}")
                        P.op("vector", (lambda g_: lambda e: e.tensor_copy(
                            tstg[g_ % 2][:, :], psB[g_ % 2][:, :]))(g),
                            reads=[pk], writes=[f"tstg:{g % 2}"], tag=f"{tagp}e{ci}_{g}")
                        dma("gpsimd", "aw", (lambda ci_, g_: lambda e: e.dma_start(
                            out=dst_hbm[512 * g_:512 * (g_ + 1), 128 * ci_:128 * (ci_ + 1)].rearrange(
                                "(k p) c -> p k c", k=4),
                            in_=tstg[g_ % 2][:, :].rearrange("p (k c) -> p k c", k=4)))(ci, g),
                            reads=[f"tstg:{g % 2}"], writes=[f"{okey}:{ci // 4}"], tag=f"{tagp}w{ci}_{g}")

                for ci in range(nxc):
                    slot = ci % 3
                    xk = f"xb:{slot}"
                    dma("scalar", "xl", (lambda ci_, s_: lambda e: e.dma_start(
                        out=xb[s_][:, :], in_=x_h[128 * ci_:128 * (ci_ + 1), :]))(ci, slot),
                        reads=[], writes=[xk], tag=f"ldx2_{ci}")
                    P.op("scalar", (lambda s_, g_: lambda e: e.activation(
                        xnb[s_ % 2][:, :], xb[s_][:, :], AF.Identity,
                        bias=st["nmr"](g_), scale=st["rstd"](g_)))(slot, ci),
                        reads=[xk, "st2"], writes=[f"xnbb:{slot % 2}"], tag=f"xn{ci}")
                    p2_tr(ci, xnb, slot % 2, xns, "xns", "xn")
                    P.op("vector", (lambda s_, g_: lambda e: e.tensor_scalar(
                        u16[:, :], xb[s_][:, :], st["a1q"](g_), st["a0q"](g_), ALU.mult, ALU.add))(slot, ci),
                        reads=[xk, "st2"], writes=["u16"], tag=f"u{ci}")
                    P.op("vector", (lambda s_, g_: lambda e: e.tensor_scalar(
                        xqb[s_ % 2][:, :], u16[:, :], st["isv"](g_), st["nis"](g_), ALU.mult, ALU.add))(slot, ci),
                        reads=["u16", "st2"], writes=[f"xqbb:{slot % 2}"], tag=f"xq{ci}")
                    p2_tr(ci, xqb, slot % 2, xqs, "xqs", "xq")

            P.barrier()
            es2 = contextlib.ExitStack()
            sb2 = lambda n, sh, dt: es2.enter_context(nc.sbuf_tensor(n, sh, dt))
            with es2:
                P.stage_ffn()
                xnT = [sb2(f"xnT{i}", [128, NKC * T], BF16) for i in range(2)]
                qT = sb2("qT", [128, NH * T], BF16)
                kT = sb2("kT", [128, NH * T], BF16)
                v_tok = [sb2(f"v_tok{i}", [128, D], BF16) for i in range(4)]
                psBt = [es2.enter_context(nc.psum_tensor(f"psBt{i}", [128, 512], F32))
                        for i in range(2)]
                gateT = sb2("gateT", [128, NH * T], BF16)
                aT = sb2("aT", [128, NH * T], BF16)
                kvb = [sb2(f"kvb{i}", [128, HD], BF16) for i in range(4)]
                qdi = [sb2(f"qdi{i}", [128, BLK], BF16) for i in range(4)]
                qkd = [sb2(f"qkd{i}", [128, 2 * BLK], BF16) for i in range(2)]
                ktok = [sb2(f"ktok{i}", [128, BLK], BF16) for i in range(4)]
                sqs0 = sb2("sqs0", [128, T], BF16)
                arow = sb2("arow", [1, T], F32)
                arow2 = sb2("arow2", [1, T], F32)
                rstdA = sb2("rstdA", [128, T], F32)
                aostg = [sb2(f"aostg{i}", [128, 512], BF16) for i in range(2)]

                for t in range(nt):
                    xp = t % 2
                    xnTc = xnT[xp]
                    for kk in range(NKC):
                        qq = ("scalar", "tr") if kk % 2 == 0 else ("sync", "trs")
                        dma(qq[0], qq[1], (lambda t_, k_, b_: lambda e: e.dma_start(
                            out=b_[:, T * k_:T * (k_ + 1)],
                            in_=xns[128 * k_:128 * (k_ + 1), T * t_:T * (t_ + 1)]))(t, kk, xnTc),
                            reads=[f"xns:{t}"], writes=[f"xnT{xp}:{kk}"], tag=f"txn{t}_{kk}")
                    for gi, (wsrc, wkey, dstT, dkey, act) in enumerate([
                            (wqs, "wqs", qT, "qT", AF.Silu),
                            (wks, "wks", kT, "kT", AF.Silu),
                            (wgs, "wgs", gateT, "gateT", AF.Sigmoid)]):
                        for c in range(NKC):
                            slot = (gi * NKC + c) % NWB
                            bank = (gi * NKC + c) % 2
                            dma("sync", "wst", (lambda w_, c_, s_: lambda e: e.dma_start(
                                out=wbuf[s_][:, :], in_=w_[c_, :, :]))(wsrc, c, slot),
                                reads=[f"{wkey}:{c}"], writes=[f"wbuf:{slot}"], tag=f"sw{t}{gi}{c}")
                            for kk in range(NKC):
                                P.op("tensor", (lambda s_, b_, k_, x_: lambda e: e.matmul(
                                    ps[b_][:, :], wbuf[s_][:, 128 * k_:128 * (k_ + 1)],
                                    x_[:, T * k_:T * (k_ + 1)],
                                    start=(k_ == 0), stop=(k_ == NKC - 1)))(slot, bank, kk, xnTc),
                                    reads=[f"wbuf:{slot}", f"xnT{xp}:{kk}"],
                                    writes=[f"ps{bank}"], tag=f"g{t}_{gi}_{c}_{kk}")
                            P.op("scalar", (lambda b_, c_, d_, a_: lambda e: e.activation(
                                d_[:, T * c_:T * (c_ + 1)], ps[b_][:, :], a_))(bank, c, dstT, act),
                                reads=[f"ps{bank}"], writes=[f"{dkey}:{c}"], tag=f"e{t}_{gi}_{c}")
                    for ec in range(4):
                        for kg in range(NKC // 4):
                            slot = kg % NWB
                            wr = [f"wvs:{4 * kg + i}" for i in range(4)]
                            dma("sync", "wst", (lambda g_, s_, e_: lambda ee: ee.dma_start(
                                out=wbuf[s_][:, :].rearrange("p (k c) -> p k c", k=4),
                                in_=wvs[e_, 4 * g_:4 * (g_ + 1), :, :].rearrange("k p c -> p k c")))(kg, slot, ec),
                                reads=wr, writes=[f"wbuf:{slot}"], tag=f"swv{t}_{ec}_{kg}")
                            for ki in range(4):
                                kk = 4 * kg + ki
                                for tc in range(4):
                                    P.op("tensor", (lambda k_, s_, ki_, tc_, x_: lambda e: e.matmul(
                                        ps[tc_][:, :], x_[:, T * k_ + 128 * tc_:T * k_ + 128 * (tc_ + 1)],
                                        wbuf[s_][:, 512 * ki_:512 * (ki_ + 1)],
                                        start=(k_ == 0), stop=(k_ == NKC - 1)))(kk, slot, ki, tc, xnTc),
                                        reads=[f"wbuf:{slot}", f"xnT{xp}:{kk}"],
                                        writes=[f"ps{tc}"], tag=f"gv{t}_{ec}_{kk}_{tc}")
                        for tc in range(4):
                            P.op("scalar", (lambda e_, tc_: lambda e: e.activation(
                                v_tok[tc_][:, 512 * e_:512 * (e_ + 1)], ps[tc_][:, :], AF.Silu))(ec, tc),
                                reads=[f"ps{tc}"], writes=[f"vt:{tc}:{ec}"], tag=f"ev{t}_{ec}_{tc}")

                    NU = (T // BLK) * NH

                    def u_hb(u):
                        b, h = divmod(u, NH)
                        return h, b, T * h + b * BLK

                    def attn_pre(u):
                        h, b, hb = u_hb(u)
                        r = u % 4
                        r2 = u % 2
                        pt = u % 2
                        for cc in range(2):
                            P.op("tensor", (lambda hb_, cc_, p_: lambda e: e.matmul(
                                psBt[p_][:, 128 * cc_:128 * (cc_ + 1)],
                                kT[:, hb_ + 128 * cc_:hb_ + 128 * (cc_ + 1)], eye[:, :],
                                start=True, stop=True))(hb, cc, pt),
                                reads=[f"kT:{h}", "c:eye"], writes=[f"psBt{pt}"], tag=f"tk{t}_{u}_{cc}")
                        for cc in range(2):
                            P.op("scalar", (lambda h_, cc_, r_, p_: lambda e: e.activation(
                                ktok[r_][:, 128 * cc_:128 * (cc_ + 1)],
                                psBt[p_][:, 128 * cc_:128 * (cc_ + 1)], AF.Copy,
                                scale=kdec[:, 2 * h_ + cc_:2 * h_ + cc_ + 1]))(h, cc, r, pt),
                                reads=[f"psBt{pt}", "kdec"], writes=[f"ktok:{r}"], tag=f"ke{t}_{u}_{cc}")
                        P.op("vector", (lambda h_, hb_, r_: lambda e: e.tensor_mul(
                            qdi[r_][:, :], qT[:, hb_:hb_ + BLK],
                            qdbc[:, BLK * h_:BLK * (h_ + 1)]))(h, hb, r),
                            reads=[f"qT:{h}", f"qdbc:{h // 2}"],
                            writes=[f"qdi:{r}"], tag=f"qd{t}_{u}")
                        P.op("scalar", (lambda h_, r_: lambda e: e.activation(
                            kvb[r_][:, :], kv[:, HD * h_:HD * (h_ + 1)], AF.Copy))(h, r),
                            reads=[f"kv:{h}"], writes=[f"kvb:{r}"], tag=f"kc{t}_{u}")

                    def attn_qk(u):
                        h, b, hb = u_hb(u)
                        pqk = u % 2
                        for cc in range(2):
                            P.op("tensor", (lambda hb_, cc_, p_: lambda e: e.matmul(
                                ps[p_][:, BLK * cc_:BLK * (cc_ + 1)],
                                kT[:, hb_ + 128 * cc_:hb_ + 128 * (cc_ + 1)],
                                qT[:, hb_:hb_ + BLK], start=True, stop=True))(hb, cc, pqk),
                                reads=[f"kT:{h}", f"qT:{h}"], writes=[f"ps{pqk}"], tag=f"qk{t}_{u}_{cc}")
                        P.op("vector", (lambda h_, u_, p_: lambda e: e.tensor_mul(
                            qkd[u_ % 2][:, :], ps[p_][:, 0:2 * BLK],
                            ddT[:, 2 * BLK * h_:2 * BLK * (h_ + 1)]))(h, u, pqk),
                            reads=[f"ps{pqk}", f"ddT:{h}:0", f"ddT:{h}:1"],
                            writes=[f"qkd:{u % 2}"], tag=f"qm{t}_{u}")

                    def attn_body(u):
                        h, b, hb = u_hb(u)
                        r = u % 4
                        r2 = u % 2
                        po = 2 + u % 2
                        pkv = 4 + u % 2
                        ve = h // 4
                        vsl = lambda cc: (2 * b + cc, h)
                        P.op("tensor", (lambda r_, p_: lambda e: e.matmul(
                            ps[p_][:, 0:BLK], kvb[r_][:, :], qdi[r_][:, :],
                            start=True, stop=False))(r, po),
                            reads=[f"kvb:{r}", f"qdi:{r}"], writes=[f"ps{po}"], tag=f"oi{t}_{u}")
                        for cc in range(2):
                            P.op("tensor", (lambda b_, cc_, h_, r_, p_: lambda e: e.matmul(
                                ps[p_][:, 0:BLK], v_tok[2 * b_ + cc_][:, 128 * h_:128 * (h_ + 1)],
                                qkd[u % 2][:, BLK * cc_:BLK * (cc_ + 1)],
                                start=False, stop=(cc_ == 1)))(b, cc, h, r, po),
                                reads=[f"vt:{2 * b + cc}:{ve}", f"qkd:{u % 2}"],
                                writes=[f"ps{po}"], tag=f"oa{t}_{u}_{cc}")
                        P.op("vector", (lambda hb_, p_, h_, b_: lambda e: e.tensor_copy(
                            aT[:, hb_:hb_ + BLK], ps[p_][:, 0:BLK]))(hb, po, h, b),
                            reads=[f"ps{po}"], writes=[f"aT:{h}:{b}"], tag=f"oe{t}_{u}")
                        for cc in range(2):
                            P.op("tensor", (lambda b_, cc_, h_, r_, p_: lambda e: e.matmul(
                                ps[p_][:, 0:HD], ktok[r_][:, 128 * cc_:128 * (cc_ + 1)],
                                v_tok[2 * b_ + cc_][:, 128 * h_:128 * (h_ + 1)],
                                start=(cc_ == 0), stop=(cc_ == 1)))(b, cc, h, r, pkv),
                                reads=[f"ktok:{r}", f"vt:{2 * b + cc}:{ve}"],
                                writes=[f"ps{pkv}"], tag=f"km{t}_{u}_{cc}")
                        P.op("vector", (lambda h_, p_: lambda e: e.scalar_tensor_tensor(
                            kv[:, HD * h_:HD * (h_ + 1)], kv[:, HD * h_:HD * (h_ + 1)],
                            bd_vec[:, h_:h_ + 1], ps[p_][:, 0:HD], ALU.mult, ALU.add))(h, pkv),
                            reads=[f"ps{pkv}", f"kv:{h}", "bd_vec"],
                            writes=[f"kv:{h}"], tag=f"ku{t}_{u}")

                    for u in range(3):
                        attn_pre(u)
                    attn_qk(0)
                    for u in range(NU):
                        if u + 1 < NU:
                            attn_qk(u + 1)
                        if u + 3 < NU:
                            attn_pre(u + 3)
                        attn_body(u)

                    sqs = [qkd[0][:, :], qkd[1][:, :], sqs0[:, :]]
                    skey = ["qkd:0", "qkd:1", "sqs:0"]
                    def av_mm(h_):
                        P.op("tensor", (lambda hh: lambda e: e.matmul(
                            ps[0][0:1, 0:T], onescol_b[:, 0:1], sqs[hh % 3],
                            start=(hh == 0), stop=(hh == NH - 1)))(h_),
                            reads=[skey[h_ % 3], "c:onescol"], writes=["ps0"], tag=f"av{t}_{h_}")
                    for h in range(NH):
                        P.op("scalar", (lambda h_: lambda e: e.activation(
                            sqs[h_ % 3], aT[:, T * h_:T * (h_ + 1)], AF.Square))(h),
                            reads=[f"aT:{h}:0", f"aT:{h}:1"], writes=[skey[h % 3]], tag=f"as{t}_{h}")
                        if h >= 2:
                            av_mm(h - 2)
                    av_mm(NH - 2)
                    av_mm(NH - 1)
                    P.op("vector", lambda e: e.tensor_single_scalar(
                        arow[0:1, 0:T], ps[0][0:1, 0:T], 1.0 / D, ALU.mult),
                        reads=["ps0"], writes=["arow"], tag=f"an1{t}")
                    P.op("scalar", lambda e: e.activation(arow2[0:1, 0:T], arow[0:1, 0:T], AF.Sqrt, bias=cb6[0:1, 0:1]),
                         reads=["arow", "c:cb6"], writes=["arow2"], tag=f"an2{t}")
                    P.op("vector", lambda e: e.reciprocal(arow[0:1, 0:T], arow2[0:1, 0:T]),
                         reads=["arow2"], writes=["arow"], tag=f"an3{t}")
                    P.op("tensor", lambda e: e.matmul(ps[1][:, 0:T], onesr[0:1, :], arow[0:1, 0:T], start=True, stop=True),
                         reads=["arow", "c:onesr"], writes=["ps1"], tag=f"an4{t}")
                    P.op("vector", lambda e: e.tensor_copy(rstdA[:, 0:T], ps[1][:, 0:T]),
                         reads=["ps1"], writes=["rstdA"], tag=f"an5{t}")
                    for h in range(NH):
                        P.op("vector", (lambda h_: lambda e: e.scalar_tensor_tensor(
                            sqs[h_ % 3], aT[:, T * h_:T * (h_ + 1)], nwa[:, h_:h_ + 1],
                            rstdA[:, 0:T], ALU.mult, ALU.mult))(h),
                            reads=[f"aT:{h}:0", f"aT:{h}:1", "rstdA", "c:nwa"],
                            writes=[skey[h % 3]], tag=f"nm{t}_{h}")
                        P.op("vector", (lambda h_: lambda e: e.tensor_mul(
                            aT[:, T * h_:T * (h_ + 1)], sqs[h_ % 3],
                            gateT[:, T * h_:T * (h_ + 1)]))(h),
                            reads=[skey[h % 3], f"gateT:{h}"],
                            writes=[f"aT:{h}:0", f"aT:{h}:1", f"ga:{h}"], tag=f"gm{t}_{h}")

                    for dchunk in range(4):
                        for kg in range(NKC // 4):
                            slot = kg % NWB
                            wr = [f"wos:{4 * kg + i}" for i in range(4)]
                            dma("sync", "wst", (lambda g_, s_, d_: lambda e: e.dma_start(
                                out=wbuf[s_][:, :].rearrange("p (k c) -> p k c", k=4),
                                in_=wos[d_, 4 * g_:4 * (g_ + 1), :, :].rearrange("k p c -> p k c")))(kg, slot, dchunk),
                                reads=wr, writes=[f"wbuf:{slot}"], tag=f"so{t}_{dchunk}_{kg}")
                            for ki in range(4):
                                kk = 4 * kg + ki
                                for tc in range(4):
                                    P.op("tensor", (lambda k_, s_, ki_, tc_: lambda e: e.matmul(
                                        ps[2 + tc_][:, :],
                                        aT[:, T * k_ + 128 * tc_:T * k_ + 128 * (tc_ + 1)],
                                        wbuf[s_][:, 512 * ki_:512 * (ki_ + 1)],
                                        start=(k_ == 0), stop=(k_ == NKC - 1)))(kk, slot, ki, tc),
                                        reads=[f"wbuf:{slot}", f"ga:{kk}"],
                                        writes=[f"ps{2 + tc}"], tag=f"op{t}_{dchunk}_{kk}_{tc}")
                        for tc in range(4):
                            so = (dchunk * 4 + tc) % 2
                            P.op("scalar", (lambda tc_, s_: lambda e: e.activation(
                                aostg[s_][:, :], ps[2 + tc_][:, :], AF.Copy))(tc, so),
                                reads=[f"ps{2 + tc}"], writes=[f"aostg:{so}"], tag=f"ae{t}_{dchunk}_{tc}")
                            dma("scalar", "ao", (lambda t_, tc_, d_, s_: lambda e: e.dma_start(
                                out=aos[T * t_ + 128 * tc_:T * t_ + 128 * (tc_ + 1),
                                        512 * d_:512 * (d_ + 1)],
                                in_=aostg[s_][:, :]))(t, tc, dchunk, so),
                                reads=[f"aostg:{so}"], writes=[f"aos:{t}"], tag=f"aw{t}_{dchunk}_{tc}")

            P.barrier()
        es3 = contextlib.ExitStack()
        sb3 = lambda n, sh, dt: es3.enter_context(nc.sbuf_tensor(n, sh, dt))
        with es3:
            xqT = sb3("xqT", [128, NKC * T], BF16)
            gat = sb3("gat", [128, NJ2 * T], BF16)
            iqT = sb3("iqT", [128, NJ2 * T], BF16)
            a_scr = [sb3(f"ascr{i}", [128, T], F32) for i in range(2)]
            sq2 = [sb3(f"sq2_{i}", [128, T], BF16) for i in range(4)]
            tmpb = [sb3(f"tmpb{i}", [128, T], BF16) for i in range(2)]
            Mb = [sb3(f"Mb{i}", [128, T], BF16) for i in range(2)]
            mbt = [sb3(f"mbt{i}", [128, 128], BF16) for i in range(2)]
            Ub = sb3("Ub", [128, T], F32)
            u16b = [sb3(f"u16b{i}", [128, T], F16) for i in range(2)]
            qrow = sb3("qrow", [1, T], F32)
            qt_ = sb3("qstats", [128, 88], F32)
            QK = ["ss", "M", "var", "sd", "rstd", "c", "rc", "s2", "is2", "U"]
            iqs = [sb3(f"iqs{i}", [128, T], F32) for i in range(2)]
            eyef = sb3("eyef_s", [128, 128], F32)
            ps6 = es3.enter_context(nc.psum_tensor("ps6", [128, 512], F32))
            ps7 = es3.enter_context(nc.psum_tensor("ps7", [128, 512], F32))
            ps67 = [ps6, ps7]
            wbuf2 = [sb3(f"wbuf2_{i}", [128, 2048], BF16) for i in range(3)]
            xres = [sb3(f"xres{i}", [128, 512], F32) for i in range(2)]
            aores = [sb3(f"aores{i}", [128, 512], BF16) for i in range(2)]
            ostg = [sb3(f"ostg{i}", [128, 512], F32) for i in range(2)]

            def p4_txq(t):
                for kk in range(NKC):
                    qq = ("scalar", "tr") if kk % 2 == 0 else ("sync", "trs")
                    dma(qq[0], qq[1], (lambda t_, k_: lambda e: e.dma_start(
                        out=xqT[:, T * k_:T * (k_ + 1)],
                        in_=xqs[128 * k_:128 * (k_ + 1), T * t_:T * (t_ + 1)]))(t, kk),
                        reads=[f"xqs:{t}"], writes=[f"xqT:{kk}"], tag=f"txq{t}_{kk}")

            def p4_w13(t):
                pq = t % 2
                P.op("vector", (lambda p_: lambda e: e.memset(Mb[p_][:, :], 0.0))(pq),
                     reads=[], writes=[f"Mb{pq}"], tag=f"mz{t}")

                def ss_mm(j_):
                    P.op("tensor", (lambda jx: lambda e: e.matmul(
                        ps[4][0:1, 0:T], onescol_b[:, 0:1], sq2[jx % 4][:, :],
                        start=(jx == 0), stop=(jx == NJ2 - 1)))(j_),
                        reads=[f"sq2:{j_ % 4}", "c:onescol"], writes=["ps4"], tag=f"ss{t}_{j_}")
                for jj in range(NJ2):
                    b1 = 2 * (jj % 2)
                    b3 = b1 + 1
                    for half, (jc, bank) in enumerate([(jj, b1), (jj + NJ2, b3)]):
                        slot = (2 * jj + half) % NWB
                        dma("sync", "wst", (lambda j_, s_: lambda e: e.dma_start(
                            out=wbuf[s_][:, :], in_=w13s[j_, :, :]))(jc, slot),
                            reads=[f"w13s:{jc}"], writes=[f"wbuf:{slot}"], tag=f"s13_{t}_{jc}")
                        for kk in range(NKC):
                            P.op("tensor", (lambda s_, b_, k_: lambda e: e.matmul(
                                ps[b_][:, :], wbuf[s_][:, 128 * k_:128 * (k_ + 1)],
                                xqT[:, T * k_:T * (k_ + 1)], start=(k_ == 0), stop=(k_ == NKC - 1)))(slot, bank, kk),
                                reads=[f"wbuf:{slot}", f"xqT:{kk}"],
                                writes=[f"ps{bank}"], tag=f"w13_{t}_{jc}_{kk}")
                    asl = jj % 2
                    P.op("scalar", (lambda b_, s_: lambda e: e.activation(
                        a_scr[s_][:, :], ps[b_][:, :], AF.Relu))(b1, asl),
                        reads=[f"ps{b1}"], writes=[f"ascr:{asl}"], tag=f"rl{t}_{jj}")
                    P.op("scalar", (lambda s_: lambda e: e.activation(
                        a_scr[s_][:, :], a_scr[s_][:, :], AF.Square))(asl),
                        reads=[f"ascr:{asl}"], writes=[f"ascr:{asl}"], tag=f"sr{t}_{jj}")
                    P.op("vector", (lambda j_, b_, s_: lambda e: e.tensor_mul(
                        gat[:, T * j_:T * (j_ + 1)], a_scr[s_][:, :], ps[b_][:, :]))(jj, b3, asl),
                        reads=[f"ascr:{asl}", f"ps{b3}"], writes=[f"gat:{jj}"], tag=f"gm{t}_{jj}")
                    P.op("gpsimd", (lambda j_, s_: lambda e: e.tensor_mul(
                        sq2[s_][:, :], gat[:, T * j_:T * (j_ + 1)],
                        gat[:, T * j_:T * (j_ + 1)]))(jj, jj % 4),
                        reads=[f"gat:{jj}"], writes=[f"sq2:{jj % 4}"], tag=f"g2{t}_{jj}")
                    if jj >= 2:
                        ss_mm(jj - 2)
                    if jj == NJ2 - 1:
                        ss_mm(NJ2 - 2)
                        ss_mm(NJ2 - 1)
                    P.op("scalar", (lambda j_, s_: lambda e: e.activation(
                        tmpb[s_][:, :], gat[:, T * j_:T * (j_ + 1)], AF.Abs,
                        scale=nwf[:, j_:j_ + 1]))(jj, asl),
                        reads=[f"gat:{jj}", "c:nwf"], writes=[f"tmpb:{asl}"], tag=f"ab{t}_{jj}")
                    P.op("vector", (lambda s_, p_: lambda e: e.tensor_max(
                        Mb[p_][:, :], Mb[p_][:, :], tmpb[s_][:, :]))(asl, pq),
                        reads=[f"tmpb:{asl}", f"Mb{pq}"], writes=[f"Mb{pq}"], tag=f"mm{t}_{jj}")
                    yield

            def p4_quant2_iq(t):
                pq = t % 2
                qt = {k: qt_[:, 8 * i + 4 * pq:8 * i + 4 * (pq + 1)] for i, k in enumerate(QK)}
                for tc in range(4):
                    dma("scalar", "trv", (lambda tc_, p_: lambda e: e.dma_start(
                        out=mbt[tc_ % 2][:, :], in_=Mb[p_][:, 128 * tc_:128 * (tc_ + 1)],
                        transpose=True))(tc, pq),
                        reads=[f"Mb{pq}"], writes=[f"mbt:{tc % 2}"], tag=f"mx{t}_{tc}")
                    P.op("vector", (lambda tc_, q_: lambda e: e.reduce_max(
                        q_["M"][:, tc_:tc_ + 1], mbt[tc_ % 2][:, :], axis=AX.X))(tc, qt),
                        reads=[f"mbt:{tc % 2}"], writes=[f"qt{pq}:M"], tag=f"mr{t}_{tc}")
                P.op("vector", lambda e: e.tensor_copy(qrow[0:1, 0:T], ps[4][0:1, 0:T]),
                     reads=["ps4"], writes=["qrow"], tag=f"sc{t}")
                for tc in range(4):
                    P.op("tensor", (lambda tc_: lambda e: e.matmul(
                        ps[5][:, tc_:tc_ + 1], qrow[0:1, 128 * tc_:128 * (tc_ + 1)],
                        onesr[0:1, 0:1], start=True, stop=True))(tc),
                        reads=["qrow", "c:onesr"], writes=["ps5"], tag=f"st{t}_{tc}")
                P.op("vector", (lambda q_: lambda e: e.tensor_copy(q_["ss"][:, 0:4], ps[5][:, 0:4]))(qt),
                     reads=["ps5"], writes=[f"qt{pq}:ss"], tag=f"se{t}")
                P.op("vector", (lambda q_: lambda e: e.tensor_single_scalar(
                    q_["var"][:, :], q_["ss"][:, :], 1.0 / DF, ALU.mult))(qt),
                    reads=[f"qt{pq}:ss"], writes=[f"qt{pq}:var"], tag=f"q1{t}")
                P.op("scalar", (lambda q_: lambda e: e.activation(
                    q_["sd"][:, :], q_["var"][:, :], AF.Sqrt, bias=cb5[:, 0:1]))(qt),
                    reads=[f"qt{pq}:var", "c:cb5"], writes=[f"qt{pq}:sd"], tag=f"q2{t}")

                def q2c(e, q_=qt):
                    e.reciprocal(q_["rstd"][:, :], q_["sd"][:, :])
                    e.drain()
                    e.tensor_mul(q_["c"][:, :], q_["rstd"][:, :], q_["M"][:, :])
                    e.drain()
                    e.tensor_single_scalar(q_["c"][:, :], q_["c"][:, :], 1e-5, ALU.max)
                    e.drain()
                    e.reciprocal(q_["rc"][:, :], q_["c"][:, :])
                    e.drain()
                    e.tensor_single_scalar(q_["s2"][:, :], q_["rc"][:, :], 127.0, ALU.mult)
                    e.tensor_single_scalar(q_["is2"][:, :], q_["c"][:, :], 1.0 / 127.0, ALU.mult)
                    e.drain()
                    e.tensor_mul(q_["U"][:, :], q_["rstd"][:, :], q_["s2"][:, :])
                    return e.drain()
                P.op("vector", q2c,
                     reads=[f"qt{pq}:sd", f"qt{pq}:M"],
                     writes=[f"qt{pq}:U", f"qt{pq}:is2"], tag=f"q3{t}")
                for tc in range(4):
                    P.op("tensor", (lambda tc_, q_: lambda e: e.transpose(
                        ps[5][0:1, 128 * tc_:128 * (tc_ + 1)],
                        q_["U"][:, tc_:tc_ + 1], eyef[:, :]))(tc, qt),
                        reads=[f"qt{pq}:U", "c:eyef"], writes=["ps5"], tag=f"ut{t}_{tc}")
                P.op("vector", lambda e: e.tensor_copy(qrow[0:1, 0:T], ps[5][0:1, 0:T]),
                     reads=["ps5"], writes=["qrow"], tag=f"ue{t}")
                P.op("tensor", lambda e: e.matmul(ps[5][:, 0:T], onesr[0:1, :], qrow[0:1, 0:T], start=True, stop=True),
                     reads=["qrow", "c:onesr"], writes=["ps5"], tag=f"ub{t}")
                P.op("vector", lambda e: e.tensor_copy(Ub[:, :], ps[5][:, 0:T]),
                     reads=["ps5"], writes=["Ub"], tag=f"uc{t}")
            def p4_iq(t):
                def i3(jj):
                    s_ = jj % 2
                    P.op("gpsimd", (lambda j_, sl_: lambda e: e.tensor_scalar(
                        iqT[:, T * j_:T * (j_ + 1)], u16b[sl_][:, :], 1.0, -R16,
                        ALU.mult, ALU.add))(jj, s_),
                        reads=[f"u16b:{s_}"], writes=[f"iqT:{jj}"], tag=f"i3{t}_{jj}")
                for jj in range(NJ2):
                    s_ = jj % 2
                    P.op("gpsimd", (lambda j_, sl_: lambda e: e.tensor_mul(
                        iqs[sl_][:, :], gat[:, T * j_:T * (j_ + 1)], Ub[:, :]))(jj, s_),
                        reads=[f"gat:{jj}", "Ub"], writes=[f"iqs:{s_}"], tag=f"i1{t}_{jj}")
                    P.op("scalar", (lambda j_, sl_: lambda e: e.activation(
                        u16b[sl_][:, :], iqs[sl_][:, :], AF.Identity, bias=cbr[:, 0:1],
                        scale=nwf[:, j_:j_ + 1]))(jj, s_),
                        reads=[f"iqs:{s_}", "c:cbr", "c:nwf"], writes=[f"u16b:{s_}"], tag=f"i2{t}_{jj}")
                    if jj >= 1:
                        i3(jj - 1)
                    if jj == NJ2 - 1:
                        i3(jj)
                    yield

            def p4_w2(t, giq):
                if giq is not None:
                    next(giq, None)
                    next(giq, None)
                pq = t % 2
                qt = {k: qt_[:, 8 * i + 4 * pq:8 * i + 4 * (pq + 1)] for i, k in enumerate(QK)}
                for dchunk in range(4):
                    for kg in range(NJ2 // 4):
                        slot = kg % 3
                        wr = [f"w2s:{4 * kg + i}" for i in range(4)]
                        dma("sync", "w2l", (lambda g_, s_, d_: lambda e: e.dma_start(
                            out=wbuf2[s_][:, :].rearrange("p (k c) -> p k c", k=4),
                            in_=w2s[d_, 4 * g_:4 * (g_ + 1), :, :].rearrange("k p c -> p k c")))(kg, slot, dchunk),
                            reads=wr, writes=[f"wbuf2:{slot}"], tag=f"s2_{t}_{dchunk}_{kg}")
                        for ki in range(4):
                            kk = 4 * kg + ki
                            if giq is not None and dchunk == 0:
                                next(giq, None)
                            for tc in range(4):
                                P.op("tensor", (lambda k_, s_, ki_, tc_: lambda e: e.matmul(
                                    ps[tc_][:, :], iqT[:, T * k_ + 128 * tc_:T * k_ + 128 * (tc_ + 1)],
                                    wbuf2[s_][:, 512 * ki_:512 * (ki_ + 1)],
                                    start=(k_ == 0), stop=(k_ == NJ2 - 1)))(kk, slot, ki, tc),
                                    reads=[f"wbuf2:{slot}", f"iqT:{kk}"],
                                    writes=[f"ps{tc}"], tag=f"w2_{t}_{dchunk}_{kk}_{tc}")
                    for tc in range(4):
                        sl = tc % 2
                        dma("scalar", "rl", (lambda t_, tc_, d_, s_: lambda e: e.dma_start(
                            out=xres[s_][:, :],
                            in_=x_h[T * t_ + 128 * tc_:T * t_ + 128 * (tc_ + 1),
                                    512 * d_:512 * (d_ + 1)]))(t, tc, dchunk, sl),
                            reads=[], writes=[f"xres:{sl}"], tag=f"lx{t}_{dchunk}_{tc}")
                        dma("scalar", "rl", (lambda t_, tc_, d_, s_: lambda e: e.dma_start(
                            out=aores[s_][:, :],
                            in_=aos[T * t_ + 128 * tc_:T * t_ + 128 * (tc_ + 1),
                                    512 * d_:512 * (d_ + 1)]))(t, tc, dchunk, sl),
                            reads=[f"aos:{t}"], writes=[f"aores:{sl}"], tag=f"la{t}_{dchunk}_{tc}")
                        P.op("vector", (lambda tc_, s_, q_: lambda e: e.scalar_tensor_tensor(
                            ostg[s_][:, :], ps[tc_][:, :], q_["is2"][:, tc_:tc_ + 1],
                            xres[s_][:, :], ALU.mult, ALU.add))(tc, sl, qt),
                            reads=[f"ps{tc}", f"xres:{sl}", f"qt{pq}:is2"],
                            writes=[f"ostg:{sl}"], tag=f"o1{t}_{dchunk}_{tc}")
                        P.op("gpsimd", (lambda s_: lambda e: e.tensor_add(
                            ostg[s_][:, :], ostg[s_][:, :], aores[s_][:, :]))(sl),
                            reads=[f"ostg:{sl}", f"aores:{sl}"], writes=[f"ostg:{sl}"], tag=f"o2{t}_{dchunk}_{tc}")
                        dma("gpsimd", "ol", (lambda t_, tc_, d_, s_: lambda e: e.dma_start(
                            out=out_h[T * t_ + 128 * tc_:T * t_ + 128 * (tc_ + 1),
                                      512 * d_:512 * (d_ + 1)],
                            in_=ostg[s_][:, :]))(t, tc, dchunk, sl),
                            reads=[f"ostg:{sl}"], writes=[f"odone:{sl}"], tag=f"wo{t}_{dchunk}_{tc}")

            dma("sync", "c", lambda e: e.dma_start(out=eyef[:, :], in_=eyef_h),
                reads=[], writes=["c:eyef"], tag="ld_eyef")
            for t in range(nt):
                p4_txq(t)
                g13 = p4_w13(t)
                for _ in g13:
                    pass
                p4_quant2_iq(t)
                giq = p4_iq(t)
                p4_w2(t, giq)
                for _ in giq:
                    pass


            P.resolve()
            with nc.Block() as blk:
                @blk.sync
                def _(eng):
                    P.emit("sync", eng, prog_sems, dma_sems)
                    r = Prog.RING["ol"]
                    tot = P.dma_tot["ol"]
                    for i in range(r):
                        n_i = tot // r + (1 if i < tot % r else 0)
                        if n_i:
                            eng.wait_ge(dma_sems[("ol", i)], 16 * n_i)

                @blk.vector
                def _(eng):
                    P.emit("vector", eng, prog_sems, dma_sems)

                @blk.scalar
                def _(eng):
                    P.emit("scalar", eng, prog_sems, dma_sems)

                @blk.tensor
                def _(eng):
                    P.emit("tensor", eng, prog_sems, dma_sems)

                @blk.gpsimd
                def _(eng):
                    P.emit("gpsimd", eng, prog_sems, dma_sems)

    return nc


def _host_consts():
    c = {}
    c["eye16"] = np.eye(128, dtype=ml_dtypes.bfloat16)
    c["eyef"] = np.eye(128, dtype=np.float32)
    c["prow"] = np.arange(128, dtype=np.float32).reshape(1, 128)
    c["mrow"] = np.arange(BLK, dtype=np.float32).reshape(1, BLK)
    c["onesrow"] = np.ones((1, 128), dtype=np.float32)
    p = np.arange(128).reshape(128, 1)
    m = np.arange(2 * BLK).reshape(1, 2 * BLK)
    c["mask"] = ((m % BLK) >= (128 * (m // BLK) + p)).astype(ml_dtypes.bfloat16)
    return c


def _blocked_lhsT(wt):
    Dd, ncol = wt.shape
    nk, ncch = Dd // 128, ncol // 128
    return np.ascontiguousarray(
        wt.reshape(nk, 128, ncch, 128).transpose(2, 1, 0, 3).reshape(ncch, 128, Dd))


def kernel(L0_attn_hidden_states, L0_attn_slope_rate, L0_attn_w_qkv,
           L0_attn_w_gate, L0_attn_w_out, L0_attn_norm_w,
           L0_ffn_w13, L0_ffn_w2, L0_ffn_norm_w):
    x = np.ascontiguousarray(np.asarray(L0_attn_hidden_states, dtype=np.float32))
    wqkv = np.asarray(L0_attn_w_qkv, dtype=np.float32).reshape(NH, 3, HD, D)
    wq = _blocked_lhsT(np.ascontiguousarray(wqkv[:, 0].reshape(D, D).T))
    wk = _blocked_lhsT(np.ascontiguousarray(wqkv[:, 1].reshape(D, D).T))
    wv = np.ascontiguousarray(wqkv[:, 2].reshape(D, D).T)
    wg = _blocked_lhsT(np.ascontiguousarray(np.asarray(L0_attn_w_gate, np.float32).T))
    wo = np.ascontiguousarray(np.asarray(L0_attn_w_out, np.float32).T)
    w13 = _blocked_lhsT(np.ascontiguousarray(np.asarray(L0_ffn_w13, np.float32).T))
    w2t = np.ascontiguousarray(np.asarray(L0_ffn_w2, np.float32).T)
    slope = np.ascontiguousarray(np.asarray(L0_attn_slope_rate, np.float32).reshape(1, NH))
    nwa = np.ascontiguousarray(np.asarray(L0_attn_norm_w, np.float32).reshape(NH, 128).T)
    nwf = np.ascontiguousarray(np.asarray(L0_ffn_norm_w, np.float32).reshape(NJ2, 128).T)
    consts = _host_consts()

    nc = build_nc(seq=S)
    base = dict(wq=wq, wk=wk, wv=wv, wg=wg, wo=wo, w13=w13, w2t=w2t,
                slope=slope, nwa=nwa, nwf=nwf, **consts)
    in_maps = [dict(base, x=np.ascontiguousarray(x[b])) for b in range(NCORES)]
    res = run_bass_kernel_spmd(nc, in_maps, core_ids=list(range(NCORES)))
    out = np.stack([np.asarray(res.results[b]["out"]) for b in range(NCORES)], axis=0)
    return out.astype(np.float32)


# version 19
# speedup vs baseline: 1.2655x; 1.0365x over previous
import numpy as np
import ml_dtypes
import contextlib
import concourse.bass as bass
import concourse.mybir as mybir
from concourse.bass_utils import run_bass_kernel_spmd
from concourse.alu_op_type import AluOpType as ALU

F32 = mybir.dt.float32
BF16 = mybir.dt.bfloat16
F16 = mybir.dt.float16
AF = mybir.ActivationFunctionType
AX = mybir.AxisListType

NCORES = 8
D = 2048
NH = 16
HD = 128
BLK = 256
DF = 5632
S = 4096
T = 512
NKC = D // 128
NJ = 2 * DF // 128
NJ2 = DF // 128
LN_EPS = 1e-5
AN_EPS = 1e-6
RMS_EPS = 1e-5
R16 = 1536.0

ENGS = ("tensor", "vector", "scalar", "gpsimd", "sync")


class Prog:

    def __init__(self):
        self.ops = []
        self.last_w = {}
        self.readers = {}
        self.bar_deps = []
        self.nbar = 0

    def op(self, engine, fn, reads=(), writes=(), dma=None, tag=""):
        idx = len(self.ops)
        deps = set()
        for b in reads:
            if b in self.last_w:
                deps.add(self.last_w[b])
        for b in writes:
            if b in self.last_w:
                deps.add(self.last_w[b])
            for r in self.readers.get(b, ()):
                deps.add(r)
        deps.update(self.bar_deps)
        deps.discard(idx)
        if dma is not None:
            if not hasattr(self, "dma_eng"):
                self.dma_eng = {}
            assert self.dma_eng.setdefault(dma, engine) == engine, \
                f"dma sem {dma} used from {engine} and {self.dma_eng[dma]}"
        self.ops.append(dict(idx=idx, engine=engine, fn=fn, deps=deps,
                             dma=dma, dma_idx=None, tag=tag, inc=None,
                             is_bar=False))
        for b in reads:
            self.readers.setdefault(b, []).append(idx)
        for b in writes:
            self.last_w[b] = idx
            self.readers[b] = []
        return idx

    def barrier(self):
        self.nbar += 1
        prior = list(range(len(self.ops)))
        bar_idxs = []
        for e in ENGS:
            idx = len(self.ops)
            self.ops.append(dict(idx=idx, engine=e, fn=None, deps=set(prior),
                                 dma=None, dma_idx=None, tag=f"bar{self.nbar}",
                                 inc=None, is_bar=True))
            bar_idxs.append(idx)
        self.bar_deps = bar_idxs

    RING = {"c": 1, "wl": 3, "ws": 2, "wst": 6, "xl": 3, "aw": 2, "tr": 4,
            "rl": 4, "ol": 2, "tb": 1, "tbw": 1, "ao": 2, "wsa": 2, "wla": 3,
            "trk": 4, "trv": 4, "trs": 4, "w2l": 4}

    def resolve(self):
        eng_seq = {}
        seq_ctr = {e: 0 for e in ENGS}
        self.dma_tot = {}
        for o in self.ops:
            eng_seq[o["idx"]] = seq_ctr[o["engine"]]
            seq_ctr[o["engine"]] += 1
            if o["dma"] is not None:
                k = self.dma_tot.get(o["dma"], 0)
                self.dma_tot[o["dma"]] = k + 1
                r = self.RING[o["dma"]]
                o["dma_sub"] = (o["dma"], k % r)
                o["dma_val"] = 16 * (k // r + 1)
        needs_inc = set()
        for o in self.ops:
            for d in o["deps"]:
                dop = self.ops[d]
                if dop["dma"] is None and not (
                        dop["engine"] == "tensor" and o["engine"] == "tensor"):
                    needs_inc.add(d)
        for o in self.ops:
            if o["is_bar"]:
                needs_inc.add(o["idx"])
        self.inc_val = {}
        ctr = {e: 0 for e in ENGS}
        for o in self.ops:
            if o["idx"] in needs_inc:
                ctr[o["engine"]] += 1
                o["inc"] = ctr[o["engine"]]
                self.inc_val[o["idx"]] = ctr[o["engine"]]
        self.eng_seq = eng_seq
        self.by_eng = {e: [o for o in self.ops if o["engine"] == e] for e in ENGS}

    def emit(self, ename, eng_obj, prog_sems, dma_sems):
        waited = {}
        drained_upto = 0
        for o in self.by_eng[ename]:
            need_drain = False
            waits = {}
            for d in sorted(o["deps"]):
                dop = self.ops[d]
                if dop["dma"] is not None:
                    key = ("dma",) + dop["dma_sub"]
                    val = dop["dma_val"]
                    if waits.get(key, 0) < val:
                        waits[key] = val
                elif dop["engine"] == ename:
                    if ename != "tensor":
                        v = self.inc_val[d]
                        key = ("prog", ename)
                        if waits.get(key, 0) < v:
                            waits[key] = v
                else:
                    v = self.inc_val[d]
                    key = ("prog", dop["engine"])
                    if waits.get(key, 0) < v:
                        waits[key] = v
            if o["dma"] is not None and o["dma_val"] > 16:
                key = ("dma",) + o["dma_sub"]
                if waits.get(key, 0) < o["dma_val"] - 16:
                    waits[key] = max(waits.get(key, 0), o["dma_val"] - 16)
            for key, val in sorted(waits.items(), key=lambda kv: str(kv[0])):
                if waited.get(key, 0) >= val:
                    continue
                sem = dma_sems[key[1:]] if key[0] == "dma" else prog_sems[key[1]]
                eng_obj.wait_ge(sem, val)
                waited[key] = val
            if o["is_bar"]:
                inst = eng_obj.drain()
                drained_upto = self.eng_seq[o["idx"]]
            else:
                inst = o["fn"](eng_obj)
            if o["inc"] is not None:
                assert inst is not None, f"op {o['tag']} returned None but needs inc"
                inst.then_inc(prog_sems[ename], 1)
            if o["dma"] is not None:
                inst.then_inc(dma_sems[o["dma_sub"]], 16)


def build_nc(seq=S):
    nt = seq // T
    nxc = seq // 128
    nc = bass.Bass("TRN2", target_bir_lowering=False, debug=False)

    x_h = nc.dram_tensor("x", [seq, D], F32, kind="ExternalInput").ap()
    wq_h = nc.dram_tensor("wq", [NKC, 128, D], F32, kind="ExternalInput").ap()
    wk_h = nc.dram_tensor("wk", [NKC, 128, D], F32, kind="ExternalInput").ap()
    wv_h = nc.dram_tensor("wv", [D, D], F32, kind="ExternalInput").ap()
    wg_h = nc.dram_tensor("wg", [NKC, 128, D], F32, kind="ExternalInput").ap()
    wo_h = nc.dram_tensor("wo", [D, D], F32, kind="ExternalInput").ap()
    w13_h = nc.dram_tensor("w13", [NJ, 128, D], F32, kind="ExternalInput").ap()
    w2_h = nc.dram_tensor("w2t", [DF, D], F32, kind="ExternalInput").ap()
    slope_h = nc.dram_tensor("slope", [1, NH], F32, kind="ExternalInput").ap()
    nwa_h = nc.dram_tensor("nwa", [128, NH], F32, kind="ExternalInput").ap()
    nwf_h = nc.dram_tensor("nwf", [128, NJ2], F32, kind="ExternalInput").ap()
    eye_h = nc.dram_tensor("eye16", [128, 128], BF16, kind="ExternalInput").ap()
    eyef_h = nc.dram_tensor("eyef", [128, 128], F32, kind="ExternalInput").ap()
    prow_h = nc.dram_tensor("prow", [1, 128], F32, kind="ExternalInput").ap()
    mrow_h = nc.dram_tensor("mrow", [1, BLK], F32, kind="ExternalInput").ap()
    onesr_h = nc.dram_tensor("onesrow", [1, 128], F32, kind="ExternalInput").ap()
    mask_h = nc.dram_tensor("mask", [128, 2 * BLK], BF16, kind="ExternalInput").ap()
    out_h = nc.dram_tensor("out", [seq, D], F32, kind="ExternalOutput").ap()

    wqs = nc.dram_tensor("wqs", [NKC, 128, D], BF16, kind="Internal").ap()
    wks = nc.dram_tensor("wks", [NKC, 128, D], BF16, kind="Internal").ap()
    wvs = nc.dram_tensor("wvs", [4, NKC, 128, 512], BF16, kind="Internal").ap()
    wgs = nc.dram_tensor("wgs", [NKC, 128, D], BF16, kind="Internal").ap()
    wos = nc.dram_tensor("wos", [4, NKC, 128, 512], BF16, kind="Internal").ap()
    w13s = nc.dram_tensor("w13s", [NJ, 128, D], BF16, kind="Internal").ap()
    w2s = nc.dram_tensor("w2s", [4, NJ2, 128, 512], BF16, kind="Internal").ap()
    xns = nc.dram_tensor("xns", [D, seq], BF16, kind="Internal").ap()
    xqs = nc.dram_tensor("xqs", [D, seq], BF16, kind="Internal").ap()
    aos = nc.dram_tensor("aos", [seq, D], BF16, kind="Internal").ap()
    tbl = nc.dram_tensor("tbl", [2, NH * BLK], F32, kind="Internal").ap()

    P = Prog()

    def dma(qeng, sem, fn, reads, writes, tag=""):
        return P.op(qeng, fn, reads=reads, writes=writes, dma=sem, tag=tag)

    es = contextlib.ExitStack()
    with es:
        sbp = lambda n, sh, dt: es.enter_context(nc.sbuf_tensor(n, sh, dt))
        eye = sbp("eye", [128, 128], BF16)

        onesr = sbp("onesr_s", [1, 128], F32)
        onescol_b = sbp("onescol_b", [128, 1], BF16)
        mask = sbp("mask_s", [128, 2 * BLK], BF16)
        nwa = sbp("nwa_s", [128, NH], F32)
        nwf = sbp("nwf_s", [128, NJ2], F32)
        srow = sbp("srow", [1, NH], F32)
        bd_vec = sbp("bd_vec", [128, NH], F32)
        cb5 = sbp("cb5", [128, 1], F32)
        cb6 = sbp("cb6", [128, 1], F32)
        cbr = sbp("cbr", [128, 1], F32)
        kdec = sbp("kdec", [128, 2 * NH], F32)
        NWB = 5
        wbuf = [sbp(f"wbuf{i}", [128, D], BF16) for i in range(NWB)]
        wbufS = [sbp(f"wbufS{i}", [128, D], BF16) for i in range(2)]
        wf32 = [sbp(f"wf32{i}", [128, D], F32) for i in range(2)]

        ps = [es.enter_context(nc.psum_tensor(f"ps{i}", [128, 512], F32))
              for i in range(6)]

        prog_sems = {e: es.enter_context(nc.semaphore(f"pg_{e}")) for e in ENGS}
        dma_sems = {}
        for k, r in Prog.RING.items():
            for i in range(r):
                dma_sems[(k, i)] = es.enter_context(nc.semaphore(f"dm_{k}{i}"))

        esAB = contextlib.ExitStack()
        sbab = lambda n, sh, dt: esAB.enter_context(nc.sbuf_tensor(n, sh, dt))
        with esAB:
            qdbc = sbab("qdbc", [128, NH * BLK], BF16)
            ddT = sbab("ddT", [128, NH * 2 * BLK], BF16)
            kv = sbab("kv", [128, NH * HD], F32)

            es1 = contextlib.ExitStack()
            sb1 = lambda n, sh, dt: es1.enter_context(nc.sbuf_tensor(n, sh, dt))
            with es1:
                rA = sb1("rowA", [1, NH * BLK], F32)
                rB = sb1("rowB", [1, NH * BLK], F32)
                rC = sb1("rowC", [1, NH * BLK], F32)
                rD = sb1("rowD", [1, NH * BLK], F32)
                ddw = sb1("ddw", [128, 2 * BLK], F32)
                t16 = sb1("t16", [16, BLK], F32)
                prow = sb1("prow_s", [1, 128], F32)
                mrow = sb1("mrow_s", [1, BLK], F32)
                ones256 = sb1("ones256", [1, BLK], F32)
                psB = [es1.enter_context(nc.psum_tensor(f"psB{i}", [128, 512], F32))
                       for i in range(2)]
                tstg = [sb1(f"tstg{i}", [128, 512], BF16) for i in range(2)]
                xb = [sb1(f"xb{i}", [128, D], F32) for i in range(3)]
                u16 = sb1("u16", [128, D], F16)
                xnb = [sb1(f"xnb{i}", [128, D], BF16) for i in range(2)]
                xqb = [sb1(f"xqb{i}", [128, D], BF16) for i in range(2)]
                NST = nxc
                stt_ = sb1("stats", [128, 18 * nxc], F32)
                SK = ["sum", "ss", "mx", "mn", "m", "var", "sd", "rstd", "t0",
                      "t1", "mab", "c", "rc", "a1q", "a0q", "isv", "nis", "nmr"]
                st = {k: (lambda i: lambda g: stt_[:, i * NST:(i + 1) * NST] if g is None
                          else stt_[:, i * NST + g:i * NST + g + 1])(i)
                      for i, k in enumerate(SK)}

                for name, dst, src in [
                        ("eye", eye, eye_h),
                        ("prow", prow, prow_h), ("mrow", mrow, mrow_h),
                        ("onesr", onesr, onesr_h), ("mask", mask, mask_h),
                        ("nwa", nwa, nwa_h), ("nwf", nwf, nwf_h),
                        ("srow", srow, slope_h)]:
                    dma("sync", "c",
                        (lambda d_, s_: lambda e: e.dma_start(out=d_[:, :], in_=s_))(dst, src),
                        reads=[], writes=[f"c:{name}"], tag=f"ld_{name}")
                P.op("vector", lambda e: e.memset(ones256[:, :], 1.0), writes=["c:ones256"])
                P.op("vector", lambda e: e.memset(onescol_b[:, :], 1.0), writes=["c:onescol"])
                P.op("vector", lambda e: e.memset(cb5[:, :], 1e-5), writes=["c:cb5"])
                P.op("vector", lambda e: e.memset(cb6[:, :], 1e-6), writes=["c:cb6"])
                P.op("vector", lambda e: e.memset(cbr[:, :], R16), writes=["c:cbr"])
                P.op("vector", lambda e: e.memset(kv[:, :], 0.0),
                     writes=["kv"] + [f"kv:{h}" for h in range(NH)])

                P.op("tensor", lambda e: e.matmul(ps[5][0:16, 0:BLK], srow[0:1, :], ones256[0:1, :], start=True, stop=True),
                     reads=["c:srow", "c:ones256"], writes=["ps5"], tag="mmS")
                P.op("vector", lambda e: e.tensor_copy(t16[:, :], ps[5][0:16, 0:BLK]),
                     reads=["ps5"], writes=["t16"], tag="evS")
                dma("gpsimd", "tbw", lambda e: e.dma_start(
                    out=tbl[0:1, :].rearrange("o (p c) -> p (o c)", p=16), in_=t16[:, :]),
                    reads=["t16"], writes=["tbl0"], tag="wS")
                dma("sync", "tb", lambda e: e.dma_start(out=rA[:, :], in_=tbl[0:1, :]),
                    reads=["tbl0"], writes=["rA"], tag="rS")
                P.op("tensor", lambda e: e.matmul(ps[5][0:16, 0:BLK], srow[0:1, :], mrow[0:1, :], start=True, stop=True),
                     reads=["c:srow", "c:mrow", "t16"], writes=["ps5"], tag="mmR")
                P.op("vector", lambda e: e.tensor_copy(t16[:, :], ps[5][0:16, 0:BLK]),
                     reads=["ps5"], writes=["t16"], tag="evR")
                dma("gpsimd", "tbw", lambda e: e.dma_start(
                    out=tbl[1:2, :].rearrange("o (p c) -> p (o c)", p=16), in_=t16[:, :]),
                    reads=["t16"], writes=["tbl1"], tag="wR")
                dma("sync", "tb", lambda e: e.dma_start(out=rB[:, :], in_=tbl[1:2, :]),
                    reads=["tbl1"], writes=["rB"], tag="rR")
                P.op("tensor", lambda e: e.matmul(ps[5][:, 0:NH], onesr[0:1, :], srow[0:1, :], start=True, stop=True),
                     reads=["c:onesr", "c:srow", "t16"], writes=["ps5"], tag="mmB")
                P.op("scalar", lambda e: e.activation(bd_vec[:, :], ps[5][:, 0:NH], AF.Exp, scale=-float(BLK)),
                     reads=["ps5"], writes=["bd_vec"], tag="bd")
                P.op("vector", lambda e: e.tensor_single_scalar(rC[:, :], rA[:, :], -255.0, ALU.mult),
                     reads=["rA"], writes=["rC"], tag="rk0")
                P.op("vector", lambda e: e.tensor_single_scalar(rD[:, :], rA[:, :], -127.0, ALU.mult),
                     reads=["rA"], writes=["rD"], tag="rk1")
                for h in range(NH):
                    for c in range(2):
                        col = 2 * h + c
                        rk = rC if c == 0 else rD
                        P.op("tensor", (lambda h_, col_: lambda e: e.matmul(
                            ps[4][:, col_:col_ + 1], prow[0:1, :],
                            rA[0:1, BLK * h_:BLK * h_ + 1], start=True, stop=False))(h, col),
                            reads=["c:prow", "rA"], writes=["ps4"], tag=f"kda{col}")
                        P.op("tensor", (lambda h_, col_, rk_: lambda e: e.matmul(
                            ps[4][:, col_:col_ + 1], onesr[0:1, :],
                            rk_[0:1, BLK * h_:BLK * h_ + 1], start=False, stop=True))(h, col, rk),
                            reads=["c:onesr", "rC", "rD"], writes=["ps4"], tag=f"kdb{col}")
                P.op("scalar", lambda e: e.activation(kdec[:, :], ps[4][:, 0:2 * NH], AF.Exp),
                     reads=["ps4"], writes=["kdec"], tag="kde")
                P.op("vector", lambda e: e.tensor_single_scalar(rC[:, :], rB[:, :], -1.0, ALU.mult),
                     reads=["rB"], writes=["rC"], tag="rc0")
                P.op("vector", lambda e: e.scalar_tensor_tensor(rD[:, :], rA[:, :], 128.0, rB[:, :], ALU.mult, ALU.subtract),
                     reads=["rA", "rB"], writes=["rD"], tag="rc1")
                for h in range(NH):
                    for c in range(2):
                        bank = 4 + (c % 2)
                        rowc = rC if c == 0 else rD
                        P.op("tensor", (lambda h_, b_: lambda e: e.matmul(
                            ps[b_][:, 0:BLK], prow[0:1, :],
                            rA[0:1, BLK * h_:BLK * (h_ + 1)], start=True, stop=False))(h, bank),
                            reads=["c:prow", "rA", "kdec"], writes=[f"ps{bank}"], tag=f"dda{h}{c}")
                        P.op("tensor", (lambda h_, b_, rc_: lambda e: e.matmul(
                            ps[b_][:, 0:BLK], onesr[0:1, :],
                            rc_[0:1, BLK * h_:BLK * (h_ + 1)], start=False, stop=True))(h, bank, rowc),
                            reads=["c:onesr", "rC", "rD"], writes=[f"ps{bank}"], tag=f"ddb{h}{c}")
                        P.op("vector", (lambda b_: lambda e: e.tensor_scalar(
                            ddw[:, 0:BLK], ps[b_][:, 0:BLK], 1.0, 0.0, ALU.mult, ALU.min))(bank),
                            reads=[f"ps{bank}"], writes=["ddw0"], tag=f"ddc{h}{c}")
                        P.op("scalar", lambda e: e.activation(ddw[:, BLK:2 * BLK], ddw[:, 0:BLK], AF.Exp),
                             reads=["ddw0"], writes=["ddw1"], tag=f"ddd{h}{c}")
                        P.op("vector", (lambda h_, c_: lambda e: e.tensor_mul(
                            ddT[:, (2 * h_ + c_) * BLK:(2 * h_ + c_ + 1) * BLK],
                            ddw[:, BLK:2 * BLK], mask[:, c_ * BLK:(c_ + 1) * BLK]))(h, c),
                            reads=["ddw1", "c:mask"], writes=[f"ddT:{h}:{c}"], tag=f"dde{h}{c}")
                P.op("vector", lambda e: e.tensor_add(rB[:, :], rB[:, :], rA[:, :]),
                     reads=["rA", "rB"], writes=["rB"], tag="r2")
                for cc in range(NH * BLK // 512):
                    bank = cc % 2
                    P.op("tensor", (lambda cc_, b_: lambda e: e.matmul(
                        ps[b_][:, :], onesr[0:1, :], rB[0:1, 512 * cc_:512 * (cc_ + 1)],
                        start=True, stop=True))(cc, bank),
                        reads=["rB", "c:onesr"], writes=[f"ps{bank}"], tag=f"qba{cc}")
                    P.op("scalar", (lambda cc_, b_: lambda e: e.activation(
                        qdbc[:, 512 * cc_:512 * (cc_ + 1)], ps[b_][:, :], AF.Exp, scale=-1.0))(cc, bank),
                        reads=[f"ps{bank}"], writes=[f"qdbc:{cc}"], tag=f"qbb{cc}")

                def stage3(src_ap, dst_ap, nchunk, key, q=("gpsimd", "wl", "gpsimd", "ws")):
                    lq, ls, wq_, ws_ = q
                    for c in range(nchunk):
                        f32b = f"wf32:{c % 2}"
                        bfb = f"wbufS:{c % 2}"
                        dma(lq, ls, (lambda c_: lambda e: e.dma_start(
                            out=wf32[c_ % 2][:, :], in_=src_ap[c_, :, :]))(c),
                            reads=[], writes=[f32b], tag=f"L{key}{c}")
                        P.op("gpsimd", (lambda c_: lambda e: e.tensor_copy(
                            wbufS[c_ % 2][:, :], wf32[c_ % 2][:, :]))(c),
                            reads=[f32b], writes=[bfb], tag=f"C{key}{c}")
                        dma(wq_, ws_, (lambda c_: lambda e: e.dma_start(
                            out=dst_ap[c_, :, :], in_=wbufS[c_ % 2][:, :]))(c),
                            reads=[bfb], writes=[f"{key}:{c}"], tag=f"S{key}{c}")

                def stage2(src_ap, dst_ap, nchunk, key, q=("gpsimd", "wl", "gpsimd", "ws")):
                    lq, ls, wq_, ws_ = q
                    for c in range(nchunk):
                        f32b = f"wf32:{c % 2}"
                        bfb = f"wbufS:{c % 2}"
                        dma(lq, ls, (lambda c_: lambda e: e.dma_start(
                            out=wf32[c_ % 2][:, :], in_=src_ap[128 * c_:128 * (c_ + 1), :]))(c),
                            reads=[], writes=[f32b], tag=f"L{key}{c}")
                        P.op("gpsimd", (lambda c_: lambda e: e.tensor_copy(
                            wbufS[c_ % 2][:, :], wf32[c_ % 2][:, :]))(c),
                            reads=[f32b], writes=[bfb], tag=f"C{key}{c}")
                        dma(wq_, ws_, (lambda c_: lambda e: e.dma_start(
                            out=dst_ap[:, c_, :, :].rearrange("d p c -> p d c"),
                            in_=wbufS[c_ % 2][:, :].rearrange("p (d c) -> p d c", d=4)))(c),
                            reads=[bfb], writes=[f"{key}:{c}"], tag=f"S{key}{c}")

                QA = ("sync", "wla", "gpsimd", "ws")
                stage3(wq_h, wqs, NKC, "wqs", QA)
                stage3(wk_h, wks, NKC, "wks", QA)
                stage2(wv_h, wvs, NKC, "wvs", QA)
                stage3(wg_h, wgs, NKC, "wgs", QA)
                stage2(wo_h, wos, NKC, "wos", QA)
                P.stage_ffn = lambda: (stage3(w13_h, w13s, NJ, "w13s"),
                                       stage2(w2_h, w2s, NJ2, "w2s"))

                for ci in range(nxc):
                    slot = ci % 3
                    xk = f"xb:{slot}"
                    dma("scalar", "xl", (lambda ci_, s_: lambda e: e.dma_start(
                        out=xb[s_][:, :], in_=x_h[128 * ci_:128 * (ci_ + 1), :]))(ci, slot),
                        reads=[], writes=[xk], tag=f"ldx{ci}")
                    P.op("scalar", (lambda s_, g_: lambda e: e.activation(
                        u16[:, :], xb[s_][:, :], AF.Square, accum_out=st["ss"](g_)))(slot, ci),
                        reads=[xk], writes=["u16", f"st1:{ci}"], tag=f"sq{ci}")
                    P.op("scalar", (lambda s_, g_: lambda e: e.activation(
                        u16[:, :], xb[s_][:, :], AF.Identity, accum_out=st["sum"](g_)))(slot, ci),
                        reads=[xk], writes=["u16", f"st1:{ci}"], tag=f"rs{ci}")
                    P.op("vector", (lambda s_, g_: lambda e: e.reduce_max(
                        st["mx"](g_), xb[s_][:, :], axis=AX.X))(slot, ci),
                        reads=[xk], writes=[f"st1:{ci}"], tag=f"rx{ci}")
                    P.op("vector", (lambda s_, g_: lambda e: e.tensor_reduce(
                        st["mn"](g_), xb[s_][:, :], AX.X, ALU.min))(slot, ci),
                        reads=[xk], writes=[f"st1:{ci}"], tag=f"rn{ci}")

                def chain_all(e):
                    s_ = lambda k: st[k](None)
                    e.tensor_single_scalar(s_("m"), s_("sum"), 1.0 / D, ALU.mult)
                    e.drain()
                    e.tensor_mul(s_("t0"), s_("m"), s_("m"))
                    e.drain()
                    e.scalar_tensor_tensor(s_("var"), s_("ss"), 1.0 / D, s_("t0"), ALU.mult, ALU.subtract)
                    return e.drain()
                P.op("vector", chain_all,
                     reads=[f"st1:{ci}" for ci in range(nxc)],
                     writes=["st2a"], tag="c1all")
                P.op("scalar", lambda e: e.activation(
                    st["sd"](None), st["var"](None), AF.Sqrt, bias=cb5[:, 0:1]),
                    reads=["st2a", "c:cb5"], writes=["st2b"], tag="c2all")

                def chain3_all(e):
                    s_ = lambda k: st[k](None)
                    e.reciprocal(s_("rstd"), s_("sd"))
                    e.drain()
                    e.tensor_sub(s_("t0"), s_("mx"), s_("m"))
                    e.tensor_sub(s_("t1"), s_("m"), s_("mn"))
                    e.drain()
                    e.tensor_max(s_("mab"), s_("t0"), s_("t1"))
                    e.drain()
                    e.tensor_mul(s_("t0"), s_("mab"), s_("rstd"))
                    e.drain()
                    e.tensor_single_scalar(s_("c"), s_("t0"), 1e-5, ALU.max)
                    e.drain()
                    e.reciprocal(s_("rc"), s_("c"))
                    e.drain()
                    e.scalar_tensor_tensor(s_("a1q"), s_("rstd"), 127.0, s_("rc"), ALU.mult, ALU.mult)
                    e.tensor_single_scalar(s_("isv"), s_("c"), 1.0 / 127.0, ALU.mult)
                    e.tensor_mul(s_("nmr"), s_("m"), s_("rstd"))
                    e.drain()
                    e.tensor_mul(s_("t1"), s_("m"), s_("a1q"))
                    e.tensor_single_scalar(s_("nis"), s_("isv"), -R16, ALU.mult)
                    e.tensor_single_scalar(s_("nmr"), s_("nmr"), -1.0, ALU.mult)
                    e.drain()
                    e.tensor_scalar(s_("a0q"), s_("t1"), -1.0, R16, ALU.mult, ALU.add)
                    return e.drain()
                P.op("vector", chain3_all, reads=["st2a", "st2b"],
                     writes=["st2"], tag="c3all")

                def p2_tr(ci, srcbuf, srckey, dst_hbm, okey, tagp):
                    for g in range(4):
                        pk = f"psB:{g % 2}"
                        for j in range(4):
                            kk = 4 * g + j
                            P.op("tensor", (lambda s_, k_, j_, g_: lambda e: e.matmul(
                                psB[g_ % 2][:, 128 * j_:128 * (j_ + 1)],
                                srcbuf[s_][:, 128 * k_:128 * (k_ + 1)], eye[:, :],
                                start=True, stop=True))(srckey, kk, j, g),
                                reads=[f"{tagp}bb:{srckey}", "c:eye"], writes=[pk], tag=f"{tagp}t{ci}_{kk}")
                        P.op("vector", (lambda g_: lambda e: e.tensor_copy(
                            tstg[g_ % 2][:, :], psB[g_ % 2][:, :]))(g),
                            reads=[pk], writes=[f"tstg:{g % 2}"], tag=f"{tagp}e{ci}_{g}")
                        dma("gpsimd", "aw", (lambda ci_, g_: lambda e: e.dma_start(
                            out=dst_hbm[512 * g_:512 * (g_ + 1), 128 * ci_:128 * (ci_ + 1)].rearrange(
                                "(k p) c -> p k c", k=4),
                            in_=tstg[g_ % 2][:, :].rearrange("p (k c) -> p k c", k=4)))(ci, g),
                            reads=[f"tstg:{g % 2}"], writes=[f"{okey}:{ci // 4}"], tag=f"{tagp}w{ci}_{g}")

                for ci in range(nxc):
                    slot = ci % 3
                    xk = f"xb:{slot}"
                    dma("scalar", "xl", (lambda ci_, s_: lambda e: e.dma_start(
                        out=xb[s_][:, :], in_=x_h[128 * ci_:128 * (ci_ + 1), :]))(ci, slot),
                        reads=[], writes=[xk], tag=f"ldx2_{ci}")
                    P.op("scalar", (lambda s_, g_: lambda e: e.activation(
                        xnb[s_ % 2][:, :], xb[s_][:, :], AF.Identity,
                        bias=st["nmr"](g_), scale=st["rstd"](g_)))(slot, ci),
                        reads=[xk, "st2"], writes=[f"xnbb:{slot % 2}"], tag=f"xn{ci}")
                    p2_tr(ci, xnb, slot % 2, xns, "xns", "xn")
                    P.op("scalar", (lambda s_, g_: lambda e: e.activation(
                        u16[:, :], xb[s_][:, :], AF.Identity,
                        bias=st["a0q"](g_), scale=st["a1q"](g_)))(slot, ci),
                        reads=[xk, "st2"], writes=["u16"], tag=f"u{ci}")
                    P.op("vector", (lambda s_, g_: lambda e: e.tensor_scalar(
                        xqb[s_ % 2][:, :], u16[:, :], st["isv"](g_), st["nis"](g_), ALU.mult, ALU.add))(slot, ci),
                        reads=["u16", "st2"], writes=[f"xqbb:{slot % 2}"], tag=f"xq{ci}")
                    p2_tr(ci, xqb, slot % 2, xqs, "xqs", "xq")

            P.barrier()
            es2 = contextlib.ExitStack()
            sb2 = lambda n, sh, dt: es2.enter_context(nc.sbuf_tensor(n, sh, dt))
            with es2:
                P.stage_ffn()
                xnT = [sb2(f"xnT{i}", [128, NKC * T], BF16) for i in range(2)]
                qT = sb2("qT", [128, NH * T], BF16)
                kT = sb2("kT", [128, NH * T], BF16)
                v_tok = [sb2(f"v_tok{i}", [128, D], BF16) for i in range(4)]
                psBt = [es2.enter_context(nc.psum_tensor(f"psBt{i}", [128, 512], F32))
                        for i in range(2)]
                gateT = sb2("gateT", [128, NH * T], BF16)
                aT = sb2("aT", [128, NH * T], BF16)
                kvb = [sb2(f"kvb{i}", [128, HD], BF16) for i in range(4)]
                qdi = [sb2(f"qdi{i}", [128, BLK], BF16) for i in range(4)]
                qkd = [sb2(f"qkd{i}", [128, 2 * BLK], BF16) for i in range(2)]
                ktok = [sb2(f"ktok{i}", [128, BLK], BF16) for i in range(4)]
                sqs0 = sb2("sqs0", [128, T], BF16)
                arow = sb2("arow", [1, T], F32)
                arow2 = sb2("arow2", [1, T], F32)
                rstdA = sb2("rstdA", [128, T], F32)
                aostg = [sb2(f"aostg{i}", [128, 512], BF16) for i in range(2)]

                for t in range(nt):
                    xp = t % 2
                    xnTc = xnT[xp]
                    for kk in range(NKC):
                        qq = ("scalar", "tr") if kk % 2 == 0 else ("sync", "trs")
                        dma(qq[0], qq[1], (lambda t_, k_, b_: lambda e: e.dma_start(
                            out=b_[:, T * k_:T * (k_ + 1)],
                            in_=xns[128 * k_:128 * (k_ + 1), T * t_:T * (t_ + 1)]))(t, kk, xnTc),
                            reads=[f"xns:{t}"], writes=[f"xnT{xp}:{kk}"], tag=f"txn{t}_{kk}")
                    for gi, (wsrc, wkey, dstT, dkey, act) in enumerate([
                            (wqs, "wqs", qT, "qT", AF.Silu),
                            (wks, "wks", kT, "kT", AF.Silu),
                            (wgs, "wgs", gateT, "gateT", AF.Sigmoid)]):
                        for c in range(NKC):
                            slot = (gi * NKC + c) % NWB
                            bank = (gi * NKC + c) % 2
                            dma("sync", "wst", (lambda w_, c_, s_: lambda e: e.dma_start(
                                out=wbuf[s_][:, :], in_=w_[c_, :, :]))(wsrc, c, slot),
                                reads=[f"{wkey}:{c}"], writes=[f"wbuf:{slot}"], tag=f"sw{t}{gi}{c}")
                            for kk in range(NKC):
                                P.op("tensor", (lambda s_, b_, k_, x_: lambda e: e.matmul(
                                    ps[b_][:, :], wbuf[s_][:, 128 * k_:128 * (k_ + 1)],
                                    x_[:, T * k_:T * (k_ + 1)],
                                    start=(k_ == 0), stop=(k_ == NKC - 1)))(slot, bank, kk, xnTc),
                                    reads=[f"wbuf:{slot}", f"xnT{xp}:{kk}"],
                                    writes=[f"ps{bank}"], tag=f"g{t}_{gi}_{c}_{kk}")
                            P.op("scalar", (lambda b_, c_, d_, a_: lambda e: e.activation(
                                d_[:, T * c_:T * (c_ + 1)], ps[b_][:, :], a_))(bank, c, dstT, act),
                                reads=[f"ps{bank}"], writes=[f"{dkey}:{c}"], tag=f"e{t}_{gi}_{c}")
                    for ec in range(4):
                        for kg in range(NKC // 4):
                            slot = kg % NWB
                            wr = [f"wvs:{4 * kg + i}" for i in range(4)]
                            dma("sync", "wst", (lambda g_, s_, e_: lambda ee: ee.dma_start(
                                out=wbuf[s_][:, :].rearrange("p (k c) -> p k c", k=4),
                                in_=wvs[e_, 4 * g_:4 * (g_ + 1), :, :].rearrange("k p c -> p k c")))(kg, slot, ec),
                                reads=wr, writes=[f"wbuf:{slot}"], tag=f"swv{t}_{ec}_{kg}")
                            for ki in range(4):
                                kk = 4 * kg + ki
                                for tc in range(4):
                                    P.op("tensor", (lambda k_, s_, ki_, tc_, x_: lambda e: e.matmul(
                                        ps[tc_][:, :], x_[:, T * k_ + 128 * tc_:T * k_ + 128 * (tc_ + 1)],
                                        wbuf[s_][:, 512 * ki_:512 * (ki_ + 1)],
                                        start=(k_ == 0), stop=(k_ == NKC - 1)))(kk, slot, ki, tc, xnTc),
                                        reads=[f"wbuf:{slot}", f"xnT{xp}:{kk}"],
                                        writes=[f"ps{tc}"], tag=f"gv{t}_{ec}_{kk}_{tc}")
                        for tc in range(4):
                            P.op("scalar", (lambda e_, tc_: lambda e: e.activation(
                                v_tok[tc_][:, 512 * e_:512 * (e_ + 1)], ps[tc_][:, :], AF.Silu))(ec, tc),
                                reads=[f"ps{tc}"], writes=[f"vt:{tc}:{ec}"], tag=f"ev{t}_{ec}_{tc}")

                    NU = (T // BLK) * NH

                    def u_hb(u):
                        b, h = divmod(u, NH)
                        return h, b, T * h + b * BLK

                    def attn_pre(u):
                        h, b, hb = u_hb(u)
                        r = u % 4
                        r2 = u % 2
                        pt = u % 2
                        for cc in range(2):
                            P.op("tensor", (lambda hb_, cc_, p_: lambda e: e.matmul(
                                psBt[p_][:, 128 * cc_:128 * (cc_ + 1)],
                                kT[:, hb_ + 128 * cc_:hb_ + 128 * (cc_ + 1)], eye[:, :],
                                start=True, stop=True))(hb, cc, pt),
                                reads=[f"kT:{h}", "c:eye"], writes=[f"psBt{pt}"], tag=f"tk{t}_{u}_{cc}")
                        for cc in range(2):
                            P.op("scalar", (lambda h_, cc_, r_, p_: lambda e: e.activation(
                                ktok[r_][:, 128 * cc_:128 * (cc_ + 1)],
                                psBt[p_][:, 128 * cc_:128 * (cc_ + 1)], AF.Copy,
                                scale=kdec[:, 2 * h_ + cc_:2 * h_ + cc_ + 1]))(h, cc, r, pt),
                                reads=[f"psBt{pt}", "kdec"], writes=[f"ktok:{r}"], tag=f"ke{t}_{u}_{cc}")
                        P.op("vector", (lambda h_, hb_, r_: lambda e: e.tensor_mul(
                            qdi[r_][:, :], qT[:, hb_:hb_ + BLK],
                            qdbc[:, BLK * h_:BLK * (h_ + 1)]))(h, hb, r),
                            reads=[f"qT:{h}", f"qdbc:{h // 2}"],
                            writes=[f"qdi:{r}"], tag=f"qd{t}_{u}")
                        P.op("scalar", (lambda h_, r_: lambda e: e.activation(
                            kvb[r_][:, :], kv[:, HD * h_:HD * (h_ + 1)], AF.Copy))(h, r),
                            reads=[f"kv:{h}"], writes=[f"kvb:{r}"], tag=f"kc{t}_{u}")

                    def attn_qk(u):
                        h, b, hb = u_hb(u)
                        pqk = u % 2
                        for cc in range(2):
                            P.op("tensor", (lambda hb_, cc_, p_: lambda e: e.matmul(
                                ps[p_][:, BLK * cc_:BLK * (cc_ + 1)],
                                kT[:, hb_ + 128 * cc_:hb_ + 128 * (cc_ + 1)],
                                qT[:, hb_:hb_ + BLK], start=True, stop=True))(hb, cc, pqk),
                                reads=[f"kT:{h}", f"qT:{h}"], writes=[f"ps{pqk}"], tag=f"qk{t}_{u}_{cc}")
                        P.op("vector", (lambda h_, u_, p_: lambda e: e.tensor_mul(
                            qkd[u_ % 2][:, :], ps[p_][:, 0:2 * BLK],
                            ddT[:, 2 * BLK * h_:2 * BLK * (h_ + 1)]))(h, u, pqk),
                            reads=[f"ps{pqk}", f"ddT:{h}:0", f"ddT:{h}:1"],
                            writes=[f"qkd:{u % 2}"], tag=f"qm{t}_{u}")

                    def attn_body(u):
                        h, b, hb = u_hb(u)
                        r = u % 4
                        r2 = u % 2
                        po = 2 + u % 2
                        pkv = 4 + u % 2
                        ve = h // 4
                        vsl = lambda cc: (2 * b + cc, h)
                        P.op("tensor", (lambda r_, p_: lambda e: e.matmul(
                            ps[p_][:, 0:BLK], kvb[r_][:, :], qdi[r_][:, :],
                            start=True, stop=False))(r, po),
                            reads=[f"kvb:{r}", f"qdi:{r}"], writes=[f"ps{po}"], tag=f"oi{t}_{u}")
                        for cc in range(2):
                            P.op("tensor", (lambda b_, cc_, h_, r_, p_: lambda e: e.matmul(
                                ps[p_][:, 0:BLK], v_tok[2 * b_ + cc_][:, 128 * h_:128 * (h_ + 1)],
                                qkd[u % 2][:, BLK * cc_:BLK * (cc_ + 1)],
                                start=False, stop=(cc_ == 1)))(b, cc, h, r, po),
                                reads=[f"vt:{2 * b + cc}:{ve}", f"qkd:{u % 2}"],
                                writes=[f"ps{po}"], tag=f"oa{t}_{u}_{cc}")
                        P.op("vector", (lambda hb_, p_, h_, b_: lambda e: e.tensor_copy(
                            aT[:, hb_:hb_ + BLK], ps[p_][:, 0:BLK]))(hb, po, h, b),
                            reads=[f"ps{po}"], writes=[f"aT:{h}:{b}"], tag=f"oe{t}_{u}")
                        for cc in range(2):
                            P.op("tensor", (lambda b_, cc_, h_, r_, p_: lambda e: e.matmul(
                                ps[p_][:, 0:HD], ktok[r_][:, 128 * cc_:128 * (cc_ + 1)],
                                v_tok[2 * b_ + cc_][:, 128 * h_:128 * (h_ + 1)],
                                start=(cc_ == 0), stop=(cc_ == 1)))(b, cc, h, r, pkv),
                                reads=[f"ktok:{r}", f"vt:{2 * b + cc}:{ve}"],
                                writes=[f"ps{pkv}"], tag=f"km{t}_{u}_{cc}")
                        P.op("vector", (lambda h_, p_: lambda e: e.scalar_tensor_tensor(
                            kv[:, HD * h_:HD * (h_ + 1)], kv[:, HD * h_:HD * (h_ + 1)],
                            bd_vec[:, h_:h_ + 1], ps[p_][:, 0:HD], ALU.mult, ALU.add))(h, pkv),
                            reads=[f"ps{pkv}", f"kv:{h}", "bd_vec"],
                            writes=[f"kv:{h}"], tag=f"ku{t}_{u}")

                    for u in range(3):
                        attn_pre(u)
                    attn_qk(0)
                    for u in range(NU):
                        if u + 1 < NU:
                            attn_qk(u + 1)
                        if u + 3 < NU:
                            attn_pre(u + 3)
                        attn_body(u)

                    sqs = [qkd[0][:, :], qkd[1][:, :], sqs0[:, :]]
                    skey = ["qkd:0", "qkd:1", "sqs:0"]
                    def av_mm(h_):
                        P.op("tensor", (lambda hh: lambda e: e.matmul(
                            ps[0][0:1, 0:T], onescol_b[:, 0:1], sqs[hh % 3],
                            start=(hh == 0), stop=(hh == NH - 1)))(h_),
                            reads=[skey[h_ % 3], "c:onescol"], writes=["ps0"], tag=f"av{t}_{h_}")
                    for h in range(NH):
                        P.op("scalar", (lambda h_: lambda e: e.activation(
                            sqs[h_ % 3], aT[:, T * h_:T * (h_ + 1)], AF.Square))(h),
                            reads=[f"aT:{h}:0", f"aT:{h}:1"], writes=[skey[h % 3]], tag=f"as{t}_{h}")
                        if h >= 2:
                            av_mm(h - 2)
                    av_mm(NH - 2)
                    av_mm(NH - 1)
                    P.op("vector", lambda e: e.tensor_single_scalar(
                        arow[0:1, 0:T], ps[0][0:1, 0:T], 1.0 / D, ALU.mult),
                        reads=["ps0"], writes=["arow"], tag=f"an1{t}")
                    P.op("scalar", lambda e: e.activation(arow2[0:1, 0:T], arow[0:1, 0:T], AF.Sqrt, bias=cb6[0:1, 0:1]),
                         reads=["arow", "c:cb6"], writes=["arow2"], tag=f"an2{t}")
                    P.op("vector", lambda e: e.reciprocal(arow[0:1, 0:T], arow2[0:1, 0:T]),
                         reads=["arow2"], writes=["arow"], tag=f"an3{t}")
                    P.op("tensor", lambda e: e.matmul(ps[1][:, 0:T], onesr[0:1, :], arow[0:1, 0:T], start=True, stop=True),
                         reads=["arow", "c:onesr"], writes=["ps1"], tag=f"an4{t}")
                    P.op("vector", lambda e: e.tensor_copy(rstdA[:, 0:T], ps[1][:, 0:T]),
                         reads=["ps1"], writes=["rstdA"], tag=f"an5{t}")
                    for h in range(NH):
                        P.op("vector", (lambda h_: lambda e: e.scalar_tensor_tensor(
                            sqs[h_ % 3], aT[:, T * h_:T * (h_ + 1)], nwa[:, h_:h_ + 1],
                            rstdA[:, 0:T], ALU.mult, ALU.mult))(h),
                            reads=[f"aT:{h}:0", f"aT:{h}:1", "rstdA", "c:nwa"],
                            writes=[skey[h % 3]], tag=f"nm{t}_{h}")
                        P.op("vector", (lambda h_: lambda e: e.tensor_mul(
                            aT[:, T * h_:T * (h_ + 1)], sqs[h_ % 3],
                            gateT[:, T * h_:T * (h_ + 1)]))(h),
                            reads=[skey[h % 3], f"gateT:{h}"],
                            writes=[f"aT:{h}:0", f"aT:{h}:1", f"ga:{h}"], tag=f"gm{t}_{h}")

                    for dchunk in range(4):
                        for kg in range(NKC // 4):
                            slot = kg % NWB
                            wr = [f"wos:{4 * kg + i}" for i in range(4)]
                            dma("sync", "wst", (lambda g_, s_, d_: lambda e: e.dma_start(
                                out=wbuf[s_][:, :].rearrange("p (k c) -> p k c", k=4),
                                in_=wos[d_, 4 * g_:4 * (g_ + 1), :, :].rearrange("k p c -> p k c")))(kg, slot, dchunk),
                                reads=wr, writes=[f"wbuf:{slot}"], tag=f"so{t}_{dchunk}_{kg}")
                            for ki in range(4):
                                kk = 4 * kg + ki
                                for tc in range(4):
                                    P.op("tensor", (lambda k_, s_, ki_, tc_: lambda e: e.matmul(
                                        ps[2 + tc_][:, :],
                                        aT[:, T * k_ + 128 * tc_:T * k_ + 128 * (tc_ + 1)],
                                        wbuf[s_][:, 512 * ki_:512 * (ki_ + 1)],
                                        start=(k_ == 0), stop=(k_ == NKC - 1)))(kk, slot, ki, tc),
                                        reads=[f"wbuf:{slot}", f"ga:{kk}"],
                                        writes=[f"ps{2 + tc}"], tag=f"op{t}_{dchunk}_{kk}_{tc}")
                        for tc in range(4):
                            so = (dchunk * 4 + tc) % 2
                            P.op("scalar", (lambda tc_, s_: lambda e: e.activation(
                                aostg[s_][:, :], ps[2 + tc_][:, :], AF.Copy))(tc, so),
                                reads=[f"ps{2 + tc}"], writes=[f"aostg:{so}"], tag=f"ae{t}_{dchunk}_{tc}")
                            dma("scalar", "ao", (lambda t_, tc_, d_, s_: lambda e: e.dma_start(
                                out=aos[T * t_ + 128 * tc_:T * t_ + 128 * (tc_ + 1),
                                        512 * d_:512 * (d_ + 1)],
                                in_=aostg[s_][:, :]))(t, tc, dchunk, so),
                                reads=[f"aostg:{so}"], writes=[f"aos:{t}"], tag=f"aw{t}_{dchunk}_{tc}")

            P.barrier()
        es3 = contextlib.ExitStack()
        sb3 = lambda n, sh, dt: es3.enter_context(nc.sbuf_tensor(n, sh, dt))
        with es3:
            xqT = sb3("xqT", [128, NKC * T], BF16)
            gat = sb3("gat", [128, NJ2 * T], BF16)
            iqT = sb3("iqT", [128, NJ2 * T], BF16)
            a_scr = [sb3(f"ascr{i}", [128, T], F32) for i in range(2)]
            sq2 = [sb3(f"sq2_{i}", [128, T], BF16) for i in range(4)]
            tmpb = [sb3(f"tmpb{i}", [128, T], BF16) for i in range(2)]
            Mb = [sb3(f"Mb{i}", [128, T], BF16) for i in range(2)]
            mbt = [sb3(f"mbt{i}", [128, 128], BF16) for i in range(2)]
            Ub = sb3("Ub", [128, T], F32)
            u16b = [sb3(f"u16b{i}", [128, T], F16) for i in range(2)]
            qrow = sb3("qrow", [1, T], F32)
            qt_ = sb3("qstats", [128, 88], F32)
            QK = ["ss", "M", "var", "sd", "rstd", "c", "rc", "s2", "is2", "U"]
            iqs = [sb3(f"iqs{i}", [128, T], F32) for i in range(2)]
            eyef = sb3("eyef_s", [128, 128], F32)
            ps6 = es3.enter_context(nc.psum_tensor("ps6", [128, 512], F32))
            ps7 = es3.enter_context(nc.psum_tensor("ps7", [128, 512], F32))
            ps67 = [ps6, ps7]
            wbuf2 = [sb3(f"wbuf2_{i}", [128, 2048], BF16) for i in range(3)]
            xres = [sb3(f"xres{i}", [128, 512], F32) for i in range(2)]
            aores = [sb3(f"aores{i}", [128, 512], BF16) for i in range(2)]
            ostg = [sb3(f"ostg{i}", [128, 512], F32) for i in range(2)]

            def p4_txq(t):
                for kk in range(NKC):
                    qq = ("scalar", "tr") if kk % 2 == 0 else ("sync", "trs")
                    dma(qq[0], qq[1], (lambda t_, k_: lambda e: e.dma_start(
                        out=xqT[:, T * k_:T * (k_ + 1)],
                        in_=xqs[128 * k_:128 * (k_ + 1), T * t_:T * (t_ + 1)]))(t, kk),
                        reads=[f"xqs:{t}"], writes=[f"xqT:{kk}"], tag=f"txq{t}_{kk}")

            def p4_w13(t):
                pq = t % 2
                P.op("vector", (lambda p_: lambda e: e.memset(Mb[p_][:, :], 0.0))(pq),
                     reads=[], writes=[f"Mb{pq}"], tag=f"mz{t}")

                def ss_mm(j_):
                    P.op("tensor", (lambda jx: lambda e: e.matmul(
                        ps[4][0:1, 0:T], onescol_b[:, 0:1], sq2[jx % 4][:, :],
                        start=(jx == 0), stop=(jx == NJ2 - 1)))(j_),
                        reads=[f"sq2:{j_ % 4}", "c:onescol"], writes=["ps4"], tag=f"ss{t}_{j_}")
                for jj in range(NJ2):
                    b1 = 2 * (jj % 2)
                    b3 = b1 + 1
                    for half, (jc, bank) in enumerate([(jj, b1), (jj + NJ2, b3)]):
                        slot = (2 * jj + half) % NWB
                        dma("sync", "wst", (lambda j_, s_: lambda e: e.dma_start(
                            out=wbuf[s_][:, :], in_=w13s[j_, :, :]))(jc, slot),
                            reads=[f"w13s:{jc}"], writes=[f"wbuf:{slot}"], tag=f"s13_{t}_{jc}")
                        for kk in range(NKC):
                            P.op("tensor", (lambda s_, b_, k_: lambda e: e.matmul(
                                ps[b_][:, :], wbuf[s_][:, 128 * k_:128 * (k_ + 1)],
                                xqT[:, T * k_:T * (k_ + 1)], start=(k_ == 0), stop=(k_ == NKC - 1)))(slot, bank, kk),
                                reads=[f"wbuf:{slot}", f"xqT:{kk}"],
                                writes=[f"ps{bank}"], tag=f"w13_{t}_{jc}_{kk}")
                    asl = jj % 2
                    P.op("scalar", (lambda b_, s_: lambda e: e.activation(
                        a_scr[s_][:, :], ps[b_][:, :], AF.Relu))(b1, asl),
                        reads=[f"ps{b1}"], writes=[f"ascr:{asl}"], tag=f"rl{t}_{jj}")
                    P.op("scalar", (lambda s_: lambda e: e.activation(
                        a_scr[s_][:, :], a_scr[s_][:, :], AF.Square))(asl),
                        reads=[f"ascr:{asl}"], writes=[f"ascr:{asl}"], tag=f"sr{t}_{jj}")
                    P.op("vector", (lambda j_, b_, s_: lambda e: e.tensor_mul(
                        gat[:, T * j_:T * (j_ + 1)], a_scr[s_][:, :], ps[b_][:, :]))(jj, b3, asl),
                        reads=[f"ascr:{asl}", f"ps{b3}"], writes=[f"gat:{jj}"], tag=f"gm{t}_{jj}")
                    P.op("gpsimd", (lambda j_, s_: lambda e: e.tensor_mul(
                        sq2[s_][:, :], gat[:, T * j_:T * (j_ + 1)],
                        gat[:, T * j_:T * (j_ + 1)]))(jj, jj % 4),
                        reads=[f"gat:{jj}"], writes=[f"sq2:{jj % 4}"], tag=f"g2{t}_{jj}")
                    if jj >= 2:
                        ss_mm(jj - 2)
                    if jj == NJ2 - 1:
                        ss_mm(NJ2 - 2)
                        ss_mm(NJ2 - 1)
                    P.op("scalar", (lambda j_, s_: lambda e: e.activation(
                        tmpb[s_][:, :], gat[:, T * j_:T * (j_ + 1)], AF.Abs,
                        scale=nwf[:, j_:j_ + 1]))(jj, asl),
                        reads=[f"gat:{jj}", "c:nwf"], writes=[f"tmpb:{asl}"], tag=f"ab{t}_{jj}")
                    P.op("vector", (lambda s_, p_: lambda e: e.tensor_max(
                        Mb[p_][:, :], Mb[p_][:, :], tmpb[s_][:, :]))(asl, pq),
                        reads=[f"tmpb:{asl}", f"Mb{pq}"], writes=[f"Mb{pq}"], tag=f"mm{t}_{jj}")
                    yield

            def p4_quant2_iq(t):
                pq = t % 2
                qt = {k: qt_[:, 8 * i + 4 * pq:8 * i + 4 * (pq + 1)] for i, k in enumerate(QK)}
                for tc in range(4):
                    dma("scalar", "trv", (lambda tc_, p_: lambda e: e.dma_start(
                        out=mbt[tc_ % 2][:, :], in_=Mb[p_][:, 128 * tc_:128 * (tc_ + 1)],
                        transpose=True))(tc, pq),
                        reads=[f"Mb{pq}"], writes=[f"mbt:{tc % 2}"], tag=f"mx{t}_{tc}")
                    P.op("vector", (lambda tc_, q_: lambda e: e.reduce_max(
                        q_["M"][:, tc_:tc_ + 1], mbt[tc_ % 2][:, :], axis=AX.X))(tc, qt),
                        reads=[f"mbt:{tc % 2}"], writes=[f"qt{pq}:M"], tag=f"mr{t}_{tc}")
                P.op("vector", lambda e: e.tensor_copy(qrow[0:1, 0:T], ps[4][0:1, 0:T]),
                     reads=["ps4"], writes=["qrow"], tag=f"sc{t}")
                for tc in range(4):
                    P.op("tensor", (lambda tc_: lambda e: e.matmul(
                        ps[5][:, tc_:tc_ + 1], qrow[0:1, 128 * tc_:128 * (tc_ + 1)],
                        onesr[0:1, 0:1], start=True, stop=True))(tc),
                        reads=["qrow", "c:onesr"], writes=["ps5"], tag=f"st{t}_{tc}")
                P.op("vector", (lambda q_: lambda e: e.tensor_copy(q_["ss"][:, 0:4], ps[5][:, 0:4]))(qt),
                     reads=["ps5"], writes=[f"qt{pq}:ss"], tag=f"se{t}")
                P.op("vector", (lambda q_: lambda e: e.tensor_single_scalar(
                    q_["var"][:, :], q_["ss"][:, :], 1.0 / DF, ALU.mult))(qt),
                    reads=[f"qt{pq}:ss"], writes=[f"qt{pq}:var"], tag=f"q1{t}")
                P.op("scalar", (lambda q_: lambda e: e.activation(
                    q_["sd"][:, :], q_["var"][:, :], AF.Sqrt, bias=cb5[:, 0:1]))(qt),
                    reads=[f"qt{pq}:var", "c:cb5"], writes=[f"qt{pq}:sd"], tag=f"q2{t}")

                def q2c(e, q_=qt):
                    e.reciprocal(q_["rstd"][:, :], q_["sd"][:, :])
                    e.drain()
                    e.tensor_mul(q_["c"][:, :], q_["rstd"][:, :], q_["M"][:, :])
                    e.drain()
                    e.tensor_single_scalar(q_["c"][:, :], q_["c"][:, :], 1e-5, ALU.max)
                    e.drain()
                    e.reciprocal(q_["rc"][:, :], q_["c"][:, :])
                    e.drain()
                    e.tensor_single_scalar(q_["s2"][:, :], q_["rc"][:, :], 127.0, ALU.mult)
                    e.tensor_single_scalar(q_["is2"][:, :], q_["c"][:, :], 1.0 / 127.0, ALU.mult)
                    e.drain()
                    e.tensor_mul(q_["U"][:, :], q_["rstd"][:, :], q_["s2"][:, :])
                    return e.drain()
                P.op("vector", q2c,
                     reads=[f"qt{pq}:sd", f"qt{pq}:M"],
                     writes=[f"qt{pq}:U", f"qt{pq}:is2"], tag=f"q3{t}")
                for tc in range(4):
                    P.op("tensor", (lambda tc_, q_: lambda e: e.transpose(
                        ps[5][0:1, 128 * tc_:128 * (tc_ + 1)],
                        q_["U"][:, tc_:tc_ + 1], eyef[:, :]))(tc, qt),
                        reads=[f"qt{pq}:U", "c:eyef"], writes=["ps5"], tag=f"ut{t}_{tc}")
                P.op("vector", lambda e: e.tensor_copy(qrow[0:1, 0:T], ps[5][0:1, 0:T]),
                     reads=["ps5"], writes=["qrow"], tag=f"ue{t}")
                P.op("tensor", lambda e: e.matmul(ps[5][:, 0:T], onesr[0:1, :], qrow[0:1, 0:T], start=True, stop=True),
                     reads=["qrow", "c:onesr"], writes=["ps5"], tag=f"ub{t}")
                P.op("vector", lambda e: e.tensor_copy(Ub[:, :], ps[5][:, 0:T]),
                     reads=["ps5"], writes=["Ub"], tag=f"uc{t}")
            def p4_iq(t):
                def i3(jj):
                    s_ = jj % 2
                    P.op("gpsimd", (lambda j_, sl_: lambda e: e.tensor_scalar(
                        iqT[:, T * j_:T * (j_ + 1)], u16b[sl_][:, :], 1.0, -R16,
                        ALU.mult, ALU.add))(jj, s_),
                        reads=[f"u16b:{s_}"], writes=[f"iqT:{jj}"], tag=f"i3{t}_{jj}")
                for jj in range(NJ2):
                    s_ = jj % 2
                    P.op("gpsimd", (lambda j_, sl_: lambda e: e.tensor_mul(
                        iqs[sl_][:, :], gat[:, T * j_:T * (j_ + 1)], Ub[:, :]))(jj, s_),
                        reads=[f"gat:{jj}", "Ub"], writes=[f"iqs:{s_}"], tag=f"i1{t}_{jj}")
                    P.op("scalar", (lambda j_, sl_: lambda e: e.activation(
                        u16b[sl_][:, :], iqs[sl_][:, :], AF.Identity, bias=cbr[:, 0:1],
                        scale=nwf[:, j_:j_ + 1]))(jj, s_),
                        reads=[f"iqs:{s_}", "c:cbr", "c:nwf"], writes=[f"u16b:{s_}"], tag=f"i2{t}_{jj}")
                    if jj >= 1:
                        i3(jj - 1)
                    if jj == NJ2 - 1:
                        i3(jj)
                    yield

            def p4_w2(t, giq):
                if giq is not None:
                    next(giq, None)
                    next(giq, None)
                pq = t % 2
                qt = {k: qt_[:, 8 * i + 4 * pq:8 * i + 4 * (pq + 1)] for i, k in enumerate(QK)}
                for dchunk in range(4):
                    for kg in range(NJ2 // 4):
                        slot = kg % 3
                        wr = [f"w2s:{4 * kg + i}" for i in range(4)]
                        dma("sync", "w2l", (lambda g_, s_, d_: lambda e: e.dma_start(
                            out=wbuf2[s_][:, :].rearrange("p (k c) -> p k c", k=4),
                            in_=w2s[d_, 4 * g_:4 * (g_ + 1), :, :].rearrange("k p c -> p k c")))(kg, slot, dchunk),
                            reads=wr, writes=[f"wbuf2:{slot}"], tag=f"s2_{t}_{dchunk}_{kg}")
                        for ki in range(4):
                            kk = 4 * kg + ki
                            if giq is not None and dchunk == 0:
                                next(giq, None)
                            for tc in range(4):
                                P.op("tensor", (lambda k_, s_, ki_, tc_: lambda e: e.matmul(
                                    ps[tc_][:, :], iqT[:, T * k_ + 128 * tc_:T * k_ + 128 * (tc_ + 1)],
                                    wbuf2[s_][:, 512 * ki_:512 * (ki_ + 1)],
                                    start=(k_ == 0), stop=(k_ == NJ2 - 1)))(kk, slot, ki, tc),
                                    reads=[f"wbuf2:{slot}", f"iqT:{kk}"],
                                    writes=[f"ps{tc}"], tag=f"w2_{t}_{dchunk}_{kk}_{tc}")
                    for tc in range(4):
                        sl = tc % 2
                        dma("scalar", "rl", (lambda t_, tc_, d_, s_: lambda e: e.dma_start(
                            out=xres[s_][:, :],
                            in_=x_h[T * t_ + 128 * tc_:T * t_ + 128 * (tc_ + 1),
                                    512 * d_:512 * (d_ + 1)]))(t, tc, dchunk, sl),
                            reads=[], writes=[f"xres:{sl}"], tag=f"lx{t}_{dchunk}_{tc}")
                        dma("scalar", "rl", (lambda t_, tc_, d_, s_: lambda e: e.dma_start(
                            out=aores[s_][:, :],
                            in_=aos[T * t_ + 128 * tc_:T * t_ + 128 * (tc_ + 1),
                                    512 * d_:512 * (d_ + 1)]))(t, tc, dchunk, sl),
                            reads=[f"aos:{t}"], writes=[f"aores:{sl}"], tag=f"la{t}_{dchunk}_{tc}")
                        P.op("vector", (lambda tc_, s_, q_: lambda e: e.scalar_tensor_tensor(
                            ostg[s_][:, :], ps[tc_][:, :], q_["is2"][:, tc_:tc_ + 1],
                            xres[s_][:, :], ALU.mult, ALU.add))(tc, sl, qt),
                            reads=[f"ps{tc}", f"xres:{sl}", f"qt{pq}:is2"],
                            writes=[f"ostg:{sl}"], tag=f"o1{t}_{dchunk}_{tc}")
                        P.op("gpsimd", (lambda s_: lambda e: e.tensor_add(
                            ostg[s_][:, :], ostg[s_][:, :], aores[s_][:, :]))(sl),
                            reads=[f"ostg:{sl}", f"aores:{sl}"], writes=[f"ostg:{sl}"], tag=f"o2{t}_{dchunk}_{tc}")
                        dma("gpsimd", "ol", (lambda t_, tc_, d_, s_: lambda e: e.dma_start(
                            out=out_h[T * t_ + 128 * tc_:T * t_ + 128 * (tc_ + 1),
                                      512 * d_:512 * (d_ + 1)],
                            in_=ostg[s_][:, :]))(t, tc, dchunk, sl),
                            reads=[f"ostg:{sl}"], writes=[f"odone:{sl}"], tag=f"wo{t}_{dchunk}_{tc}")

            dma("sync", "c", lambda e: e.dma_start(out=eyef[:, :], in_=eyef_h),
                reads=[], writes=["c:eyef"], tag="ld_eyef")
            for t in range(nt):
                p4_txq(t)
                g13 = p4_w13(t)
                for _ in g13:
                    pass
                p4_quant2_iq(t)
                giq = p4_iq(t)
                p4_w2(t, giq)
                for _ in giq:
                    pass


            P.resolve()
            with nc.Block() as blk:
                @blk.sync
                def _(eng):
                    P.emit("sync", eng, prog_sems, dma_sems)
                    r = Prog.RING["ol"]
                    tot = P.dma_tot["ol"]
                    for i in range(r):
                        n_i = tot // r + (1 if i < tot % r else 0)
                        if n_i:
                            eng.wait_ge(dma_sems[("ol", i)], 16 * n_i)

                @blk.vector
                def _(eng):
                    P.emit("vector", eng, prog_sems, dma_sems)

                @blk.scalar
                def _(eng):
                    P.emit("scalar", eng, prog_sems, dma_sems)

                @blk.tensor
                def _(eng):
                    P.emit("tensor", eng, prog_sems, dma_sems)

                @blk.gpsimd
                def _(eng):
                    P.emit("gpsimd", eng, prog_sems, dma_sems)

    return nc


def _host_consts():
    c = {}
    c["eye16"] = np.eye(128, dtype=ml_dtypes.bfloat16)
    c["eyef"] = np.eye(128, dtype=np.float32)
    c["prow"] = np.arange(128, dtype=np.float32).reshape(1, 128)
    c["mrow"] = np.arange(BLK, dtype=np.float32).reshape(1, BLK)
    c["onesrow"] = np.ones((1, 128), dtype=np.float32)
    p = np.arange(128).reshape(128, 1)
    m = np.arange(2 * BLK).reshape(1, 2 * BLK)
    c["mask"] = ((m % BLK) >= (128 * (m // BLK) + p)).astype(ml_dtypes.bfloat16)
    return c


def _blocked_lhsT(wt):
    Dd, ncol = wt.shape
    nk, ncch = Dd // 128, ncol // 128
    return np.ascontiguousarray(
        wt.reshape(nk, 128, ncch, 128).transpose(2, 1, 0, 3).reshape(ncch, 128, Dd))


def kernel(L0_attn_hidden_states, L0_attn_slope_rate, L0_attn_w_qkv,
           L0_attn_w_gate, L0_attn_w_out, L0_attn_norm_w,
           L0_ffn_w13, L0_ffn_w2, L0_ffn_norm_w):
    x = np.ascontiguousarray(np.asarray(L0_attn_hidden_states, dtype=np.float32))
    wqkv = np.asarray(L0_attn_w_qkv, dtype=np.float32).reshape(NH, 3, HD, D)
    wq = _blocked_lhsT(np.ascontiguousarray(wqkv[:, 0].reshape(D, D).T))
    wk = _blocked_lhsT(np.ascontiguousarray(wqkv[:, 1].reshape(D, D).T))
    wv = np.ascontiguousarray(wqkv[:, 2].reshape(D, D).T)
    wg = _blocked_lhsT(np.ascontiguousarray(np.asarray(L0_attn_w_gate, np.float32).T))
    wo = np.ascontiguousarray(np.asarray(L0_attn_w_out, np.float32).T)
    w13 = _blocked_lhsT(np.ascontiguousarray(np.asarray(L0_ffn_w13, np.float32).T))
    w2t = np.ascontiguousarray(np.asarray(L0_ffn_w2, np.float32).T)
    slope = np.ascontiguousarray(np.asarray(L0_attn_slope_rate, np.float32).reshape(1, NH))
    nwa = np.ascontiguousarray(np.asarray(L0_attn_norm_w, np.float32).reshape(NH, 128).T)
    nwf = np.ascontiguousarray(np.asarray(L0_ffn_norm_w, np.float32).reshape(NJ2, 128).T)
    consts = _host_consts()

    nc = build_nc(seq=S)
    base = dict(wq=wq, wk=wk, wv=wv, wg=wg, wo=wo, w13=w13, w2t=w2t,
                slope=slope, nwa=nwa, nwf=nwf, **consts)
    in_maps = [dict(base, x=np.ascontiguousarray(x[b])) for b in range(NCORES)]
    res = run_bass_kernel_spmd(nc, in_maps, core_ids=list(range(NCORES)))
    out = np.stack([np.asarray(res.results[b]["out"]) for b in range(NCORES)], axis=0)
    return out.astype(np.float32)
